# version 01
# baseline (speedup 1.0000x reference)
import os
import numpy as np
import concourse.bass as bass
import concourse.mybir as mybir
from concourse.alu_op_type import AluOpType
from concourse.bass_types import AP
from concourse.bass_utils import run_bass_kernel_spmd

F32 = mybir.dt.float32
BF16 = mybir.dt.bfloat16
AF = mybir.ActivationFunctionType
AX = mybir.AxisListType.X
OP = AluOpType
MAGIC = float(1.5 * 2 ** 23)
NEG = -1.0e30
FCONST_VALS = [1e-5, 1.0, -1.0, 0.125, MAGIC, -MAGIC, 1e-4, 0.0]

T = 2048
D = 1024
PT = D // 128
N_LAYERS = int(os.environ.get("MK_NLAYERS", "4"))
MK_PARTS = os.environ.get("MK_PARTS", "both")

BASE = 16512
X_B = BASE + 21504
XN_B = X_B + 65536
ARENA_B = XN_B + 32768
ARENA_E = 229344


class _SkipRest(Exception):
    pass


class Buf:
    __slots__ = ("w", "r")
    def __init__(self, deps=()):
        self.w = None
        self.r = list(deps)


MK_SYNC = os.environ.get("MK_SYNC", "0") == "1"


class Prog:
    ENGS = ("pe", "dve", "act", "gp", "sp")

    def __init__(self, nc):
        self.nc = nc
        self.h = {"pe": nc.tensor, "dve": nc.vector, "act": nc.scalar,
                  "gp": nc.gpsimd, "sp": nc.sync}
        self.sem = {}
        self.cnt = {e: 0 for e in self.ENGS}
        self.lastw = {e: {} for e in self.ENGS}
        self.bub = None
        self.dve_seq = 0
        self.dve_hist = {}

    def init_sems(self, stack):
        for e in self.ENGS:
            self.sem[e] = stack.enter_context(self.nc.semaphore("sem_" + e))

    def _wait(self, eng, tok):
        if tok is None:
            return
        e2, v, isasync = tok
        if e2 == eng and not isasync:
            return
        if isasync:
            v = self.cnt[e2]
        if self.lastw[eng].get(e2, 0) >= v:
            return
        self.h[eng].wait_ge(self.sem[e2], v)
        self.lastw[eng][e2] = v

    def emit(self, eng, fn, ins=(), outs=(), inc=1, isasync=None, hz=True, hzin=False):
        if MK_SYNC:
            for e2 in self.ENGS:
                if e2 != eng and self.cnt[e2] > self.lastw[eng].get(e2, 0):
                    self.h[eng].wait_ge(self.sem[e2], self.cnt[e2])
                    self.lastw[eng][e2] = self.cnt[e2]
        for b in ins:
            if b is not None:
                self._wait(eng, b.w)
        for b in outs:
            if b is not None:
                self._wait(eng, b.w)
                for t in b.r:
                    self._wait(eng, t)
        if eng == "dve" and self.bub is not None:
            need = 0
            for b in list(ins) + list(outs):
                if b is not None and b.w is not None and b.w[0] == "dve":
                    info = self.dve_hist.get(b.w[1])
                    if info is not None and info[1]:
                        d = self.dve_seq - info[0]
                        if d < 5:
                            need = max(need, 5 - d)
            for _ in range(need):
                self.h["dve"].memset(self.bub, 0.0)
                self.dve_seq += 1
        instr = fn(self.h[eng])
        instr.then_inc(self.sem[eng], inc)
        self.cnt[eng] += inc
        if isasync is None:
            isasync = inc >= 16
        tok = (eng, self.cnt[eng], isasync)
        if eng == "dve":
            self.dve_seq += 1
            self.dve_hist[self.cnt[eng]] = (self.dve_seq, hz)
            if len(self.dve_hist) > 64:
                for k in list(self.dve_hist)[:32]:
                    del self.dve_hist[k]
        for b in ins:
            if b is not None:
                b.r.append(tok)
        for b in outs:
            if b is not None:
                b.w = tok
                b.r = []
        return tok


class Arena:
    def __init__(self, nc, base, end):
        self.nc = nc
        self.base = base
        self.end = end
        self.off = base
        self.uid = 0
        self.tenants = []

    def alloc(self, shape, dtype, tag=""):
        esz = 4 if dtype == F32 else 2
        nbytes = (shape[1] * esz + 31) & ~31
        start, stop = self.off, self.off + nbytes
        assert stop <= self.end, \
            f"arena overflow {tag}: need {stop - self.base}, have {self.end - self.base}"
        self.uid += 1
        t = self.nc.alloc_sbuf_tensor_at(f"ar_{tag}_{self.uid}", list(shape), dtype,
                                         offset=start)
        deps = []
        for (s0, s1, b) in self.tenants:
            if s0 < stop and start < s1:
                if b.w is not None:
                    deps.append(b.w)
                deps.extend(b.r)
        buf = Buf(deps)
        self.tenants.append((start, stop, buf))
        self.off = stop
        return t, buf

    def mark(self):
        return self.off

    def release(self, m):
        self.off = m
        if len(self.tenants) > 512:
            self.tenants = self.tenants[-256:]


def host_consts():
    c = {}
    c["identf"] = np.eye(128, dtype=np.float32)
    c["onesf"] = np.ones((128, 128), np.float32)
    invf = (1.0 / 10000.0 ** (np.arange(0, 64, 2, dtype=np.float64) / 64.0))
    ang = np.arange(T, dtype=np.float64)[None, :] * np.tile(invf, 4)[:, None]
    import ml_dtypes
    c["ropec"] = np.cos(ang).astype(ml_dtypes.bfloat16)
    c["ropes"] = np.sin(ang).astype(ml_dtypes.bfloat16)
    kk = np.arange(128)[:, None]
    uu = np.arange(896)[None, :]
    m = np.where(kk <= uu - 384, 0.0, NEG).astype(np.float32)
    import ml_dtypes
    c["l1mask"] = m.astype(ml_dtypes.bfloat16)
    kk = np.arange(128)[:, None]
    qq = np.arange(128)[None, :]
    sm = np.zeros((128, 256), np.float32)
    sm[:, 0:128] = np.where(kk > qq, 0.0, NEG)
    sm[:, 128:256] = np.where(kk <= qq, 0.0, NEG)
    c["swamask"] = sm.astype(ml_dtypes.bfloat16)
    c["trilT"] = np.where(kk <= qq, 1.0, 0.0).astype(np.float32)
    blk16 = np.zeros((128, 8), np.float32)
    for p in range(128):
        blk16[p, (p % 64) // 8] = 1.0
    c["blk16"] = blk16
    c["blk8"] = np.ascontiguousarray(blk16.T)
    fc = np.array(FCONST_VALS, np.float32)
    c["fconsts"] = np.tile(fc[None, :], (128, 1))
    return c


def build_program(nlayers):
    nc = bass.Bass("TRN2", target_bir_lowering=False)
    di = {}

    def dram_in(name, shape, dt=F32):
        di[name] = nc.dram_tensor(name, list(shape), dt, kind="ExternalInput")

    dram_in("x_fm", (D, T))
    dram_in("ipw1T", (D, 2048)); dram_in("ipw2T", (D, 2048))
    dram_in("conv_w", (2048, 4)); dram_in("conv_b", (2048,))
    dram_in("dt_wT", (2048, 2048)); dram_in("dt_b", (2048,))
    dram_in("A_log", (2048, 16))
    dram_in("B_wT", (2048, 16)); dram_in("C_wT", (2048, 16))
    dram_in("out_wT", (2048, D))
    dram_in("WrT", (D, D)); dram_in("WkT", (D, D)); dram_in("WvT", (D, D))
    dram_in("mu_r", (D,)); dram_in("mu_k", (D,))
    if nlayers >= 2:
        dram_in("l1_wqT", (D, D)); dram_in("l1_wkT", (D, D)); dram_in("l1_wvT", (D, D))
        dram_in("l1_w2T", (D, D))
        dram_in("l1f_w1T", (D, 4096)); dram_in("l1f_b1", (4096,))
        dram_in("l1f_w2T", (4096, D)); dram_in("l1f_b2", (D,))
    if nlayers >= 3:
        dram_in("l2_wqT", (D, D)); dram_in("l2_wkT", (D, D)); dram_in("l2_wvT", (D, D))
        dram_in("l2_bq", (D,)); dram_in("l2_bk", (D,)); dram_in("l2_bv", (D,))
        dram_in("l2_woT", (D, D)); dram_in("l2_bo", (D,))
        dram_in("l2f_ws1", (D, D)); dram_in("l2f_bs1", (D,))
        dram_in("l2f_ws2", (D, D)); dram_in("l2f_bs2", (D,))
        dram_in("l2f_We1", (31, D, D)); dram_in("l2f_Be1", (31, D))
        dram_in("l2f_We2", (31, D, D)); dram_in("l2f_Be2", (31, D))
        dram_in("l2f_Wr", (D, 31)); dram_in("l2f_br", (31,))
    if nlayers >= 4:
        dram_in("l3_g1", (D,)); dram_in("l3_b1", (D,))
        dram_in("l3_Wqkv_a", (D, 3 * D)); dram_in("l3_bqkv_a", (3 * D,))
        dram_in("l3_Wp_a", (D, D)); dram_in("l3_bp_a", (D,))
        dram_in("l3_Wqkv_s", (D, 3 * D)); dram_in("l3_bqkv_s", (3 * D,))
        dram_in("l3_Wo_s", (D, D)); dram_in("l3_bo_s", (D,))
        dram_in("l3_Wo", (D, D)); dram_in("l3_bo", (D,))
        dram_in("l3_gm", (D,)); dram_in("l3_bm", (D,))
        dram_in("l3_W1", (D, 4 * D)); dram_in("l3_b1m", (4 * D,))
        dram_in("l3_W2", (4 * D, D)); dram_in("l3_b2m", (D,))
        dram_in("l3f_Wg", (D, 8))
        dram_in("l3f_W1", (8, D, 2048)); dram_in("l3f_W2", (8, 2048, D))
    dram_in("identf", (128, 128)); dram_in("onesf", (128, 128))
    dram_in("ropec", (128, T), BF16); dram_in("ropes", (128, T), BF16)
    dram_in("l1mask", (128, 896), BF16); dram_in("swamask", (128, 256), BF16)
    dram_in("trilT", (128, 128)); dram_in("blk16", (128, 8)); dram_in("blk8", (8, 128))
    dram_in("fconsts", (128, 8))

    out_d = nc.dram_tensor("out_fm", [D, T], F32, kind="ExternalOutput")
    dbg_d = nc.dram_tensor("dbg", [1, 64], F32, kind="ExternalOutput")

    xin_sp = nc.dram_tensor("xin_sp", [2048, T], BF16)
    bc_sp = nc.dram_tensor("bc_sp", [16, 2 * T], BF16)
    dt_sp = nc.dram_tensor("dt_sp", [2048, T], BF16)
    y_sp = nc.dram_tensor("y_sp", [2048, T], BF16)
    h_sp = nc.dram_tensor("h_sp", [4096, T], F32)
    xq_sp = nc.dram_tensor("xq_sp", [D, T], BF16)
    gates_sp = nc.dram_tensor("gates_sp", [31, T], BF16)
    swp_sp = nc.dram_tensor("swp_sp", [D, T], BF16)
    mix_sp = nc.dram_tensor("mix_sp", [D, T], BF16)
    h2_sp = nc.dram_tensor("h2_sp", [4096, T], BF16)
    w2q_sp = nc.dram_tensor("w2q_sp", [4096, D], BF16)
    ar_in = nc.dram_tensor("ar_in", [1, 4], F32)
    ar_out = nc.dram_tensor("ar_out", [1, 4], F32)
    ar2_in = nc.dram_tensor("ar2_in", [1, 4], F32)
    ar2_out = nc.dram_tensor("ar2_out", [1, 4], F32)
    flip_sp = nc.dram_tensor("flip_sp", [4, 128], F32)
    ag_in = nc.dram_tensor("ag_in", [8, T], F32)
    ag_out = nc.dram_tensor("ag_out", [64, T], F32)

    import contextlib
    stack = contextlib.ExitStack()
    with stack:
        P = Prog(nc)
        P.init_sems(stack)
        E = P.emit

        off = BASE
        def fixed(name, shape, dtype):
            nonlocal off
            t = nc.alloc_sbuf_tensor_at(name, list(shape), dtype, offset=off)
            off += (shape[1] * (4 if dtype == F32 else 2) + 31) & ~31
            return t

        identf = fixed("identf_s", (128, 128), F32)
        onesf = fixed("onesf_s", (128, 128), F32)
        ones16 = fixed("ones16_s", (128, 128), BF16)
        ropec = fixed("ropec_s", (128, T), BF16)
        ropes = fixed("ropes_s", (128, T), BF16)
        l1mask = fixed("l1mask_s", (128, 896), BF16)
        swamask = fixed("swamask_s", (128, 256), BF16)
        trilT16 = fixed("trilT16_s", (128, 128), BF16)
        bv3 = fixed("bv3_s", (1, 1024), BF16)
        blk16 = fixed("blk16_s", (128, 8), F32)
        blk8 = fixed("blk8_s", (8, 128), F32)
        fconsts = fixed("fconsts_s", (128, 8), F32)
        misc = fixed("misc_s", (128, 1100), F32)
        misc2 = fixed("misc2_s", (128, 512), F32)
        assert off <= X_B, f"const region overflow: {off}"
        P.bub = misc[0:1, 1096:1100]

        X = nc.alloc_sbuf_tensor_at("X_s", [128, PT * T], F32, offset=X_B)
        XN = nc.alloc_sbuf_tensor_at("XN_s", [128, PT * T], BF16, offset=XN_B)
        Xb = [Buf() for _ in range(PT)]
        XNb = [Buf() for _ in range(PT)]

        A = Arena(nc, ARENA_B, ARENA_E)

        pbig = stack.enter_context(nc.psum_tensor("pbig", [128, 4096], F32))
        pbB = [Buf() for _ in range(8)]
        def pbank(i, n0=0, nlen=512):
            return pbig[:, i * 512 + n0:i * 512 + n0 + nlen]

        CB = Buf()
        flipD = Buf()
        xinD, dtD, yD, hD, bcD = Buf(), Buf(), Buf(), Buf(), Buf()
        xqD, w2qD, gD = Buf(), Buf(), Buf()
        swpD, mixD, h2D, agD, agD2 = Buf(), Buf(), Buf(), Buf(), Buf()
        miscB = Buf()
        misc2B = Buf()

        blk = stack.enter_context(nc.Block())

        for i, v in enumerate(FCONST_VALS):
            nc.const_aps.register(F32, float(v), fconsts[:, i:i + 1])
        for t_, nm in ((identf, "identf"), (onesf, "onesf"), (ropec, "ropec"),
                       (ropes, "ropes"), (l1mask, "l1mask"), (swamask, "swamask"),
                       (blk16, "blk16"), (blk8, "blk8"), (fconsts, "fconsts")):
            E("sp", lambda e, t_=t_, nm=nm: e.dma_start(out=t_[:], in_=di[nm].ap()),
              outs=[CB], inc=16)
        E("gp", lambda e: e.dma_start(out=ones16[:], in_=di["onesf"].ap()), outs=[CB], inc=16)
        E("gp", lambda e: e.dma_start(out=trilT16[:], in_=di["trilT"].ap()), outs=[CB], inc=16)
        bv3b = Buf()
        for k in range(PT):
            E("sp", lambda e, k=k: e.dma_start(
                out=X[:, k * T:(k + 1) * T], in_=di["x_fm"].ap()[k * 128:(k + 1) * 128, :]),
              outs=[Xb[k]], inc=16)

        misc_off = [0]
        def mslot(n):
            o = misc_off[0]
            misc_off[0] += n
            assert misc_off[0] <= 1096
            return o

        def load_vec(dname, n, slot=None):
            nt = (n + 127) // 128
            o = mslot(nt) if slot is None else slot
            E("sp", lambda e: e.dma_start(out=misc[:, o:o + nt],
                                          in_=AP(di[dname].ap().tensor, 0, [[1, 128], [128, nt]]),
                                          allow_slow_non_contiguous=True),
              outs=[miscB], inc=16)
            return o

        def load_rows(dname, nrows, ncols, o=None):
            nt = nrows // 128
            o = mslot(nt * ncols) if o is None else o
            for i in range(nt):
                E("sp", lambda e, i=i: e.dma_start(
                    out=misc[:, o + i * ncols:o + (i + 1) * ncols],
                    in_=di[dname].ap()[i * 128:(i + 1) * 128, :]), outs=[miscB], inc=16)
            return o

        def wload(dst, dstb, dname, kt0, nk, out_w, col0=0, ncols=None, eng="gp"):
            if ncols is None:
                ncols = out_w
            for i in range(nk):
                r0 = (kt0 + i) * 128
                E(eng, lambda e, i=i, r0=r0: e.dma_start(
                    out=dst[:, i * ncols:(i + 1) * ncols],
                    in_=di[dname].ap()[r0:r0 + 128, col0:col0 + ncols]),
                  outs=[dstb], inc=16)

        def mm(bank, lT, lTsl, lb, rT, rsl, rb, start, stop, m0=0, mlen=128, n0=0, nlen=512):
            E("pe", lambda e: e.matmul(pbank(bank, n0, nlen)[m0:m0 + mlen, :],
                                       lT[lTsl[0]:lTsl[1], lTsl[2]:lTsl[3]],
                                       rT[rsl[0]:rsl[1], rsl[2]:rsl[3]],
                                       start=start, stop=stop),
              ins=[lb, rb], outs=[pbB[bank]])

        def layer_norm(want32=False):
            xn32, xn32b = (A.alloc((128, PT * T), F32, "xn32") if want32 else (None, None))
            sq, sqb = A.alloc((128, 512), F32, "lnsq")
            for ch in range(4):
                c0 = ch * 512
                for k in range(PT):
                    mm(0, onesf, (0, 128, 0, 1), CB, X, (0, 128, k * T + c0, k * T + c0 + 512),
                       Xb[k], start=(k == 0), stop=(k == PT - 1), mlen=1)
                for k in range(PT):
                    E("act", lambda e, k=k: e.activation(
                        sq[:], X[:, k * T + c0:k * T + c0 + 512], AF.Square),
                      ins=[Xb[k]], outs=[sqb])
                    mm(1, onesf, (0, 128, 0, 1), CB, sq, (0, 128, 0, 512), sqb,
                       start=(k == 0), stop=(k == PT - 1), mlen=1)
                E("dve", lambda e: e.tensor_scalar(misc2[0:1, 0:512], pbank(0)[0:1, :],
                                                   1.0 / D, None, OP.mult),
                  ins=[pbB[0]], outs=[misc2B])
                mm(2, onesf, (0, 1, 0, 128), CB, misc2, (0, 1, 0, 512), misc2B, True, True)
                E("dve", lambda e: e.tensor_tensor(misc2[0:1, 0:512], misc2[0:1, 0:512],
                                                   misc2[0:1, 0:512], OP.mult),
                  ins=[misc2B], outs=[misc2B])
                E("dve", lambda e: e.scalar_tensor_tensor(
                    misc2[0:1, 0:512], pbank(1)[0:1, :], 1.0 / D, misc2[0:1, 0:512],
                    OP.mult, OP.subtract), ins=[pbB[1], misc2B], outs=[misc2B])
                E("act", lambda e: e.activation(misc2[0:1, 0:512], misc2[0:1, 0:512],
                                                AF.Sqrt, bias=1e-5),
                  ins=[misc2B], outs=[misc2B])
                E("dve", lambda e: e.reciprocal(misc2[0:1, 0:512], misc2[0:1, 0:512]),
                  ins=[misc2B], outs=[misc2B], hz=True)
                mm(3, onesf, (0, 1, 0, 128), CB, misc2, (0, 1, 0, 512), misc2B, True, True)
                for k in range(PT):
                    E("dve", lambda e, k=k: e.tensor_tensor(
                        sq[:], X[:, k * T + c0:k * T + c0 + 512], pbank(2), OP.subtract),
                      ins=[Xb[k], pbB[2]], outs=[sqb])
                    if want32:
                        E("dve", lambda e, k=k: e.tensor_tensor(
                            xn32[:, k * T + c0:k * T + c0 + 512], sq[:], pbank(3), OP.mult),
                          ins=[sqb, pbB[3]], outs=[xn32b])
                        E("dve", lambda e, k=k: e.tensor_copy(
                            XN[:, k * T + c0:k * T + c0 + 512],
                            xn32[:, k * T + c0:k * T + c0 + 512]),
                          ins=[xn32b], outs=[XNb[k]])
                    else:
                        E("dve", lambda e, k=k: e.tensor_tensor(
                            XN[:, k * T + c0:k * T + c0 + 512], sq[:], pbank(3), OP.mult),
                          ins=[sqb, pbB[3]], outs=[XNb[k]])
            return xn32, xn32b

        def x_add_psum(bank, k, c0, nlen=512):
            E("dve", lambda e: e.tensor_tensor(
                X[:, k * T + c0:k * T + c0 + nlen],
                X[:, k * T + c0:k * T + c0 + nlen], pbank(bank, 0, nlen), OP.add),
              ins=[pbB[bank]], outs=[Xb[k]])

        def l0():
            o_A = load_rows("A_log", 2048, 16)
            o_cw = load_rows("conv_w", 2048, 4)
            o_cb = load_vec("conv_b", 2048)
            o_db = load_vec("dt_b", 2048)
            o_mur = load_vec("mu_r", D)
            o_muk = load_vec("mu_k", D)
            o_omr = mslot(8)
            o_omk = mslot(8)
            E("act", lambda e: e.activation(misc[:, o_A:o_A + 256], misc[:, o_A:o_A + 256],
                                            AF.Exp), ins=[miscB], outs=[miscB])
            E("dve", lambda e: e.tensor_scalar(misc[:, o_A:o_A + 256], misc[:, o_A:o_A + 256],
                                               -1.0, None, OP.mult), ins=[miscB], outs=[miscB])
            E("dve", lambda e: e.tensor_scalar(misc[:, o_omr:o_omr + 8],
                                               misc[:, o_mur:o_mur + 8],
                                               -1.0, 1.0, OP.mult, OP.add),
              ins=[miscB], outs=[miscB])
            E("dve", lambda e: e.tensor_scalar(misc[:, o_omk:o_omk + 8],
                                               misc[:, o_muk:o_muk + 8],
                                               -1.0, 1.0, OP.mult, OP.add),
              ins=[miscB], outs=[miscB])

            mpre = A.mark()
            layer_norm(False)
            A.release(mpre)
            m0 = A.mark()
            do_attn = MK_PARTS in ("both", "attn", "f0xin", "f0dt", "f0y") or N_LAYERS > 1
            do_ffn = MK_PARTS in ("both", "ffn", "f0r", "f0k") or N_LAYERS > 1
            if MK_PARTS == "ln":
                for k in range(PT):
                    E("dve", lambda e, k=k: e.tensor_copy(X[:, k * T:(k + 1) * T],
                                                          XN[:, k * T:(k + 1) * T]),
                      ins=[XNb[k]], outs=[Xb[k]])
            def dump_sp(spt, spb):
                tmp, tmpb = A.alloc((128, T), BF16, "dump")
                for k in range(PT):
                    E("gp", lambda e, k=k, spt=spt: e.dma_start(
                        out=tmp[:], in_=spt.ap()[k * 128:(k + 1) * 128, :]),
                      ins=[spb], outs=[tmpb], inc=16)
                    E("dve", lambda e, k=k: e.tensor_copy(X[:, k * T:(k + 1) * T], tmp[:]),
                      ins=[tmpb], outs=[Xb[k]])

            if do_attn:
                w1, w1b = A.alloc((128, PT * 2048), BF16, "ipw1")
                stg, stgb = A.alloc((128, 520), F32, "cstg")
                cvt, cvtb = A.alloc((128, 512), F32, "cvt")
                cv2, cv2b = A.alloc((128, 512), F32, "cv2")
                xo, xob = A.alloc((128, 512), BF16, "xo")
                wload(w1, w1b, "ipw1T", 0, PT, 2048)
                for dt16 in range(16):
                    for ch in range(4):
                        c0 = ch * 512
                        bank = ch % 2
                        for k in range(PT):
                            mm(bank, w1, (0, 128, k * 2048 + dt16 * 128, k * 2048 + dt16 * 128 + 128),
                               w1b, XN, (0, 128, k * T + c0, k * T + c0 + 512), XNb[k],
                               start=(k == 0), stop=(k == PT - 1))
                        if ch == 0:
                            E("dve", lambda e: e.memset(stg[:, 0:3], 0.0), outs=[stgb])
                        E("act", lambda e, bank=bank: e.copy(stg[:, 3:515], pbank(bank)),
                          ins=[pbB[bank]], outs=[stgb])
                        E("dve", lambda e, dt16=dt16: e.tensor_single_scalar(
                            cvt[:], stg[:, 0:512], misc[:, o_cw + dt16 * 4:o_cw + dt16 * 4 + 1],
                            OP.mult), ins=[stgb, miscB], outs=[cvtb], hz=False)
                        for kk in range(1, 4):
                            E("dve", lambda e, dt16=dt16, kk=kk: e.tensor_single_scalar(
                                cv2[:], stg[:, kk:kk + 512],
                                misc[:, o_cw + dt16 * 4 + kk:o_cw + dt16 * 4 + kk + 1], OP.mult),
                              ins=[stgb, miscB], outs=[cv2b], hz=False)
                            E("dve", lambda e: e.tensor_tensor(cvt[:], cvt[:], cv2[:], OP.add),
                              ins=[cv2b], outs=[cvtb], hz=False)
                        if ch < 3:
                            E("dve", lambda e: e.tensor_copy(stg[:, 0:3], stg[:, 512:515]),
                              ins=[stgb], outs=[stgb])
                        E("act", lambda e, dt16=dt16: e.activation(
                            xo[:], cvt[:], AF.Silu, bias=misc[:, o_cb + dt16:o_cb + dt16 + 1]),
                          ins=[cvtb, miscB], outs=[xob])
                        E("sp", lambda e, dt16=dt16, c0=c0: e.dma_start(
                            out=xin_sp.ap()[dt16 * 128:(dt16 + 1) * 128, c0:c0 + 512], in_=xo[:]),
                          ins=[xob], outs=[xinD], inc=16)
                if MK_PARTS == "f0xin":
                    dump_sp(xin_sp, xinD)
                A.release(m0)

                if MK_PARTS == "f0xin":
                    raise _SkipRest
                BmCm, BmCmb = A.alloc((128, 512), BF16, "BmCm")
                bw, bwb = A.alloc((128, 16 * 32), BF16, "bcw")
                for i in range(16):
                    E("gp", lambda e, i=i: e.dma_start(
                        out=bw[:, i * 32:i * 32 + 16],
                        in_=di["B_wT"].ap()[i * 128:(i + 1) * 128, :]), outs=[bwb], inc=16)
                    E("gp", lambda e, i=i: e.dma_start(
                        out=bw[:, i * 32 + 16:i * 32 + 32],
                        in_=di["C_wT"].ap()[i * 128:(i + 1) * 128, :]), outs=[bwb], inc=16)
                m1 = A.mark()
                dw, dwb = A.alloc((128, 16 * 1024), BF16, "dtw")
                xr = [A.alloc((128, 16 * 512), BF16, f"xr{i}") for i in range(2)]
                sp1, sp1b = A.alloc((128, 512), F32, "sp1")
                sp2, sp2b = A.alloc((128, 512), F32, "sp2")
                dto, dtob = A.alloc((128, 512), BF16, "dto")
                for half in range(2):
                    wload(dw, dwb, "dt_wT", 0, 16, 1024, col0=half * 1024, ncols=1024)
                    for ch in range(4):
                        c0 = ch * 512
                        xt, xtb = xr[ch % 2]
                        for i in range(16):
                            E("gp", lambda e, i=i, c0=c0: e.dma_start(
                                out=xt[:, i * 512:(i + 1) * 512],
                                in_=xin_sp.ap()[i * 128:(i + 1) * 128, c0:c0 + 512]),
                              ins=[xinD], outs=[xtb], inc=16)
                        if half == 0:
                            for i in range(16):
                                mm(6, bw, (0, 128, i * 32, i * 32 + 16), bwb,
                                   xt, (0, 128, i * 512, i * 512 + 512), xtb,
                                   start=(i == 0), stop=(i == 15), mlen=16)
                            E("act", lambda e: e.copy(BmCm[0:16, 0:512], pbank(6)[0:16, :]),
                              ins=[pbB[6]], outs=[BmCmb])
                            E("sp", lambda e, c0=c0: e.dma_start(
                                out=bc_sp.ap()[:, c0:c0 + 512], in_=BmCm[0:16, 0:512]),
                              ins=[BmCmb], outs=[bcD], inc=16)
                            for i in range(16):
                                mm(7, bw, (0, 128, i * 32 + 16, i * 32 + 32), bwb,
                                   xt, (0, 128, i * 512, i * 512 + 512), xtb,
                                   start=(i == 0), stop=(i == 15), mlen=16)
                            E("act", lambda e: e.copy(BmCm[0:16, 0:512], pbank(7)[0:16, :]),
                              ins=[pbB[7]], outs=[BmCmb])
                            E("sp", lambda e, c0=c0: e.dma_start(
                                out=bc_sp.ap()[:, T + c0:T + c0 + 512], in_=BmCm[0:16, 0:512]),
                              ins=[BmCmb], outs=[bcD], inc=16)
                        for mt in range(8):
                            dt16 = half * 8 + mt
                            bank = mt % 2 + 2
                            for i in range(16):
                                mm(bank, dw, (0, 128, i * 1024 + mt * 128, i * 1024 + mt * 128 + 128),
                                   dwb, xt, (0, 128, i * 512, i * 512 + 512), xtb,
                                   start=(i == 0), stop=(i == 15))
                            dbs = misc[:, o_db + dt16:o_db + dt16 + 1]
                            E("act", lambda e, bank=bank, dbs=dbs: e.activation(
                                sp1[:], pbank(bank), AF.Abs, bias=dbs),
                              ins=[pbB[bank], miscB], outs=[sp1b])
                            E("act", lambda e: e.activation(sp1[:], sp1[:], AF.Exp, scale=-1.0),
                              ins=[sp1b], outs=[sp1b])
                            E("act", lambda e: e.activation(sp1[:], sp1[:], AF.Ln, bias=1.0),
                              ins=[sp1b], outs=[sp1b])
                            E("act", lambda e, bank=bank, dbs=dbs: e.activation(
                                sp2[:], pbank(bank), AF.Relu, bias=dbs),
                              ins=[pbB[bank], miscB], outs=[sp2b])
                            E("dve", lambda e: e.tensor_tensor(dto[:], sp1[:], sp2[:], OP.add),
                              ins=[sp1b, sp2b], outs=[dtob], hz=False)
                            E("sp", lambda e, dt16=dt16, c0=c0: e.dma_start(
                                out=dt_sp.ap()[dt16 * 128:(dt16 + 1) * 128, c0:c0 + 512],
                                in_=dto[:]), ins=[dtob], outs=[dtD], inc=16)
                A.release(m1)
                if MK_PARTS == "f0dt":
                    dump_sp(dt_sp, dtD)

                if MK_PARTS == "f0dt":
                    raise _SkipRest
                xi = [A.alloc((128, T), BF16, f"sxi{i}") for i in range(2)]
                dti = [A.alloc((128, T), BF16, f"sdt{i}") for i in range(2)]
                br = [A.alloc((1, 2 * T), BF16, f"sbr{i}") for i in range(2)]
                ut, utb = A.alloc((128, T), F32, "su")
                bt2, bt2b = A.alloc((128, T), F32, "sb2")
                at, atb = A.alloc((128, T), F32, "sa")
                bt, btb = A.alloc((128, T), F32, "sb")
                st, stb = A.alloc((128, T), F32, "ss")
                ya, yab = A.alloc((128, T), F32, "sy")
                y16, y16b = A.alloc((128, T), BF16, "sy16")
                for pt16 in range(16):
                    xt, xtb = xi[pt16 % 2]
                    dt_, dtb_ = dti[pt16 % 2]
                    E("gp", lambda e, pt16=pt16: e.dma_start(
                        out=xt[:], in_=xin_sp.ap()[pt16 * 128:(pt16 + 1) * 128, :]),
                      ins=[xinD], outs=[xtb], inc=16)
                    E("gp", lambda e, pt16=pt16: e.dma_start(
                        out=dt_[:], in_=dt_sp.ap()[pt16 * 128:(pt16 + 1) * 128, :]),
                      ins=[dtD], outs=[dtb_], inc=16)
                    E("dve", lambda e: e.tensor_tensor(ut[:], dt_[:], xt[:], OP.mult),
                      ins=[dtb_, xtb], outs=[utb], hz=False)
                    for s in range(16):
                        brt, brtb = br[s % 2]
                        E("gp", lambda e, s=s, brt=brt: e.dma_start(
                            out=brt[:], in_=bc_sp.ap()[s:s + 1, :]),
                          ins=[bcD], outs=[brtb], inc=16)
                        for cc in range(4):
                            mm(cc, ones16, (0, 1, 0, 128), CB, brt,
                               (0, 1, cc * 512, cc * 512 + 512), brtb, True, True)
                            mm(4 + cc, ones16, (0, 1, 0, 128), CB, brt,
                               (0, 1, T + cc * 512, T + cc * 512 + 512), brtb, True, True)
                        Asl = misc[:, o_A + pt16 * 16 + s:o_A + pt16 * 16 + s + 1]
                        E("act", lambda e, Asl=Asl: e.activation(at[:], dt_[:], AF.Exp, scale=Asl),
                          ins=[dtb_, miscB], outs=[atb])
                        E("dve", lambda e: e.tensor_tensor(bt[:], ut[:], pbig[:, 0:2048], OP.mult),
                          ins=[utb, pbB[0], pbB[1], pbB[2], pbB[3]], outs=[btb], hz=False)
                        E("dve", lambda e: e.tensor_tensor_scan(st[:], at[:], bt[:], 0.0,
                                                                OP.mult, OP.add),
                          ins=[atb, btb], outs=[stb], hz=True, hzin=True)
                        E("dve", lambda e: e.tensor_tensor(bt2[:], st[:], pbig[:, 2048:4096],
                                                           OP.mult),
                          ins=[stb, pbB[4], pbB[5], pbB[6], pbB[7]], outs=[bt2b], hz=False)
                        if s == 0:
                            if os.environ.get("MK_SCANDBG", "") == "s0":
                                E("dve", lambda e: e.tensor_copy(ya[:], st[:]), ins=[stb], outs=[yab])
                            else:
                                E("dve", lambda e: e.tensor_copy(ya[:], bt2[:]), ins=[bt2b], outs=[yab], hz=False)
                        elif os.environ.get("MK_SCANDBG", "") != "s0":
                            E("dve", lambda e: e.tensor_tensor(ya[:], ya[:], bt2[:], OP.add),
                              ins=[bt2b], outs=[yab], hz=False)
                    sdbg = os.environ.get("MK_SCANDBG", "")
                    if sdbg not in ("a", "b", "u", "s", "s0", "B", "C"):
                        sdbg = ""
                    if sdbg == "B":
                        E("dve", lambda e: e.tensor_copy(y16[:], pbig[:, 0:2048]),
                          ins=[pbB[0], pbB[1], pbB[2], pbB[3]], outs=[y16b])
                    elif sdbg == "C":
                        E("dve", lambda e: e.tensor_copy(y16[:], pbig[:, 2048:4096]),
                          ins=[pbB[4], pbB[5], pbB[6], pbB[7]], outs=[y16b])
                    else:
                        dbgsrc = {"a": (at, atb), "b": (bt, btb), "u": (ut, utb),
                                  "s": (st, stb), "s0": (ya, yab), "": (ya, yab)}[sdbg]
                        E("dve", lambda e: e.tensor_copy(y16[:], dbgsrc[0][:]),
                          ins=[dbgsrc[1]], outs=[y16b], hz=False)
                    E("sp", lambda e, pt16=pt16: e.dma_start(
                        out=y_sp.ap()[pt16 * 128:(pt16 + 1) * 128, :], in_=y16[:]),
                      ins=[y16b], outs=[yD], inc=16)
                if MK_PARTS == "f0y":
                    dump_sp(y_sp, yD)
                A.release(m0)

                if MK_PARTS == "f0y":
                    raise _SkipRest
                w2, w2b = A.alloc((128, PT * 2048), BF16, "ipw2")
                ow, owb = A.alloc((128, 16 * 1024), BF16, "outw")
                yt, ytb = A.alloc((128, 16 * 512), BF16, "yt")
                sr, srb = A.alloc((128, 512), BF16, "sres")
                wload(w2, w2b, "ipw2T", 0, PT, 2048)
                wload(ow, owb, "out_wT", 0, 16, 1024)
                for ch in range(4):
                    c0 = ch * 512
                    for i in range(16):
                        E("gp", lambda e, i=i, c0=c0: e.dma_start(
                            out=yt[:, i * 512:(i + 1) * 512],
                            in_=y_sp.ap()[i * 128:(i + 1) * 128, c0:c0 + 512]),
                          ins=[yD], outs=[ytb], inc=16)
                    for dt16 in range(16):
                        bank = dt16 % 2
                        for k in range(PT):
                            mm(bank, w2, (0, 128, k * 2048 + dt16 * 128, k * 2048 + dt16 * 128 + 128),
                               w2b, XN, (0, 128, k * T + c0, k * T + c0 + 512), XNb[k],
                               start=(k == 0), stop=(k == PT - 1))
                        E("act", lambda e, bank=bank: e.activation(sr[:], pbank(bank), AF.Silu),
                          ins=[pbB[bank]], outs=[srb])
                        E("dve", lambda e, dt16=dt16: e.tensor_tensor(
                            yt[:, dt16 * 512:(dt16 + 1) * 512],
                            yt[:, dt16 * 512:(dt16 + 1) * 512], sr[:], OP.mult),
                          ins=[srb], outs=[ytb])
                    for mt in range(8):
                        bank = mt % 2 + 2
                        for i in range(16):
                            mm(bank, ow, (0, 128, i * 1024 + mt * 128, i * 1024 + mt * 128 + 128),
                               owb, yt, (0, 128, i * 512, i * 512 + 512), ytb,
                               start=(i == 0), stop=(i == 15))
                        x_add_psum(bank, mt, c0)
                A.release(m0)


            if do_ffn:
                def shift_mix(dst, dstb, o_mu, o_om, k, c0):
                    mu = misc[:, o_mu + k:o_mu + k + 1]
                    om = misc[:, o_om + k:o_om + k + 1]
                    E("dve", lambda e: e.tensor_single_scalar(
                        cvt[:], XN[:, k * T + c0:k * T + c0 + 512], mu, OP.mult),
                      ins=[XNb[k], miscB], outs=[cvtb])
                    if c0 == 0:
                        E("dve", lambda e: e.tensor_single_scalar(
                            cv2[:, 1:512], XN[:, k * T:k * T + 511], om, OP.mult),
                          ins=[XNb[k], miscB], outs=[cv2b])
                        E("dve", lambda e: e.memset(cv2[:, 0:1], 0.0), outs=[cv2b])
                    else:
                        E("dve", lambda e: e.tensor_single_scalar(
                            cv2[:], XN[:, k * T + c0 - 1:k * T + c0 + 511], om, OP.mult),
                          ins=[XNb[k], miscB], outs=[cv2b])
                    E("dve", lambda e: e.tensor_tensor(dst[:], cvt[:], cv2[:], OP.add),
                      ins=[cvtb, cv2b], outs=[dstb])

                cvt, cvtb = A.alloc((128, 512), F32, "f0a")
                cv2, cv2b = A.alloc((128, 512), F32, "f0b")
                xmix, xmixb = A.alloc((128, PT * 512), BF16, "f0mix")
                r16, r16b = A.alloc((128, PT * T), BF16, "f0r")
                k16, k16b = A.alloc((128, PT * T), BF16, "f0k")
                wv, wvb = A.alloc((128, PT * 512), BF16, "f0w")
                for half in range(2):
                    wload(wv, wvb, "WkT", 0, PT, 512, col0=half * 512, ncols=512)
                    for ch in range(4):
                        c0 = ch * 512
                        for k in range(PT):
                            shift_mix(xmix[:, k * 512:(k + 1) * 512], xmixb, o_muk, o_omk, k, c0)
                        for mt4 in range(4):
                            mt = half * 4 + mt4
                            bank = mt4 % 2
                            for k in range(PT):
                                mm(bank, wv, (0, 128, k * 512 + mt4 * 128, k * 512 + mt4 * 128 + 128),
                                   wvb, xmix, (0, 128, k * 512, k * 512 + 512), xmixb,
                                   start=(k == 0), stop=(k == PT - 1))
                            E("act", lambda e, bank=bank: e.activation(cvt[:], pbank(bank), AF.Relu),
                              ins=[pbB[bank]], outs=[cvtb])
                            E("act", lambda e, mt=mt, c0=c0: e.activation(
                                k16[:, mt * T + c0:mt * T + c0 + 512], cvt[:], AF.Square),
                              ins=[cvtb], outs=[k16b])
                for half in range(2):
                    wload(wv, wvb, "WrT", 0, PT, 512, col0=half * 512, ncols=512)
                    for ch in range(4):
                        c0 = ch * 512
                        for k in range(PT):
                            shift_mix(xmix[:, k * 512:(k + 1) * 512], xmixb, o_mur, o_omr, k, c0)
                        for mt4 in range(4):
                            mt = half * 4 + mt4
                            bank = mt4 % 2
                            for k in range(PT):
                                mm(bank, wv, (0, 128, k * 512 + mt4 * 128, k * 512 + mt4 * 128 + 128),
                                   wvb, xmix, (0, 128, k * 512, k * 512 + 512), xmixb,
                                   start=(k == 0), stop=(k == PT - 1))
                            E("act", lambda e, bank=bank, mt=mt, c0=c0: e.activation(
                                r16[:, mt * T + c0:mt * T + c0 + 512], pbank(bank), AF.Sigmoid),
                              ins=[pbB[bank]], outs=[r16b])
                if MK_PARTS in ("f0r", "f0k"):
                    srct = r16 if MK_PARTS == "f0r" else k16
                    for k in range(PT):
                        E("dve", lambda e, k=k: e.tensor_copy(
                            X[:, k * T:(k + 1) * T], srct[:, k * T:(k + 1) * T]),
                          ins=[r16b if MK_PARTS == "f0r" else k16b], outs=[Xb[k]])
                for half in range(2):
                    wload(wv, wvb, "WvT", 0, PT, 512, col0=half * 512, ncols=512)
                    for ch in range(4):
                        c0 = ch * 512
                        for mt4 in range(4):
                            mt = half * 4 + mt4
                            bank = mt4 % 2
                            for k in range(PT):
                                mm(bank, wv, (0, 128, k * 512 + mt4 * 128, k * 512 + mt4 * 128 + 128),
                                   wvb, k16, (0, 128, k * T + c0, k * T + c0 + 512), k16b,
                                   start=(k == 0), stop=(k == PT - 1))
                            E("dve", lambda e, bank=bank, mt=mt, c0=c0: e.scalar_tensor_tensor(
                                cvt[:], pbank(bank), 1.0, r16[:, mt * T + c0:mt * T + c0 + 512],
                                OP.mult, OP.mult), ins=[pbB[bank], r16b], outs=[cvtb])
                            E("dve", lambda e, mt=mt, c0=c0: e.tensor_tensor(
                                X[:, mt * T + c0:mt * T + c0 + 512],
                                X[:, mt * T + c0:mt * T + c0 + 512], cvt[:], OP.add),
                              ins=[cvtb], outs=[Xb[mt]])
                A.release(m0)


        def rope_attn(wq_nm, wk_nm, wv_nm, w2_nm, causal, o_bq=None, o_bk=None,
                      bv16=None, bv16b=None, o_bo=None, o_sc4=0):
            for hg in range(2):
                mhg = A.mark()
                ao, aob = A.alloc((128, 4 * T), BF16, "ao")
                q16, q16b = A.alloc((128, 4 * T), BF16, "q")
                k16, k16b = A.alloc((128, 4 * T), BF16, "k")
                vt, vtb = A.alloc((128, 16 * 520), BF16, "v")
                msc = A.mark()
                rstg, rstgb = A.alloc((128, 512), F32, "rs")
                rot, rotb = A.alloc((128, 512), F32, "rot")
                t1, t1b = A.alloc((128, 512), F32, "t1")
                ws, wsb = A.alloc((128, 8 * 512), BF16, "w")
                pst = [A.alloc((128, 512), BF16, f"p{i}") for i in range(2)]
                nrm, nrmb = A.alloc((128, 520), BF16, "nrm")
                nr32, nr32b = rstg, rstgb

                for which, wnm, dst, dstb, o_bias in ((0, wq_nm, q16, q16b, o_bq),
                                                      (1, wk_nm, k16, k16b, o_bk)):
                    wload(ws, wsb, wnm, 0, PT, 512, col0=hg * 512, ncols=512)
                    for mt4 in range(4):
                        for ch in range(4):
                            c0 = ch * 512
                            bank = (mt4 * 4 + ch) % 2
                            for k in range(PT):
                                mm(bank, ws, (0, 128, k * 512 + mt4 * 128,
                                              k * 512 + mt4 * 128 + 128), wsb,
                                   XN, (0, 128, k * T + c0, k * T + c0 + 512), XNb[k],
                                   start=(k == 0), stop=(k == PT - 1))
                            if o_bias is None:
                                E("act", lambda e, bank=bank: e.copy(rstg[:], pbank(bank)),
                                  ins=[pbB[bank]], outs=[rstgb])
                            else:
                                bc = o_bias + hg * 4 + mt4
                                E("act", lambda e, bank=bank, bc=bc: e.activation(
                                    rstg[:], pbank(bank), AF.Identity, scale=1.0,
                                    bias=misc[:, bc:bc + 1]),
                                  ins=[pbB[bank], miscB], outs=[rstgb])
                            for blk2 in range(2):
                                b0 = blk2 * 64
                                E("dve", lambda e, b0=b0: e.tensor_scalar(
                                    rot[b0:b0 + 32, :], rstg[b0 + 32:b0 + 64, :],
                                    -1.0, None, OP.mult), ins=[rstgb], outs=[rotb], hz=False)
                                E("dve", lambda e, b0=b0: e.tensor_copy(
                                    rot[b0 + 32:b0 + 64, :], rstg[b0:b0 + 32, :]),
                                  ins=[rstgb], outs=[rotb], hz=False)
                            E("dve", lambda e, c0=c0: e.tensor_tensor(
                                t1[:], rstg[:], ropec[:, c0:c0 + 512], OP.mult),
                              ins=[rstgb, CB], outs=[t1b], hz=False)
                            E("dve", lambda e, c0=c0: e.tensor_tensor(
                                rot[:], rot[:], ropes[:, c0:c0 + 512], OP.mult),
                              ins=[rotb, CB], outs=[rotb], hz=False)
                            E("dve", lambda e, mt4=mt4, c0=c0, dst=dst: e.tensor_tensor(
                                dst[:, mt4 * T + c0:mt4 * T + c0 + 512], t1[:], rot[:],
                                OP.add), ins=[t1b, rotb], outs=[dstb], hz=False)
                E("dve", lambda e: e.memset(vt[:], 1.0), outs=[vtb], hz=False)
                wload(ws, wsb, wv_nm, 0, PT, 512, col0=hg * 512, ncols=512)
                for tt in range(16):
                    bank = tt % 2 + 2
                    for k in range(PT):
                        mm(bank, XN, (0, 128, k * T + tt * 128, k * T + tt * 128 + 128),
                           XNb[k], ws, (0, 128, k * 512, k * 512 + 512), wsb,
                           start=(k == 0), stop=(k == PT - 1 and bv16 is None))
                    if bv16 is not None:
                        mm(bank, ones16, (0, 1, 0, 128), CB, bv16,
                           (0, 1, hg * 512, hg * 512 + 512), bv16b,
                           start=False, stop=True)
                    for h8 in range(8):
                        E("act", lambda e, bank=bank, tt=tt, h8=h8: e.copy(
                            vt[:, tt * 520 + h8 * 65:tt * 520 + h8 * 65 + 64],
                            pbank(bank, h8 * 64, 64)), ins=[pbB[bank]], outs=[vtb])
                for h8 in range(8):
                    hb = (h8 % 2) * 64
                    tq = (h8 // 2)
                    qoff = hb * (4 * T) + tq * T
                    E("pe", lambda e, qoff=qoff: e.matmul(
                        pbank(6)[0:128, 0:128],
                        AP(k16.ap().tensor, qoff, [[4 * T, 64], [16, 128]]),
                        AP(q16.ap().tensor, qoff, [[4 * T, 64], [16, 128]]),
                        start=True, stop=True),
                      ins=[k16b, q16b], outs=[pbB[6]])
                    E("dve", lambda e: e.reduce_max(misc2[:, 28:29], pbank(6)[:, 0:128],
                                                    axis=AX),
                      ins=[pbB[6]], outs=[misc2B], hz=True)
                    E("sp", lambda e: e.dma_start(
                        out=flip_sp.ap()[0:1, :].rearrange("a b -> (a b)"),
                        in_=misc2[:, 28:29]), ins=[misc2B], outs=[flipD], inc=16)
                    E("sp", lambda e: e.dma_start(out=misc2[0:1, 32:160],
                                                  in_=flip_sp.ap()[0:1, :]),
                      ins=[flipD], outs=[misc2B], inc=16)
                    E("dve", lambda e: e.reduce_max(misc2[0:1, 30:31],
                                                    misc2[0:1, 32:160], axis=AX),
                      ins=[misc2B], outs=[misc2B], hz=True)
                    E("dve", lambda e: e.tensor_scalar(misc2[0:1, 30:31],
                                                       misc2[0:1, 30:31], -0.125, None,
                                                       OP.mult),
                      ins=[misc2B], outs=[misc2B])
                    mm(6, onesf, (0, 1, 0, 128), CB, misc2, (0, 1, 30, 31), misc2B,
                       True, True, nlen=1)
                    E("act", lambda e: e.copy(misc[:, o_sc4:o_sc4 + 1], pbank(6, 0, 1)),
                      ins=[pbB[6]], outs=[miscB])
                    if os.environ.get("MK_SCANDBG", "") == "nb" and not causal:
                        E("sp", lambda e, hg=hg, h8=h8: e.dma_start(
                            out=dbg_d.ap()[0:1, hg * 8 + h8:hg * 8 + h8 + 1],
                            in_=misc2[0:1, 30:31]), ins=[misc2B], inc=16)
                        E("sp", lambda e, hg=hg, h8=h8: e.dma_start(
                            out=dbg_d.ap()[0:1, 16 + hg * 8 + h8:17 + hg * 8 + h8],
                            in_=misc2[0:1, 28:29]), ins=[misc2B], inc=16)
                    for qc in range(4):
                        q0 = qc * 512
                        nkt = (qc * 4 + 4) if causal else 16
                        for kt in range(nkt):
                            bank = kt % 2
                            mm(bank, k16, (hb, hb + 64, tq * T + kt * 128,
                                           tq * T + kt * 128 + 128), k16b,
                               q16, (hb, hb + 64, tq * T + q0, tq * T + q0 + 512), q16b,
                               start=True, stop=True)
                            if causal:
                                delta = kt * 128 - q0
                                if delta >= 0:
                                    mi = delta // 128
                                    b0 = 384 - mi * 128
                                    E("dve", lambda e, bank=bank, b0=b0: e.tensor_tensor(
                                        pbank(bank), pbank(bank),
                                        l1mask[:, b0:b0 + 512], OP.add),
                                      ins=[CB], outs=[pbB[bank]], hz=False)
                            ps, psb = pst[kt % 2]
                            E("act", lambda e, bank=bank, ps=ps: e.activation(
                                ps[:], pbank(bank), AF.Exp, scale=0.125,
                                bias=misc[:, o_sc4:o_sc4 + 1]),
                              ins=[pbB[bank], miscB], outs=[psb])
                            mm(2, vt, (0, 128, kt * 520 + h8 * 65, kt * 520 + h8 * 65 + 65),
                               vtb, ps, (0, 128, 0, 512), psb,
                               start=(kt == 0), stop=(kt == nkt - 1), mlen=65)
                        E("dve", lambda e: e.reciprocal(nr32[64:65, 0:512],
                                                        pbank(2)[64:65, 0:512]),
                          ins=[pbB[2]], outs=[nr32b], hz=True)
                        E("dve", lambda e: e.tensor_copy(nrm[64:65, 0:512],
                                                         nr32[64:65, 0:512]),
                          ins=[nr32b], outs=[nrmb], hz=True)
                        mm(3, ones16, (64, 65, 0, 64), CB, nrm, (64, 65, 0, 512), nrmb,
                           True, True, mlen=64)
                        E("act", lambda e: e.copy(rstg[0:64, :], pbank(2)[0:64, :]),
                          ins=[pbB[2]], outs=[rstgb])
                        E("dve", lambda e, h8=h8, q0=q0: e.tensor_tensor(
                            ao[(h8 % 2) * 64:(h8 % 2) * 64 + 64,
                               (h8 // 2) * T + q0:(h8 // 2) * T + q0 + 512],
                            rstg[0:64, :], pbank(3)[0:64, :], OP.mult),
                          ins=[rstgb, pbB[3]], outs=[aob], hz=False)
                A.release(msc)
                w2s, w2sb = A.alloc((128, 4 * 1024), BF16, "w2p")
                ev2, ev2b = A.alloc((128, 512), F32, "w2ev")
                wload(w2s, w2sb, w2_nm, hg * 4, 4, 1024)
                for ch in range(4):
                    c0 = ch * 512
                    for mt in range(8):
                        bank = mt % 2
                        for k in range(4):
                            mm(bank, w2s, (0, 128, k * 1024 + mt * 128,
                                           k * 1024 + mt * 128 + 128),
                               w2sb, ao, (0, 128, k * T + c0, k * T + c0 + 512), aob,
                               start=(k == 0), stop=(k == 3))
                        if o_bo is not None and hg == 0:
                            E("act", lambda e, bank=bank, mt=mt: e.activation(
                                ev2[:], pbank(bank), AF.Identity, scale=1.0,
                                bias=misc[:, o_bo + mt:o_bo + mt + 1]),
                              ins=[pbB[bank], miscB], outs=[ev2b])
                            E("dve", lambda e, mt=mt, c0=c0: e.tensor_tensor(
                                X[:, mt * T + c0:mt * T + c0 + 512],
                                X[:, mt * T + c0:mt * T + c0 + 512], ev2[:], OP.add),
                              ins=[ev2b], outs=[Xb[mt]], hz=False)
                        else:
                            x_add_psum(bank, mt, c0)
                A.release(mhg)

        def l1():
            o_b1 = load_vec("l1f_b1", 4096)
            o_b2 = load_vec("l1f_b2", D)
            o_b1s = mslot(32)
            o_b2s = mslot(8)
            o_sc = mslot(8)

            mpre = A.mark()
            xn32, xn32b = layer_norm(want32=True)

            arD, ar2D = Buf(), Buf()
            ax, axb = A.alloc((128, 16), F32, "amax")
            wstg, wstgb = A.alloc((128, 2048), F32, "awstg")
            wstg2, wstg2b = A.alloc((128, 2048), F32, "awstg2")
            for k in range(PT):
                E("dve", lambda e, k=k: e.reduce_max(ax[:, k:k + 1], xn32[:, k * T:(k + 1) * T],
                                                     axis=AX, apply_absolute_value=True),
                  ins=[xn32b], outs=[axb], hz=True)
            w1v = di["l1f_w1T"].ap().rearrange("a (b c) -> (a b) c", c=2048)
            for i in range(16):
                wt = wstg if i % 2 == 0 else wstg2
                wtb = wstgb if i % 2 == 0 else wstg2b
                E("sp", lambda e, i=i, wt=wt: e.dma_start(
                    out=wt[:], in_=w1v[i * 128:(i + 1) * 128, :]),
                  outs=[wtb], inc=16)
                E("dve", lambda e, wt=wt: e.reduce_max(ax[:, 9:10], wt[:], axis=AX,
                                                       apply_absolute_value=True),
                  ins=[wtb], outs=[axb], hz=True)
                E("dve", lambda e, i=i: (e.tensor_copy(ax[:, 8:9], ax[:, 9:10]) if i == 0
                                         else e.tensor_tensor(ax[:, 8:9], ax[:, 8:9],
                                                              ax[:, 9:10], OP.max)),
                  ins=[axb], outs=[axb])
            for i in range(16):
                wt = wstg if i % 2 == 0 else wstg2
                wtb = wstgb if i % 2 == 0 else wstg2b
                E("sp", lambda e, i=i, wt=wt: e.dma_start(
                    out=wt[:, 0:1024], in_=di["l1f_w2T"].ap()[i * 256:i * 256 + 128, :]),
                  outs=[wtb], inc=16)
                E("sp", lambda e, i=i, wt=wt: e.dma_start(
                    out=wt[:, 1024:2048], in_=di["l1f_w2T"].ap()[i * 256 + 128:i * 256 + 256, :]),
                  outs=[wtb], inc=16)
                E("dve", lambda e, wt=wt: e.reduce_max(ax[:, 9:10], wt[:], axis=AX,
                                                       apply_absolute_value=True),
                  ins=[wtb], outs=[axb], hz=True)
                E("dve", lambda e, i=i: (e.tensor_copy(ax[:, 10:11], ax[:, 9:10]) if i == 0
                                         else e.tensor_tensor(ax[:, 10:11], ax[:, 10:11],
                                                              ax[:, 9:10], OP.max)),
                  ins=[axb], outs=[axb])
            for k in range(1, PT):
                E("dve", lambda e, k=k: e.tensor_tensor(ax[:, 0:1], ax[:, 0:1], ax[:, k:k + 1],
                                                        OP.max), ins=[axb], outs=[axb])
            flD = flipD
            for j, col in enumerate((0, 8, 10)):
                E("sp", lambda e, j=j, col=col: e.dma_start(
                    out=flip_sp.ap()[j:j + 1, :].rearrange("a b -> (a b)"),
                    in_=ax[:, col:col + 1]), ins=[axb], outs=[flD], inc=16)
            for j in range(3):
                E("sp", lambda e, j=j: e.dma_start(
                    out=misc2[0:1, 32 + j * 128:32 + (j + 1) * 128],
                    in_=flip_sp.ap()[j:j + 1, :]), ins=[flD], outs=[misc2B], inc=16)
            for j in range(3):
                E("dve", lambda e, j=j: e.reduce_max(
                    misc2[0:1, 16 + j:17 + j], misc2[0:1, 32 + j * 128:32 + (j + 1) * 128],
                    axis=AX), ins=[misc2B], outs=[misc2B], hz=True)
            E("dve", lambda e: e.memset(misc2[0:1, 19:20], 0.0), outs=[misc2B])
            E("gp", lambda e: e.dma_start(out=ar_in.ap()[0:1, 0:4], in_=misc2[0:1, 16:20]),
              ins=[misc2B], outs=[arD], inc=16)
            E("gp", lambda e: e.collective_compute(
                "AllReduce", OP.max, replica_groups=[list(range(8))],
                ins=[ar_in.ap().opt()], outs=[ar_out.ap().opt()]),
              ins=[arD], outs=[ar2D], isasync=True)
            E("gp", lambda e: e.dma_start(out=misc2[0:1, 16:19], in_=ar_out.ap()[0:1, 0:3]),
              ins=[ar2D], outs=[misc2B], inc=16)
            E("dve", lambda e: e.tensor_scalar(misc2[0:1, 20:23], misc2[0:1, 16:19],
                                               1e-6, 1.0 / 127.0, OP.max, OP.mult),
              ins=[misc2B], outs=[misc2B])
            E("dve", lambda e: e.reciprocal(misc2[0:1, 24:25], misc2[0:1, 20:21]),
              ins=[misc2B], outs=[misc2B], hz=True)
            E("dve", lambda e: e.tensor_tensor(misc2[0:1, 25:26], misc2[0:1, 20:21],
                                               misc2[0:1, 21:22], OP.mult),
              ins=[misc2B], outs=[misc2B])
            mm(1, onesf, (0, 1, 0, 128), CB, misc2, (0, 1, 24, 25), misc2B, True, True, nlen=1)
            E("act", lambda e: e.copy(misc[:, o_sc:o_sc + 1], pbank(1, 0, 1)),
              ins=[pbB[1]], outs=[miscB])
            mm(1, onesf, (0, 1, 0, 128), CB, misc2, (0, 1, 25, 26), misc2B, True, True, nlen=1)
            E("act", lambda e: e.copy(misc[:, o_sc + 1:o_sc + 2], pbank(1, 0, 1)),
              ins=[pbB[1]], outs=[miscB])
            E("dve", lambda e: e.reciprocal(misc2[0:1, 29:30], misc2[0:1, 21:22]),
              ins=[misc2B], outs=[misc2B], hz=True)
            mm(1, onesf, (0, 1, 0, 128), CB, misc2, (0, 1, 29, 30), misc2B, True, True, nlen=1)
            E("act", lambda e: e.copy(misc[:, o_sc + 5:o_sc + 6], pbank(1, 0, 1)),
              ins=[pbB[1]], outs=[miscB])
            E("dve", lambda e: e.reciprocal(misc2[0:1, 29:30], misc2[0:1, 22:23]),
              ins=[misc2B], outs=[misc2B], hz=True)
            mm(1, onesf, (0, 1, 0, 128), CB, misc2, (0, 1, 29, 30), misc2B, True, True, nlen=1)
            E("act", lambda e: e.copy(misc[:, o_sc + 6:o_sc + 7], pbank(1, 0, 1)),
              ins=[pbB[1]], outs=[miscB])
            E("dve", lambda e: e.tensor_single_scalar(misc[:, o_b1s:o_b1s + 32],
                                                      misc[:, o_b1:o_b1 + 32],
                                                      misc[:, o_sc + 1:o_sc + 2], OP.mult),
              ins=[miscB], outs=[miscB])
            if os.environ.get("MK_SCANDBG", "") == "sc":
                E("dve", lambda e: e.tensor_copy(X[0:1, 0:16], misc2[0:1, 16:32]),
                  ins=[misc2B], outs=[Xb[0]])
                E("dve", lambda e: e.tensor_copy(X[0:1, 16:24], misc[0:1, o_sc:o_sc + 8]),
                  ins=[miscB], outs=[Xb[0]])
                E("sp", lambda e: e.dma_start(out=X[0:1, 40:44], in_=ar_out.ap()[0:1, 0:4]),
                  ins=[ar2D], outs=[Xb[0]], inc=16)
                E("sp", lambda e: e.dma_start(out=X[0:1, 44:48], in_=ar_in.ap()[0:1, 0:4]),
                  ins=[arD], outs=[Xb[0]], inc=16)
                E("dve", lambda e: e.tensor_copy(X[0:1, 70:198], misc2[0:1, 160:288]),
                  ins=[misc2B], outs=[Xb[0]])
                raise _SkipRest
            xqs, xqsb = A.alloc((128, 512), F32, "xqs")
            xq16, xq16b = A.alloc((128, 512), BF16, "xq16")
            for k in range(PT):
                for ch in range(4):
                    c0 = ch * 512
                    E("dve", lambda e, k=k, c0=c0: e.tensor_single_scalar(
                        xqs[:], xn32[:, k * T + c0:k * T + c0 + 512],
                        misc[:, o_sc:o_sc + 1], OP.mult),
                      ins=[xn32b, miscB], outs=[xqsb])
                    E("dve", lambda e: e.tensor_scalar(xqs[:], xqs[:], MAGIC, None, OP.add),
                      ins=[xqsb], outs=[xqsb])
                    E("dve", lambda e: e.tensor_scalar(xq16[:], xqs[:], MAGIC, None,
                                                       OP.subtract),
                      ins=[xqsb], outs=[xq16b])
                    E("sp", lambda e, k=k, c0=c0: e.dma_start(
                        out=xq_sp.ap()[k * 128:(k + 1) * 128, c0:c0 + 512], in_=xq16[:]),
                      ins=[xq16b], outs=[xqD], inc=16)
            A.release(mpre)


            do_attn1 = not (N_LAYERS == 2 and MK_PARTS == "ffn")
            do_ffn1 = not (N_LAYERS == 2 and MK_PARTS == "attn")
            m0 = A.mark()
            if do_attn1:
                rope_attn("l1_wqT", "l1_wkT", "l1_wvT", "l1_w2T", True, o_sc4=o_sc + 4)
            A.release(m0)

            if not do_ffn1:
                raise _SkipRest
            m0 = A.mark()
            w1q, w1qb = A.alloc((128, 8 * 512), BF16, "w1q")
            fstg, fstgb = A.alloc((128, 512), F32, "fstg")
            f2, f2b = A.alloc((128, 512), F32, "f2")
            xqv = [A.alloc((128, 8 * 512), BF16, f"xqv{i}") for i in range(2)]
            hmax, hmaxb = A.alloc((128, 4), F32, "hmax")
            hst = [A.alloc((128, 512), F32, f"hst{i}") for i in range(2)]
            for oc in range(8):
                for kt in range(PT):
                    E("sp", lambda e, oc=oc, kt=kt: e.dma_start(
                        out=fstg[:], in_=di["l1f_w1T"].ap()[kt * 128:(kt + 1) * 128,
                                                            oc * 512:(oc + 1) * 512]),
                      outs=[fstgb], inc=16)
                    E("dve", lambda e, kt=kt: e.tensor_single_scalar(
                        f2[:], fstg[:], misc[:, o_sc + 5:o_sc + 6], OP.mult),
                      ins=[fstgb, miscB], outs=[f2b])
                    E("dve", lambda e: e.tensor_scalar(f2[:], f2[:], MAGIC, None, OP.add),
                      ins=[f2b], outs=[f2b])
                    E("dve", lambda e, kt=kt: e.tensor_scalar(
                        w1q[:, kt * 512:(kt + 1) * 512], f2[:], MAGIC, None, OP.subtract),
                      ins=[f2b], outs=[w1qb])
                for ch in range(4):
                    c0 = ch * 512
                    xqt, xqtb = xqv[ch % 2]
                    for kt in range(PT):
                        E("gp", lambda e, kt=kt, c0=c0: e.dma_start(
                            out=xqt[:, kt * 512:(kt + 1) * 512],
                            in_=xq_sp.ap()[kt * 128:(kt + 1) * 128, c0:c0 + 512]),
                          ins=[xqD], outs=[xqtb], inc=16)
                    for mt4 in range(4):
                        dglob = oc * 4 + mt4
                        bank = (mt4 + ch) % 2
                        for kt in range(PT):
                            mm(bank, w1q, (0, 128, kt * 512 + mt4 * 128,
                                           kt * 512 + mt4 * 128 + 128), w1qb,
                               xqt, (0, 128, kt * 512, kt * 512 + 512), xqtb,
                               start=(kt == 0), stop=(kt == PT - 1))
                        hs, hsb = hst[bank]
                        E("act", lambda e, bank=bank, dglob=dglob, hs=hs: e.activation(
                            hs[:], pbank(bank), AF.Gelu,
                            scale=misc[:, o_sc + 1:o_sc + 2],
                            bias=misc[:, o_b1s + dglob:o_b1s + dglob + 1]),
                          ins=[pbB[bank], miscB], outs=[hsb])
                        E("dve", lambda e, hs=hs: e.reduce_max(hmax[:, 1:2], hs[:], axis=AX,
                                                               apply_absolute_value=True),
                          ins=[hsb], outs=[hmaxb], hz=True)
                        E("dve", lambda e, oc=oc, ch=ch, mt4=mt4: (
                            e.tensor_copy(hmax[:, 0:1], hmax[:, 1:2])
                            if (oc == 0 and ch == 0 and mt4 == 0) else
                            e.tensor_tensor(hmax[:, 0:1], hmax[:, 0:1], hmax[:, 1:2],
                                            OP.max)), ins=[hmaxb], outs=[hmaxb])
                        E("sp", lambda e, dglob=dglob, c0=c0, hs=hs: e.dma_start(
                            out=h_sp.ap()[dglob * 128:(dglob + 1) * 128, c0:c0 + 512],
                            in_=hs[:]), ins=[hsb], outs=[hD], inc=16)
            flD2 = flipD
            E("sp", lambda e: e.dma_start(
                out=flip_sp.ap()[3:4, :].rearrange("a b -> (a b)"), in_=hmax[:, 0:1]),
              ins=[hmaxb], outs=[flD2], inc=16)
            E("sp", lambda e: e.dma_start(out=misc2[0:1, 32:160], in_=flip_sp.ap()[3:4, :]),
              ins=[flD2], outs=[misc2B], inc=16)
            E("dve", lambda e: e.reduce_max(misc2[0:1, 16:17], misc2[0:1, 32:160], axis=AX),
              ins=[misc2B], outs=[misc2B], hz=True)
            E("gp", lambda e: e.dma_start(out=ar2_in.ap()[0:1, 0:1], in_=misc2[0:1, 16:17]),
              ins=[misc2B], outs=[arD], inc=16)
            E("gp", lambda e: e.collective_compute(
                "AllReduce", OP.max, replica_groups=[list(range(8))],
                ins=[ar2_in.ap().opt()], outs=[ar2_out.ap().opt()]),
              ins=[arD], outs=[ar2D], isasync=True)
            E("gp", lambda e: e.dma_start(out=misc2[0:1, 16:17], in_=ar2_out.ap()[0:1, 0:1]),
              ins=[ar2D], outs=[misc2B], inc=16)
            E("dve", lambda e: e.tensor_scalar(misc2[0:1, 26:27], misc2[0:1, 16:17],
                                               1e-6, 1.0 / 127.0, OP.max, OP.mult),
              ins=[misc2B], outs=[misc2B])
            E("dve", lambda e: e.reciprocal(misc2[0:1, 27:28], misc2[0:1, 26:27]),
              ins=[misc2B], outs=[misc2B], hz=True)
            E("dve", lambda e: e.tensor_tensor(misc2[0:1, 28:29], misc2[0:1, 26:27],
                                               misc2[0:1, 22:23], OP.mult),
              ins=[misc2B], outs=[misc2B])
            mm(1, onesf, (0, 1, 0, 128), CB, misc2, (0, 1, 27, 28), misc2B, True, True, nlen=1)
            E("act", lambda e: e.copy(misc[:, o_sc + 2:o_sc + 3], pbank(1, 0, 1)),
              ins=[pbB[1]], outs=[miscB])
            mm(1, onesf, (0, 1, 0, 128), CB, misc2, (0, 1, 28, 29), misc2B, True, True, nlen=1)
            E("act", lambda e: e.copy(misc[:, o_sc + 3:o_sc + 4], pbank(1, 0, 1)),
              ins=[pbB[1]], outs=[miscB])
            E("dve", lambda e: e.tensor_single_scalar(misc[:, o_b2s:o_b2s + 8],
                                                      misc[:, o_b2:o_b2 + 8],
                                                      misc[:, o_sc + 3:o_sc + 4], OP.mult),
              ins=[miscB], outs=[miscB])
            for kt in range(32):
                E("sp", lambda e, kt=kt: e.dma_start(
                    out=fstg[:], in_=di["l1f_w2T"].ap()[kt * 128:(kt + 1) * 128, 0:512]),
                  outs=[fstgb], inc=16)
                E("sp", lambda e, kt=kt: e.dma_start(
                    out=f2[:], in_=di["l1f_w2T"].ap()[kt * 128:(kt + 1) * 128, 512:1024]),
                  outs=[f2b], inc=16)
                for half, (src_t, src_b) in enumerate(((fstg, fstgb), (f2, f2b))):
                    E("dve", lambda e, src_t=src_t: e.tensor_single_scalar(
                        hst[0][0][:], src_t[:], misc[:, o_sc + 6:o_sc + 7], OP.mult),
                      ins=[src_b, miscB], outs=[hst[0][1]])
                    E("dve", lambda e: e.tensor_scalar(hst[0][0][:], hst[0][0][:], MAGIC,
                                                       None, OP.add),
                      ins=[hst[0][1]], outs=[hst[0][1]])
                    E("dve", lambda e: e.tensor_scalar(xqv[0][0][:, 0:512], hst[0][0][:],
                                                       MAGIC, None, OP.subtract),
                      ins=[hst[0][1]], outs=[xqv[0][1]])
                    E("sp", lambda e, kt=kt, half=half: e.dma_start(
                        out=w2q_sp.ap()[kt * 128:(kt + 1) * 128,
                                        half * 512:(half + 1) * 512],
                        in_=xqv[0][0][:, 0:512]), ins=[xqv[0][1]], outs=[w2qD], inc=16)
            w2qs, w2qsb = A.alloc((128, 4 * 1024), BF16, "w2qs")
            hq16, hq16b = A.alloc((128, 4 * 512), BF16, "hq16")
            for ch in range(4):
                c0 = ch * 512
                first = True
                for kg in range(8):
                    for kt4 in range(4):
                        kt = kg * 4 + kt4
                        E("gp", lambda e, kt=kt, kt4=kt4: e.dma_start(
                            out=w2qs[:, kt4 * 1024:(kt4 + 1) * 1024],
                            in_=w2q_sp.ap()[kt * 128:(kt + 1) * 128, :]),
                          ins=[w2qD], outs=[w2qsb], inc=16)
                        E("sp", lambda e, kt=kt, c0=c0: e.dma_start(
                            out=fstg[:], in_=h_sp.ap()[kt * 128:(kt + 1) * 128, c0:c0 + 512]),
                          ins=[hD], outs=[fstgb], inc=16)
                        E("dve", lambda e: e.tensor_single_scalar(
                            f2[:], fstg[:], misc[:, o_sc + 2:o_sc + 3], OP.mult),
                          ins=[fstgb, miscB], outs=[f2b])
                        E("dve", lambda e: e.tensor_scalar(f2[:], f2[:], MAGIC, None, OP.add),
                          ins=[f2b], outs=[f2b])
                        E("dve", lambda e, kt4=kt4: e.tensor_scalar(
                            hq16[:, kt4 * 512:(kt4 + 1) * 512], f2[:], MAGIC, None,
                            OP.subtract), ins=[f2b], outs=[hq16b])
                    for mt in range(8):
                        for kt4 in range(4):
                            mm(mt, w2qs, (0, 128, kt4 * 1024 + mt * 128,
                                          kt4 * 1024 + mt * 128 + 128), w2qsb,
                               hq16, (0, 128, kt4 * 512, kt4 * 512 + 512), hq16b,
                               start=(kg == 0 and kt4 == 0),
                               stop=(kg == 7 and kt4 == 3))
                for mt in range(8):
                    E("act", lambda e, mt=mt: e.activation(
                        fstg[:], pbank(mt), AF.Identity, scale=misc[:, o_sc + 3:o_sc + 4],
                        bias=misc[:, o_b2s + mt:o_b2s + mt + 1]),
                      ins=[pbB[mt], miscB], outs=[fstgb])
                    E("dve", lambda e, mt=mt, c0=c0: e.tensor_tensor(
                        X[:, mt * T + c0:mt * T + c0 + 512],
                        X[:, mt * T + c0:mt * T + c0 + 512], fstg[:], OP.add),
                      ins=[fstgb], outs=[Xb[mt]])
            A.release(m0)


        def l2():
            o_bq2 = load_vec("l2_bq", D)
            o_bk2 = load_vec("l2_bk", D)
            o_bo2 = load_vec("l2_bo", D)
            o_bs1 = load_vec("l2f_bs1", D)
            o_bs2 = load_vec("l2f_bs2", D)
            o_br = mslot(1)
            E("sp", lambda e: e.dma_start(out=misc[0:31, o_br:o_br + 1],
                                          in_=AP(di["l2f_br"].ap().tensor, 0, [[1, 31], [1, 1]]),
                                          allow_slow_non_contiguous=True),
              outs=[miscB], inc=16)
            bv16, bv16b = A.alloc((1, 1024), BF16, "l2bv16")
            E("gp", lambda e: e.dma_start(out=bv16[:],
                                          in_=di["l2_bv"].ap().rearrange("(a b) -> a b", a=1)),
              outs=[bv16b], inc=16)

            mpre = A.mark()
            layer_norm(False)
            A.release(mpre)
            do_attn2 = not (N_LAYERS == 3 and MK_PARTS == "ffn")
            do_ffn2 = not (N_LAYERS == 3 and MK_PARTS == "attn")
            m0 = A.mark()
            if do_attn2:
                rope_attn("l2_wqT", "l2_wkT", "l2_wvT", "l2_woT", False,
                          o_bq=o_bq2, o_bk=o_bk2, bv16=bv16, bv16b=bv16b, o_bo=o_bo2,
                          o_sc4=o_br + 1 if False else mslot(1))
            A.release(m0)
            if not do_ffn2:
                return

            m0 = A.mark()
            cvt, cvtb = A.alloc((128, 512), F32, "l2cv")
            for mt in range(8):
                for ch in range(4):
                    c0 = ch * 512
                    E("dve", lambda e, mt=mt, c0=c0: e.tensor_tensor(
                        X[:, mt * T + c0:mt * T + c0 + 512],
                        X[:, mt * T + c0:mt * T + c0 + 512],
                        XN[:, mt * T + c0:mt * T + c0 + 512], OP.add),
                      ins=[XNb[mt]], outs=[Xb[mt]], hz=False)
            ws, wsb = A.alloc((128, PT * 1024), BF16, "l2ws")
            h16, h16b = A.alloc((128, PT * T), BF16, "l2h")
            wload(ws, wsb, "l2f_ws1", 0, PT, 1024)
            for ch in range(4):
                c0 = ch * 512
                for mt in range(8):
                    bank = mt % 2
                    for k in range(PT):
                        mm(bank, ws, (0, 128, k * 1024 + mt * 128, k * 1024 + mt * 128 + 128),
                           wsb, XN, (0, 128, k * T + c0, k * T + c0 + 512), XNb[k],
                           start=(k == 0), stop=(k == PT - 1))
                    E("act", lambda e, bank=bank, mt=mt, c0=c0: e.activation(
                        h16[:, mt * T + c0:mt * T + c0 + 512], pbank(bank), AF.Gelu,
                        bias=misc[:, o_bs1 + mt:o_bs1 + mt + 1]),
                      ins=[pbB[bank], miscB], outs=[h16b])
            wload(ws, wsb, "l2f_ws2", 0, PT, 1024)
            for ch in range(4):
                c0 = ch * 512
                for mt in range(8):
                    bank = mt % 2
                    for k in range(PT):
                        mm(bank, ws, (0, 128, k * 1024 + mt * 128, k * 1024 + mt * 128 + 128),
                           wsb, h16, (0, 128, k * T + c0, k * T + c0 + 512), h16b,
                           start=(k == 0), stop=(k == PT - 1))
                    E("act", lambda e, bank=bank, mt=mt: e.activation(
                        cvt[:], pbank(bank), AF.Identity, scale=1.0,
                        bias=misc[:, o_bs2 + mt:o_bs2 + mt + 1]),
                      ins=[pbB[bank], miscB], outs=[cvtb])
                    E("dve", lambda e, mt=mt, c0=c0: e.tensor_tensor(
                        X[:, mt * T + c0:mt * T + c0 + 512],
                        X[:, mt * T + c0:mt * T + c0 + 512], cvt[:], OP.add),
                      ins=[cvtb], outs=[Xb[mt]], hz=False)
            wr, wrb = A.alloc((128, PT * 31), BF16, "l2wr")
            affs, affsb = A.alloc((128, T), F32, "l2aff")
            ptm, ptmb = A.alloc((128, 31), F32, "l2ptm")
            gtm, gtmb = A.alloc((128, 31), F32, "l2gtm")
            s1, s1b = A.alloc((128, 8), F32, "l2s1")
            gfm, gfmb = A.alloc((128, 128), BF16, "l2gfm")
            E("dve", lambda e: e.memset(affs[:], 0.0), outs=[affsb], hz=False)
            wload(wr, wrb, "l2f_Wr", 0, PT, 31)
            for ch in range(4):
                c0 = ch * 512
                bank = ch % 2
                for k in range(PT):
                    mm(bank, wr, (0, 128, k * 31, k * 31 + 31), wrb,
                       XN, (0, 128, k * T + c0, k * T + c0 + 512), XNb[k],
                       start=(k == 0), stop=(k == PT - 1), mlen=31)
                E("act", lambda e, bank=bank, c0=c0: e.activation(
                    affs[0:31, c0:c0 + 512], pbank(bank)[0:31, :], AF.Identity,
                    scale=1.0, bias=misc[0:31, o_br:o_br + 1]),
                  ins=[pbB[bank], miscB], outs=[affsb])
            for tt in range(16):
                E("pe", lambda e, tt=tt: e.transpose(pbank(4)[0:128, 0:128],
                                                     affs[:, tt * 128:(tt + 1) * 128],
                                                     identf[:]),
                  ins=[affsb, CB], outs=[pbB[4]])
                E("dve", lambda e: e.reduce_max(s1[:, 0:1], pbank(4)[:, 0:31], axis=AX,
                                                negate=True),
                  ins=[pbB[4]], outs=[s1b], hz=True)
                E("act", lambda e: e.activation(ptm[:], pbank(4)[:, 0:31], AF.Exp,
                                                bias=s1[:, 0:1]),
                  ins=[pbB[4], s1b], outs=[ptmb])
                E("dve", lambda e: e.reduce_sum(s1[:, 1:2], ptm[:], axis=AX),
                  ins=[ptmb], outs=[s1b], hz=True)
                E("dve", lambda e: e.reciprocal(s1[:, 2:3], s1[:, 1:2]),
                  ins=[s1b], outs=[s1b], hz=True)
                E("dve", lambda e: e.tensor_single_scalar(ptm[:], ptm[:], s1[:, 2:3],
                                                          OP.mult),
                  ins=[s1b], outs=[ptmb])
                E("dve", lambda e: e.reduce_max(s1[:, 3:4], ptm[:], axis=AX),
                  ins=[ptmb], outs=[s1b], hz=True)
                E("dve", lambda e: e.tensor_single_scalar(gtm[:], ptm[:], s1[:, 3:4],
                                                          OP.is_ge),
                  ins=[ptmb, s1b], outs=[gtmb])
                E("dve", lambda e: e.scalar_tensor_tensor(gtm[:], gtm[:], -2.0, ptm[:],
                                                          OP.mult, OP.add),
                  ins=[ptmb], outs=[gtmb], hz=False)
                E("dve", lambda e: e.reduce_max(s1[:, 4:5], gtm[:], axis=AX),
                  ins=[gtmb], outs=[s1b], hz=True)
                E("dve", lambda e: e.tensor_single_scalar(cvt[:, 0:31], gtm[:], s1[:, 4:5],
                                                          OP.is_ge),
                  ins=[gtmb, s1b], outs=[cvtb])
                E("dve", lambda e: e.scalar_tensor_tensor(cvt[:, 0:31], cvt[:, 0:31], -2.0,
                                                          gtm[:], OP.mult, OP.add),
                  ins=[gtmb], outs=[cvtb], hz=False)
                E("dve", lambda e: e.reduce_max(s1[:, 5:6], cvt[:, 0:31], axis=AX),
                  ins=[cvtb], outs=[s1b], hz=True)
                E("dve", lambda e: e.tensor_single_scalar(gtm[:], ptm[:], s1[:, 5:6],
                                                          OP.is_ge),
                  ins=[ptmb, s1b], outs=[gtmb])
                E("dve", lambda e: e.tensor_tensor(gtm[:], gtm[:], ptm[:], OP.mult),
                  ins=[ptmb], outs=[gtmb])
                E("pe", lambda e, tt=tt: e.transpose(pbank(5)[0:31, 0:128], gtm[:, 0:31],
                                                     identf[:]),
                  ins=[gtmb, CB], outs=[pbB[5]])
                E("act", lambda e: e.copy(gfm[0:31, 0:128], pbank(5)[0:31, 0:128]),
                  ins=[pbB[5]], outs=[gfmb])
                E("sp", lambda e, tt=tt: e.dma_start(
                    out=gates_sp.ap()[:, tt * 128:(tt + 1) * 128], in_=gfm[0:31, 0:128]),
                  ins=[gfmb], outs=[gD], inc=16)
            A.release(m0)
            m0 = A.mark()
            cvt, cvtb = A.alloc((128, 512), F32, "l2cv2")
            he, heb = A.alloc((128, PT * T), BF16, "l2he")
            wv = [A.alloc((128, PT * 1024), BF16, f"l2we{i}") for i in range(2)]
            be1r, be1rb = A.alloc((128, 16), F32, "l2be1")
            be2r, be2rb = A.alloc((1, 1024), BF16, "l2be2")
            grow, growb = A.alloc((1, T), BF16, "l2grow")
            for e_ in range(31):
                wt, wtb = wv[e_ % 2]
                for i in range(PT):
                    E("gp", lambda e, e_=e_, i=i, wt=wt: e.dma_start(
                        out=wt[:, i * 1024:(i + 1) * 1024],
                        in_=di["l2f_We1"].ap()[e_, i * 128:(i + 1) * 128, :]),
                      outs=[wtb], inc=16)
                E("sp", lambda e, e_=e_: e.dma_start(
                    out=be1r[:, 0:8],
                    in_=AP(di["l2f_Be1"].ap().tensor, e_ * 1024, [[1, 128], [128, 8]]),
                    allow_slow_non_contiguous=True),
                  outs=[be1rb], inc=16)
                E("gp", lambda e, e_=e_: e.dma_start(
                    out=grow[:], in_=gates_sp.ap()[e_:e_ + 1, :]),
                  ins=[gD], outs=[growb], inc=16)
                for ch in range(4):
                    c0 = ch * 512
                    for mt in range(8):
                        bank = mt % 2
                        for k in range(PT):
                            mm(bank, wt, (0, 128, k * 1024 + mt * 128,
                                          k * 1024 + mt * 128 + 128), wtb,
                               XN, (0, 128, k * T + c0, k * T + c0 + 512), XNb[k],
                               start=(k == 0), stop=(k == PT - 1))
                        E("act", lambda e, bank=bank, mt=mt, c0=c0: e.activation(
                            he[:, mt * T + c0:mt * T + c0 + 512], pbank(bank), AF.Gelu,
                            bias=be1r[:, mt:mt + 1]),
                          ins=[pbB[bank], be1rb], outs=[heb])
                wt2, wt2b = wv[e_ % 2]
                for i in range(PT):
                    E("gp", lambda e, e_=e_, i=i, wt2=wt2: e.dma_start(
                        out=wt2[:, i * 1024:(i + 1) * 1024],
                        in_=di["l2f_We2"].ap()[e_, i * 128:(i + 1) * 128, :]),
                      outs=[wt2b], inc=16)
                E("gp", lambda e, e_=e_: e.dma_start(
                    out=be2r[:], in_=di["l2f_Be2"].ap()[e_:e_ + 1, :]),
                  outs=[be2rb], inc=16)
                for ch in range(4):
                    c0 = ch * 512
                    mm(4, ones16, (0, 1, 0, 128), CB, grow, (0, 1, c0, c0 + 512), growb,
                       True, True)
                    for mt in range(8):
                        bank = 2 + mt % 2
                        for k in range(PT):
                            mm(bank, wt2, (0, 128, k * 1024 + mt * 128,
                                           k * 1024 + mt * 128 + 128), wt2b,
                               he, (0, 128, k * T + c0, k * T + c0 + 512), heb,
                               start=(k == 0), stop=False)
                        mm(bank, be2r, (0, 1, mt * 128, mt * 128 + 128), be2rb,
                           grow, (0, 1, c0, c0 + 512), growb, start=False, stop=True)
                        E("act", lambda e, bank=bank: e.copy(cvt[:], pbank(bank)),
                          ins=[pbB[bank]], outs=[cvtb])
                        E("dve", lambda e: e.tensor_tensor(cvt[:], cvt[:], pbank(4),
                                                           OP.mult),
                          ins=[cvtb, pbB[4]], outs=[cvtb], hz=False)
                        E("dve", lambda e, mt=mt, c0=c0: e.tensor_tensor(
                            X[:, mt * T + c0:mt * T + c0 + 512],
                            X[:, mt * T + c0:mt * T + c0 + 512], cvt[:], OP.add),
                          ins=[cvtb], outs=[Xb[mt]], hz=False)
            A.release(m0)


        def l3():
            o_g1 = load_vec("l3_g1", D); o_b1 = load_vec("l3_b1", D)
            o_gm = load_vec("l3_gm", D); o_bm = load_vec("l3_bm", D)
            o_bqa = load_vec("l3_bqkv_a", 3 * D)
            o_bqs = load_vec("l3_bqkv_s", 3 * D)
            o_bpa = load_vec("l3_bp_a", D)
            o_bos = load_vec("l3_bo_s", D)
            o_bo3 = load_vec("l3_bo", D)
            o_b1m = load_vec("l3_b1m", 4 * D)
            o_b2m = load_vec("l3_b2m", D)
            o_sc3 = mslot(4)
            o_bqs8 = mslot(8)
            E("dve", lambda e: e.tensor_scalar(misc[:, o_bqs8:o_bqs8 + 8],
                                               misc[:, o_bqs:o_bqs + 8], 0.125, None,
                                               OP.mult), ins=[miscB], outs=[miscB])

            mpre = A.mark()
            layer_norm(False)
            A.release(mpre)
            do_attn3 = not (N_LAYERS == 4 and MK_PARTS == "ffn")
            do_ffn3 = not (N_LAYERS == 4 and MK_PARTS == "attn")

            def ln_of(src, srcbl, dst, dstb, o_g, o_b):
                sq2, sq2b = A.alloc((128, 512), F32, "lnq")
                for ch in range(4):
                    c0 = ch * 512
                    for k in range(PT):
                        mm(0, ones16, (0, 128, 0, 1), CB, src,
                           (0, 128, k * T + c0, k * T + c0 + 512), srcbl[k],
                           start=(k == 0), stop=(k == PT - 1), mlen=1)
                    for k in range(PT):
                        E("act", lambda e, k=k: e.activation(
                            sq2[:], src[:, k * T + c0:k * T + c0 + 512], AF.Square),
                          ins=[srcbl[k]], outs=[sq2b])
                        mm(1, onesf, (0, 128, 0, 1), CB, sq2, (0, 128, 0, 512), sq2b,
                           start=(k == 0), stop=(k == PT - 1), mlen=1)
                    E("dve", lambda e: e.tensor_scalar(misc2[0:1, 0:512], pbank(0)[0:1, :],
                                                       1.0 / D, None, OP.mult),
                      ins=[pbB[0]], outs=[misc2B])
                    mm(2, onesf, (0, 1, 0, 128), CB, misc2, (0, 1, 0, 512), misc2B,
                       True, True)
                    E("dve", lambda e: e.tensor_tensor(misc2[0:1, 0:512], misc2[0:1, 0:512],
                                                       misc2[0:1, 0:512], OP.mult),
                      ins=[misc2B], outs=[misc2B])
                    E("dve", lambda e: e.scalar_tensor_tensor(
                        misc2[0:1, 0:512], pbank(1)[0:1, :], 1.0 / D, misc2[0:1, 0:512],
                        OP.mult, OP.subtract), ins=[pbB[1], misc2B], outs=[misc2B])
                    E("act", lambda e: e.activation(misc2[0:1, 0:512], misc2[0:1, 0:512],
                                                    AF.Sqrt, bias=1e-5),
                      ins=[misc2B], outs=[misc2B])
                    E("dve", lambda e: e.reciprocal(misc2[0:1, 0:512], misc2[0:1, 0:512]),
                      ins=[misc2B], outs=[misc2B], hz=True)
                    mm(3, onesf, (0, 1, 0, 128), CB, misc2, (0, 1, 0, 512), misc2B,
                       True, True)
                    for k in range(PT):
                        E("dve", lambda e, k=k: e.tensor_tensor(
                            sq2[:], src[:, k * T + c0:k * T + c0 + 512], pbank(2),
                            OP.subtract), ins=[srcbl[k], pbB[2]], outs=[sq2b], hz=False)
                        E("dve", lambda e: e.tensor_tensor(sq2[:], sq2[:], pbank(3),
                                                           OP.mult),
                          ins=[sq2b, pbB[3]], outs=[sq2b], hz=False)
                        E("dve", lambda e, k=k: e.tensor_single_scalar(
                            sq2[:], sq2[:], misc[:, o_g + k:o_g + k + 1], OP.mult),
                          ins=[sq2b, miscB], outs=[sq2b], hz=False)
                        E("dve", lambda e, k=k: e.tensor_single_scalar(
                            dst[:, k * T + c0:k * T + c0 + 512], sq2[:],
                            misc[:, o_b + k:o_b + k + 1], OP.add),
                          ins=[sq2b, miscB], outs=[dstb], hz=False)

            if do_attn3:
                m0 = A.mark()
                xni, xnib = A.alloc((128, PT * T), BF16, "xni")
                ln_of(XN, XNb, xni, [xnib] if False else xnib, o_g1, o_b1)
                m_ao = A.mark()
                ao, aob = A.alloc((128, PT * T), BF16, "l3ao")
                E("gp", lambda e: e.dma_start(
                    out=bv3[:], in_=di["l3_bqkv_a"].ap()
                    .rearrange("(a b) -> a b", a=3)[2:3, :]),
                  outs=[bv3b], inc=16)
                msw = A.mark()
                q16, q16b = A.alloc((128, T), BF16, "swq")
                k16, k16b = A.alloc((128, T), BF16, "swk")
                vth, vthb = A.alloc((128, 16 * 65), BF16, "swv")
                ws3, ws3b = A.alloc((128, 8 * 192), BF16, "sww")
                rst, rstb = A.alloc((128, 512), F32, "swr")
                ps3 = [A.alloc((128, 128), BF16, f"swp{i}") for i in range(2)]
                nr3, nr3b = A.alloc((128, 130), BF16, "swn")
                for h in range(16):
                    for j, wnm in enumerate(("l3_Wqkv_a",)):
                        for kk in range(PT):
                            E("gp", lambda e, kk=kk, h=h: e.dma_start(
                                out=ws3[:, kk * 192:kk * 192 + 64],
                                in_=di["l3_Wqkv_a"].ap()[kk * 128:(kk + 1) * 128,
                                                         h * 64:h * 64 + 64]),
                              outs=[ws3b], inc=16)
                            E("gp", lambda e, kk=kk, h=h: e.dma_start(
                                out=ws3[:, kk * 192 + 64:kk * 192 + 128],
                                in_=di["l3_Wqkv_a"].ap()[kk * 128:(kk + 1) * 128,
                                                         1024 + h * 64:1024 + h * 64 + 64]),
                              outs=[ws3b], inc=16)
                            E("gp", lambda e, kk=kk, h=h: e.dma_start(
                                out=ws3[:, kk * 192 + 128:kk * 192 + 192],
                                in_=di["l3_Wqkv_a"].ap()[kk * 128:(kk + 1) * 128,
                                                         2048 + h * 64:2048 + h * 64 + 64]),
                              outs=[ws3b], inc=16)
                    for which, dst3, dstb3, bofs in ((0, q16, q16b, o_bqa),
                                                     (1, k16, k16b, o_bqa + 8)):
                        for ch in range(4):
                            c0 = ch * 512
                            bank = ch % 2
                            for kk in range(PT):
                                mm(bank, ws3, (0, 128, kk * 192 + which * 64,
                                               kk * 192 + which * 64 + 64), ws3b,
                                   xni, (0, 128, kk * T + c0, kk * T + c0 + 512), xnib,
                                   start=(kk == 0), stop=(kk == PT - 1), mlen=64)
                            bc = bofs + h // 2
                            hb2 = (h % 2) * 64
                            E("act", lambda e, bank=bank, bc=bc, hb2=hb2, dst3=dst3, c0=c0:
                              e.activation(dst3[0:64, c0:c0 + 512], pbank(bank)[0:64, :],
                                           AF.Identity, scale=1.0,
                                           bias=misc[hb2:hb2 + 64, bc:bc + 1]),
                              ins=[pbB[bank], miscB], outs=[dstb3])
                    E("dve", lambda e: e.memset(vth[:], 1.0), outs=[vthb], hz=False)
                    for tt in range(16):
                        bank = tt % 2 + 2
                        for kk in range(PT):
                            mm(bank, xni, (0, 128, kk * T + tt * 128, kk * T + tt * 128 + 128),
                               xnib, ws3, (0, 128, kk * 192 + 128, kk * 192 + 192), ws3b,
                               start=(kk == 0), stop=False, nlen=64)
                        mm(bank, ones16, (0, 1, 0, 128), CB, bv3,
                           (0, 1, h * 64, h * 64 + 64), bv3b, start=False, stop=True,
                           nlen=64)
                        E("act", lambda e, bank=bank, tt=tt: e.copy(
                            vth[:, tt * 65:tt * 65 + 64], pbank(bank, 0, 64)),
                          ins=[pbB[bank]], outs=[vthb])
                    E("pe", lambda e: e.matmul(
                        pbank(6)[0:128, 0:128],
                        AP(k16.ap().tensor, 0, [[T, 64], [16, 128]]),
                        AP(q16.ap().tensor, 0, [[T, 64], [16, 128]]),
                        start=True, stop=True), ins=[k16b, q16b], outs=[pbB[6]])
                    E("dve", lambda e: e.reduce_max(misc2[:, 28:29], pbank(6)[:, 0:128],
                                                    axis=AX),
                      ins=[pbB[6]], outs=[misc2B], hz=True)
                    E("sp", lambda e: e.dma_start(
                        out=flip_sp.ap()[0:1, :].rearrange("a b -> (a b)"),
                        in_=misc2[:, 28:29]), ins=[misc2B], outs=[flipD], inc=16)
                    E("sp", lambda e: e.dma_start(out=misc2[0:1, 32:160],
                                                  in_=flip_sp.ap()[0:1, :]),
                      ins=[flipD], outs=[misc2B], inc=16)
                    E("dve", lambda e: e.reduce_max(misc2[0:1, 30:31], misc2[0:1, 32:160],
                                                    axis=AX),
                      ins=[misc2B], outs=[misc2B], hz=True)
                    E("dve", lambda e: e.tensor_scalar(misc2[0:1, 30:31], misc2[0:1, 30:31],
                                                       -0.125, None, OP.mult),
                      ins=[misc2B], outs=[misc2B])
                    mm(6, onesf, (0, 1, 0, 128), CB, misc2, (0, 1, 30, 31), misc2B,
                       True, True, nlen=1)
                    E("act", lambda e: e.copy(misc[:, o_sc3:o_sc3 + 1], pbank(6, 0, 1)),
                      ins=[pbB[6]], outs=[miscB])
                    for n in range(16):
                        kts = ([n] if n == 0 else [n - 1, n])
                        nk = len(kts)
                        for j, ktn in enumerate(kts):
                            bank = j % 2
                            mm(bank, k16, (0, 64, ktn * 128, ktn * 128 + 128), k16b,
                               q16, (0, 64, n * 128, n * 128 + 128), q16b,
                               start=True, stop=True, nlen=128)
                            moff = 128 if (nk == 1 or j == 1) else 0
                            E("dve", lambda e, bank=bank, moff=moff: e.tensor_tensor(
                                pbank(bank, 0, 128), pbank(bank, 0, 128),
                                swamask[:, moff:moff + 128], OP.add),
                              ins=[CB], outs=[pbB[bank]], hz=False)
                            ps, psb = ps3[j]
                            E("act", lambda e, bank=bank, ps=ps: e.activation(
                                ps[:], pbank(bank, 0, 128), AF.Exp, scale=0.125,
                                bias=misc[:, o_sc3:o_sc3 + 1]),
                              ins=[pbB[bank], miscB], outs=[psb])
                            mm(4, vth, (0, 128, ktn * 65, ktn * 65 + 65), vthb,
                               ps, (0, 128, 0, 128), psb,
                               start=(j == 0), stop=(j == nk - 1), mlen=65, nlen=128)
                        E("dve", lambda e: e.tensor_scalar(rst[64:65, 0:128],
                                                           pbank(4)[64:65, 0:128],
                                                           1e-30, None, OP.add),
                          ins=[pbB[4]], outs=[rstb])
                        E("dve", lambda e: e.reciprocal(rst[64:65, 128:256],
                                                        rst[64:65, 0:128]),
                          ins=[rstb], outs=[rstb], hz=True)
                        E("dve", lambda e: e.tensor_copy(nr3[64:65, 0:128],
                                                         rst[64:65, 128:256]),
                          ins=[rstb], outs=[nr3b], hz=True)
                        mm(5, ones16, (64, 65, 0, 64), CB, nr3, (64, 65, 0, 128), nr3b,
                           True, True, mlen=64, nlen=128)
                        E("act", lambda e: e.copy(rst[0:64, 0:128], pbank(4)[0:64, 0:128]),
                          ins=[pbB[4]], outs=[rstb])
                        E("dve", lambda e, h=h, n=n: e.tensor_tensor(
                            ao[(h % 2) * 64:(h % 2) * 64 + 64,
                               (h // 2) * T + n * 128:(h // 2) * T + n * 128 + 128],
                            rst[0:64, 0:128], pbank(5)[0:64, 0:128], OP.mult),
                          ins=[rstb, pbB[5]], outs=[aob], hz=False)
                A.release(msw)
                wsp, wspb = A.alloc((128, PT * 1024), BF16, "wsp")
                ev3, ev3b = A.alloc((128, 512), BF16, "ev3")
                wload(wsp, wspb, "l3_Wp_a", 0, PT, 1024)
                for ch in range(4):
                    c0 = ch * 512
                    for mt in range(8):
                        bank = mt % 2
                        for kk in range(PT):
                            mm(bank, wsp, (0, 128, kk * 1024 + mt * 128,
                                           kk * 1024 + mt * 128 + 128), wspb,
                               ao, (0, 128, kk * T + c0, kk * T + c0 + 512), aob,
                               start=(kk == 0), stop=(kk == PT - 1))
                        E("act", lambda e, bank=bank, mt=mt: e.activation(
                            ev3[:], pbank(bank), AF.Identity, scale=1.0,
                            bias=misc[:, o_bpa + mt:o_bpa + mt + 1]),
                          ins=[pbB[bank], miscB], outs=[ev3b])
                        E("sp", lambda e, mt=mt, c0=c0: e.dma_start(
                            out=swp_sp.ap()[mt * 128:(mt + 1) * 128, c0:c0 + 512],
                            in_=ev3[:]), ins=[ev3b], outs=[swpD], inc=16)
                A.release(m_ao)
                ao2, ao2b = A.alloc((128, PT * T), BF16, "l3ao2")
                E("gp", lambda e: e.dma_start(
                    out=bv3[:], in_=di["l3_bqkv_s"].ap()
                    .rearrange("(a b) -> a b", a=3)[2:3, :]),
                  outs=[bv3b], inc=16)
                mln = A.mark()
                q16, q16b = A.alloc((128, T), BF16, "liq")
                k16, k16b = A.alloc((128, T), BF16, "lik")
                k32, k32b = A.alloc((128, 512), F32, "lik32")
                ktm, ktmb = A.alloc((128, 64), BF16, "liktm")
                vth, vthb = A.alloc((128, 16 * 65), BF16, "livt")
                ws3b2 = ws3, ws3b = A.alloc((128, 8 * 192), BF16, "liw")
                rst, rstb = A.alloc((128, 512), F32, "lir")
                el1, el1b = A.alloc((128, 512), F32, "liel")
                Ssb, Ssbb = A.alloc((128, 66), BF16, "liS")
                Sf, Sfb = A.alloc((128, 66), F32, "liSf")
                at3, at3b = A.alloc((128, 128), BF16, "liA")
                nr3, nr3b = A.alloc((128, 130), BF16, "lin")
                for h in range(16):
                    for kk in range(PT):
                        for j in range(3):
                            E("gp", lambda e, kk=kk, h=h, j=j: e.dma_start(
                                out=ws3[:, kk * 192 + j * 64:kk * 192 + j * 64 + 64],
                                in_=di["l3_Wqkv_s"].ap()[kk * 128:(kk + 1) * 128,
                                                         j * 1024 + h * 64:j * 1024 + h * 64 + 64]),
                              outs=[ws3b], inc=16)
                    for which, dst3, dstb3, bofs, sc in ((0, q16, q16b, o_bqs8, 0.125),
                                                         (1, k16, k16b, o_bqs + 8, 1.0)):
                        for ch in range(4):
                            c0 = ch * 512
                            bank = ch % 2
                            for kk in range(PT):
                                mm(bank, ws3, (0, 128, kk * 192 + which * 64,
                                               kk * 192 + which * 64 + 64), ws3b,
                                   xni, (0, 128, kk * T + c0, kk * T + c0 + 512), xnib,
                                   start=(kk == 0), stop=(kk == PT - 1), mlen=64)
                            bc = bofs + h // 2
                            hb2 = (h % 2) * 64
                            E("act", lambda e, bank=bank, bc=bc, hb2=hb2, sc=sc:
                              e.activation(rst[0:64, :], pbank(bank)[0:64, :],
                                           AF.Identity, scale=sc,
                                           bias=misc[hb2:hb2 + 64, bc:bc + 1]),
                              ins=[pbB[bank], miscB], outs=[rstb])
                            E("dve", lambda e: e.tensor_scalar(el1[0:64, :], rst[0:64, :],
                                                               0.0, None, OP.min),
                              ins=[rstb], outs=[el1b], hz=False)
                            E("act", lambda e: e.activation(el1[0:64, :], el1[0:64, :],
                                                            AF.Exp),
                              ins=[el1b], outs=[el1b])
                            E("dve", lambda e: e.scalar_tensor_tensor(
                                el1[0:64, :], rst[0:64, :], 0.0, el1[0:64, :],
                                OP.max, OP.add), ins=[rstb, el1b], outs=[el1b], hz=False)
                            E("dve", lambda e, dst3=dst3, c0=c0: e.tensor_scalar(
                                dst3[0:64, c0:c0 + 512], el1[0:64, :], 1e-4, None,
                                OP.add), ins=[el1b], outs=[dstb3], hz=False)
                    E("dve", lambda e: e.memset(vth[:], 1.0), outs=[vthb], hz=False)
                    for tt in range(16):
                        bank = tt % 2 + 2
                        for kk in range(PT):
                            mm(bank, xni, (0, 128, kk * T + tt * 128, kk * T + tt * 128 + 128),
                               xnib, ws3, (0, 128, kk * 192 + 128, kk * 192 + 192), ws3b,
                               start=(kk == 0), stop=False, nlen=64)
                        mm(bank, ones16, (0, 1, 0, 128), CB, bv3,
                           (0, 1, h * 64, h * 64 + 64), bv3b, start=False, stop=True,
                           nlen=64)
                        E("act", lambda e, bank=bank, tt=tt: e.copy(
                            vth[:, tt * 65:tt * 65 + 64], pbank(bank, 0, 64)),
                          ins=[pbB[bank]], outs=[vthb])
                    E("dve", lambda e: e.memset(Sf[:], 0.0), outs=[Sfb], hz=False)
                    E("dve", lambda e: e.memset(Ssb[:], 0.0), outs=[Ssbb], hz=False)
                    for n in range(16):
                        mm(0, k16, (0, 64, n * 128, n * 128 + 128), k16b,
                           q16, (0, 64, n * 128, n * 128 + 128), q16b, True, True,
                           nlen=128)
                        E("dve", lambda e: e.tensor_tensor(at3[:, 0:128], pbank(0, 0, 128),
                                                           trilT16[:], OP.mult),
                          ins=[pbB[0], CB], outs=[at3b], hz=False)
                        mm(1, Ssb, (0, 64, 0, 65), Ssbb,
                           q16, (0, 64, n * 128, n * 128 + 128), q16b,
                           start=True, stop=False, mlen=65, nlen=128)
                        mm(1, vth, (0, 128, n * 65, n * 65 + 65), vthb,
                           at3, (0, 128, 0, 128), at3b, start=False, stop=True,
                           mlen=65, nlen=128)
                        E("dve", lambda e: e.tensor_scalar(rst[64:65, 0:128],
                                                           pbank(1)[64:65, 0:128],
                                                           1e-6, None, OP.add),
                          ins=[pbB[1]], outs=[rstb])
                        E("dve", lambda e: e.reciprocal(rst[64:65, 128:256],
                                                        rst[64:65, 0:128]),
                          ins=[rstb], outs=[rstb], hz=True)
                        E("dve", lambda e: e.tensor_copy(nr3[64:65, 0:128],
                                                         rst[64:65, 128:256]),
                          ins=[rstb], outs=[nr3b], hz=True)
                        mm(5, ones16, (64, 65, 0, 64), CB, nr3, (64, 65, 0, 128), nr3b,
                           True, True, mlen=64, nlen=128)
                        E("act", lambda e: e.copy(rst[0:64, 0:128], pbank(1)[0:64, 0:128]),
                          ins=[pbB[1]], outs=[rstb])
                        E("dve", lambda e, h=h, n=n: e.tensor_tensor(
                            ao2[(h % 2) * 64:(h % 2) * 64 + 64,
                                (h // 2) * T + n * 128:(h // 2) * T + n * 128 + 128],
                            rst[0:64, 0:128], pbank(5)[0:64, 0:128], OP.mult),
                          ins=[rstb, pbB[5]], outs=[ao2b], hz=False)
                        E("act", lambda e, n=n: e.copy(k32[0:64, 0:128],
                                                       k16[0:64, n * 128:n * 128 + 128]),
                          ins=[k16b], outs=[k32b])
                        E("pe", lambda e: e.transpose(pbank(6)[0:128, 0:64],
                                                      k32[0:64, 0:128], identf[0:64, 0:64]),
                          ins=[k32b, CB], outs=[pbB[6]])
                        E("act", lambda e: e.copy(ktm[:, 0:64], pbank(6)[:, 0:64]),
                          ins=[pbB[6]], outs=[ktmb])
                        mm(7, ktm, (0, 128, 0, 64), ktmb, vth,
                           (0, 128, n * 65, n * 65 + 65), vthb, True, True,
                           mlen=64, nlen=65)
                        E("dve", lambda e: e.tensor_tensor(Sf[0:64, 0:65], Sf[0:64, 0:65],
                                                           pbank(7, 0, 65)[0:64, :],
                                                           OP.add),
                          ins=[pbB[7]], outs=[Sfb], hz=False)
                        E("dve", lambda e: e.tensor_copy(Ssb[0:64, 0:65], Sf[0:64, 0:65]),
                          ins=[Sfb], outs=[Ssbb], hz=False)
                A.release(mln)
                wsp, wspb = A.alloc((128, PT * 1024), BF16, "wos")
                ev3, ev3b = A.alloc((128, 512), F32, "ev3b")
                sw16, sw16b = A.alloc((128, 512), BF16, "sw16")
                mx16, mx16b = A.alloc((128, 512), BF16, "mx16")
                wload(wsp, wspb, "l3_Wo_s", 0, PT, 1024)
                for ch in range(4):
                    c0 = ch * 512
                    for mt in range(8):
                        bank = mt % 2
                        for kk in range(PT):
                            mm(bank, wsp, (0, 128, kk * 1024 + mt * 128,
                                           kk * 1024 + mt * 128 + 128), wspb,
                               ao2, (0, 128, kk * T + c0, kk * T + c0 + 512), ao2b,
                               start=(kk == 0), stop=(kk == PT - 1))
                        E("act", lambda e, bank=bank, mt=mt: e.activation(
                            ev3[:], pbank(bank), AF.Identity, scale=1.0,
                            bias=misc[:, o_bos + mt:o_bos + mt + 1]),
                          ins=[pbB[bank], miscB], outs=[ev3b])
                        E("sp", lambda e, mt=mt, c0=c0, sw16=sw16: e.dma_start(
                            out=sw16[:], in_=swp_sp.ap()[mt * 128:(mt + 1) * 128,
                                                         c0:c0 + 512]),
                          ins=[swpD], outs=[sw16b], inc=16)
                        E("dve", lambda e: e.tensor_tensor(mx16[:], ev3[:], sw16[:],
                                                           OP.add),
                          ins=[ev3b, sw16b], outs=[mx16b], hz=False)
                        E("sp", lambda e, mt=mt, c0=c0: e.dma_start(
                            out=mix_sp.ap()[mt * 128:(mt + 1) * 128, c0:c0 + 512],
                            in_=mx16[:]), ins=[mx16b], outs=[mixD], inc=16)
                A.release(m0)
                mx1 = A.mark()
                x1, x1b = A.alloc((128, PT * T), BF16, "l3x1")
                m0 = A.mark()
                wsp, wspb = A.alloc((128, PT * 1024), BF16, "wo3")
                mxr, mxrb = A.alloc((128, 8 * 512), BF16, "mxr")
                ev3, ev3b = A.alloc((128, 512), F32, "ev3c")
                wload(wsp, wspb, "l3_Wo", 0, PT, 1024)
                for ch in range(4):
                    c0 = ch * 512
                    for kk in range(PT):
                        E("gp", lambda e, kk=kk, c0=c0: e.dma_start(
                            out=mxr[:, kk * 512:(kk + 1) * 512],
                            in_=mix_sp.ap()[kk * 128:(kk + 1) * 128, c0:c0 + 512]),
                          ins=[mixD], outs=[mxrb], inc=16)
                    for mt in range(8):
                        bank = mt % 2
                        for kk in range(PT):
                            mm(bank, wsp, (0, 128, kk * 1024 + mt * 128,
                                           kk * 1024 + mt * 128 + 128), wspb,
                               mxr, (0, 128, kk * 512, kk * 512 + 512), mxrb,
                               start=(kk == 0), stop=(kk == PT - 1))
                        E("act", lambda e, bank=bank, mt=mt: e.activation(
                            ev3[:], pbank(bank), AF.Identity, scale=1.0,
                            bias=misc[:, o_bo3 + mt:o_bo3 + mt + 1]),
                          ins=[pbB[bank], miscB], outs=[ev3b])
                        E("dve", lambda e, mt=mt, c0=c0: e.tensor_tensor(
                            x1[:, mt * T + c0:mt * T + c0 + 512], ev3[:],
                            XN[:, mt * T + c0:mt * T + c0 + 512], OP.add),
                          ins=[ev3b, XNb[mt]], outs=[x1b], hz=False)
                        E("dve", lambda e, mt=mt, c0=c0: e.tensor_tensor(
                            X[:, mt * T + c0:mt * T + c0 + 512],
                            X[:, mt * T + c0:mt * T + c0 + 512],
                            x1[:, mt * T + c0:mt * T + c0 + 512], OP.add),
                          ins=[x1b], outs=[Xb[mt]], hz=False)
                A.release(m0)
                ln_of(x1, [x1b] * PT, x1, x1b, o_gm, o_bm)
                h2, h2b = x1, x1b
                m0 = A.mark()
                wsp, wspb = A.alloc((128, PT * 1024), BF16, "w13")
                hstg3, hstg3b = A.alloc((128, 512), BF16, "hst3")
                for half in range(4):
                    wload(wsp, wspb, "l3_W1", 0, PT, 1024, col0=half * 1024, ncols=1024)
                    for ch in range(4):
                        c0 = ch * 512
                        for mt in range(8):
                            dglob = half * 8 + mt
                            bank = mt % 2
                            for kk in range(PT):
                                mm(bank, wsp, (0, 128, kk * 1024 + mt * 128,
                                               kk * 1024 + mt * 128 + 128), wspb,
                                   h2, (0, 128, kk * T + c0, kk * T + c0 + 512), h2b,
                                   start=(kk == 0), stop=(kk == PT - 1))
                            E("act", lambda e, bank=bank, dglob=dglob: e.activation(
                                hstg3[:], pbank(bank), AF.Gelu,
                                bias=misc[:, o_b1m + dglob:o_b1m + dglob + 1]),
                              ins=[pbB[bank], miscB], outs=[hstg3b])
                            E("sp", lambda e, dglob=dglob, c0=c0: e.dma_start(
                                out=h2_sp.ap()[dglob * 128:(dglob + 1) * 128, c0:c0 + 512],
                                in_=hstg3[:]), ins=[hstg3b], outs=[h2D], inc=16)
                A.release(m0)
                m0 = A.mark()
                wsp, wspb = A.alloc((128, 16 * 1024), BF16, "w23")
                hr, hrb = A.alloc((128, 16 * 512), BF16, "hr3")
                ev3, ev3b = A.alloc((128, 512), F32, "ev3d")
                for half in range(2):
                    wload(wsp, wspb, "l3_W2", half * 16, 16, 1024)
                    for ch in range(4):
                        c0 = ch * 512
                        for kk in range(16):
                            E("sp", lambda e, kk=kk, half=half, c0=c0: e.dma_start(
                                out=hr[:, kk * 512:(kk + 1) * 512],
                                in_=h2_sp.ap()[(half * 16 + kk) * 128:
                                               (half * 16 + kk + 1) * 128, c0:c0 + 512]),
                              ins=[h2D], outs=[hrb], inc=16)
                        for mt in range(8):
                            bank = mt % 2
                            for kk in range(16):
                                mm(bank, wsp, (0, 128, kk * 1024 + mt * 128,
                                               kk * 1024 + mt * 128 + 128), wspb,
                                   hr, (0, 128, kk * 512, kk * 512 + 512), hrb,
                                   start=(kk == 0), stop=(kk == 15))
                            if half == 0:
                                E("act", lambda e, bank=bank, mt=mt: e.activation(
                                    ev3[:], pbank(bank), AF.Identity, scale=1.0,
                                    bias=misc[:, o_b2m + mt:o_b2m + mt + 1]),
                                  ins=[pbB[bank], miscB], outs=[ev3b])
                                E("dve", lambda e, mt=mt, c0=c0: e.tensor_tensor(
                                    X[:, mt * T + c0:mt * T + c0 + 512],
                                    X[:, mt * T + c0:mt * T + c0 + 512], ev3[:], OP.add),
                                  ins=[ev3b], outs=[Xb[mt]], hz=False)
                            else:
                                x_add_psum(bank, mt, c0)
                A.release(mx1)

            if not do_ffn3:
                return
            m0 = A.mark()
            wg, wgb = A.alloc((128, PT * 8), BF16, "l3wg")
            ptm3, ptm3b = A.alloc((128, 16 * 8), F32, "l3ptm")
            s3, s3b = A.alloc((128, 16), F32, "l3s")
            pfm, pfmb = A.alloc((128, 128), F32, "l3pfm")
            wload(wg, wgb, "l3f_Wg", 0, PT, 8)
            for tt in range(16):
                bank = tt % 2
                for kk in range(PT):
                    mm(bank, XN, (0, 128, kk * T + tt * 128, kk * T + tt * 128 + 128),
                       XNb[kk], wg, (0, 128, kk * 8, kk * 8 + 8), wgb,
                       start=(kk == 0), stop=(kk == PT - 1), nlen=8)
                E("dve", lambda e, bank=bank: e.reduce_max(s3[:, 0:1], pbank(bank)[:, 0:8],
                                                           axis=AX, negate=True),
                  ins=[pbB[bank]], outs=[s3b], hz=True)
                E("act", lambda e, bank=bank, tt=tt: e.activation(
                    ptm3[:, tt * 8:tt * 8 + 8], pbank(bank)[:, 0:8], AF.Exp,
                    bias=s3[:, 0:1]), ins=[pbB[bank], s3b], outs=[ptm3b])
                E("dve", lambda e, tt=tt: e.reduce_sum(s3[:, 1:2],
                                                       ptm3[:, tt * 8:tt * 8 + 8],
                                                       axis=AX),
                  ins=[ptm3b], outs=[s3b], hz=True)
                E("dve", lambda e: e.reciprocal(s3[:, 2:3], s3[:, 1:2]),
                  ins=[s3b], outs=[s3b], hz=True)
                E("dve", lambda e, tt=tt: e.tensor_single_scalar(
                    ptm3[:, tt * 8:tt * 8 + 8], ptm3[:, tt * 8:tt * 8 + 8], s3[:, 2:3],
                    OP.mult), ins=[s3b], outs=[ptm3b])
                E("pe", lambda e, tt=tt: e.transpose(pbank(4)[0:8, 0:128],
                                                     ptm3[:, tt * 8:tt * 8 + 8],
                                                     identf[:]),
                  ins=[ptm3b, CB], outs=[pbB[4]])
                E("act", lambda e: e.copy(pfm[0:8, 0:128], pbank(4)[0:8, 0:128]),
                  ins=[pbB[4]], outs=[pfmb])
                E("sp", lambda e, tt=tt: e.dma_start(
                    out=ag_in.ap()[:, tt * 128:(tt + 1) * 128], in_=pfm[0:8, 0:128]),
                  ins=[pfmb], outs=[agD], inc=16)
            E("gp", lambda e: e.collective_compute(
                "AllGather", OP.bypass, replica_groups=[list(range(8))],
                ins=[ag_in.ap().opt()], outs=[ag_out.ap().opt()]),
              ins=[agD], outs=[agD2], isasync=True)
            pall, pallb = A.alloc((128, 1024), F32, "l3pall")
            for hh in range(2):
                E("sp", lambda e, hh=hh: e.dma_start(
                    out=pall[hh * 64:(hh + 1) * 64, :],
                    in_=AP(ag_out.ap().tensor, hh * 1024,
                           [[T, 8], [8 * T, 8], [1, 1024]])),
                  ins=[agD2], outs=[pallb], inc=16)
            lo_c, hi_c, mid_c, cnt_c, prd_c = 4, 5, 6, 7, 8
            E("dve", lambda e: e.memset(s3[:, lo_c:lo_c + 1], 0.0), outs=[s3b], hz=False)
            E("dve", lambda e: e.memset(s3[:, hi_c:hi_c + 1], 1.0), outs=[s3b], hz=False)
            cm, cmb = A.alloc((128, 1024), F32, "l3cm")
            for it in range(48):
                E("dve", lambda e: e.scalar_tensor_tensor(
                    s3[:, mid_c:mid_c + 1], s3[:, lo_c:lo_c + 1], 0.5,
                    s3[:, hi_c:hi_c + 1], OP.add, OP.bypass) if False else
                    e.tensor_tensor(s3[:, mid_c:mid_c + 1], s3[:, lo_c:lo_c + 1],
                                    s3[:, hi_c:hi_c + 1], OP.add),
                  ins=[s3b], outs=[s3b])
                E("dve", lambda e: e.tensor_scalar(s3[:, mid_c:mid_c + 1],
                                                   s3[:, mid_c:mid_c + 1], 0.5, None,
                                                   OP.mult), ins=[s3b], outs=[s3b])
                E("dve", lambda e: e.tensor_single_scalar(cm[:], pall[:],
                                                          s3[:, mid_c:mid_c + 1],
                                                          OP.is_ge),
                  ins=[pallb, s3b], outs=[cmb])
                E("dve", lambda e: e.reduce_sum(s3[:, cnt_c:cnt_c + 1], cm[:], axis=AX),
                  ins=[cmb], outs=[s3b], hz=True)
                mm(6, blk16, (0, 128, 0, 8), CB, s3, (0, 128, cnt_c, cnt_c + 1), s3b,
                   True, True, mlen=8, nlen=1)
                E("act", lambda e: e.copy(s3[0:8, 15:16], pbank(6, 0, 1)[0:8, :]),
                  ins=[pbB[6]], outs=[s3b])
                mm(7, blk8, (0, 8, 0, 128), CB, s3, (0, 8, 15, 16), s3b,
                   True, True, mlen=128, nlen=1)
                E("dve", lambda e: e.tensor_scalar(s3[:, cnt_c:cnt_c + 1],
                                                   pbank(7, 0, 1), 2048.0, None,
                                                   OP.is_ge),
                  ins=[pbB[7]], outs=[s3b])
                E("dve", lambda e: e.tensor_tensor(s3[:, prd_c:prd_c + 1],
                                                   s3[:, mid_c:mid_c + 1],
                                                   s3[:, lo_c:lo_c + 1], OP.subtract),
                  ins=[s3b], outs=[s3b])
                E("dve", lambda e: e.tensor_tensor(s3[:, prd_c:prd_c + 1],
                                                   s3[:, prd_c:prd_c + 1],
                                                   s3[:, cnt_c:cnt_c + 1], OP.mult),
                  ins=[s3b], outs=[s3b])
                E("dve", lambda e: e.tensor_tensor(s3[:, lo_c:lo_c + 1],
                                                   s3[:, lo_c:lo_c + 1],
                                                   s3[:, prd_c:prd_c + 1], OP.add),
                  ins=[s3b], outs=[s3b])
                E("dve", lambda e: e.tensor_scalar(s3[:, 9:10], s3[:, cnt_c:cnt_c + 1],
                                                   -1.0, 1.0, OP.mult, OP.add),
                  ins=[s3b], outs=[s3b])
                E("dve", lambda e: e.tensor_tensor(s3[:, prd_c:prd_c + 1],
                                                   s3[:, hi_c:hi_c + 1],
                                                   s3[:, mid_c:mid_c + 1], OP.subtract),
                  ins=[s3b], outs=[s3b])
                E("dve", lambda e: e.tensor_tensor(s3[:, prd_c:prd_c + 1],
                                                   s3[:, prd_c:prd_c + 1],
                                                   s3[:, 9:10], OP.mult),
                  ins=[s3b], outs=[s3b])
                E("dve", lambda e: e.tensor_tensor(s3[:, hi_c:hi_c + 1],
                                                   s3[:, hi_c:hi_c + 1],
                                                   s3[:, prd_c:prd_c + 1], OP.subtract),
                  ins=[s3b], outs=[s3b])
            if os.environ.get("MK_SCANDBG", "") == "th":
                E("sp", lambda e: e.dma_start(
                    out=flip_sp.ap()[0:1, :].rearrange("a b -> (a b)"),
                    in_=s3[:, lo_c:lo_c + 1]), ins=[s3b], outs=[flipD], inc=16)
                E("sp", lambda e: e.dma_start(out=dbg_d.ap()[0:1, 0:64],
                                              in_=flip_sp.ap()[0:1, 0:64]),
                  ins=[flipD], inc=16)
            mm(6, blk16, (0, 128, 0, 8), CB, s3, (0, 128, lo_c, lo_c + 1), s3b,
               True, True, mlen=8, nlen=1)
            E("act", lambda e: e.activation(pfm[0:8, 0:1], pbank(6, 0, 1)[0:8, :],
                                            AF.Identity, scale=1.0 / 16.0, bias=0.0),
              ins=[pbB[6]], outs=[pfmb])
            E("pe", lambda e: e.transpose(pbank(4)[0:1, 0:8], pfm[0:8, 0:1],
                                          identf[0:8, 0:8]),
              ins=[pfmb, CB], outs=[pbB[4]])
            E("act", lambda e: e.copy(pfm[0:1, 8:16], pbank(4)[0:1, 0:8]),
              ins=[pbB[4]], outs=[pfmb])
            mm(5, onesf, (0, 1, 0, 128), CB, pfm, (0, 1, 8, 16), pfmb, True, True, nlen=8)
            g1t, g1tb = A.alloc((128, 8), F32, "l3g1")
            g2t, g2tb = A.alloc((128, 8), F32, "l3g2")
            gfm3, gfm3b = A.alloc((128, 128), BF16, "l3gf")
            for tt in range(16):
                E("dve", lambda e, tt=tt: e.tensor_tensor(
                    g1t[:], ptm3[:, tt * 8:tt * 8 + 8], pbank(5, 0, 8), OP.is_ge),
                  ins=[ptm3b, pbB[5]], outs=[g1tb])
                E("dve", lambda e, tt=tt: e.tensor_tensor(
                    g2t[:], ptm3[:, tt * 8:tt * 8 + 8], g1t[:], OP.mult),
                  ins=[ptm3b, g1tb], outs=[g2tb])
                E("dve", lambda e: e.tensor_scalar(g1t[:], g1t[:], 1e30, -1e30,
                                                   OP.mult, OP.add),
                  ins=[g1tb], outs=[g1tb])
                E("dve", lambda e: e.tensor_tensor(g1t[:], g2t[:], g1t[:], OP.add),
                  ins=[g2tb], outs=[g1tb])
                E("dve", lambda e: e.reduce_max(s3[:, 9:10], g1t[:], axis=AX),
                  ins=[g1tb], outs=[s3b], hz=True)
                E("dve", lambda e: e.tensor_scalar(s3[:, 10:11], s3[:, 9:10], -1e29,
                                                   None, OP.is_gt),
                  ins=[s3b], outs=[s3b])
                E("dve", lambda e, tt=tt: e.reduce_max(s3[:, 11:12],
                                                       ptm3[:, tt * 8:tt * 8 + 8],
                                                       axis=AX),
                  ins=[ptm3b], outs=[s3b], hz=True)
                E("dve", lambda e: e.tensor_tensor(s3[:, 12:13], s3[:, 11:12],
                                                   s3[:, 9:10], OP.subtract),
                  ins=[s3b], outs=[s3b])
                E("dve", lambda e: e.tensor_tensor(s3[:, 12:13], s3[:, 12:13],
                                                   s3[:, 10:11], OP.mult),
                  ins=[s3b], outs=[s3b])
                E("dve", lambda e: e.tensor_tensor(s3[:, 12:13], s3[:, 11:12],
                                                   s3[:, 12:13], OP.subtract),
                  ins=[s3b], outs=[s3b])
                E("dve", lambda e, tt=tt: e.tensor_tensor(
                    g2t[:], g1t[:], ptm3[:, tt * 8:tt * 8 + 8], OP.subtract),
                  ins=[g1tb, ptm3b], outs=[g2tb])
                E("dve", lambda e: e.tensor_single_scalar(g2t[:], g2t[:], s3[:, 10:11],
                                                          OP.mult),
                  ins=[s3b], outs=[g2tb])
                E("dve", lambda e, tt=tt: e.tensor_tensor(
                    g2t[:], g2t[:], ptm3[:, tt * 8:tt * 8 + 8], OP.add),
                  ins=[ptm3b], outs=[g2tb])
                E("dve", lambda e: e.tensor_single_scalar(g2t[:], g2t[:], s3[:, 12:13],
                                                          OP.is_ge),
                  ins=[s3b], outs=[g2tb])
                E("dve", lambda e, tt=tt: e.tensor_tensor(
                    g2t[:], g2t[:], ptm3[:, tt * 8:tt * 8 + 8], OP.mult),
                  ins=[ptm3b], outs=[g2tb])
                E("pe", lambda e: e.transpose(pbank(4)[0:8, 0:128], g2t[:, 0:8],
                                              identf[:]),
                  ins=[g2tb, CB], outs=[pbB[4]])
                E("act", lambda e, tt=tt: e.copy(gfm3[0:8, 0:128], pbank(4)[0:8, 0:128]),
                  ins=[pbB[4]], outs=[gfm3b])
                E("sp", lambda e, tt=tt: e.dma_start(
                    out=gates_sp.ap()[0:8, tt * 128:(tt + 1) * 128], in_=gfm3[0:8, 0:128]),
                  ins=[gfm3b], outs=[gD], inc=16)
            A.release(m0)
            m0 = A.mark()
            w1e, w1eb = A.alloc((128, 8 * 2048), BF16, "l3w1e")
            w2e, w2eb = A.alloc((128, 16 * 1024), BF16, "l3w2e")
            hech, hechb = A.alloc((128, 16 * 512), BF16, "l3hech")
            grow, growb = A.alloc((1, T), BF16, "l3grow")
            cve, cveb = A.alloc((128, 512), F32, "l3cve")
            for e_ in range(8):
                for i in range(PT):
                    E("gp", lambda e, e_=e_, i=i: e.dma_start(
                        out=w1e[:, i * 2048:(i + 1) * 2048],
                        in_=di["l3f_W1"].ap()[e_, i * 128:(i + 1) * 128, :]),
                      outs=[w1eb], inc=16)
                for i in range(16):
                    E("gp", lambda e, e_=e_, i=i: e.dma_start(
                        out=w2e[:, i * 1024:(i + 1) * 1024],
                        in_=di["l3f_W2"].ap()[e_, i * 128:(i + 1) * 128, :]),
                      outs=[w2eb], inc=16)
                E("gp", lambda e, e_=e_: e.dma_start(
                    out=grow[:], in_=gates_sp.ap()[e_:e_ + 1, :]),
                  ins=[gD], outs=[growb], inc=16)
                for ch in range(4):
                    c0 = ch * 512
                    for mt in range(16):
                        bank = mt % 2
                        for kk in range(PT):
                            mm(bank, w1e, (0, 128, kk * 2048 + mt * 128,
                                           kk * 2048 + mt * 128 + 128), w1eb,
                               XN, (0, 128, kk * T + c0, kk * T + c0 + 512), XNb[kk],
                               start=(kk == 0), stop=(kk == PT - 1))
                        E("act", lambda e, bank=bank, mt=mt: e.activation(
                            hech[:, mt * 512:(mt + 1) * 512], pbank(bank), AF.Gelu),
                          ins=[pbB[bank]], outs=[hechb])
                    mm(4, ones16, (0, 1, 0, 128), CB, grow, (0, 1, c0, c0 + 512), growb,
                       True, True)
                    for mt in range(8):
                        bank = 2 + mt % 2
                        for kk in range(16):
                            mm(bank, w2e, (0, 128, kk * 1024 + mt * 128,
                                           kk * 1024 + mt * 128 + 128), w2eb,
                               hech, (0, 128, kk * 512, kk * 512 + 512), hechb,
                               start=(kk == 0), stop=(kk == 15))
                        E("act", lambda e, bank=bank: e.copy(cve[:], pbank(bank)),
                          ins=[pbB[bank]], outs=[cveb])
                        E("dve", lambda e: e.tensor_tensor(cve[:], cve[:], pbank(4),
                                                           OP.mult),
                          ins=[cveb, pbB[4]], outs=[cveb], hz=False)
                        E("dve", lambda e, mt=mt, c0=c0: e.tensor_tensor(
                            X[:, mt * T + c0:mt * T + c0 + 512],
                            X[:, mt * T + c0:mt * T + c0 + 512], cve[:], OP.add),
                          ins=[cveb], outs=[Xb[mt]], hz=False)
            A.release(m0)

        try:
            l0()
            if nlayers >= 2:
                l1()
            if nlayers >= 3:
                l2()
            if nlayers >= 4:
                l3()
        except _SkipRest:
            pass

        fin = Buf()
        for k in range(PT):
            E("sp", lambda e, k=k: e.dma_start(
                out=out_d.ap()[k * 128:(k + 1) * 128, :], in_=X[:, k * T:(k + 1) * T]),
              ins=[Xb[k]], outs=[fin], inc=16)
        nc.sync.wait_ge(P.sem["sp"], P.cnt["sp"])
        nc.vector.wait_ge(P.sem["sp"], P.cnt["sp"])

    return nc


def _prep_inputs(inputs, nlayers):
    ct = host_consts()
    w = {}
    w.update({k: np.ascontiguousarray(v) for k, v in ct.items()})
    f = lambda a: np.ascontiguousarray(np.asarray(a, dtype=np.float32))
    tr = lambda a: np.ascontiguousarray(np.asarray(a, dtype=np.float32).T)
    w["ipw1T"] = tr(inputs["L0_attn_in_proj_w"][:2048])
    w["ipw2T"] = tr(inputs["L0_attn_in_proj_w"][2048:])
    w["conv_w"] = f(inputs["L0_attn_conv_w"])
    w["conv_b"] = f(inputs["L0_attn_conv_b"])
    w["dt_wT"] = tr(inputs["L0_attn_dt_w"])
    w["dt_b"] = f(inputs["L0_attn_dt_b"])
    w["A_log"] = f(inputs["L0_attn_A_log"])
    w["B_wT"] = tr(inputs["L0_attn_B_w"])
    w["C_wT"] = tr(inputs["L0_attn_C_w"])
    w["out_wT"] = tr(inputs["L0_attn_out_w"])
    w["WrT"] = tr(inputs["L0_ffn_W_r"])
    w["WkT"] = tr(inputs["L0_ffn_W_k"])
    w["WvT"] = tr(inputs["L0_ffn_W_v"])
    w["mu_r"] = f(inputs["L0_ffn_mu_r"])
    w["mu_k"] = f(inputs["L0_ffn_mu_k"])
    if nlayers >= 2:
        W1 = np.asarray(inputs["L1_attn_W1"], np.float32)
        w["l1_wqT"] = tr(W1[0:1024]); w["l1_wkT"] = tr(W1[1024:2048]); w["l1_wvT"] = tr(W1[2048:3072])
        w["l1_w2T"] = tr(inputs["L1_attn_W2"])
        w["l1f_w1T"] = tr(inputs["L1_ffn_w1"]); w["l1f_b1"] = f(inputs["L1_ffn_b1"])
        w["l1f_w2T"] = tr(inputs["L1_ffn_w2"]); w["l1f_b2"] = f(inputs["L1_ffn_b2"])
    if nlayers >= 3:
        qkv = np.asarray(inputs["L2_attn_qkv_w"], np.float32)
        qkb = np.asarray(inputs["L2_attn_qkv_b"], np.float32)
        perm = np.concatenate([h * 64 + np.concatenate([np.arange(0, 64, 2),
                                                        np.arange(1, 64, 2)])
                               for h in range(16)])
        w["l2_wqT"] = tr(qkv[0:1024][perm])
        w["l2_wkT"] = tr(qkv[1024:2048][perm])
        w["l2_wvT"] = tr(qkv[2048:3072])
        w["l2_bq"] = f(qkb[0:1024][perm]); w["l2_bk"] = f(qkb[1024:2048][perm])
        w["l2_bv"] = f(qkb[2048:3072])
        w["l2_woT"] = tr(inputs["L2_attn_out_w"]); w["l2_bo"] = f(inputs["L2_attn_out_b"])
        w["l2f_ws1"] = f(inputs["L2_ffn_ws1"]); w["l2f_bs1"] = f(inputs["L2_ffn_bs1"])
        w["l2f_ws2"] = f(inputs["L2_ffn_ws2"]); w["l2f_bs2"] = f(inputs["L2_ffn_bs2"])
        w["l2f_We1"] = f(inputs["L2_ffn_We1"]); w["l2f_Be1"] = f(inputs["L2_ffn_Be1"])
        w["l2f_We2"] = f(inputs["L2_ffn_We2"]); w["l2f_Be2"] = f(inputs["L2_ffn_Be2"])
        w["l2f_Wr"] = f(inputs["L2_ffn_Wr"]); w["l2f_br"] = f(inputs["L2_ffn_br"])
    if nlayers >= 4:
        w["l3_g1"] = f(inputs["L3_attn_g1"]); w["l3_b1"] = f(inputs["L3_attn_b1"])
        w["l3_Wqkv_a"] = f(inputs["L3_attn_Wqkv_a"]); w["l3_bqkv_a"] = f(inputs["L3_attn_bqkv_a"])
        w["l3_Wp_a"] = f(inputs["L3_attn_Wp_a"]); w["l3_bp_a"] = f(inputs["L3_attn_bp_a"])
        w["l3_Wqkv_s"] = f(inputs["L3_attn_Wqkv_s"]); w["l3_bqkv_s"] = f(inputs["L3_attn_bqkv_s"])
        w["l3_Wo_s"] = f(inputs["L3_attn_Wo_s"]); w["l3_bo_s"] = f(inputs["L3_attn_bo_s"])
        w["l3_Wo"] = f(inputs["L3_attn_Wo"]); w["l3_bo"] = f(inputs["L3_attn_bo"])
        w["l3_gm"] = f(inputs["L3_attn_gm"]); w["l3_bm"] = f(inputs["L3_attn_bm"])
        w["l3_W1"] = f(inputs["L3_attn_W1"]); w["l3_b1m"] = f(inputs["L3_attn_b1m"])
        w["l3_W2"] = f(inputs["L3_attn_W2"]); w["l3_b2m"] = f(inputs["L3_attn_b2m"])
        w["l3f_Wg"] = f(inputs["L3_ffn_Wg"])
        w["l3f_W1"] = f(inputs["L3_ffn_W1"]); w["l3f_W2"] = f(inputs["L3_ffn_W2"])
    return w


def kernel(**inputs):
    nlayers = N_LAYERS
    w = _prep_inputs(inputs, nlayers)
    x = np.asarray(inputs["L0_attn_x"], np.float32)
    nc = build_program(nlayers)
    in_maps = []
    for c in range(8):
        m = dict(w)
        m["x_fm"] = np.ascontiguousarray(x[c].T)
        in_maps.append(m)
    res = run_bass_kernel_spmd(nc, in_maps, core_ids=list(range(8)))
    global _LAST_RES
    _LAST_RES = res
    out = np.stack([np.ascontiguousarray(res.results[c]["out_fm"].T) for c in range(8)])
    return out


# version 02 (final)
# speedup vs baseline: 1.0065x; 1.0065x over previous
import os
import numpy as np
import concourse.bass as bass
import concourse.mybir as mybir
from concourse.alu_op_type import AluOpType
from concourse.bass_types import AP
from concourse.bass_utils import run_bass_kernel_spmd

F32 = mybir.dt.float32
BF16 = mybir.dt.bfloat16
AF = mybir.ActivationFunctionType
AX = mybir.AxisListType.X
OP = AluOpType
MAGIC = float(1.5 * 2 ** 23)
NEG = -1.0e30
FCONST_VALS = [1e-5, 1.0, -1.0, 0.125, MAGIC, -MAGIC, 1e-4, 0.0]

T = 2048
D = 1024
PT = D // 128
N_LAYERS = int(os.environ.get("MK_NLAYERS", "4"))
MK_PARTS = os.environ.get("MK_PARTS", "both")

BASE = 16512
X_B = BASE + 21504
XN_B = X_B + 65536
ARENA_B = XN_B + 32768
ARENA_E = 229344


class _SkipRest(Exception):
    pass


class Buf:
    __slots__ = ("w", "r")
    def __init__(self, deps=()):
        self.w = None
        self.r = list(deps)


MK_SYNC = os.environ.get("MK_SYNC", "0") == "1"


class Prog:
    ENGS = ("pe", "dve", "act", "gp", "sp")

    def __init__(self, nc):
        self.nc = nc
        self.h = {"pe": nc.tensor, "dve": nc.vector, "act": nc.scalar,
                  "gp": nc.gpsimd, "sp": nc.sync}
        self.sem = {}
        self.cnt = {e: 0 for e in self.ENGS}
        self.lastw = {e: {} for e in self.ENGS}
        self.bub = None
        self.dve_seq = 0
        self.dve_hist = {}

    def init_sems(self, stack):
        for e in self.ENGS:
            self.sem[e] = stack.enter_context(self.nc.semaphore("sem_" + e))

    def _wait(self, eng, tok):
        if tok is None:
            return
        e2, v, isasync = tok
        if e2 == eng and not isasync:
            return
        if isasync:
            v = self.cnt[e2]
        if self.lastw[eng].get(e2, 0) >= v:
            return
        self.h[eng].wait_ge(self.sem[e2], v)
        self.lastw[eng][e2] = v

    def emit(self, eng, fn, ins=(), outs=(), inc=1, isasync=None, hz=True, hzin=False):
        if MK_SYNC:
            for e2 in self.ENGS:
                if e2 != eng and self.cnt[e2] > self.lastw[eng].get(e2, 0):
                    self.h[eng].wait_ge(self.sem[e2], self.cnt[e2])
                    self.lastw[eng][e2] = self.cnt[e2]
        for b in ins:
            if b is not None:
                self._wait(eng, b.w)
        for b in outs:
            if b is not None:
                self._wait(eng, b.w)
                for t in b.r:
                    self._wait(eng, t)
        if eng == "dve" and self.bub is not None:
            need = 0
            for b in list(ins) + list(outs):
                if b is not None and b.w is not None and b.w[0] == "dve":
                    info = self.dve_hist.get(b.w[1])
                    if info is not None and info[1]:
                        d = self.dve_seq - info[0]
                        if d < 5:
                            need = max(need, 5 - d)
            for _ in range(need):
                self.h["dve"].memset(self.bub, 0.0)
                self.dve_seq += 1
        instr = fn(self.h[eng])
        instr.then_inc(self.sem[eng], inc)
        self.cnt[eng] += inc
        if isasync is None:
            isasync = inc >= 16
        tok = (eng, self.cnt[eng], isasync)
        if eng == "dve":
            self.dve_seq += 1
            self.dve_hist[self.cnt[eng]] = (self.dve_seq, hz)
            if len(self.dve_hist) > 64:
                for k in list(self.dve_hist)[:32]:
                    del self.dve_hist[k]
        for b in ins:
            if b is not None:
                b.r.append(tok)
        for b in outs:
            if b is not None:
                b.w = tok
                b.r = []
        return tok


class Arena:
    def __init__(self, nc, base, end):
        self.nc = nc
        self.base = base
        self.end = end
        self.off = base
        self.uid = 0
        self.tenants = []

    def alloc(self, shape, dtype, tag=""):
        esz = 4 if dtype == F32 else 2
        nbytes = (shape[1] * esz + 31) & ~31
        start, stop = self.off, self.off + nbytes
        assert stop <= self.end, \
            f"arena overflow {tag}: need {stop - self.base}, have {self.end - self.base}"
        self.uid += 1
        t = self.nc.alloc_sbuf_tensor_at(f"ar_{tag}_{self.uid}", list(shape), dtype,
                                         offset=start)
        deps = []
        for (s0, s1, b) in self.tenants:
            if s0 < stop and start < s1:
                if b.w is not None:
                    deps.append(b.w)
                deps.extend(b.r)
        buf = Buf(deps)
        self.tenants.append((start, stop, buf))
        self.off = stop
        return t, buf

    def mark(self):
        return self.off

    def release(self, m):
        self.off = m
        if len(self.tenants) > 512:
            self.tenants = self.tenants[-256:]


def host_consts():
    c = {}
    c["identf"] = np.eye(128, dtype=np.float32)
    c["onesf"] = np.ones((128, 128), np.float32)
    invf = (1.0 / 10000.0 ** (np.arange(0, 64, 2, dtype=np.float64) / 64.0))
    ang = np.arange(T, dtype=np.float64)[None, :] * np.tile(invf, 4)[:, None]
    import ml_dtypes
    c["ropec"] = np.cos(ang).astype(ml_dtypes.bfloat16)
    c["ropes"] = np.sin(ang).astype(ml_dtypes.bfloat16)
    kk = np.arange(128)[:, None]
    uu = np.arange(896)[None, :]
    m = np.where(kk <= uu - 384, 0.0, NEG).astype(np.float32)
    import ml_dtypes
    c["l1mask"] = m.astype(ml_dtypes.bfloat16)
    kk = np.arange(128)[:, None]
    qq = np.arange(128)[None, :]
    sm = np.zeros((128, 256), np.float32)
    sm[:, 0:128] = np.where(kk > qq, 0.0, NEG)
    sm[:, 128:256] = np.where(kk <= qq, 0.0, NEG)
    c["swamask"] = sm.astype(ml_dtypes.bfloat16)
    c["trilT"] = np.where(kk <= qq, 1.0, 0.0).astype(np.float32)
    blk16 = np.zeros((128, 8), np.float32)
    for p in range(128):
        blk16[p, (p % 64) // 8] = 1.0
    c["blk16"] = blk16
    c["blk8"] = np.ascontiguousarray(blk16.T)
    fc = np.array(FCONST_VALS, np.float32)
    c["fconsts"] = np.tile(fc[None, :], (128, 1))
    return c


def build_program(nlayers):
    nc = bass.Bass("TRN2", target_bir_lowering=False)
    di = {}

    def dram_in(name, shape, dt=F32):
        di[name] = nc.dram_tensor(name, list(shape), dt, kind="ExternalInput")

    dram_in("x_fm", (D, T))
    dram_in("ipw1T", (D, 2048)); dram_in("ipw2T", (D, 2048))
    dram_in("conv_w", (2048, 4)); dram_in("conv_b", (2048,))
    dram_in("dt_wT", (2048, 2048)); dram_in("dt_b", (2048,))
    dram_in("A_log", (2048, 16))
    dram_in("B_wT", (2048, 16)); dram_in("C_wT", (2048, 16))
    dram_in("out_wT", (2048, D))
    dram_in("WrT", (D, D)); dram_in("WkT", (D, D)); dram_in("WvT", (D, D))
    dram_in("mu_r", (D,)); dram_in("mu_k", (D,))
    if nlayers >= 2:
        dram_in("l1_wqT", (D, D)); dram_in("l1_wkT", (D, D)); dram_in("l1_wvT", (D, D))
        dram_in("l1_w2T", (D, D))
        dram_in("l1f_w1T", (D, 4096)); dram_in("l1f_b1", (4096,))
        dram_in("l1f_w2T", (4096, D)); dram_in("l1f_b2", (D,))
    if nlayers >= 3:
        dram_in("l2_wqT", (D, D)); dram_in("l2_wkT", (D, D)); dram_in("l2_wvT", (D, D))
        dram_in("l2_bq", (D,)); dram_in("l2_bk", (D,)); dram_in("l2_bv", (D,))
        dram_in("l2_woT", (D, D)); dram_in("l2_bo", (D,))
        dram_in("l2f_ws1", (D, D)); dram_in("l2f_bs1", (D,))
        dram_in("l2f_ws2", (D, D)); dram_in("l2f_bs2", (D,))
        dram_in("l2f_We1", (31, D, D)); dram_in("l2f_Be1", (31, D))
        dram_in("l2f_We2", (31, D, D)); dram_in("l2f_Be2", (31, D))
        dram_in("l2f_Wr", (D, 31)); dram_in("l2f_br", (31,))
    if nlayers >= 4:
        dram_in("l3_g1", (D,)); dram_in("l3_b1", (D,))
        dram_in("l3_Wqkv_a", (D, 3 * D)); dram_in("l3_bqkv_a", (3 * D,))
        dram_in("l3_Wp_a", (D, D)); dram_in("l3_bp_a", (D,))
        dram_in("l3_Wqkv_s", (D, 3 * D)); dram_in("l3_bqkv_s", (3 * D,))
        dram_in("l3_Wo_s", (D, D)); dram_in("l3_bo_s", (D,))
        dram_in("l3_Wo", (D, D)); dram_in("l3_bo", (D,))
        dram_in("l3_gm", (D,)); dram_in("l3_bm", (D,))
        dram_in("l3_W1", (D, 4 * D)); dram_in("l3_b1m", (4 * D,))
        dram_in("l3_W2", (4 * D, D)); dram_in("l3_b2m", (D,))
        dram_in("l3f_Wg", (D, 8))
        dram_in("l3f_W1", (8, D, 2048)); dram_in("l3f_W2", (8, 2048, D))
    dram_in("identf", (128, 128)); dram_in("onesf", (128, 128))
    dram_in("ropec", (128, T), BF16); dram_in("ropes", (128, T), BF16)
    dram_in("l1mask", (128, 896), BF16); dram_in("swamask", (128, 256), BF16)
    dram_in("trilT", (128, 128)); dram_in("blk16", (128, 8)); dram_in("blk8", (8, 128))
    dram_in("fconsts", (128, 8))

    out_d = nc.dram_tensor("out_fm", [D, T], F32, kind="ExternalOutput")
    dbg_d = nc.dram_tensor("dbg", [1, 64], F32, kind="ExternalOutput")

    xin_sp = nc.dram_tensor("xin_sp", [2048, T], BF16)
    bc_sp = nc.dram_tensor("bc_sp", [16, 2 * T], BF16)
    dt_sp = nc.dram_tensor("dt_sp", [2048, T], BF16)
    y_sp = nc.dram_tensor("y_sp", [2048, T], BF16)
    h_sp = nc.dram_tensor("h_sp", [4096, T], F32)
    xq_sp = nc.dram_tensor("xq_sp", [D, T], BF16)
    gates_sp = nc.dram_tensor("gates_sp", [31, T], BF16)
    swp_sp = nc.dram_tensor("swp_sp", [D, T], BF16)
    mix_sp = nc.dram_tensor("mix_sp", [D, T], BF16)
    h2_sp = nc.dram_tensor("h2_sp", [4096, T], BF16)
    w2q_sp = nc.dram_tensor("w2q_sp", [4096, D], BF16)
    ar_in = nc.dram_tensor("ar_in", [1, 4], F32)
    ar_out = nc.dram_tensor("ar_out", [1, 4], F32)
    ar2_in = nc.dram_tensor("ar2_in", [1, 4], F32)
    ar2_out = nc.dram_tensor("ar2_out", [1, 4], F32)
    flip_sp = nc.dram_tensor("flip_sp", [4, 128], F32)
    ag_in = nc.dram_tensor("ag_in", [8, T], F32)
    ag_out = nc.dram_tensor("ag_out", [64, T], F32)

    import contextlib
    stack = contextlib.ExitStack()
    with stack:
        P = Prog(nc)
        P.init_sems(stack)
        E = P.emit

        off = BASE
        def fixed(name, shape, dtype):
            nonlocal off
            t = nc.alloc_sbuf_tensor_at(name, list(shape), dtype, offset=off)
            off += (shape[1] * (4 if dtype == F32 else 2) + 31) & ~31
            return t

        identf = fixed("identf_s", (128, 128), F32)
        onesf = fixed("onesf_s", (128, 128), F32)
        ones16 = fixed("ones16_s", (128, 128), BF16)
        ropec = fixed("ropec_s", (128, T), BF16)
        ropes = fixed("ropes_s", (128, T), BF16)
        l1mask = fixed("l1mask_s", (128, 896), BF16)
        swamask = fixed("swamask_s", (128, 256), BF16)
        trilT16 = fixed("trilT16_s", (128, 128), BF16)
        bv3 = fixed("bv3_s", (1, 1024), BF16)
        blk16 = fixed("blk16_s", (128, 8), F32)
        blk8 = fixed("blk8_s", (8, 128), F32)
        fconsts = fixed("fconsts_s", (128, 8), F32)
        misc = fixed("misc_s", (128, 1100), F32)
        misc2 = fixed("misc2_s", (128, 512), F32)
        assert off <= X_B, f"const region overflow: {off}"
        P.bub = misc[0:1, 1096:1100]

        X = nc.alloc_sbuf_tensor_at("X_s", [128, PT * T], F32, offset=X_B)
        XN = nc.alloc_sbuf_tensor_at("XN_s", [128, PT * T], BF16, offset=XN_B)
        Xb = [Buf() for _ in range(PT)]
        XNb = [Buf() for _ in range(PT)]

        A = Arena(nc, ARENA_B, ARENA_E)

        pbig = stack.enter_context(nc.psum_tensor("pbig", [128, 4096], F32))
        pbB = [Buf() for _ in range(8)]
        def pbank(i, n0=0, nlen=512):
            return pbig[:, i * 512 + n0:i * 512 + n0 + nlen]

        CB = Buf()
        flipD = Buf()
        xinD, dtD, yD, hD, bcD = Buf(), Buf(), Buf(), Buf(), Buf()
        xqD, w2qD, gD = Buf(), Buf(), Buf()
        swpD, mixD, h2D, agD, agD2 = Buf(), Buf(), Buf(), Buf(), Buf()
        miscB = Buf()
        misc2B = Buf()

        blk = stack.enter_context(nc.Block())

        for i, v in enumerate(FCONST_VALS):
            nc.const_aps.register(F32, float(v), fconsts[:, i:i + 1])
        for t_, nm in ((identf, "identf"), (onesf, "onesf"), (ropec, "ropec"),
                       (ropes, "ropes"), (l1mask, "l1mask"), (swamask, "swamask"),
                       (blk16, "blk16"), (blk8, "blk8"), (fconsts, "fconsts")):
            E("sp", lambda e, t_=t_, nm=nm: e.dma_start(out=t_[:], in_=di[nm].ap()),
              outs=[CB], inc=16)
        E("gp", lambda e: e.dma_start(out=ones16[:], in_=di["onesf"].ap()), outs=[CB], inc=16)
        E("gp", lambda e: e.dma_start(out=trilT16[:], in_=di["trilT"].ap()), outs=[CB], inc=16)
        bv3b = Buf()
        for k in range(PT):
            E("sp", lambda e, k=k: e.dma_start(
                out=X[:, k * T:(k + 1) * T], in_=di["x_fm"].ap()[k * 128:(k + 1) * 128, :]),
              outs=[Xb[k]], inc=16)

        misc_off = [0]
        def mslot(n):
            o = misc_off[0]
            misc_off[0] += n
            assert misc_off[0] <= 1096
            return o

        def load_vec(dname, n, slot=None):
            nt = (n + 127) // 128
            o = mslot(nt) if slot is None else slot
            E("sp", lambda e: e.dma_start(out=misc[:, o:o + nt],
                                          in_=AP(di[dname].ap().tensor, 0, [[1, 128], [128, nt]]),
                                          allow_slow_non_contiguous=True),
              outs=[miscB], inc=16)
            return o

        def load_rows(dname, nrows, ncols, o=None):
            nt = nrows // 128
            o = mslot(nt * ncols) if o is None else o
            for i in range(nt):
                E("sp", lambda e, i=i: e.dma_start(
                    out=misc[:, o + i * ncols:o + (i + 1) * ncols],
                    in_=di[dname].ap()[i * 128:(i + 1) * 128, :]), outs=[miscB], inc=16)
            return o

        def wload(dst, dstb, dname, kt0, nk, out_w, col0=0, ncols=None, eng="gp"):
            if ncols is None:
                ncols = out_w
            for i in range(nk):
                r0 = (kt0 + i) * 128
                E(eng, lambda e, i=i, r0=r0: e.dma_start(
                    out=dst[:, i * ncols:(i + 1) * ncols],
                    in_=di[dname].ap()[r0:r0 + 128, col0:col0 + ncols]),
                  outs=[dstb], inc=16)

        def mm(bank, lT, lTsl, lb, rT, rsl, rb, start, stop, m0=0, mlen=128, n0=0, nlen=512):
            E("pe", lambda e: e.matmul(pbank(bank, n0, nlen)[m0:m0 + mlen, :],
                                       lT[lTsl[0]:lTsl[1], lTsl[2]:lTsl[3]],
                                       rT[rsl[0]:rsl[1], rsl[2]:rsl[3]],
                                       start=start, stop=stop),
              ins=[lb, rb], outs=[pbB[bank]])

        def layer_norm(want32=False):
            xn32, xn32b = (A.alloc((128, PT * T), F32, "xn32") if want32 else (None, None))
            sq, sqb = A.alloc((128, 512), F32, "lnsq")
            for ch in range(4):
                c0 = ch * 512
                for k in range(PT):
                    mm(0, onesf, (0, 128, 0, 1), CB, X, (0, 128, k * T + c0, k * T + c0 + 512),
                       Xb[k], start=(k == 0), stop=(k == PT - 1), mlen=1)
                for k in range(PT):
                    E("act", lambda e, k=k: e.activation(
                        sq[:], X[:, k * T + c0:k * T + c0 + 512], AF.Square),
                      ins=[Xb[k]], outs=[sqb])
                    mm(1, onesf, (0, 128, 0, 1), CB, sq, (0, 128, 0, 512), sqb,
                       start=(k == 0), stop=(k == PT - 1), mlen=1)
                E("dve", lambda e: e.tensor_scalar(misc2[0:1, 0:512], pbank(0)[0:1, :],
                                                   1.0 / D, None, OP.mult),
                  ins=[pbB[0]], outs=[misc2B])
                mm(2, onesf, (0, 1, 0, 128), CB, misc2, (0, 1, 0, 512), misc2B, True, True)
                E("dve", lambda e: e.tensor_tensor(misc2[0:1, 0:512], misc2[0:1, 0:512],
                                                   misc2[0:1, 0:512], OP.mult),
                  ins=[misc2B], outs=[misc2B])
                E("dve", lambda e: e.scalar_tensor_tensor(
                    misc2[0:1, 0:512], pbank(1)[0:1, :], 1.0 / D, misc2[0:1, 0:512],
                    OP.mult, OP.subtract), ins=[pbB[1], misc2B], outs=[misc2B])
                E("act", lambda e: e.activation(misc2[0:1, 0:512], misc2[0:1, 0:512],
                                                AF.Sqrt, bias=1e-5),
                  ins=[misc2B], outs=[misc2B])
                E("dve", lambda e: e.reciprocal(misc2[0:1, 0:512], misc2[0:1, 0:512]),
                  ins=[misc2B], outs=[misc2B], hz=True)
                mm(3, onesf, (0, 1, 0, 128), CB, misc2, (0, 1, 0, 512), misc2B, True, True)
                for k in range(PT):
                    E("dve", lambda e, k=k: e.tensor_tensor(
                        sq[:], X[:, k * T + c0:k * T + c0 + 512], pbank(2), OP.subtract),
                      ins=[Xb[k], pbB[2]], outs=[sqb])
                    if want32:
                        E("dve", lambda e, k=k: e.tensor_tensor(
                            xn32[:, k * T + c0:k * T + c0 + 512], sq[:], pbank(3), OP.mult),
                          ins=[sqb, pbB[3]], outs=[xn32b])
                        E("dve", lambda e, k=k: e.tensor_copy(
                            XN[:, k * T + c0:k * T + c0 + 512],
                            xn32[:, k * T + c0:k * T + c0 + 512]),
                          ins=[xn32b], outs=[XNb[k]])
                    else:
                        E("dve", lambda e, k=k: e.tensor_tensor(
                            XN[:, k * T + c0:k * T + c0 + 512], sq[:], pbank(3), OP.mult),
                          ins=[sqb, pbB[3]], outs=[XNb[k]])
            return xn32, xn32b

        def x_add_psum(bank, k, c0, nlen=512):
            E("dve", lambda e: e.tensor_tensor(
                X[:, k * T + c0:k * T + c0 + nlen],
                X[:, k * T + c0:k * T + c0 + nlen], pbank(bank, 0, nlen), OP.add),
              ins=[pbB[bank]], outs=[Xb[k]])

        def l0():
            o_A = load_rows("A_log", 2048, 16)
            o_cw = load_rows("conv_w", 2048, 4)
            o_cb = load_vec("conv_b", 2048)
            o_db = load_vec("dt_b", 2048)
            o_mur = load_vec("mu_r", D)
            o_muk = load_vec("mu_k", D)
            o_omr = mslot(8)
            o_omk = mslot(8)
            E("act", lambda e: e.activation(misc[:, o_A:o_A + 256], misc[:, o_A:o_A + 256],
                                            AF.Exp), ins=[miscB], outs=[miscB])
            E("dve", lambda e: e.tensor_scalar(misc[:, o_A:o_A + 256], misc[:, o_A:o_A + 256],
                                               -1.0, None, OP.mult), ins=[miscB], outs=[miscB])
            E("dve", lambda e: e.tensor_scalar(misc[:, o_omr:o_omr + 8],
                                               misc[:, o_mur:o_mur + 8],
                                               -1.0, 1.0, OP.mult, OP.add),
              ins=[miscB], outs=[miscB])
            E("dve", lambda e: e.tensor_scalar(misc[:, o_omk:o_omk + 8],
                                               misc[:, o_muk:o_muk + 8],
                                               -1.0, 1.0, OP.mult, OP.add),
              ins=[miscB], outs=[miscB])

            mpre = A.mark()
            layer_norm(False)
            A.release(mpre)
            m0 = A.mark()
            do_attn = MK_PARTS in ("both", "attn", "f0xin", "f0dt", "f0y") or N_LAYERS > 1
            do_ffn = MK_PARTS in ("both", "ffn", "f0r", "f0k") or N_LAYERS > 1
            if MK_PARTS == "ln":
                for k in range(PT):
                    E("dve", lambda e, k=k: e.tensor_copy(X[:, k * T:(k + 1) * T],
                                                          XN[:, k * T:(k + 1) * T]),
                      ins=[XNb[k]], outs=[Xb[k]])
            def dump_sp(spt, spb):
                tmp, tmpb = A.alloc((128, T), BF16, "dump")
                for k in range(PT):
                    E("gp", lambda e, k=k, spt=spt: e.dma_start(
                        out=tmp[:], in_=spt.ap()[k * 128:(k + 1) * 128, :]),
                      ins=[spb], outs=[tmpb], inc=16)
                    E("dve", lambda e, k=k: e.tensor_copy(X[:, k * T:(k + 1) * T], tmp[:]),
                      ins=[tmpb], outs=[Xb[k]])

            if do_attn:
                w1, w1b = A.alloc((128, PT * 2048), BF16, "ipw1")
                stg, stgb = A.alloc((128, 520), F32, "cstg")
                cvt, cvtb = A.alloc((128, 512), F32, "cvt")
                cv2, cv2b = A.alloc((128, 512), F32, "cv2")
                xo, xob = A.alloc((128, 512), BF16, "xo")
                wload(w1, w1b, "ipw1T", 0, PT, 2048)
                for dt16 in range(16):
                    for ch in range(4):
                        c0 = ch * 512
                        bank = ch % 2
                        for k in range(PT):
                            mm(bank, w1, (0, 128, k * 2048 + dt16 * 128, k * 2048 + dt16 * 128 + 128),
                               w1b, XN, (0, 128, k * T + c0, k * T + c0 + 512), XNb[k],
                               start=(k == 0), stop=(k == PT - 1))
                        if ch == 0:
                            E("dve", lambda e: e.memset(stg[:, 0:3], 0.0), outs=[stgb])
                        E("act", lambda e, bank=bank: e.copy(stg[:, 3:515], pbank(bank)),
                          ins=[pbB[bank]], outs=[stgb])
                        E("dve", lambda e, dt16=dt16: e.tensor_single_scalar(
                            cvt[:], stg[:, 0:512], misc[:, o_cw + dt16 * 4:o_cw + dt16 * 4 + 1],
                            OP.mult), ins=[stgb, miscB], outs=[cvtb], hz=False)
                        for kk in range(1, 4):
                            E("dve", lambda e, dt16=dt16, kk=kk: e.tensor_single_scalar(
                                cv2[:], stg[:, kk:kk + 512],
                                misc[:, o_cw + dt16 * 4 + kk:o_cw + dt16 * 4 + kk + 1], OP.mult),
                              ins=[stgb, miscB], outs=[cv2b], hz=False)
                            E("dve", lambda e: e.tensor_tensor(cvt[:], cvt[:], cv2[:], OP.add),
                              ins=[cv2b], outs=[cvtb], hz=False)
                        if ch < 3:
                            E("dve", lambda e: e.tensor_copy(stg[:, 0:3], stg[:, 512:515]),
                              ins=[stgb], outs=[stgb])
                        E("act", lambda e, dt16=dt16: e.activation(
                            xo[:], cvt[:], AF.Silu, bias=misc[:, o_cb + dt16:o_cb + dt16 + 1]),
                          ins=[cvtb, miscB], outs=[xob])
                        E("sp", lambda e, dt16=dt16, c0=c0: e.dma_start(
                            out=xin_sp.ap()[dt16 * 128:(dt16 + 1) * 128, c0:c0 + 512], in_=xo[:]),
                          ins=[xob], outs=[xinD], inc=16)
                if MK_PARTS == "f0xin":
                    dump_sp(xin_sp, xinD)
                A.release(m0)

                if MK_PARTS == "f0xin":
                    raise _SkipRest
                BmCm, BmCmb = A.alloc((128, 512), BF16, "BmCm")
                bw, bwb = A.alloc((128, 16 * 32), BF16, "bcw")
                for i in range(16):
                    E("gp", lambda e, i=i: e.dma_start(
                        out=bw[:, i * 32:i * 32 + 16],
                        in_=di["B_wT"].ap()[i * 128:(i + 1) * 128, :]), outs=[bwb], inc=16)
                    E("gp", lambda e, i=i: e.dma_start(
                        out=bw[:, i * 32 + 16:i * 32 + 32],
                        in_=di["C_wT"].ap()[i * 128:(i + 1) * 128, :]), outs=[bwb], inc=16)
                m1 = A.mark()
                dw, dwb = A.alloc((128, 16 * 1024), BF16, "dtw")
                xr = [A.alloc((128, 16 * 512), BF16, f"xr{i}") for i in range(2)]
                sp1, sp1b = A.alloc((128, 512), F32, "sp1")
                sp2, sp2b = A.alloc((128, 512), F32, "sp2")
                dto, dtob = A.alloc((128, 512), BF16, "dto")
                for half in range(2):
                    wload(dw, dwb, "dt_wT", 0, 16, 1024, col0=half * 1024, ncols=1024)
                    for ch in range(4):
                        c0 = ch * 512
                        xt, xtb = xr[ch % 2]
                        for i in range(16):
                            E("gp", lambda e, i=i, c0=c0: e.dma_start(
                                out=xt[:, i * 512:(i + 1) * 512],
                                in_=xin_sp.ap()[i * 128:(i + 1) * 128, c0:c0 + 512]),
                              ins=[xinD], outs=[xtb], inc=16)
                        if half == 0:
                            for i in range(16):
                                mm(6, bw, (0, 128, i * 32, i * 32 + 16), bwb,
                                   xt, (0, 128, i * 512, i * 512 + 512), xtb,
                                   start=(i == 0), stop=(i == 15), mlen=16)
                            E("act", lambda e: e.copy(BmCm[0:16, 0:512], pbank(6)[0:16, :]),
                              ins=[pbB[6]], outs=[BmCmb])
                            E("sp", lambda e, c0=c0: e.dma_start(
                                out=bc_sp.ap()[:, c0:c0 + 512], in_=BmCm[0:16, 0:512]),
                              ins=[BmCmb], outs=[bcD], inc=16)
                            for i in range(16):
                                mm(7, bw, (0, 128, i * 32 + 16, i * 32 + 32), bwb,
                                   xt, (0, 128, i * 512, i * 512 + 512), xtb,
                                   start=(i == 0), stop=(i == 15), mlen=16)
                            E("act", lambda e: e.copy(BmCm[0:16, 0:512], pbank(7)[0:16, :]),
                              ins=[pbB[7]], outs=[BmCmb])
                            E("sp", lambda e, c0=c0: e.dma_start(
                                out=bc_sp.ap()[:, T + c0:T + c0 + 512], in_=BmCm[0:16, 0:512]),
                              ins=[BmCmb], outs=[bcD], inc=16)
                        for mt in range(8):
                            dt16 = half * 8 + mt
                            bank = mt % 2 + 2
                            for i in range(16):
                                mm(bank, dw, (0, 128, i * 1024 + mt * 128, i * 1024 + mt * 128 + 128),
                                   dwb, xt, (0, 128, i * 512, i * 512 + 512), xtb,
                                   start=(i == 0), stop=(i == 15))
                            dbs = misc[:, o_db + dt16:o_db + dt16 + 1]
                            E("act", lambda e, bank=bank, dbs=dbs: e.activation(
                                sp1[:], pbank(bank), AF.Abs, bias=dbs),
                              ins=[pbB[bank], miscB], outs=[sp1b])
                            E("act", lambda e: e.activation(sp1[:], sp1[:], AF.Exp, scale=-1.0),
                              ins=[sp1b], outs=[sp1b])
                            E("act", lambda e: e.activation(sp1[:], sp1[:], AF.Ln, bias=1.0),
                              ins=[sp1b], outs=[sp1b])
                            E("act", lambda e, bank=bank, dbs=dbs: e.activation(
                                sp2[:], pbank(bank), AF.Relu, bias=dbs),
                              ins=[pbB[bank], miscB], outs=[sp2b])
                            E("dve", lambda e: e.tensor_tensor(dto[:], sp1[:], sp2[:], OP.add),
                              ins=[sp1b, sp2b], outs=[dtob], hz=False)
                            E("sp", lambda e, dt16=dt16, c0=c0: e.dma_start(
                                out=dt_sp.ap()[dt16 * 128:(dt16 + 1) * 128, c0:c0 + 512],
                                in_=dto[:]), ins=[dtob], outs=[dtD], inc=16)
                A.release(m1)
                if MK_PARTS == "f0dt":
                    dump_sp(dt_sp, dtD)

                if MK_PARTS == "f0dt":
                    raise _SkipRest
                xi = [A.alloc((128, T), BF16, f"sxi{i}") for i in range(2)]
                dti = [A.alloc((128, T), BF16, f"sdt{i}") for i in range(2)]
                br = [A.alloc((1, 2 * T), BF16, f"sbr{i}") for i in range(2)]
                ut, utb = A.alloc((128, T), F32, "su")
                bt2, bt2b = A.alloc((128, T), F32, "sb2")
                at, atb = A.alloc((128, T), F32, "sa")
                bt, btb = A.alloc((128, T), F32, "sb")
                st, stb = A.alloc((128, T), F32, "ss")
                ya, yab = A.alloc((128, T), F32, "sy")
                y16, y16b = A.alloc((128, T), BF16, "sy16")
                for pt16 in range(16):
                    xt, xtb = xi[pt16 % 2]
                    dt_, dtb_ = dti[pt16 % 2]
                    E("gp", lambda e, pt16=pt16: e.dma_start(
                        out=xt[:], in_=xin_sp.ap()[pt16 * 128:(pt16 + 1) * 128, :]),
                      ins=[xinD], outs=[xtb], inc=16)
                    E("gp", lambda e, pt16=pt16: e.dma_start(
                        out=dt_[:], in_=dt_sp.ap()[pt16 * 128:(pt16 + 1) * 128, :]),
                      ins=[dtD], outs=[dtb_], inc=16)
                    E("dve", lambda e: e.tensor_tensor(ut[:], dt_[:], xt[:], OP.mult),
                      ins=[dtb_, xtb], outs=[utb], hz=False)
                    for s in range(16):
                        brt, brtb = br[s % 2]
                        E("gp", lambda e, s=s, brt=brt: e.dma_start(
                            out=brt[:], in_=bc_sp.ap()[s:s + 1, :]),
                          ins=[bcD], outs=[brtb], inc=16)
                        for cc in range(4):
                            mm(cc, ones16, (0, 1, 0, 128), CB, brt,
                               (0, 1, cc * 512, cc * 512 + 512), brtb, True, True)
                            mm(4 + cc, ones16, (0, 1, 0, 128), CB, brt,
                               (0, 1, T + cc * 512, T + cc * 512 + 512), brtb, True, True)
                        Asl = misc[:, o_A + pt16 * 16 + s:o_A + pt16 * 16 + s + 1]
                        E("act", lambda e, Asl=Asl: e.activation(at[:], dt_[:], AF.Exp, scale=Asl),
                          ins=[dtb_, miscB], outs=[atb])
                        E("dve", lambda e: e.tensor_tensor(bt[:], ut[:], pbig[:, 0:2048], OP.mult),
                          ins=[utb, pbB[0], pbB[1], pbB[2], pbB[3]], outs=[btb], hz=False)
                        E("dve", lambda e: e.tensor_tensor_scan(st[:], at[:], bt[:], 0.0,
                                                                OP.mult, OP.add),
                          ins=[atb, btb], outs=[stb], hz=True, hzin=True)
                        E("dve", lambda e: e.tensor_tensor(bt2[:], st[:], pbig[:, 2048:4096],
                                                           OP.mult),
                          ins=[stb, pbB[4], pbB[5], pbB[6], pbB[7]], outs=[bt2b], hz=False)
                        if s == 0:
                            if os.environ.get("MK_SCANDBG", "") == "s0":
                                E("dve", lambda e: e.tensor_copy(ya[:], st[:]), ins=[stb], outs=[yab])
                            else:
                                E("dve", lambda e: e.tensor_copy(ya[:], bt2[:]), ins=[bt2b], outs=[yab], hz=False)
                        elif os.environ.get("MK_SCANDBG", "") != "s0":
                            E("dve", lambda e: e.tensor_tensor(ya[:], ya[:], bt2[:], OP.add),
                              ins=[bt2b], outs=[yab], hz=False)
                    sdbg = os.environ.get("MK_SCANDBG", "")
                    if sdbg not in ("a", "b", "u", "s", "s0", "B", "C"):
                        sdbg = ""
                    if sdbg == "B":
                        E("dve", lambda e: e.tensor_copy(y16[:], pbig[:, 0:2048]),
                          ins=[pbB[0], pbB[1], pbB[2], pbB[3]], outs=[y16b])
                    elif sdbg == "C":
                        E("dve", lambda e: e.tensor_copy(y16[:], pbig[:, 2048:4096]),
                          ins=[pbB[4], pbB[5], pbB[6], pbB[7]], outs=[y16b])
                    else:
                        dbgsrc = {"a": (at, atb), "b": (bt, btb), "u": (ut, utb),
                                  "s": (st, stb), "s0": (ya, yab), "": (ya, yab)}[sdbg]
                        E("dve", lambda e: e.tensor_copy(y16[:], dbgsrc[0][:]),
                          ins=[dbgsrc[1]], outs=[y16b], hz=False)
                    E("sp", lambda e, pt16=pt16: e.dma_start(
                        out=y_sp.ap()[pt16 * 128:(pt16 + 1) * 128, :], in_=y16[:]),
                      ins=[y16b], outs=[yD], inc=16)
                if MK_PARTS == "f0y":
                    dump_sp(y_sp, yD)
                A.release(m0)

                if MK_PARTS == "f0y":
                    raise _SkipRest
                w2, w2b = A.alloc((128, PT * 2048), BF16, "ipw2")
                ow, owb = A.alloc((128, 16 * 1024), BF16, "outw")
                yt, ytb = A.alloc((128, 16 * 512), BF16, "yt")
                sr, srb = A.alloc((128, 512), BF16, "sres")
                wload(w2, w2b, "ipw2T", 0, PT, 2048)
                wload(ow, owb, "out_wT", 0, 16, 1024)
                for ch in range(4):
                    c0 = ch * 512
                    for i in range(16):
                        E("gp", lambda e, i=i, c0=c0: e.dma_start(
                            out=yt[:, i * 512:(i + 1) * 512],
                            in_=y_sp.ap()[i * 128:(i + 1) * 128, c0:c0 + 512]),
                          ins=[yD], outs=[ytb], inc=16)
                    for dt16 in range(16):
                        bank = dt16 % 2
                        for k in range(PT):
                            mm(bank, w2, (0, 128, k * 2048 + dt16 * 128, k * 2048 + dt16 * 128 + 128),
                               w2b, XN, (0, 128, k * T + c0, k * T + c0 + 512), XNb[k],
                               start=(k == 0), stop=(k == PT - 1))
                        E("act", lambda e, bank=bank: e.activation(sr[:], pbank(bank), AF.Silu),
                          ins=[pbB[bank]], outs=[srb])
                        E("dve", lambda e, dt16=dt16: e.tensor_tensor(
                            yt[:, dt16 * 512:(dt16 + 1) * 512],
                            yt[:, dt16 * 512:(dt16 + 1) * 512], sr[:], OP.mult),
                          ins=[srb], outs=[ytb])
                    for mt in range(8):
                        bank = mt % 2 + 2
                        for i in range(16):
                            mm(bank, ow, (0, 128, i * 1024 + mt * 128, i * 1024 + mt * 128 + 128),
                               owb, yt, (0, 128, i * 512, i * 512 + 512), ytb,
                               start=(i == 0), stop=(i == 15))
                        x_add_psum(bank, mt, c0)
                A.release(m0)


            if do_ffn:
                def shift_mix(dst, dstb, o_mu, o_om, k, c0):
                    mu = misc[:, o_mu + k:o_mu + k + 1]
                    om = misc[:, o_om + k:o_om + k + 1]
                    E("dve", lambda e: e.tensor_single_scalar(
                        cvt[:], XN[:, k * T + c0:k * T + c0 + 512], mu, OP.mult),
                      ins=[XNb[k], miscB], outs=[cvtb])
                    if c0 == 0:
                        E("dve", lambda e: e.tensor_single_scalar(
                            cv2[:, 1:512], XN[:, k * T:k * T + 511], om, OP.mult),
                          ins=[XNb[k], miscB], outs=[cv2b])
                        E("dve", lambda e: e.memset(cv2[:, 0:1], 0.0), outs=[cv2b])
                    else:
                        E("dve", lambda e: e.tensor_single_scalar(
                            cv2[:], XN[:, k * T + c0 - 1:k * T + c0 + 511], om, OP.mult),
                          ins=[XNb[k], miscB], outs=[cv2b])
                    E("dve", lambda e: e.tensor_tensor(dst[:], cvt[:], cv2[:], OP.add),
                      ins=[cvtb, cv2b], outs=[dstb])

                cvt, cvtb = A.alloc((128, 512), F32, "f0a")
                cv2, cv2b = A.alloc((128, 512), F32, "f0b")
                xmix, xmixb = A.alloc((128, PT * 512), BF16, "f0mix")
                r16, r16b = A.alloc((128, PT * T), BF16, "f0r")
                k16, k16b = A.alloc((128, PT * T), BF16, "f0k")
                wv, wvb = A.alloc((128, PT * 512), BF16, "f0w")
                for half in range(2):
                    wload(wv, wvb, "WkT", 0, PT, 512, col0=half * 512, ncols=512)
                    for ch in range(4):
                        c0 = ch * 512
                        for k in range(PT):
                            shift_mix(xmix[:, k * 512:(k + 1) * 512], xmixb, o_muk, o_omk, k, c0)
                        for mt4 in range(4):
                            mt = half * 4 + mt4
                            bank = mt4 % 2
                            for k in range(PT):
                                mm(bank, wv, (0, 128, k * 512 + mt4 * 128, k * 512 + mt4 * 128 + 128),
                                   wvb, xmix, (0, 128, k * 512, k * 512 + 512), xmixb,
                                   start=(k == 0), stop=(k == PT - 1))
                            E("act", lambda e, bank=bank: e.activation(cvt[:], pbank(bank), AF.Relu),
                              ins=[pbB[bank]], outs=[cvtb])
                            E("act", lambda e, mt=mt, c0=c0: e.activation(
                                k16[:, mt * T + c0:mt * T + c0 + 512], cvt[:], AF.Square),
                              ins=[cvtb], outs=[k16b])
                for half in range(2):
                    wload(wv, wvb, "WrT", 0, PT, 512, col0=half * 512, ncols=512)
                    for ch in range(4):
                        c0 = ch * 512
                        for k in range(PT):
                            shift_mix(xmix[:, k * 512:(k + 1) * 512], xmixb, o_mur, o_omr, k, c0)
                        for mt4 in range(4):
                            mt = half * 4 + mt4
                            bank = mt4 % 2
                            for k in range(PT):
                                mm(bank, wv, (0, 128, k * 512 + mt4 * 128, k * 512 + mt4 * 128 + 128),
                                   wvb, xmix, (0, 128, k * 512, k * 512 + 512), xmixb,
                                   start=(k == 0), stop=(k == PT - 1))
                            E("act", lambda e, bank=bank, mt=mt, c0=c0: e.activation(
                                r16[:, mt * T + c0:mt * T + c0 + 512], pbank(bank), AF.Sigmoid),
                              ins=[pbB[bank]], outs=[r16b])
                if MK_PARTS in ("f0r", "f0k"):
                    srct = r16 if MK_PARTS == "f0r" else k16
                    for k in range(PT):
                        E("dve", lambda e, k=k: e.tensor_copy(
                            X[:, k * T:(k + 1) * T], srct[:, k * T:(k + 1) * T]),
                          ins=[r16b if MK_PARTS == "f0r" else k16b], outs=[Xb[k]])
                for half in range(2):
                    wload(wv, wvb, "WvT", 0, PT, 512, col0=half * 512, ncols=512)
                    for ch in range(4):
                        c0 = ch * 512
                        for mt4 in range(4):
                            mt = half * 4 + mt4
                            bank = mt4 % 2
                            for k in range(PT):
                                mm(bank, wv, (0, 128, k * 512 + mt4 * 128, k * 512 + mt4 * 128 + 128),
                                   wvb, k16, (0, 128, k * T + c0, k * T + c0 + 512), k16b,
                                   start=(k == 0), stop=(k == PT - 1))
                            E("dve", lambda e, bank=bank, mt=mt, c0=c0: e.scalar_tensor_tensor(
                                cvt[:], pbank(bank), 1.0, r16[:, mt * T + c0:mt * T + c0 + 512],
                                OP.mult, OP.mult), ins=[pbB[bank], r16b], outs=[cvtb])
                            E("dve", lambda e, mt=mt, c0=c0: e.tensor_tensor(
                                X[:, mt * T + c0:mt * T + c0 + 512],
                                X[:, mt * T + c0:mt * T + c0 + 512], cvt[:], OP.add),
                              ins=[cvtb], outs=[Xb[mt]])
                A.release(m0)


        def rope_attn(wq_nm, wk_nm, wv_nm, w2_nm, causal, o_bq=None, o_bk=None,
                      bv16=None, bv16b=None, o_bo=None, o_sc4=0):
            for hg in range(2):
                mhg = A.mark()
                ao, aob = A.alloc((128, 4 * T), BF16, "ao")
                q16, q16b = A.alloc((128, 4 * T), BF16, "q")
                k16, k16b = A.alloc((128, 4 * T), BF16, "k")
                vt, vtb = A.alloc((128, 16 * 520), BF16, "v")
                msc = A.mark()
                rstg, rstgb = A.alloc((128, 512), F32, "rs")
                rot, rotb = A.alloc((128, 512), F32, "rot")
                t1, t1b = A.alloc((128, 512), F32, "t1")
                ws, wsb = A.alloc((128, 8 * 512), BF16, "w")
                pst = [A.alloc((128, 512), BF16, f"p{i}") for i in range(2)]
                nrm, nrmb = A.alloc((128, 520), BF16, "nrm")
                nr32, nr32b = rstg, rstgb

                for which, wnm, dst, dstb, o_bias in ((0, wq_nm, q16, q16b, o_bq),
                                                      (1, wk_nm, k16, k16b, o_bk)):
                    wload(ws, wsb, wnm, 0, PT, 512, col0=hg * 512, ncols=512)
                    for mt4 in range(4):
                        for ch in range(4):
                            c0 = ch * 512
                            bank = (mt4 * 4 + ch) % 2
                            for k in range(PT):
                                mm(bank, ws, (0, 128, k * 512 + mt4 * 128,
                                              k * 512 + mt4 * 128 + 128), wsb,
                                   XN, (0, 128, k * T + c0, k * T + c0 + 512), XNb[k],
                                   start=(k == 0), stop=(k == PT - 1))
                            if o_bias is None:
                                E("act", lambda e, bank=bank: e.copy(rstg[:], pbank(bank)),
                                  ins=[pbB[bank]], outs=[rstgb])
                            else:
                                bc = o_bias + hg * 4 + mt4
                                E("act", lambda e, bank=bank, bc=bc: e.activation(
                                    rstg[:], pbank(bank), AF.Identity, scale=1.0,
                                    bias=misc[:, bc:bc + 1]),
                                  ins=[pbB[bank], miscB], outs=[rstgb])
                            for blk2 in range(2):
                                b0 = blk2 * 64
                                E("dve", lambda e, b0=b0: e.tensor_scalar(
                                    rot[b0:b0 + 32, :], rstg[b0 + 32:b0 + 64, :],
                                    -1.0, None, OP.mult), ins=[rstgb], outs=[rotb], hz=False)
                                E("dve", lambda e, b0=b0: e.tensor_copy(
                                    rot[b0 + 32:b0 + 64, :], rstg[b0:b0 + 32, :]),
                                  ins=[rstgb], outs=[rotb], hz=False)
                            E("dve", lambda e, c0=c0: e.tensor_tensor(
                                t1[:], rstg[:], ropec[:, c0:c0 + 512], OP.mult),
                              ins=[rstgb, CB], outs=[t1b], hz=False)
                            E("dve", lambda e, c0=c0: e.tensor_tensor(
                                rot[:], rot[:], ropes[:, c0:c0 + 512], OP.mult),
                              ins=[rotb, CB], outs=[rotb], hz=False)
                            E("dve", lambda e, mt4=mt4, c0=c0, dst=dst: e.tensor_tensor(
                                dst[:, mt4 * T + c0:mt4 * T + c0 + 512], t1[:], rot[:],
                                OP.add), ins=[t1b, rotb], outs=[dstb], hz=False)
                E("dve", lambda e: e.memset(vt[:], 1.0), outs=[vtb], hz=False)
                wload(ws, wsb, wv_nm, 0, PT, 512, col0=hg * 512, ncols=512)
                for tt in range(16):
                    bank = tt % 2 + 2
                    for k in range(PT):
                        mm(bank, XN, (0, 128, k * T + tt * 128, k * T + tt * 128 + 128),
                           XNb[k], ws, (0, 128, k * 512, k * 512 + 512), wsb,
                           start=(k == 0), stop=(k == PT - 1 and bv16 is None))
                    if bv16 is not None:
                        mm(bank, ones16, (0, 1, 0, 128), CB, bv16,
                           (0, 1, hg * 512, hg * 512 + 512), bv16b,
                           start=False, stop=True)
                    for h8 in range(8):
                        E("act", lambda e, bank=bank, tt=tt, h8=h8: e.copy(
                            vt[:, tt * 520 + h8 * 65:tt * 520 + h8 * 65 + 64],
                            pbank(bank, h8 * 64, 64)), ins=[pbB[bank]], outs=[vtb])
                for h8 in range(8):
                    hb = (h8 % 2) * 64
                    tq = (h8 // 2)
                    qoff = hb * (4 * T) + tq * T
                    E("pe", lambda e, qoff=qoff: e.matmul(
                        pbank(6)[0:128, 0:128],
                        AP(k16.ap().tensor, qoff, [[4 * T, 64], [16, 128]]),
                        AP(q16.ap().tensor, qoff, [[4 * T, 64], [16, 128]]),
                        start=True, stop=True),
                      ins=[k16b, q16b], outs=[pbB[6]])
                    E("dve", lambda e: e.reduce_max(misc2[:, 28:29], pbank(6)[:, 0:128],
                                                    axis=AX),
                      ins=[pbB[6]], outs=[misc2B], hz=True)
                    E("sp", lambda e: e.dma_start(
                        out=flip_sp.ap()[0:1, :].rearrange("a b -> (a b)"),
                        in_=misc2[:, 28:29]), ins=[misc2B], outs=[flipD], inc=16)
                    E("sp", lambda e: e.dma_start(out=misc2[0:1, 32:160],
                                                  in_=flip_sp.ap()[0:1, :]),
                      ins=[flipD], outs=[misc2B], inc=16)
                    E("dve", lambda e: e.reduce_max(misc2[0:1, 30:31],
                                                    misc2[0:1, 32:160], axis=AX),
                      ins=[misc2B], outs=[misc2B], hz=True)
                    E("dve", lambda e: e.tensor_scalar(misc2[0:1, 30:31],
                                                       misc2[0:1, 30:31], -0.125, None,
                                                       OP.mult),
                      ins=[misc2B], outs=[misc2B])
                    mm(6, onesf, (0, 1, 0, 128), CB, misc2, (0, 1, 30, 31), misc2B,
                       True, True, nlen=1)
                    E("act", lambda e: e.copy(misc[:, o_sc4:o_sc4 + 1], pbank(6, 0, 1)),
                      ins=[pbB[6]], outs=[miscB])
                    if os.environ.get("MK_SCANDBG", "") == "nb" and not causal:
                        E("sp", lambda e, hg=hg, h8=h8: e.dma_start(
                            out=dbg_d.ap()[0:1, hg * 8 + h8:hg * 8 + h8 + 1],
                            in_=misc2[0:1, 30:31]), ins=[misc2B], inc=16)
                        E("sp", lambda e, hg=hg, h8=h8: e.dma_start(
                            out=dbg_d.ap()[0:1, 16 + hg * 8 + h8:17 + hg * 8 + h8],
                            in_=misc2[0:1, 28:29]), ins=[misc2B], inc=16)
                    for qc in range(4):
                        q0 = qc * 512
                        nkt = (qc * 4 + 4) if causal else 16
                        for kt in range(nkt):
                            bank = kt % 2
                            mm(bank, k16, (hb, hb + 64, tq * T + kt * 128,
                                           tq * T + kt * 128 + 128), k16b,
                               q16, (hb, hb + 64, tq * T + q0, tq * T + q0 + 512), q16b,
                               start=True, stop=True)
                            if causal:
                                delta = kt * 128 - q0
                                if delta >= 0:
                                    mi = delta // 128
                                    b0 = 384 - mi * 128
                                    E("dve", lambda e, bank=bank, b0=b0: e.tensor_tensor(
                                        pbank(bank), pbank(bank),
                                        l1mask[:, b0:b0 + 512], OP.add),
                                      ins=[CB], outs=[pbB[bank]], hz=False)
                            ps, psb = pst[kt % 2]
                            E("act", lambda e, bank=bank, ps=ps: e.activation(
                                ps[:], pbank(bank), AF.Exp, scale=0.125,
                                bias=misc[:, o_sc4:o_sc4 + 1]),
                              ins=[pbB[bank], miscB], outs=[psb])
                            mm(2, vt, (0, 128, kt * 520 + h8 * 65, kt * 520 + h8 * 65 + 65),
                               vtb, ps, (0, 128, 0, 512), psb,
                               start=(kt == 0), stop=(kt == nkt - 1), mlen=65)
                        E("dve", lambda e: e.reciprocal(nr32[64:65, 0:512],
                                                        pbank(2)[64:65, 0:512]),
                          ins=[pbB[2]], outs=[nr32b], hz=True)
                        E("dve", lambda e: e.tensor_copy(nrm[64:65, 0:512],
                                                         nr32[64:65, 0:512]),
                          ins=[nr32b], outs=[nrmb], hz=True)
                        mm(3, ones16, (64, 65, 0, 64), CB, nrm, (64, 65, 0, 512), nrmb,
                           True, True, mlen=64)
                        E("act", lambda e: e.copy(rstg[0:64, :], pbank(2)[0:64, :]),
                          ins=[pbB[2]], outs=[rstgb])
                        E("dve", lambda e, h8=h8, q0=q0: e.tensor_tensor(
                            ao[(h8 % 2) * 64:(h8 % 2) * 64 + 64,
                               (h8 // 2) * T + q0:(h8 // 2) * T + q0 + 512],
                            rstg[0:64, :], pbank(3)[0:64, :], OP.mult),
                          ins=[rstgb, pbB[3]], outs=[aob], hz=False)
                A.release(msc)
                w2s, w2sb = A.alloc((128, 4 * 1024), BF16, "w2p")
                ev2, ev2b = A.alloc((128, 512), F32, "w2ev")
                wload(w2s, w2sb, w2_nm, hg * 4, 4, 1024)
                for ch in range(4):
                    c0 = ch * 512
                    for mt in range(8):
                        bank = mt % 2
                        for k in range(4):
                            mm(bank, w2s, (0, 128, k * 1024 + mt * 128,
                                           k * 1024 + mt * 128 + 128),
                               w2sb, ao, (0, 128, k * T + c0, k * T + c0 + 512), aob,
                               start=(k == 0), stop=(k == 3))
                        if o_bo is not None and hg == 0:
                            E("act", lambda e, bank=bank, mt=mt: e.activation(
                                ev2[:], pbank(bank), AF.Identity, scale=1.0,
                                bias=misc[:, o_bo + mt:o_bo + mt + 1]),
                              ins=[pbB[bank], miscB], outs=[ev2b])
                            E("dve", lambda e, mt=mt, c0=c0: e.tensor_tensor(
                                X[:, mt * T + c0:mt * T + c0 + 512],
                                X[:, mt * T + c0:mt * T + c0 + 512], ev2[:], OP.add),
                              ins=[ev2b], outs=[Xb[mt]], hz=False)
                        else:
                            x_add_psum(bank, mt, c0)
                A.release(mhg)

        def l1():
            o_b1 = load_vec("l1f_b1", 4096)
            o_b2 = load_vec("l1f_b2", D)
            o_b1s = mslot(32)
            o_b2s = mslot(8)
            o_sc = mslot(8)

            mpre = A.mark()
            xn32, xn32b = layer_norm(want32=True)

            arD, ar2D = Buf(), Buf()
            ax, axb = A.alloc((128, 16), F32, "amax")
            wstg, wstgb = A.alloc((128, 2048), F32, "awstg")
            wstg2, wstg2b = A.alloc((128, 2048), F32, "awstg2")
            for k in range(PT):
                E("dve", lambda e, k=k: e.reduce_max(ax[:, k:k + 1], xn32[:, k * T:(k + 1) * T],
                                                     axis=AX, apply_absolute_value=True),
                  ins=[xn32b], outs=[axb], hz=True)
            w1v = di["l1f_w1T"].ap().rearrange("a (b c) -> (a b) c", c=2048)
            for i in range(16):
                wt = wstg if i % 2 == 0 else wstg2
                wtb = wstgb if i % 2 == 0 else wstg2b
                E("sp", lambda e, i=i, wt=wt: e.dma_start(
                    out=wt[:], in_=w1v[i * 128:(i + 1) * 128, :]),
                  outs=[wtb], inc=16)
                E("dve", lambda e, wt=wt: e.reduce_max(ax[:, 9:10], wt[:], axis=AX,
                                                       apply_absolute_value=True),
                  ins=[wtb], outs=[axb], hz=True)
                E("dve", lambda e, i=i: (e.tensor_copy(ax[:, 8:9], ax[:, 9:10]) if i == 0
                                         else e.tensor_tensor(ax[:, 8:9], ax[:, 8:9],
                                                              ax[:, 9:10], OP.max)),
                  ins=[axb], outs=[axb])
            for i in range(16):
                wt = wstg if i % 2 == 0 else wstg2
                wtb = wstgb if i % 2 == 0 else wstg2b
                E("sp", lambda e, i=i, wt=wt: e.dma_start(
                    out=wt[:, 0:1024], in_=di["l1f_w2T"].ap()[i * 256:i * 256 + 128, :]),
                  outs=[wtb], inc=16)
                E("sp", lambda e, i=i, wt=wt: e.dma_start(
                    out=wt[:, 1024:2048], in_=di["l1f_w2T"].ap()[i * 256 + 128:i * 256 + 256, :]),
                  outs=[wtb], inc=16)
                E("dve", lambda e, wt=wt: e.reduce_max(ax[:, 9:10], wt[:], axis=AX,
                                                       apply_absolute_value=True),
                  ins=[wtb], outs=[axb], hz=True)
                E("dve", lambda e, i=i: (e.tensor_copy(ax[:, 10:11], ax[:, 9:10]) if i == 0
                                         else e.tensor_tensor(ax[:, 10:11], ax[:, 10:11],
                                                              ax[:, 9:10], OP.max)),
                  ins=[axb], outs=[axb])
            for k in range(1, PT):
                E("dve", lambda e, k=k: e.tensor_tensor(ax[:, 0:1], ax[:, 0:1], ax[:, k:k + 1],
                                                        OP.max), ins=[axb], outs=[axb])
            flD = flipD
            for j, col in enumerate((0, 8, 10)):
                E("sp", lambda e, j=j, col=col: e.dma_start(
                    out=flip_sp.ap()[j:j + 1, :].rearrange("a b -> (a b)"),
                    in_=ax[:, col:col + 1]), ins=[axb], outs=[flD], inc=16)
            for j in range(3):
                E("sp", lambda e, j=j: e.dma_start(
                    out=misc2[0:1, 32 + j * 128:32 + (j + 1) * 128],
                    in_=flip_sp.ap()[j:j + 1, :]), ins=[flD], outs=[misc2B], inc=16)
            for j in range(3):
                E("dve", lambda e, j=j: e.reduce_max(
                    misc2[0:1, 16 + j:17 + j], misc2[0:1, 32 + j * 128:32 + (j + 1) * 128],
                    axis=AX), ins=[misc2B], outs=[misc2B], hz=True)
            E("dve", lambda e: e.memset(misc2[0:1, 19:20], 0.0), outs=[misc2B])
            E("gp", lambda e: e.dma_start(out=ar_in.ap()[0:1, 0:4], in_=misc2[0:1, 16:20]),
              ins=[misc2B], outs=[arD], inc=16)
            E("gp", lambda e: e.collective_compute(
                "AllReduce", OP.max, replica_groups=[list(range(8))],
                ins=[ar_in.ap().opt()], outs=[ar_out.ap().opt()]),
              ins=[arD], outs=[ar2D], isasync=True)
            E("gp", lambda e: e.dma_start(out=misc2[0:1, 16:19], in_=ar_out.ap()[0:1, 0:3]),
              ins=[ar2D], outs=[misc2B], inc=16)
            E("dve", lambda e: e.tensor_scalar(misc2[0:1, 20:23], misc2[0:1, 16:19],
                                               1e-6, 1.0 / 127.0, OP.max, OP.mult),
              ins=[misc2B], outs=[misc2B])
            E("dve", lambda e: e.reciprocal(misc2[0:1, 24:25], misc2[0:1, 20:21]),
              ins=[misc2B], outs=[misc2B], hz=True)
            E("dve", lambda e: e.tensor_tensor(misc2[0:1, 25:26], misc2[0:1, 20:21],
                                               misc2[0:1, 21:22], OP.mult),
              ins=[misc2B], outs=[misc2B])
            mm(1, onesf, (0, 1, 0, 128), CB, misc2, (0, 1, 24, 25), misc2B, True, True, nlen=1)
            E("act", lambda e: e.copy(misc[:, o_sc:o_sc + 1], pbank(1, 0, 1)),
              ins=[pbB[1]], outs=[miscB])
            mm(1, onesf, (0, 1, 0, 128), CB, misc2, (0, 1, 25, 26), misc2B, True, True, nlen=1)
            E("act", lambda e: e.copy(misc[:, o_sc + 1:o_sc + 2], pbank(1, 0, 1)),
              ins=[pbB[1]], outs=[miscB])
            E("dve", lambda e: e.reciprocal(misc2[0:1, 29:30], misc2[0:1, 21:22]),
              ins=[misc2B], outs=[misc2B], hz=True)
            mm(1, onesf, (0, 1, 0, 128), CB, misc2, (0, 1, 29, 30), misc2B, True, True, nlen=1)
            E("act", lambda e: e.copy(misc[:, o_sc + 5:o_sc + 6], pbank(1, 0, 1)),
              ins=[pbB[1]], outs=[miscB])
            E("dve", lambda e: e.reciprocal(misc2[0:1, 29:30], misc2[0:1, 22:23]),
              ins=[misc2B], outs=[misc2B], hz=True)
            mm(1, onesf, (0, 1, 0, 128), CB, misc2, (0, 1, 29, 30), misc2B, True, True, nlen=1)
            E("act", lambda e: e.copy(misc[:, o_sc + 6:o_sc + 7], pbank(1, 0, 1)),
              ins=[pbB[1]], outs=[miscB])
            E("dve", lambda e: e.tensor_single_scalar(misc[:, o_b1s:o_b1s + 32],
                                                      misc[:, o_b1:o_b1 + 32],
                                                      misc[:, o_sc + 1:o_sc + 2], OP.mult),
              ins=[miscB], outs=[miscB])
            if os.environ.get("MK_SCANDBG", "") == "sc":
                E("dve", lambda e: e.tensor_copy(X[0:1, 0:16], misc2[0:1, 16:32]),
                  ins=[misc2B], outs=[Xb[0]])
                E("dve", lambda e: e.tensor_copy(X[0:1, 16:24], misc[0:1, o_sc:o_sc + 8]),
                  ins=[miscB], outs=[Xb[0]])
                E("sp", lambda e: e.dma_start(out=X[0:1, 40:44], in_=ar_out.ap()[0:1, 0:4]),
                  ins=[ar2D], outs=[Xb[0]], inc=16)
                E("sp", lambda e: e.dma_start(out=X[0:1, 44:48], in_=ar_in.ap()[0:1, 0:4]),
                  ins=[arD], outs=[Xb[0]], inc=16)
                E("dve", lambda e: e.tensor_copy(X[0:1, 70:198], misc2[0:1, 160:288]),
                  ins=[misc2B], outs=[Xb[0]])
                raise _SkipRest
            xqs, xqsb = A.alloc((128, 512), F32, "xqs")
            xq16, xq16b = A.alloc((128, 512), BF16, "xq16")
            for k in range(PT):
                for ch in range(4):
                    c0 = ch * 512
                    E("dve", lambda e, k=k, c0=c0: e.tensor_single_scalar(
                        xqs[:], xn32[:, k * T + c0:k * T + c0 + 512],
                        misc[:, o_sc:o_sc + 1], OP.mult),
                      ins=[xn32b, miscB], outs=[xqsb])
                    E("dve", lambda e: e.tensor_scalar(xqs[:], xqs[:], MAGIC, None, OP.add),
                      ins=[xqsb], outs=[xqsb])
                    E("dve", lambda e: e.tensor_scalar(xq16[:], xqs[:], MAGIC, None,
                                                       OP.subtract),
                      ins=[xqsb], outs=[xq16b])
                    E("sp", lambda e, k=k, c0=c0: e.dma_start(
                        out=xq_sp.ap()[k * 128:(k + 1) * 128, c0:c0 + 512], in_=xq16[:]),
                      ins=[xq16b], outs=[xqD], inc=16)
            A.release(mpre)


            do_attn1 = not (N_LAYERS == 2 and MK_PARTS == "ffn")
            do_ffn1 = not (N_LAYERS == 2 and MK_PARTS == "attn")
            m0 = A.mark()
            if do_attn1:
                rope_attn("l1_wqT", "l1_wkT", "l1_wvT", "l1_w2T", True, o_sc4=o_sc + 4)
            A.release(m0)

            if not do_ffn1:
                raise _SkipRest
            m0 = A.mark()
            w1q, w1qb = A.alloc((128, 8 * 512), BF16, "w1q")
            fstg, fstgb = A.alloc((128, 512), F32, "fstg")
            f2, f2b = A.alloc((128, 512), F32, "f2")
            xqv = [A.alloc((128, 8 * 512), BF16, f"xqv{i}") for i in range(2)]
            hmax, hmaxb = A.alloc((128, 4), F32, "hmax")
            hst = [A.alloc((128, 512), F32, f"hst{i}") for i in range(2)]
            for oc in range(8):
                for kt in range(PT):
                    E("sp", lambda e, oc=oc, kt=kt: e.dma_start(
                        out=fstg[:], in_=di["l1f_w1T"].ap()[kt * 128:(kt + 1) * 128,
                                                            oc * 512:(oc + 1) * 512]),
                      outs=[fstgb], inc=16)
                    E("dve", lambda e, kt=kt: e.tensor_single_scalar(
                        f2[:], fstg[:], misc[:, o_sc + 5:o_sc + 6], OP.mult),
                      ins=[fstgb, miscB], outs=[f2b])
                    E("dve", lambda e: e.tensor_scalar(f2[:], f2[:], MAGIC, None, OP.add),
                      ins=[f2b], outs=[f2b])
                    E("dve", lambda e, kt=kt: e.tensor_scalar(
                        w1q[:, kt * 512:(kt + 1) * 512], f2[:], MAGIC, None, OP.subtract),
                      ins=[f2b], outs=[w1qb])
                for ch in range(4):
                    c0 = ch * 512
                    xqt, xqtb = xqv[ch % 2]
                    for kt in range(PT):
                        E("gp", lambda e, kt=kt, c0=c0: e.dma_start(
                            out=xqt[:, kt * 512:(kt + 1) * 512],
                            in_=xq_sp.ap()[kt * 128:(kt + 1) * 128, c0:c0 + 512]),
                          ins=[xqD], outs=[xqtb], inc=16)
                    for mt4 in range(4):
                        dglob = oc * 4 + mt4
                        bank = (mt4 + ch) % 2
                        for kt in range(PT):
                            mm(bank, w1q, (0, 128, kt * 512 + mt4 * 128,
                                           kt * 512 + mt4 * 128 + 128), w1qb,
                               xqt, (0, 128, kt * 512, kt * 512 + 512), xqtb,
                               start=(kt == 0), stop=(kt == PT - 1))
                        hs, hsb = hst[bank]
                        E("act", lambda e, bank=bank, dglob=dglob, hs=hs: e.activation(
                            hs[:], pbank(bank), AF.Gelu,
                            scale=misc[:, o_sc + 1:o_sc + 2],
                            bias=misc[:, o_b1s + dglob:o_b1s + dglob + 1]),
                          ins=[pbB[bank], miscB], outs=[hsb])
                        E("dve", lambda e, hs=hs: e.reduce_max(hmax[:, 1:2], hs[:], axis=AX,
                                                               apply_absolute_value=True),
                          ins=[hsb], outs=[hmaxb], hz=True)
                        E("dve", lambda e, oc=oc, ch=ch, mt4=mt4: (
                            e.tensor_copy(hmax[:, 0:1], hmax[:, 1:2])
                            if (oc == 0 and ch == 0 and mt4 == 0) else
                            e.tensor_tensor(hmax[:, 0:1], hmax[:, 0:1], hmax[:, 1:2],
                                            OP.max)), ins=[hmaxb], outs=[hmaxb])
                        E("sp", lambda e, dglob=dglob, c0=c0, hs=hs: e.dma_start(
                            out=h_sp.ap()[dglob * 128:(dglob + 1) * 128, c0:c0 + 512],
                            in_=hs[:]), ins=[hsb], outs=[hD], inc=16)
            flD2 = flipD
            E("sp", lambda e: e.dma_start(
                out=flip_sp.ap()[3:4, :].rearrange("a b -> (a b)"), in_=hmax[:, 0:1]),
              ins=[hmaxb], outs=[flD2], inc=16)
            E("sp", lambda e: e.dma_start(out=misc2[0:1, 32:160], in_=flip_sp.ap()[3:4, :]),
              ins=[flD2], outs=[misc2B], inc=16)
            E("dve", lambda e: e.reduce_max(misc2[0:1, 16:17], misc2[0:1, 32:160], axis=AX),
              ins=[misc2B], outs=[misc2B], hz=True)
            E("gp", lambda e: e.dma_start(out=ar2_in.ap()[0:1, 0:1], in_=misc2[0:1, 16:17]),
              ins=[misc2B], outs=[arD], inc=16)
            E("gp", lambda e: e.collective_compute(
                "AllReduce", OP.max, replica_groups=[list(range(8))],
                ins=[ar2_in.ap().opt()], outs=[ar2_out.ap().opt()]),
              ins=[arD], outs=[ar2D], isasync=True)
            E("gp", lambda e: e.dma_start(out=misc2[0:1, 16:17], in_=ar2_out.ap()[0:1, 0:1]),
              ins=[ar2D], outs=[misc2B], inc=16)
            E("dve", lambda e: e.tensor_scalar(misc2[0:1, 26:27], misc2[0:1, 16:17],
                                               1e-6, 1.0 / 127.0, OP.max, OP.mult),
              ins=[misc2B], outs=[misc2B])
            E("dve", lambda e: e.reciprocal(misc2[0:1, 27:28], misc2[0:1, 26:27]),
              ins=[misc2B], outs=[misc2B], hz=True)
            E("dve", lambda e: e.tensor_tensor(misc2[0:1, 28:29], misc2[0:1, 26:27],
                                               misc2[0:1, 22:23], OP.mult),
              ins=[misc2B], outs=[misc2B])
            mm(1, onesf, (0, 1, 0, 128), CB, misc2, (0, 1, 27, 28), misc2B, True, True, nlen=1)
            E("act", lambda e: e.copy(misc[:, o_sc + 2:o_sc + 3], pbank(1, 0, 1)),
              ins=[pbB[1]], outs=[miscB])
            mm(1, onesf, (0, 1, 0, 128), CB, misc2, (0, 1, 28, 29), misc2B, True, True, nlen=1)
            E("act", lambda e: e.copy(misc[:, o_sc + 3:o_sc + 4], pbank(1, 0, 1)),
              ins=[pbB[1]], outs=[miscB])
            E("dve", lambda e: e.tensor_single_scalar(misc[:, o_b2s:o_b2s + 8],
                                                      misc[:, o_b2:o_b2 + 8],
                                                      misc[:, o_sc + 3:o_sc + 4], OP.mult),
              ins=[miscB], outs=[miscB])
            for kt in range(32):
                E("sp", lambda e, kt=kt: e.dma_start(
                    out=fstg[:], in_=di["l1f_w2T"].ap()[kt * 128:(kt + 1) * 128, 0:512]),
                  outs=[fstgb], inc=16)
                E("sp", lambda e, kt=kt: e.dma_start(
                    out=f2[:], in_=di["l1f_w2T"].ap()[kt * 128:(kt + 1) * 128, 512:1024]),
                  outs=[f2b], inc=16)
                for half, (src_t, src_b) in enumerate(((fstg, fstgb), (f2, f2b))):
                    E("dve", lambda e, src_t=src_t: e.tensor_single_scalar(
                        hst[0][0][:], src_t[:], misc[:, o_sc + 6:o_sc + 7], OP.mult),
                      ins=[src_b, miscB], outs=[hst[0][1]])
                    E("dve", lambda e: e.tensor_scalar(hst[0][0][:], hst[0][0][:], MAGIC,
                                                       None, OP.add),
                      ins=[hst[0][1]], outs=[hst[0][1]])
                    E("dve", lambda e: e.tensor_scalar(xqv[0][0][:, 0:512], hst[0][0][:],
                                                       MAGIC, None, OP.subtract),
                      ins=[hst[0][1]], outs=[xqv[0][1]])
                    E("sp", lambda e, kt=kt, half=half: e.dma_start(
                        out=w2q_sp.ap()[kt * 128:(kt + 1) * 128,
                                        half * 512:(half + 1) * 512],
                        in_=xqv[0][0][:, 0:512]), ins=[xqv[0][1]], outs=[w2qD], inc=16)
            w2qs, w2qsb = A.alloc((128, 4 * 1024), BF16, "w2qs")
            hq16, hq16b = A.alloc((128, 4 * 512), BF16, "hq16")
            for ch in range(4):
                c0 = ch * 512
                first = True
                for kg in range(8):
                    for kt4 in range(4):
                        kt = kg * 4 + kt4
                        E("gp", lambda e, kt=kt, kt4=kt4: e.dma_start(
                            out=w2qs[:, kt4 * 1024:(kt4 + 1) * 1024],
                            in_=w2q_sp.ap()[kt * 128:(kt + 1) * 128, :]),
                          ins=[w2qD], outs=[w2qsb], inc=16)
                        E("sp", lambda e, kt=kt, c0=c0: e.dma_start(
                            out=fstg[:], in_=h_sp.ap()[kt * 128:(kt + 1) * 128, c0:c0 + 512]),
                          ins=[hD], outs=[fstgb], inc=16)
                        E("dve", lambda e: e.tensor_single_scalar(
                            f2[:], fstg[:], misc[:, o_sc + 2:o_sc + 3], OP.mult),
                          ins=[fstgb, miscB], outs=[f2b])
                        E("dve", lambda e: e.tensor_scalar(f2[:], f2[:], MAGIC, None, OP.add),
                          ins=[f2b], outs=[f2b])
                        E("dve", lambda e, kt4=kt4: e.tensor_scalar(
                            hq16[:, kt4 * 512:(kt4 + 1) * 512], f2[:], MAGIC, None,
                            OP.subtract), ins=[f2b], outs=[hq16b])
                    for mt in range(8):
                        for kt4 in range(4):
                            mm(mt, w2qs, (0, 128, kt4 * 1024 + mt * 128,
                                          kt4 * 1024 + mt * 128 + 128), w2qsb,
                               hq16, (0, 128, kt4 * 512, kt4 * 512 + 512), hq16b,
                               start=(kg == 0 and kt4 == 0),
                               stop=(kg == 7 and kt4 == 3))
                for mt in range(8):
                    E("act", lambda e, mt=mt: e.activation(
                        fstg[:], pbank(mt), AF.Identity, scale=misc[:, o_sc + 3:o_sc + 4],
                        bias=misc[:, o_b2s + mt:o_b2s + mt + 1]),
                      ins=[pbB[mt], miscB], outs=[fstgb])
                    E("dve", lambda e, mt=mt, c0=c0: e.tensor_tensor(
                        X[:, mt * T + c0:mt * T + c0 + 512],
                        X[:, mt * T + c0:mt * T + c0 + 512], fstg[:], OP.add),
                      ins=[fstgb], outs=[Xb[mt]])
            A.release(m0)


        def l2():
            o_bq2 = load_vec("l2_bq", D)
            o_bk2 = load_vec("l2_bk", D)
            o_bo2 = load_vec("l2_bo", D)
            o_bs1 = load_vec("l2f_bs1", D)
            o_bs2 = load_vec("l2f_bs2", D)
            o_br = mslot(1)
            E("sp", lambda e: e.dma_start(out=misc[0:31, o_br:o_br + 1],
                                          in_=AP(di["l2f_br"].ap().tensor, 0, [[1, 31], [1, 1]]),
                                          allow_slow_non_contiguous=True),
              outs=[miscB], inc=16)
            bv16, bv16b = A.alloc((1, 1024), BF16, "l2bv16")
            E("gp", lambda e: e.dma_start(out=bv16[:],
                                          in_=di["l2_bv"].ap().rearrange("(a b) -> a b", a=1)),
              outs=[bv16b], inc=16)

            mpre = A.mark()
            layer_norm(False)
            A.release(mpre)
            do_attn2 = not (N_LAYERS == 3 and MK_PARTS == "ffn")
            do_ffn2 = not (N_LAYERS == 3 and MK_PARTS == "attn")
            m0 = A.mark()
            if do_attn2:
                rope_attn("l2_wqT", "l2_wkT", "l2_wvT", "l2_woT", False,
                          o_bq=o_bq2, o_bk=o_bk2, bv16=bv16, bv16b=bv16b, o_bo=o_bo2,
                          o_sc4=o_br + 1 if False else mslot(1))
            A.release(m0)
            if not do_ffn2:
                return

            m0 = A.mark()
            cvt, cvtb = A.alloc((128, 512), F32, "l2cv")
            for mt in range(8):
                for ch in range(4):
                    c0 = ch * 512
                    E("dve", lambda e, mt=mt, c0=c0: e.tensor_tensor(
                        X[:, mt * T + c0:mt * T + c0 + 512],
                        X[:, mt * T + c0:mt * T + c0 + 512],
                        XN[:, mt * T + c0:mt * T + c0 + 512], OP.add),
                      ins=[XNb[mt]], outs=[Xb[mt]], hz=False)
            ws, wsb = A.alloc((128, PT * 1024), BF16, "l2ws")
            h16, h16b = A.alloc((128, PT * T), BF16, "l2h")
            wload(ws, wsb, "l2f_ws1", 0, PT, 1024)
            for ch in range(4):
                c0 = ch * 512
                for mt in range(8):
                    bank = mt % 2
                    for k in range(PT):
                        mm(bank, ws, (0, 128, k * 1024 + mt * 128, k * 1024 + mt * 128 + 128),
                           wsb, XN, (0, 128, k * T + c0, k * T + c0 + 512), XNb[k],
                           start=(k == 0), stop=(k == PT - 1))
                    E("act", lambda e, bank=bank, mt=mt, c0=c0: e.activation(
                        h16[:, mt * T + c0:mt * T + c0 + 512], pbank(bank), AF.Gelu,
                        bias=misc[:, o_bs1 + mt:o_bs1 + mt + 1]),
                      ins=[pbB[bank], miscB], outs=[h16b])
            wload(ws, wsb, "l2f_ws2", 0, PT, 1024)
            for ch in range(4):
                c0 = ch * 512
                for mt in range(8):
                    bank = mt % 2
                    for k in range(PT):
                        mm(bank, ws, (0, 128, k * 1024 + mt * 128, k * 1024 + mt * 128 + 128),
                           wsb, h16, (0, 128, k * T + c0, k * T + c0 + 512), h16b,
                           start=(k == 0), stop=(k == PT - 1))
                    E("act", lambda e, bank=bank, mt=mt: e.activation(
                        cvt[:], pbank(bank), AF.Identity, scale=1.0,
                        bias=misc[:, o_bs2 + mt:o_bs2 + mt + 1]),
                      ins=[pbB[bank], miscB], outs=[cvtb])
                    E("dve", lambda e, mt=mt, c0=c0: e.tensor_tensor(
                        X[:, mt * T + c0:mt * T + c0 + 512],
                        X[:, mt * T + c0:mt * T + c0 + 512], cvt[:], OP.add),
                      ins=[cvtb], outs=[Xb[mt]], hz=False)
            wr, wrb = A.alloc((128, PT * 31), BF16, "l2wr")
            affs, affsb = A.alloc((128, T), F32, "l2aff")
            ptm, ptmb = A.alloc((128, 31), F32, "l2ptm")
            gtm, gtmb = A.alloc((128, 31), F32, "l2gtm")
            s1, s1b = A.alloc((128, 8), F32, "l2s1")
            gfm, gfmb = A.alloc((128, 128), BF16, "l2gfm")
            E("dve", lambda e: e.memset(affs[:], 0.0), outs=[affsb], hz=False)
            wload(wr, wrb, "l2f_Wr", 0, PT, 31)
            for ch in range(4):
                c0 = ch * 512
                bank = ch % 2
                for k in range(PT):
                    mm(bank, wr, (0, 128, k * 31, k * 31 + 31), wrb,
                       XN, (0, 128, k * T + c0, k * T + c0 + 512), XNb[k],
                       start=(k == 0), stop=(k == PT - 1), mlen=31)
                E("act", lambda e, bank=bank, c0=c0: e.activation(
                    affs[0:31, c0:c0 + 512], pbank(bank)[0:31, :], AF.Identity,
                    scale=1.0, bias=misc[0:31, o_br:o_br + 1]),
                  ins=[pbB[bank], miscB], outs=[affsb])
            for tt in range(16):
                E("pe", lambda e, tt=tt: e.transpose(pbank(4)[0:128, 0:128],
                                                     affs[:, tt * 128:(tt + 1) * 128],
                                                     identf[:]),
                  ins=[affsb, CB], outs=[pbB[4]])
                E("dve", lambda e: e.reduce_max(s1[:, 0:1], pbank(4)[:, 0:31], axis=AX,
                                                negate=True),
                  ins=[pbB[4]], outs=[s1b], hz=True)
                E("act", lambda e: e.activation(ptm[:], pbank(4)[:, 0:31], AF.Exp,
                                                bias=s1[:, 0:1]),
                  ins=[pbB[4], s1b], outs=[ptmb])
                E("dve", lambda e: e.reduce_sum(s1[:, 1:2], ptm[:], axis=AX),
                  ins=[ptmb], outs=[s1b], hz=True)
                E("dve", lambda e: e.reciprocal(s1[:, 2:3], s1[:, 1:2]),
                  ins=[s1b], outs=[s1b], hz=True)
                E("dve", lambda e: e.tensor_single_scalar(ptm[:], ptm[:], s1[:, 2:3],
                                                          OP.mult),
                  ins=[s1b], outs=[ptmb])
                E("dve", lambda e: e.reduce_max(s1[:, 3:4], ptm[:], axis=AX),
                  ins=[ptmb], outs=[s1b], hz=True)
                E("dve", lambda e: e.tensor_single_scalar(gtm[:], ptm[:], s1[:, 3:4],
                                                          OP.is_ge),
                  ins=[ptmb, s1b], outs=[gtmb])
                E("dve", lambda e: e.scalar_tensor_tensor(gtm[:], gtm[:], -2.0, ptm[:],
                                                          OP.mult, OP.add),
                  ins=[ptmb], outs=[gtmb], hz=False)
                E("dve", lambda e: e.reduce_max(s1[:, 4:5], gtm[:], axis=AX),
                  ins=[gtmb], outs=[s1b], hz=True)
                E("dve", lambda e: e.tensor_single_scalar(cvt[:, 0:31], gtm[:], s1[:, 4:5],
                                                          OP.is_ge),
                  ins=[gtmb, s1b], outs=[cvtb])
                E("dve", lambda e: e.scalar_tensor_tensor(cvt[:, 0:31], cvt[:, 0:31], -2.0,
                                                          gtm[:], OP.mult, OP.add),
                  ins=[gtmb], outs=[cvtb], hz=False)
                E("dve", lambda e: e.reduce_max(s1[:, 5:6], cvt[:, 0:31], axis=AX),
                  ins=[cvtb], outs=[s1b], hz=True)
                E("dve", lambda e: e.tensor_single_scalar(gtm[:], ptm[:], s1[:, 5:6],
                                                          OP.is_ge),
                  ins=[ptmb, s1b], outs=[gtmb])
                E("dve", lambda e: e.tensor_tensor(gtm[:], gtm[:], ptm[:], OP.mult),
                  ins=[ptmb], outs=[gtmb])
                E("pe", lambda e, tt=tt: e.transpose(pbank(5)[0:31, 0:128], gtm[:, 0:31],
                                                     identf[:]),
                  ins=[gtmb, CB], outs=[pbB[5]])
                E("act", lambda e: e.copy(gfm[0:31, 0:128], pbank(5)[0:31, 0:128]),
                  ins=[pbB[5]], outs=[gfmb])
                E("sp", lambda e, tt=tt: e.dma_start(
                    out=gates_sp.ap()[:, tt * 128:(tt + 1) * 128], in_=gfm[0:31, 0:128]),
                  ins=[gfmb], outs=[gD], inc=16)
            A.release(m0)
            m0 = A.mark()
            cvt, cvtb = A.alloc((128, 512), F32, "l2cv2")
            he, heb = A.alloc((128, PT * T), BF16, "l2he")
            wv = [A.alloc((128, PT * 1024), BF16, f"l2we{i}") for i in range(2)]
            be1r, be1rb = A.alloc((128, 16), F32, "l2be1")
            be2r, be2rb = A.alloc((1, 1024), BF16, "l2be2")
            grow, growb = A.alloc((1, T), BF16, "l2grow")
            for e_ in range(31):
                wt, wtb = wv[e_ % 2]
                for i in range(PT):
                    E("gp", lambda e, e_=e_, i=i, wt=wt: e.dma_start(
                        out=wt[:, i * 1024:(i + 1) * 1024],
                        in_=di["l2f_We1"].ap()[e_, i * 128:(i + 1) * 128, :]),
                      outs=[wtb], inc=16)
                E("sp", lambda e, e_=e_: e.dma_start(
                    out=be1r[:, 0:8],
                    in_=AP(di["l2f_Be1"].ap().tensor, e_ * 1024, [[1, 128], [128, 8]]),
                    allow_slow_non_contiguous=True),
                  outs=[be1rb], inc=16)
                E("gp", lambda e, e_=e_: e.dma_start(
                    out=grow[:], in_=gates_sp.ap()[e_:e_ + 1, :]),
                  ins=[gD], outs=[growb], inc=16)
                for ch in range(4):
                    c0 = ch * 512
                    for mt in range(8):
                        bank = mt % 2
                        for k in range(PT):
                            mm(bank, wt, (0, 128, k * 1024 + mt * 128,
                                          k * 1024 + mt * 128 + 128), wtb,
                               XN, (0, 128, k * T + c0, k * T + c0 + 512), XNb[k],
                               start=(k == 0), stop=(k == PT - 1))
                        E("act", lambda e, bank=bank, mt=mt, c0=c0: e.activation(
                            he[:, mt * T + c0:mt * T + c0 + 512], pbank(bank), AF.Gelu,
                            bias=be1r[:, mt:mt + 1]),
                          ins=[pbB[bank], be1rb], outs=[heb])
                wt2, wt2b = wv[(e_ + 1) % 2]
                for i in range(PT):
                    E("gp", lambda e, e_=e_, i=i, wt2=wt2: e.dma_start(
                        out=wt2[:, i * 1024:(i + 1) * 1024],
                        in_=di["l2f_We2"].ap()[e_, i * 128:(i + 1) * 128, :]),
                      outs=[wt2b], inc=16)
                E("gp", lambda e, e_=e_: e.dma_start(
                    out=be2r[:], in_=di["l2f_Be2"].ap()[e_:e_ + 1, :]),
                  outs=[be2rb], inc=16)
                for ch in range(4):
                    c0 = ch * 512
                    mm(4, ones16, (0, 1, 0, 128), CB, grow, (0, 1, c0, c0 + 512), growb,
                       True, True)
                    for mt in range(8):
                        bank = 2 + mt % 2
                        for k in range(PT):
                            mm(bank, wt2, (0, 128, k * 1024 + mt * 128,
                                           k * 1024 + mt * 128 + 128), wt2b,
                               he, (0, 128, k * T + c0, k * T + c0 + 512), heb,
                               start=(k == 0), stop=False)
                        mm(bank, be2r, (0, 1, mt * 128, mt * 128 + 128), be2rb,
                           grow, (0, 1, c0, c0 + 512), growb, start=False, stop=True)
                        E("act", lambda e, bank=bank: e.copy(cvt[:], pbank(bank)),
                          ins=[pbB[bank]], outs=[cvtb])
                        E("dve", lambda e: e.tensor_tensor(cvt[:], cvt[:], pbank(4),
                                                           OP.mult),
                          ins=[cvtb, pbB[4]], outs=[cvtb], hz=False)
                        E("dve", lambda e, mt=mt, c0=c0: e.tensor_tensor(
                            X[:, mt * T + c0:mt * T + c0 + 512],
                            X[:, mt * T + c0:mt * T + c0 + 512], cvt[:], OP.add),
                          ins=[cvtb], outs=[Xb[mt]], hz=False)
            A.release(m0)


        def l3():
            o_g1 = load_vec("l3_g1", D); o_b1 = load_vec("l3_b1", D)
            o_gm = load_vec("l3_gm", D); o_bm = load_vec("l3_bm", D)
            o_bqa = load_vec("l3_bqkv_a", 3 * D)
            o_bqs = load_vec("l3_bqkv_s", 3 * D)
            o_bpa = load_vec("l3_bp_a", D)
            o_bos = load_vec("l3_bo_s", D)
            o_bo3 = load_vec("l3_bo", D)
            o_b1m = load_vec("l3_b1m", 4 * D)
            o_b2m = load_vec("l3_b2m", D)
            o_sc3 = mslot(4)
            o_bqs8 = mslot(8)
            E("dve", lambda e: e.tensor_scalar(misc[:, o_bqs8:o_bqs8 + 8],
                                               misc[:, o_bqs:o_bqs + 8], 0.125, None,
                                               OP.mult), ins=[miscB], outs=[miscB])

            mpre = A.mark()
            layer_norm(False)
            A.release(mpre)
            do_attn3 = not (N_LAYERS == 4 and MK_PARTS == "ffn")
            do_ffn3 = not (N_LAYERS == 4 and MK_PARTS == "attn")

            def ln_of(src, srcbl, dst, dstb, o_g, o_b):
                sq2, sq2b = A.alloc((128, 512), F32, "lnq")
                for ch in range(4):
                    c0 = ch * 512
                    for k in range(PT):
                        mm(0, ones16, (0, 128, 0, 1), CB, src,
                           (0, 128, k * T + c0, k * T + c0 + 512), srcbl[k],
                           start=(k == 0), stop=(k == PT - 1), mlen=1)
                    for k in range(PT):
                        E("act", lambda e, k=k: e.activation(
                            sq2[:], src[:, k * T + c0:k * T + c0 + 512], AF.Square),
                          ins=[srcbl[k]], outs=[sq2b])
                        mm(1, onesf, (0, 128, 0, 1), CB, sq2, (0, 128, 0, 512), sq2b,
                           start=(k == 0), stop=(k == PT - 1), mlen=1)
                    E("dve", lambda e: e.tensor_scalar(misc2[0:1, 0:512], pbank(0)[0:1, :],
                                                       1.0 / D, None, OP.mult),
                      ins=[pbB[0]], outs=[misc2B])
                    mm(2, onesf, (0, 1, 0, 128), CB, misc2, (0, 1, 0, 512), misc2B,
                       True, True)
                    E("dve", lambda e: e.tensor_tensor(misc2[0:1, 0:512], misc2[0:1, 0:512],
                                                       misc2[0:1, 0:512], OP.mult),
                      ins=[misc2B], outs=[misc2B])
                    E("dve", lambda e: e.scalar_tensor_tensor(
                        misc2[0:1, 0:512], pbank(1)[0:1, :], 1.0 / D, misc2[0:1, 0:512],
                        OP.mult, OP.subtract), ins=[pbB[1], misc2B], outs=[misc2B])
                    E("act", lambda e: e.activation(misc2[0:1, 0:512], misc2[0:1, 0:512],
                                                    AF.Sqrt, bias=1e-5),
                      ins=[misc2B], outs=[misc2B])
                    E("dve", lambda e: e.reciprocal(misc2[0:1, 0:512], misc2[0:1, 0:512]),
                      ins=[misc2B], outs=[misc2B], hz=True)
                    mm(3, onesf, (0, 1, 0, 128), CB, misc2, (0, 1, 0, 512), misc2B,
                       True, True)
                    for k in range(PT):
                        E("dve", lambda e, k=k: e.tensor_tensor(
                            sq2[:], src[:, k * T + c0:k * T + c0 + 512], pbank(2),
                            OP.subtract), ins=[srcbl[k], pbB[2]], outs=[sq2b], hz=False)
                        E("dve", lambda e: e.tensor_tensor(sq2[:], sq2[:], pbank(3),
                                                           OP.mult),
                          ins=[sq2b, pbB[3]], outs=[sq2b], hz=False)
                        E("dve", lambda e, k=k: e.tensor_single_scalar(
                            sq2[:], sq2[:], misc[:, o_g + k:o_g + k + 1], OP.mult),
                          ins=[sq2b, miscB], outs=[sq2b], hz=False)
                        E("dve", lambda e, k=k: e.tensor_single_scalar(
                            dst[:, k * T + c0:k * T + c0 + 512], sq2[:],
                            misc[:, o_b + k:o_b + k + 1], OP.add),
                          ins=[sq2b, miscB], outs=[dstb], hz=False)

            if do_attn3:
                m0 = A.mark()
                xni, xnib = A.alloc((128, PT * T), BF16, "xni")
                ln_of(XN, XNb, xni, [xnib] if False else xnib, o_g1, o_b1)
                m_ao = A.mark()
                ao, aob = A.alloc((128, PT * T), BF16, "l3ao")
                E("gp", lambda e: e.dma_start(
                    out=bv3[:], in_=di["l3_bqkv_a"].ap()
                    .rearrange("(a b) -> a b", a=3)[2:3, :]),
                  outs=[bv3b], inc=16)
                msw = A.mark()
                q16, q16b = A.alloc((128, T), BF16, "swq")
                k16, k16b = A.alloc((128, T), BF16, "swk")
                vth, vthb = A.alloc((128, 16 * 65), BF16, "swv")
                ws3, ws3b = A.alloc((128, 8 * 192), BF16, "sww")
                rst, rstb = A.alloc((128, 512), F32, "swr")
                ps3 = [A.alloc((128, 128), BF16, f"swp{i}") for i in range(2)]
                nr3, nr3b = A.alloc((128, 130), BF16, "swn")
                for h in range(16):
                    for j, wnm in enumerate(("l3_Wqkv_a",)):
                        for kk in range(PT):
                            E("gp", lambda e, kk=kk, h=h: e.dma_start(
                                out=ws3[:, kk * 192:kk * 192 + 64],
                                in_=di["l3_Wqkv_a"].ap()[kk * 128:(kk + 1) * 128,
                                                         h * 64:h * 64 + 64]),
                              outs=[ws3b], inc=16)
                            E("gp", lambda e, kk=kk, h=h: e.dma_start(
                                out=ws3[:, kk * 192 + 64:kk * 192 + 128],
                                in_=di["l3_Wqkv_a"].ap()[kk * 128:(kk + 1) * 128,
                                                         1024 + h * 64:1024 + h * 64 + 64]),
                              outs=[ws3b], inc=16)
                            E("gp", lambda e, kk=kk, h=h: e.dma_start(
                                out=ws3[:, kk * 192 + 128:kk * 192 + 192],
                                in_=di["l3_Wqkv_a"].ap()[kk * 128:(kk + 1) * 128,
                                                         2048 + h * 64:2048 + h * 64 + 64]),
                              outs=[ws3b], inc=16)
                    for which, dst3, dstb3, bofs in ((0, q16, q16b, o_bqa),
                                                     (1, k16, k16b, o_bqa + 8)):
                        for ch in range(4):
                            c0 = ch * 512
                            bank = ch % 2
                            for kk in range(PT):
                                mm(bank, ws3, (0, 128, kk * 192 + which * 64,
                                               kk * 192 + which * 64 + 64), ws3b,
                                   xni, (0, 128, kk * T + c0, kk * T + c0 + 512), xnib,
                                   start=(kk == 0), stop=(kk == PT - 1), mlen=64)
                            bc = bofs + h // 2
                            hb2 = (h % 2) * 64
                            E("act", lambda e, bank=bank, bc=bc, hb2=hb2, dst3=dst3, c0=c0:
                              e.activation(dst3[0:64, c0:c0 + 512], pbank(bank)[0:64, :],
                                           AF.Identity, scale=1.0,
                                           bias=misc[hb2:hb2 + 64, bc:bc + 1]),
                              ins=[pbB[bank], miscB], outs=[dstb3])
                    E("dve", lambda e: e.memset(vth[:], 1.0), outs=[vthb], hz=False)
                    for tt in range(16):
                        bank = tt % 2 + 2
                        for kk in range(PT):
                            mm(bank, xni, (0, 128, kk * T + tt * 128, kk * T + tt * 128 + 128),
                               xnib, ws3, (0, 128, kk * 192 + 128, kk * 192 + 192), ws3b,
                               start=(kk == 0), stop=False, nlen=64)
                        mm(bank, ones16, (0, 1, 0, 128), CB, bv3,
                           (0, 1, h * 64, h * 64 + 64), bv3b, start=False, stop=True,
                           nlen=64)
                        E("act", lambda e, bank=bank, tt=tt: e.copy(
                            vth[:, tt * 65:tt * 65 + 64], pbank(bank, 0, 64)),
                          ins=[pbB[bank]], outs=[vthb])
                    E("pe", lambda e: e.matmul(
                        pbank(6)[0:128, 0:128],
                        AP(k16.ap().tensor, 0, [[T, 64], [16, 128]]),
                        AP(q16.ap().tensor, 0, [[T, 64], [16, 128]]),
                        start=True, stop=True), ins=[k16b, q16b], outs=[pbB[6]])
                    E("dve", lambda e: e.reduce_max(misc2[:, 28:29], pbank(6)[:, 0:128],
                                                    axis=AX),
                      ins=[pbB[6]], outs=[misc2B], hz=True)
                    E("sp", lambda e: e.dma_start(
                        out=flip_sp.ap()[0:1, :].rearrange("a b -> (a b)"),
                        in_=misc2[:, 28:29]), ins=[misc2B], outs=[flipD], inc=16)
                    E("sp", lambda e: e.dma_start(out=misc2[0:1, 32:160],
                                                  in_=flip_sp.ap()[0:1, :]),
                      ins=[flipD], outs=[misc2B], inc=16)
                    E("dve", lambda e: e.reduce_max(misc2[0:1, 30:31], misc2[0:1, 32:160],
                                                    axis=AX),
                      ins=[misc2B], outs=[misc2B], hz=True)
                    E("dve", lambda e: e.tensor_scalar(misc2[0:1, 30:31], misc2[0:1, 30:31],
                                                       -0.125, None, OP.mult),
                      ins=[misc2B], outs=[misc2B])
                    mm(6, onesf, (0, 1, 0, 128), CB, misc2, (0, 1, 30, 31), misc2B,
                       True, True, nlen=1)
                    E("act", lambda e: e.copy(misc[:, o_sc3:o_sc3 + 1], pbank(6, 0, 1)),
                      ins=[pbB[6]], outs=[miscB])
                    for n in range(16):
                        kts = ([n] if n == 0 else [n - 1, n])
                        nk = len(kts)
                        for j, ktn in enumerate(kts):
                            bank = j % 2
                            mm(bank, k16, (0, 64, ktn * 128, ktn * 128 + 128), k16b,
                               q16, (0, 64, n * 128, n * 128 + 128), q16b,
                               start=True, stop=True, nlen=128)
                            moff = 128 if (nk == 1 or j == 1) else 0
                            E("dve", lambda e, bank=bank, moff=moff: e.tensor_tensor(
                                pbank(bank, 0, 128), pbank(bank, 0, 128),
                                swamask[:, moff:moff + 128], OP.add),
                              ins=[CB], outs=[pbB[bank]], hz=False)
                            ps, psb = ps3[j]
                            E("act", lambda e, bank=bank, ps=ps: e.activation(
                                ps[:], pbank(bank, 0, 128), AF.Exp, scale=0.125,
                                bias=misc[:, o_sc3:o_sc3 + 1]),
                              ins=[pbB[bank], miscB], outs=[psb])
                            mm(4, vth, (0, 128, ktn * 65, ktn * 65 + 65), vthb,
                               ps, (0, 128, 0, 128), psb,
                               start=(j == 0), stop=(j == nk - 1), mlen=65, nlen=128)
                        E("dve", lambda e: e.tensor_scalar(rst[64:65, 0:128],
                                                           pbank(4)[64:65, 0:128],
                                                           1e-30, None, OP.add),
                          ins=[pbB[4]], outs=[rstb])
                        E("dve", lambda e: e.reciprocal(rst[64:65, 128:256],
                                                        rst[64:65, 0:128]),
                          ins=[rstb], outs=[rstb], hz=True)
                        E("dve", lambda e: e.tensor_copy(nr3[64:65, 0:128],
                                                         rst[64:65, 128:256]),
                          ins=[rstb], outs=[nr3b], hz=True)
                        mm(5, ones16, (64, 65, 0, 64), CB, nr3, (64, 65, 0, 128), nr3b,
                           True, True, mlen=64, nlen=128)
                        E("act", lambda e: e.copy(rst[0:64, 0:128], pbank(4)[0:64, 0:128]),
                          ins=[pbB[4]], outs=[rstb])
                        E("dve", lambda e, h=h, n=n: e.tensor_tensor(
                            ao[(h % 2) * 64:(h % 2) * 64 + 64,
                               (h // 2) * T + n * 128:(h // 2) * T + n * 128 + 128],
                            rst[0:64, 0:128], pbank(5)[0:64, 0:128], OP.mult),
                          ins=[rstb, pbB[5]], outs=[aob], hz=False)
                A.release(msw)
                wsp, wspb = A.alloc((128, PT * 1024), BF16, "wsp")
                ev3, ev3b = A.alloc((128, 512), BF16, "ev3")
                wload(wsp, wspb, "l3_Wp_a", 0, PT, 1024)
                for ch in range(4):
                    c0 = ch * 512
                    for mt in range(8):
                        bank = mt % 2
                        for kk in range(PT):
                            mm(bank, wsp, (0, 128, kk * 1024 + mt * 128,
                                           kk * 1024 + mt * 128 + 128), wspb,
                               ao, (0, 128, kk * T + c0, kk * T + c0 + 512), aob,
                               start=(kk == 0), stop=(kk == PT - 1))
                        E("act", lambda e, bank=bank, mt=mt: e.activation(
                            ev3[:], pbank(bank), AF.Identity, scale=1.0,
                            bias=misc[:, o_bpa + mt:o_bpa + mt + 1]),
                          ins=[pbB[bank], miscB], outs=[ev3b])
                        E("sp", lambda e, mt=mt, c0=c0: e.dma_start(
                            out=swp_sp.ap()[mt * 128:(mt + 1) * 128, c0:c0 + 512],
                            in_=ev3[:]), ins=[ev3b], outs=[swpD], inc=16)
                A.release(m_ao)
                ao2, ao2b = A.alloc((128, PT * T), BF16, "l3ao2")
                E("gp", lambda e: e.dma_start(
                    out=bv3[:], in_=di["l3_bqkv_s"].ap()
                    .rearrange("(a b) -> a b", a=3)[2:3, :]),
                  outs=[bv3b], inc=16)
                mln = A.mark()
                q16, q16b = A.alloc((128, T), BF16, "liq")
                k16, k16b = A.alloc((128, T), BF16, "lik")
                k32, k32b = A.alloc((128, 512), F32, "lik32")
                ktm, ktmb = A.alloc((128, 64), BF16, "liktm")
                vth, vthb = A.alloc((128, 16 * 65), BF16, "livt")
                ws3b2 = ws3, ws3b = A.alloc((128, 8 * 192), BF16, "liw")
                rst, rstb = A.alloc((128, 512), F32, "lir")
                el1, el1b = A.alloc((128, 512), F32, "liel")
                Ssb, Ssbb = A.alloc((128, 66), BF16, "liS")
                Sf, Sfb = A.alloc((128, 66), F32, "liSf")
                at3, at3b = A.alloc((128, 128), BF16, "liA")
                nr3, nr3b = A.alloc((128, 130), BF16, "lin")
                for h in range(16):
                    for kk in range(PT):
                        for j in range(3):
                            E("gp", lambda e, kk=kk, h=h, j=j: e.dma_start(
                                out=ws3[:, kk * 192 + j * 64:kk * 192 + j * 64 + 64],
                                in_=di["l3_Wqkv_s"].ap()[kk * 128:(kk + 1) * 128,
                                                         j * 1024 + h * 64:j * 1024 + h * 64 + 64]),
                              outs=[ws3b], inc=16)
                    for which, dst3, dstb3, bofs, sc in ((0, q16, q16b, o_bqs8, 0.125),
                                                         (1, k16, k16b, o_bqs + 8, 1.0)):
                        for ch in range(4):
                            c0 = ch * 512
                            bank = ch % 2
                            for kk in range(PT):
                                mm(bank, ws3, (0, 128, kk * 192 + which * 64,
                                               kk * 192 + which * 64 + 64), ws3b,
                                   xni, (0, 128, kk * T + c0, kk * T + c0 + 512), xnib,
                                   start=(kk == 0), stop=(kk == PT - 1), mlen=64)
                            bc = bofs + h // 2
                            hb2 = (h % 2) * 64
                            E("act", lambda e, bank=bank, bc=bc, hb2=hb2, sc=sc:
                              e.activation(rst[0:64, :], pbank(bank)[0:64, :],
                                           AF.Identity, scale=sc,
                                           bias=misc[hb2:hb2 + 64, bc:bc + 1]),
                              ins=[pbB[bank], miscB], outs=[rstb])
                            E("dve", lambda e: e.tensor_scalar(el1[0:64, :], rst[0:64, :],
                                                               0.0, None, OP.min),
                              ins=[rstb], outs=[el1b], hz=False)
                            E("act", lambda e: e.activation(el1[0:64, :], el1[0:64, :],
                                                            AF.Exp),
                              ins=[el1b], outs=[el1b])
                            E("dve", lambda e: e.scalar_tensor_tensor(
                                el1[0:64, :], rst[0:64, :], 0.0, el1[0:64, :],
                                OP.max, OP.add), ins=[rstb, el1b], outs=[el1b], hz=False)
                            E("dve", lambda e, dst3=dst3, c0=c0: e.tensor_scalar(
                                dst3[0:64, c0:c0 + 512], el1[0:64, :], 1e-4, None,
                                OP.add), ins=[el1b], outs=[dstb3], hz=False)
                    E("dve", lambda e: e.memset(vth[:], 1.0), outs=[vthb], hz=False)
                    for tt in range(16):
                        bank = tt % 2 + 2
                        for kk in range(PT):
                            mm(bank, xni, (0, 128, kk * T + tt * 128, kk * T + tt * 128 + 128),
                               xnib, ws3, (0, 128, kk * 192 + 128, kk * 192 + 192), ws3b,
                               start=(kk == 0), stop=False, nlen=64)
                        mm(bank, ones16, (0, 1, 0, 128), CB, bv3,
                           (0, 1, h * 64, h * 64 + 64), bv3b, start=False, stop=True,
                           nlen=64)
                        E("act", lambda e, bank=bank, tt=tt: e.copy(
                            vth[:, tt * 65:tt * 65 + 64], pbank(bank, 0, 64)),
                          ins=[pbB[bank]], outs=[vthb])
                    E("dve", lambda e: e.memset(Sf[:], 0.0), outs=[Sfb], hz=False)
                    E("dve", lambda e: e.memset(Ssb[:], 0.0), outs=[Ssbb], hz=False)
                    for n in range(16):
                        mm(0, k16, (0, 64, n * 128, n * 128 + 128), k16b,
                           q16, (0, 64, n * 128, n * 128 + 128), q16b, True, True,
                           nlen=128)
                        E("dve", lambda e: e.tensor_tensor(at3[:, 0:128], pbank(0, 0, 128),
                                                           trilT16[:], OP.mult),
                          ins=[pbB[0], CB], outs=[at3b], hz=False)
                        mm(1, Ssb, (0, 64, 0, 65), Ssbb,
                           q16, (0, 64, n * 128, n * 128 + 128), q16b,
                           start=True, stop=False, mlen=65, nlen=128)
                        mm(1, vth, (0, 128, n * 65, n * 65 + 65), vthb,
                           at3, (0, 128, 0, 128), at3b, start=False, stop=True,
                           mlen=65, nlen=128)
                        E("dve", lambda e: e.tensor_scalar(rst[64:65, 0:128],
                                                           pbank(1)[64:65, 0:128],
                                                           1e-6, None, OP.add),
                          ins=[pbB[1]], outs=[rstb])
                        E("dve", lambda e: e.reciprocal(rst[64:65, 128:256],
                                                        rst[64:65, 0:128]),
                          ins=[rstb], outs=[rstb], hz=True)
                        E("dve", lambda e: e.tensor_copy(nr3[64:65, 0:128],
                                                         rst[64:65, 128:256]),
                          ins=[rstb], outs=[nr3b], hz=True)
                        mm(5, ones16, (64, 65, 0, 64), CB, nr3, (64, 65, 0, 128), nr3b,
                           True, True, mlen=64, nlen=128)
                        E("act", lambda e: e.copy(rst[0:64, 0:128], pbank(1)[0:64, 0:128]),
                          ins=[pbB[1]], outs=[rstb])
                        E("dve", lambda e, h=h, n=n: e.tensor_tensor(
                            ao2[(h % 2) * 64:(h % 2) * 64 + 64,
                                (h // 2) * T + n * 128:(h // 2) * T + n * 128 + 128],
                            rst[0:64, 0:128], pbank(5)[0:64, 0:128], OP.mult),
                          ins=[rstb, pbB[5]], outs=[ao2b], hz=False)
                        E("act", lambda e, n=n: e.copy(k32[0:64, 0:128],
                                                       k16[0:64, n * 128:n * 128 + 128]),
                          ins=[k16b], outs=[k32b])
                        E("pe", lambda e: e.transpose(pbank(6)[0:128, 0:64],
                                                      k32[0:64, 0:128], identf[0:64, 0:64]),
                          ins=[k32b, CB], outs=[pbB[6]])
                        E("act", lambda e: e.copy(ktm[:, 0:64], pbank(6)[:, 0:64]),
                          ins=[pbB[6]], outs=[ktmb])
                        mm(7, ktm, (0, 128, 0, 64), ktmb, vth,
                           (0, 128, n * 65, n * 65 + 65), vthb, True, True,
                           mlen=64, nlen=65)
                        E("dve", lambda e: e.tensor_tensor(Sf[0:64, 0:65], Sf[0:64, 0:65],
                                                           pbank(7, 0, 65)[0:64, :],
                                                           OP.add),
                          ins=[pbB[7]], outs=[Sfb], hz=False)
                        E("dve", lambda e: e.tensor_copy(Ssb[0:64, 0:65], Sf[0:64, 0:65]),
                          ins=[Sfb], outs=[Ssbb], hz=False)
                A.release(mln)
                wsp, wspb = A.alloc((128, PT * 1024), BF16, "wos")
                ev3, ev3b = A.alloc((128, 512), F32, "ev3b")
                sw16, sw16b = A.alloc((128, 512), BF16, "sw16")
                mx16, mx16b = A.alloc((128, 512), BF16, "mx16")
                wload(wsp, wspb, "l3_Wo_s", 0, PT, 1024)
                for ch in range(4):
                    c0 = ch * 512
                    for mt in range(8):
                        bank = mt % 2
                        for kk in range(PT):
                            mm(bank, wsp, (0, 128, kk * 1024 + mt * 128,
                                           kk * 1024 + mt * 128 + 128), wspb,
                               ao2, (0, 128, kk * T + c0, kk * T + c0 + 512), ao2b,
                               start=(kk == 0), stop=(kk == PT - 1))
                        E("act", lambda e, bank=bank, mt=mt: e.activation(
                            ev3[:], pbank(bank), AF.Identity, scale=1.0,
                            bias=misc[:, o_bos + mt:o_bos + mt + 1]),
                          ins=[pbB[bank], miscB], outs=[ev3b])
                        E("sp", lambda e, mt=mt, c0=c0, sw16=sw16: e.dma_start(
                            out=sw16[:], in_=swp_sp.ap()[mt * 128:(mt + 1) * 128,
                                                         c0:c0 + 512]),
                          ins=[swpD], outs=[sw16b], inc=16)
                        E("dve", lambda e: e.tensor_tensor(mx16[:], ev3[:], sw16[:],
                                                           OP.add),
                          ins=[ev3b, sw16b], outs=[mx16b], hz=False)
                        E("sp", lambda e, mt=mt, c0=c0: e.dma_start(
                            out=mix_sp.ap()[mt * 128:(mt + 1) * 128, c0:c0 + 512],
                            in_=mx16[:]), ins=[mx16b], outs=[mixD], inc=16)
                A.release(m0)
                mx1 = A.mark()
                x1, x1b = A.alloc((128, PT * T), BF16, "l3x1")
                m0 = A.mark()
                wsp, wspb = A.alloc((128, PT * 1024), BF16, "wo3")
                mxr, mxrb = A.alloc((128, 8 * 512), BF16, "mxr")
                ev3, ev3b = A.alloc((128, 512), F32, "ev3c")
                wload(wsp, wspb, "l3_Wo", 0, PT, 1024)
                for ch in range(4):
                    c0 = ch * 512
                    for kk in range(PT):
                        E("gp", lambda e, kk=kk, c0=c0: e.dma_start(
                            out=mxr[:, kk * 512:(kk + 1) * 512],
                            in_=mix_sp.ap()[kk * 128:(kk + 1) * 128, c0:c0 + 512]),
                          ins=[mixD], outs=[mxrb], inc=16)
                    for mt in range(8):
                        bank = mt % 2
                        for kk in range(PT):
                            mm(bank, wsp, (0, 128, kk * 1024 + mt * 128,
                                           kk * 1024 + mt * 128 + 128), wspb,
                               mxr, (0, 128, kk * 512, kk * 512 + 512), mxrb,
                               start=(kk == 0), stop=(kk == PT - 1))
                        E("act", lambda e, bank=bank, mt=mt: e.activation(
                            ev3[:], pbank(bank), AF.Identity, scale=1.0,
                            bias=misc[:, o_bo3 + mt:o_bo3 + mt + 1]),
                          ins=[pbB[bank], miscB], outs=[ev3b])
                        E("dve", lambda e, mt=mt, c0=c0: e.tensor_tensor(
                            x1[:, mt * T + c0:mt * T + c0 + 512], ev3[:],
                            XN[:, mt * T + c0:mt * T + c0 + 512], OP.add),
                          ins=[ev3b, XNb[mt]], outs=[x1b], hz=False)
                        E("dve", lambda e, mt=mt, c0=c0: e.tensor_tensor(
                            X[:, mt * T + c0:mt * T + c0 + 512],
                            X[:, mt * T + c0:mt * T + c0 + 512],
                            x1[:, mt * T + c0:mt * T + c0 + 512], OP.add),
                          ins=[x1b], outs=[Xb[mt]], hz=False)
                A.release(m0)
                ln_of(x1, [x1b] * PT, x1, x1b, o_gm, o_bm)
                h2, h2b = x1, x1b
                m0 = A.mark()
                wsp, wspb = A.alloc((128, PT * 1024), BF16, "w13")
                hstg3, hstg3b = A.alloc((128, 512), BF16, "hst3")
                for half in range(4):
                    wload(wsp, wspb, "l3_W1", 0, PT, 1024, col0=half * 1024, ncols=1024)
                    for ch in range(4):
                        c0 = ch * 512
                        for mt in range(8):
                            dglob = half * 8 + mt
                            bank = mt % 2
                            for kk in range(PT):
                                mm(bank, wsp, (0, 128, kk * 1024 + mt * 128,
                                               kk * 1024 + mt * 128 + 128), wspb,
                                   h2, (0, 128, kk * T + c0, kk * T + c0 + 512), h2b,
                                   start=(kk == 0), stop=(kk == PT - 1))
                            E("act", lambda e, bank=bank, dglob=dglob: e.activation(
                                hstg3[:], pbank(bank), AF.Gelu,
                                bias=misc[:, o_b1m + dglob:o_b1m + dglob + 1]),
                              ins=[pbB[bank], miscB], outs=[hstg3b])
                            E("sp", lambda e, dglob=dglob, c0=c0: e.dma_start(
                                out=h2_sp.ap()[dglob * 128:(dglob + 1) * 128, c0:c0 + 512],
                                in_=hstg3[:]), ins=[hstg3b], outs=[h2D], inc=16)
                A.release(m0)
                m0 = A.mark()
                wsp, wspb = A.alloc((128, 16 * 1024), BF16, "w23")
                hr, hrb = A.alloc((128, 16 * 512), BF16, "hr3")
                ev3, ev3b = A.alloc((128, 512), F32, "ev3d")
                for half in range(2):
                    wload(wsp, wspb, "l3_W2", half * 16, 16, 1024)
                    for ch in range(4):
                        c0 = ch * 512
                        for kk in range(16):
                            E("sp", lambda e, kk=kk, half=half, c0=c0: e.dma_start(
                                out=hr[:, kk * 512:(kk + 1) * 512],
                                in_=h2_sp.ap()[(half * 16 + kk) * 128:
                                               (half * 16 + kk + 1) * 128, c0:c0 + 512]),
                              ins=[h2D], outs=[hrb], inc=16)
                        for mt in range(8):
                            bank = mt % 2
                            for kk in range(16):
                                mm(bank, wsp, (0, 128, kk * 1024 + mt * 128,
                                               kk * 1024 + mt * 128 + 128), wspb,
                                   hr, (0, 128, kk * 512, kk * 512 + 512), hrb,
                                   start=(kk == 0), stop=(kk == 15))
                            if half == 0:
                                E("act", lambda e, bank=bank, mt=mt: e.activation(
                                    ev3[:], pbank(bank), AF.Identity, scale=1.0,
                                    bias=misc[:, o_b2m + mt:o_b2m + mt + 1]),
                                  ins=[pbB[bank], miscB], outs=[ev3b])
                                E("dve", lambda e, mt=mt, c0=c0: e.tensor_tensor(
                                    X[:, mt * T + c0:mt * T + c0 + 512],
                                    X[:, mt * T + c0:mt * T + c0 + 512], ev3[:], OP.add),
                                  ins=[ev3b], outs=[Xb[mt]], hz=False)
                            else:
                                x_add_psum(bank, mt, c0)
                A.release(mx1)

            if not do_ffn3:
                return
            m0 = A.mark()
            wg, wgb = A.alloc((128, PT * 8), BF16, "l3wg")
            ptm3, ptm3b = A.alloc((128, 16 * 8), F32, "l3ptm")
            s3, s3b = A.alloc((128, 16), F32, "l3s")
            pfm, pfmb = A.alloc((128, 128), F32, "l3pfm")
            wload(wg, wgb, "l3f_Wg", 0, PT, 8)
            for tt in range(16):
                bank = tt % 2
                for kk in range(PT):
                    mm(bank, XN, (0, 128, kk * T + tt * 128, kk * T + tt * 128 + 128),
                       XNb[kk], wg, (0, 128, kk * 8, kk * 8 + 8), wgb,
                       start=(kk == 0), stop=(kk == PT - 1), nlen=8)
                E("dve", lambda e, bank=bank: e.reduce_max(s3[:, 0:1], pbank(bank)[:, 0:8],
                                                           axis=AX, negate=True),
                  ins=[pbB[bank]], outs=[s3b], hz=True)
                E("act", lambda e, bank=bank, tt=tt: e.activation(
                    ptm3[:, tt * 8:tt * 8 + 8], pbank(bank)[:, 0:8], AF.Exp,
                    bias=s3[:, 0:1]), ins=[pbB[bank], s3b], outs=[ptm3b])
                E("dve", lambda e, tt=tt: e.reduce_sum(s3[:, 1:2],
                                                       ptm3[:, tt * 8:tt * 8 + 8],
                                                       axis=AX),
                  ins=[ptm3b], outs=[s3b], hz=True)
                E("dve", lambda e: e.reciprocal(s3[:, 2:3], s3[:, 1:2]),
                  ins=[s3b], outs=[s3b], hz=True)
                E("dve", lambda e, tt=tt: e.tensor_single_scalar(
                    ptm3[:, tt * 8:tt * 8 + 8], ptm3[:, tt * 8:tt * 8 + 8], s3[:, 2:3],
                    OP.mult), ins=[s3b], outs=[ptm3b])
                E("pe", lambda e, tt=tt: e.transpose(pbank(4)[0:8, 0:128],
                                                     ptm3[:, tt * 8:tt * 8 + 8],
                                                     identf[:]),
                  ins=[ptm3b, CB], outs=[pbB[4]])
                E("act", lambda e: e.copy(pfm[0:8, 0:128], pbank(4)[0:8, 0:128]),
                  ins=[pbB[4]], outs=[pfmb])
                E("sp", lambda e, tt=tt: e.dma_start(
                    out=ag_in.ap()[:, tt * 128:(tt + 1) * 128], in_=pfm[0:8, 0:128]),
                  ins=[pfmb], outs=[agD], inc=16)
            E("gp", lambda e: e.collective_compute(
                "AllGather", OP.bypass, replica_groups=[list(range(8))],
                ins=[ag_in.ap().opt()], outs=[ag_out.ap().opt()]),
              ins=[agD], outs=[agD2], isasync=True)
            pall, pallb = A.alloc((128, 1024), F32, "l3pall")
            for hh in range(2):
                E("sp", lambda e, hh=hh: e.dma_start(
                    out=pall[hh * 64:(hh + 1) * 64, :],
                    in_=AP(ag_out.ap().tensor, hh * 1024,
                           [[T, 8], [8 * T, 8], [1, 1024]])),
                  ins=[agD2], outs=[pallb], inc=16)
            lo_c, hi_c, mid_c, cnt_c, prd_c = 4, 5, 6, 7, 8
            E("dve", lambda e: e.memset(s3[:, lo_c:lo_c + 1], 0.0), outs=[s3b], hz=False)
            E("dve", lambda e: e.memset(s3[:, hi_c:hi_c + 1], 1.0), outs=[s3b], hz=False)
            cm, cmb = A.alloc((128, 1024), F32, "l3cm")
            for it in range(48):
                E("dve", lambda e: e.scalar_tensor_tensor(
                    s3[:, mid_c:mid_c + 1], s3[:, lo_c:lo_c + 1], 0.5,
                    s3[:, hi_c:hi_c + 1], OP.add, OP.bypass) if False else
                    e.tensor_tensor(s3[:, mid_c:mid_c + 1], s3[:, lo_c:lo_c + 1],
                                    s3[:, hi_c:hi_c + 1], OP.add),
                  ins=[s3b], outs=[s3b])
                E("dve", lambda e: e.tensor_scalar(s3[:, mid_c:mid_c + 1],
                                                   s3[:, mid_c:mid_c + 1], 0.5, None,
                                                   OP.mult), ins=[s3b], outs=[s3b])
                E("dve", lambda e: e.tensor_single_scalar(cm[:], pall[:],
                                                          s3[:, mid_c:mid_c + 1],
                                                          OP.is_ge),
                  ins=[pallb, s3b], outs=[cmb])
                E("dve", lambda e: e.reduce_sum(s3[:, cnt_c:cnt_c + 1], cm[:], axis=AX),
                  ins=[cmb], outs=[s3b], hz=True)
                mm(6, blk16, (0, 128, 0, 8), CB, s3, (0, 128, cnt_c, cnt_c + 1), s3b,
                   True, True, mlen=8, nlen=1)
                E("act", lambda e: e.copy(s3[0:8, 15:16], pbank(6, 0, 1)[0:8, :]),
                  ins=[pbB[6]], outs=[s3b])
                mm(7, blk8, (0, 8, 0, 128), CB, s3, (0, 8, 15, 16), s3b,
                   True, True, mlen=128, nlen=1)
                E("dve", lambda e: e.tensor_scalar(s3[:, cnt_c:cnt_c + 1],
                                                   pbank(7, 0, 1), 2048.0, None,
                                                   OP.is_ge),
                  ins=[pbB[7]], outs=[s3b])
                E("dve", lambda e: e.tensor_tensor(s3[:, prd_c:prd_c + 1],
                                                   s3[:, mid_c:mid_c + 1],
                                                   s3[:, lo_c:lo_c + 1], OP.subtract),
                  ins=[s3b], outs=[s3b])
                E("dve", lambda e: e.tensor_tensor(s3[:, prd_c:prd_c + 1],
                                                   s3[:, prd_c:prd_c + 1],
                                                   s3[:, cnt_c:cnt_c + 1], OP.mult),
                  ins=[s3b], outs=[s3b])
                E("dve", lambda e: e.tensor_tensor(s3[:, lo_c:lo_c + 1],
                                                   s3[:, lo_c:lo_c + 1],
                                                   s3[:, prd_c:prd_c + 1], OP.add),
                  ins=[s3b], outs=[s3b])
                E("dve", lambda e: e.tensor_scalar(s3[:, 9:10], s3[:, cnt_c:cnt_c + 1],
                                                   -1.0, 1.0, OP.mult, OP.add),
                  ins=[s3b], outs=[s3b])
                E("dve", lambda e: e.tensor_tensor(s3[:, prd_c:prd_c + 1],
                                                   s3[:, hi_c:hi_c + 1],
                                                   s3[:, mid_c:mid_c + 1], OP.subtract),
                  ins=[s3b], outs=[s3b])
                E("dve", lambda e: e.tensor_tensor(s3[:, prd_c:prd_c + 1],
                                                   s3[:, prd_c:prd_c + 1],
                                                   s3[:, 9:10], OP.mult),
                  ins=[s3b], outs=[s3b])
                E("dve", lambda e: e.tensor_tensor(s3[:, hi_c:hi_c + 1],
                                                   s3[:, hi_c:hi_c + 1],
                                                   s3[:, prd_c:prd_c + 1], OP.subtract),
                  ins=[s3b], outs=[s3b])
            if os.environ.get("MK_SCANDBG", "") == "th":
                E("sp", lambda e: e.dma_start(
                    out=flip_sp.ap()[0:1, :].rearrange("a b -> (a b)"),
                    in_=s3[:, lo_c:lo_c + 1]), ins=[s3b], outs=[flipD], inc=16)
                E("sp", lambda e: e.dma_start(out=dbg_d.ap()[0:1, 0:64],
                                              in_=flip_sp.ap()[0:1, 0:64]),
                  ins=[flipD], inc=16)
            mm(6, blk16, (0, 128, 0, 8), CB, s3, (0, 128, lo_c, lo_c + 1), s3b,
               True, True, mlen=8, nlen=1)
            E("act", lambda e: e.activation(pfm[0:8, 0:1], pbank(6, 0, 1)[0:8, :],
                                            AF.Identity, scale=1.0 / 16.0, bias=0.0),
              ins=[pbB[6]], outs=[pfmb])
            E("pe", lambda e: e.transpose(pbank(4)[0:1, 0:8], pfm[0:8, 0:1],
                                          identf[0:8, 0:8]),
              ins=[pfmb, CB], outs=[pbB[4]])
            E("act", lambda e: e.copy(pfm[0:1, 8:16], pbank(4)[0:1, 0:8]),
              ins=[pbB[4]], outs=[pfmb])
            mm(5, onesf, (0, 1, 0, 128), CB, pfm, (0, 1, 8, 16), pfmb, True, True, nlen=8)
            g1t, g1tb = A.alloc((128, 8), F32, "l3g1")
            g2t, g2tb = A.alloc((128, 8), F32, "l3g2")
            gfm3, gfm3b = A.alloc((128, 128), BF16, "l3gf")
            for tt in range(16):
                E("dve", lambda e, tt=tt: e.tensor_tensor(
                    g1t[:], ptm3[:, tt * 8:tt * 8 + 8], pbank(5, 0, 8), OP.is_ge),
                  ins=[ptm3b, pbB[5]], outs=[g1tb])
                E("dve", lambda e, tt=tt: e.tensor_tensor(
                    g2t[:], ptm3[:, tt * 8:tt * 8 + 8], g1t[:], OP.mult),
                  ins=[ptm3b, g1tb], outs=[g2tb])
                E("dve", lambda e: e.tensor_scalar(g1t[:], g1t[:], 1e30, -1e30,
                                                   OP.mult, OP.add),
                  ins=[g1tb], outs=[g1tb])
                E("dve", lambda e: e.tensor_tensor(g1t[:], g2t[:], g1t[:], OP.add),
                  ins=[g2tb], outs=[g1tb])
                E("dve", lambda e: e.reduce_max(s3[:, 9:10], g1t[:], axis=AX),
                  ins=[g1tb], outs=[s3b], hz=True)
                E("dve", lambda e: e.tensor_scalar(s3[:, 10:11], s3[:, 9:10], -1e29,
                                                   None, OP.is_gt),
                  ins=[s3b], outs=[s3b])
                E("dve", lambda e, tt=tt: e.reduce_max(s3[:, 11:12],
                                                       ptm3[:, tt * 8:tt * 8 + 8],
                                                       axis=AX),
                  ins=[ptm3b], outs=[s3b], hz=True)
                E("dve", lambda e: e.tensor_tensor(s3[:, 12:13], s3[:, 11:12],
                                                   s3[:, 9:10], OP.subtract),
                  ins=[s3b], outs=[s3b])
                E("dve", lambda e: e.tensor_tensor(s3[:, 12:13], s3[:, 12:13],
                                                   s3[:, 10:11], OP.mult),
                  ins=[s3b], outs=[s3b])
                E("dve", lambda e: e.tensor_tensor(s3[:, 12:13], s3[:, 11:12],
                                                   s3[:, 12:13], OP.subtract),
                  ins=[s3b], outs=[s3b])
                E("dve", lambda e, tt=tt: e.tensor_tensor(
                    g2t[:], g1t[:], ptm3[:, tt * 8:tt * 8 + 8], OP.subtract),
                  ins=[g1tb, ptm3b], outs=[g2tb])
                E("dve", lambda e: e.tensor_single_scalar(g2t[:], g2t[:], s3[:, 10:11],
                                                          OP.mult),
                  ins=[s3b], outs=[g2tb])
                E("dve", lambda e, tt=tt: e.tensor_tensor(
                    g2t[:], g2t[:], ptm3[:, tt * 8:tt * 8 + 8], OP.add),
                  ins=[ptm3b], outs=[g2tb])
                E("dve", lambda e: e.tensor_single_scalar(g2t[:], g2t[:], s3[:, 12:13],
                                                          OP.is_ge),
                  ins=[s3b], outs=[g2tb])
                E("dve", lambda e, tt=tt: e.tensor_tensor(
                    g2t[:], g2t[:], ptm3[:, tt * 8:tt * 8 + 8], OP.mult),
                  ins=[ptm3b], outs=[g2tb])
                E("pe", lambda e: e.transpose(pbank(4)[0:8, 0:128], g2t[:, 0:8],
                                              identf[:]),
                  ins=[g2tb, CB], outs=[pbB[4]])
                E("act", lambda e, tt=tt: e.copy(gfm3[0:8, 0:128], pbank(4)[0:8, 0:128]),
                  ins=[pbB[4]], outs=[gfm3b])
                E("sp", lambda e, tt=tt: e.dma_start(
                    out=gates_sp.ap()[0:8, tt * 128:(tt + 1) * 128], in_=gfm3[0:8, 0:128]),
                  ins=[gfm3b], outs=[gD], inc=16)
            A.release(m0)
            m0 = A.mark()
            w1e, w1eb = A.alloc((128, 8 * 2048), BF16, "l3w1e")
            w2e, w2eb = A.alloc((128, 16 * 1024), BF16, "l3w2e")
            hech, hechb = A.alloc((128, 16 * 512), BF16, "l3hech")
            grow, growb = A.alloc((1, T), BF16, "l3grow")
            cve, cveb = A.alloc((128, 512), F32, "l3cve")
            for e_ in range(8):
                for i in range(PT):
                    E("gp", lambda e, e_=e_, i=i: e.dma_start(
                        out=w1e[:, i * 2048:(i + 1) * 2048],
                        in_=di["l3f_W1"].ap()[e_, i * 128:(i + 1) * 128, :]),
                      outs=[w1eb], inc=16)
                for i in range(16):
                    E("gp", lambda e, e_=e_, i=i: e.dma_start(
                        out=w2e[:, i * 1024:(i + 1) * 1024],
                        in_=di["l3f_W2"].ap()[e_, i * 128:(i + 1) * 128, :]),
                      outs=[w2eb], inc=16)
                E("gp", lambda e, e_=e_: e.dma_start(
                    out=grow[:], in_=gates_sp.ap()[e_:e_ + 1, :]),
                  ins=[gD], outs=[growb], inc=16)
                for ch in range(4):
                    c0 = ch * 512
                    for mt in range(16):
                        bank = mt % 2
                        for kk in range(PT):
                            mm(bank, w1e, (0, 128, kk * 2048 + mt * 128,
                                           kk * 2048 + mt * 128 + 128), w1eb,
                               XN, (0, 128, kk * T + c0, kk * T + c0 + 512), XNb[kk],
                               start=(kk == 0), stop=(kk == PT - 1))
                        E("act", lambda e, bank=bank, mt=mt: e.activation(
                            hech[:, mt * 512:(mt + 1) * 512], pbank(bank), AF.Gelu),
                          ins=[pbB[bank]], outs=[hechb])
                    mm(4, ones16, (0, 1, 0, 128), CB, grow, (0, 1, c0, c0 + 512), growb,
                       True, True)
                    for mt in range(8):
                        bank = 2 + mt % 2
                        for kk in range(16):
                            mm(bank, w2e, (0, 128, kk * 1024 + mt * 128,
                                           kk * 1024 + mt * 128 + 128), w2eb,
                               hech, (0, 128, kk * 512, kk * 512 + 512), hechb,
                               start=(kk == 0), stop=(kk == 15))
                        E("act", lambda e, bank=bank: e.copy(cve[:], pbank(bank)),
                          ins=[pbB[bank]], outs=[cveb])
                        E("dve", lambda e: e.tensor_tensor(cve[:], cve[:], pbank(4),
                                                           OP.mult),
                          ins=[cveb, pbB[4]], outs=[cveb], hz=False)
                        E("dve", lambda e, mt=mt, c0=c0: e.tensor_tensor(
                            X[:, mt * T + c0:mt * T + c0 + 512],
                            X[:, mt * T + c0:mt * T + c0 + 512], cve[:], OP.add),
                          ins=[cveb], outs=[Xb[mt]], hz=False)
            A.release(m0)

        try:
            l0()
            if nlayers >= 2:
                l1()
            if nlayers >= 3:
                l2()
            if nlayers >= 4:
                l3()
        except _SkipRest:
            pass

        fin = Buf()
        for k in range(PT):
            E("sp", lambda e, k=k: e.dma_start(
                out=out_d.ap()[k * 128:(k + 1) * 128, :], in_=X[:, k * T:(k + 1) * T]),
              ins=[Xb[k]], outs=[fin], inc=16)
        nc.sync.wait_ge(P.sem["sp"], P.cnt["sp"])
        nc.vector.wait_ge(P.sem["sp"], P.cnt["sp"])

    return nc


def _prep_inputs(inputs, nlayers):
    ct = host_consts()
    w = {}
    w.update({k: np.ascontiguousarray(v) for k, v in ct.items()})
    f = lambda a: np.ascontiguousarray(np.asarray(a, dtype=np.float32))
    tr = lambda a: np.ascontiguousarray(np.asarray(a, dtype=np.float32).T)
    w["ipw1T"] = tr(inputs["L0_attn_in_proj_w"][:2048])
    w["ipw2T"] = tr(inputs["L0_attn_in_proj_w"][2048:])
    w["conv_w"] = f(inputs["L0_attn_conv_w"])
    w["conv_b"] = f(inputs["L0_attn_conv_b"])
    w["dt_wT"] = tr(inputs["L0_attn_dt_w"])
    w["dt_b"] = f(inputs["L0_attn_dt_b"])
    w["A_log"] = f(inputs["L0_attn_A_log"])
    w["B_wT"] = tr(inputs["L0_attn_B_w"])
    w["C_wT"] = tr(inputs["L0_attn_C_w"])
    w["out_wT"] = tr(inputs["L0_attn_out_w"])
    w["WrT"] = tr(inputs["L0_ffn_W_r"])
    w["WkT"] = tr(inputs["L0_ffn_W_k"])
    w["WvT"] = tr(inputs["L0_ffn_W_v"])
    w["mu_r"] = f(inputs["L0_ffn_mu_r"])
    w["mu_k"] = f(inputs["L0_ffn_mu_k"])
    if nlayers >= 2:
        W1 = np.asarray(inputs["L1_attn_W1"], np.float32)
        w["l1_wqT"] = tr(W1[0:1024]); w["l1_wkT"] = tr(W1[1024:2048]); w["l1_wvT"] = tr(W1[2048:3072])
        w["l1_w2T"] = tr(inputs["L1_attn_W2"])
        w["l1f_w1T"] = tr(inputs["L1_ffn_w1"]); w["l1f_b1"] = f(inputs["L1_ffn_b1"])
        w["l1f_w2T"] = tr(inputs["L1_ffn_w2"]); w["l1f_b2"] = f(inputs["L1_ffn_b2"])
    if nlayers >= 3:
        qkv = np.asarray(inputs["L2_attn_qkv_w"], np.float32)
        qkb = np.asarray(inputs["L2_attn_qkv_b"], np.float32)
        perm = np.concatenate([h * 64 + np.concatenate([np.arange(0, 64, 2),
                                                        np.arange(1, 64, 2)])
                               for h in range(16)])
        w["l2_wqT"] = tr(qkv[0:1024][perm])
        w["l2_wkT"] = tr(qkv[1024:2048][perm])
        w["l2_wvT"] = tr(qkv[2048:3072])
        w["l2_bq"] = f(qkb[0:1024][perm]); w["l2_bk"] = f(qkb[1024:2048][perm])
        w["l2_bv"] = f(qkb[2048:3072])
        w["l2_woT"] = tr(inputs["L2_attn_out_w"]); w["l2_bo"] = f(inputs["L2_attn_out_b"])
        w["l2f_ws1"] = f(inputs["L2_ffn_ws1"]); w["l2f_bs1"] = f(inputs["L2_ffn_bs1"])
        w["l2f_ws2"] = f(inputs["L2_ffn_ws2"]); w["l2f_bs2"] = f(inputs["L2_ffn_bs2"])
        w["l2f_We1"] = f(inputs["L2_ffn_We1"]); w["l2f_Be1"] = f(inputs["L2_ffn_Be1"])
        w["l2f_We2"] = f(inputs["L2_ffn_We2"]); w["l2f_Be2"] = f(inputs["L2_ffn_Be2"])
        w["l2f_Wr"] = f(inputs["L2_ffn_Wr"]); w["l2f_br"] = f(inputs["L2_ffn_br"])
    if nlayers >= 4:
        w["l3_g1"] = f(inputs["L3_attn_g1"]); w["l3_b1"] = f(inputs["L3_attn_b1"])
        w["l3_Wqkv_a"] = f(inputs["L3_attn_Wqkv_a"]); w["l3_bqkv_a"] = f(inputs["L3_attn_bqkv_a"])
        w["l3_Wp_a"] = f(inputs["L3_attn_Wp_a"]); w["l3_bp_a"] = f(inputs["L3_attn_bp_a"])
        w["l3_Wqkv_s"] = f(inputs["L3_attn_Wqkv_s"]); w["l3_bqkv_s"] = f(inputs["L3_attn_bqkv_s"])
        w["l3_Wo_s"] = f(inputs["L3_attn_Wo_s"]); w["l3_bo_s"] = f(inputs["L3_attn_bo_s"])
        w["l3_Wo"] = f(inputs["L3_attn_Wo"]); w["l3_bo"] = f(inputs["L3_attn_bo"])
        w["l3_gm"] = f(inputs["L3_attn_gm"]); w["l3_bm"] = f(inputs["L3_attn_bm"])
        w["l3_W1"] = f(inputs["L3_attn_W1"]); w["l3_b1m"] = f(inputs["L3_attn_b1m"])
        w["l3_W2"] = f(inputs["L3_attn_W2"]); w["l3_b2m"] = f(inputs["L3_attn_b2m"])
        w["l3f_Wg"] = f(inputs["L3_ffn_Wg"])
        w["l3f_W1"] = f(inputs["L3_ffn_W1"]); w["l3f_W2"] = f(inputs["L3_ffn_W2"])
    return w


def kernel(**inputs):
    nlayers = N_LAYERS
    w = _prep_inputs(inputs, nlayers)
    x = np.asarray(inputs["L0_attn_x"], np.float32)
    nc = build_program(nlayers)
    in_maps = []
    for c in range(8):
        m = dict(w)
        m["x_fm"] = np.ascontiguousarray(x[c].T)
        in_maps.append(m)
    res = run_bass_kernel_spmd(nc, in_maps, core_ids=list(range(8)))
    global _LAST_RES
    _LAST_RES = res
    out = np.stack([np.ascontiguousarray(res.results[c]["out_fm"].T) for c in range(8)])
    return out


# version 03
# speedup vs baseline: 1.0077x; 1.0012x over previous
import os
import numpy as np
import concourse.bass as bass
import concourse.mybir as mybir
from concourse.alu_op_type import AluOpType
from concourse.bass_types import AP
from concourse.bass_utils import run_bass_kernel_spmd

F32 = mybir.dt.float32
BF16 = mybir.dt.bfloat16
AF = mybir.ActivationFunctionType
AX = mybir.AxisListType.X
OP = AluOpType
MAGIC = float(1.5 * 2 ** 23)
NEG = -1.0e30
FCONST_VALS = [1e-5, 1.0, -1.0, 0.125, MAGIC, -MAGIC, 1e-4, 0.0]

T = 2048
D = 1024
PT = D // 128
N_LAYERS = int(os.environ.get("MK_NLAYERS", "4"))
MK_PARTS = os.environ.get("MK_PARTS", "both")

BASE = 16512
X_B = BASE + 21504
XN_B = X_B + 65536
ARENA_B = XN_B + 32768
ARENA_E = 229344


class _SkipRest(Exception):
    pass


class Buf:
    __slots__ = ("w", "r")
    def __init__(self, deps=()):
        self.w = None
        self.r = list(deps)


MK_SYNC = os.environ.get("MK_SYNC", "0") == "1"


class Prog:
    ENGS = ("pe", "dve", "act", "gp", "sp")

    def __init__(self, nc):
        self.nc = nc
        self.h = {"pe": nc.tensor, "dve": nc.vector, "act": nc.scalar,
                  "gp": nc.gpsimd, "sp": nc.sync}
        self.sem = {}
        self.cnt = {e: 0 for e in self.ENGS}
        self.lastw = {e: {} for e in self.ENGS}
        self.bub = None
        self.dve_seq = 0
        self.dve_hist = {}

    def init_sems(self, stack):
        for e in self.ENGS:
            self.sem[e] = stack.enter_context(self.nc.semaphore("sem_" + e))

    def _wait(self, eng, tok):
        if tok is None:
            return
        e2, v, isasync = tok
        if e2 == eng and not isasync:
            return
        if isasync:
            v = self.cnt[e2]
        if self.lastw[eng].get(e2, 0) >= v:
            return
        self.h[eng].wait_ge(self.sem[e2], v)
        self.lastw[eng][e2] = v

    def emit(self, eng, fn, ins=(), outs=(), inc=1, isasync=None, hz=True, hzin=False):
        if MK_SYNC:
            for e2 in self.ENGS:
                if e2 != eng and self.cnt[e2] > self.lastw[eng].get(e2, 0):
                    self.h[eng].wait_ge(self.sem[e2], self.cnt[e2])
                    self.lastw[eng][e2] = self.cnt[e2]
        for b in ins:
            if b is not None:
                self._wait(eng, b.w)
        for b in outs:
            if b is not None:
                self._wait(eng, b.w)
                for t in b.r:
                    self._wait(eng, t)
        if eng == "dve" and self.bub is not None:
            need = 0
            for b in list(ins) + list(outs):
                if b is not None and b.w is not None and b.w[0] == "dve":
                    info = self.dve_hist.get(b.w[1])
                    if info is not None and info[1]:
                        d = self.dve_seq - info[0]
                        if d < 5:
                            need = max(need, 5 - d)
            for _ in range(need):
                self.h["dve"].memset(self.bub, 0.0)
                self.dve_seq += 1
        instr = fn(self.h[eng])
        instr.then_inc(self.sem[eng], inc)
        self.cnt[eng] += inc
        if isasync is None:
            isasync = inc >= 16
        tok = (eng, self.cnt[eng], isasync)
        if eng == "dve":
            self.dve_seq += 1
            self.dve_hist[self.cnt[eng]] = (self.dve_seq, hz)
            if len(self.dve_hist) > 64:
                for k in list(self.dve_hist)[:32]:
                    del self.dve_hist[k]
        for b in ins:
            if b is not None:
                b.r.append(tok)
        for b in outs:
            if b is not None:
                b.w = tok
                b.r = []
        return tok


class Arena:
    def __init__(self, nc, base, end):
        self.nc = nc
        self.base = base
        self.end = end
        self.off = base
        self.uid = 0
        self.tenants = []

    def alloc(self, shape, dtype, tag=""):
        esz = 4 if dtype == F32 else 2
        nbytes = (shape[1] * esz + 31) & ~31
        start, stop = self.off, self.off + nbytes
        assert stop <= self.end, \
            f"arena overflow {tag}: need {stop - self.base}, have {self.end - self.base}"
        self.uid += 1
        t = self.nc.alloc_sbuf_tensor_at(f"ar_{tag}_{self.uid}", list(shape), dtype,
                                         offset=start)
        deps = []
        for (s0, s1, b) in self.tenants:
            if s0 < stop and start < s1:
                if b.w is not None:
                    deps.append(b.w)
                deps.extend(b.r)
        buf = Buf(deps)
        self.tenants.append((start, stop, buf))
        self.off = stop
        return t, buf

    def mark(self):
        return self.off

    def release(self, m):
        self.off = m
        if len(self.tenants) > 512:
            self.tenants = self.tenants[-256:]


def host_consts():
    c = {}
    c["identf"] = np.eye(128, dtype=np.float32)
    c["onesf"] = np.ones((128, 128), np.float32)
    invf = (1.0 / 10000.0 ** (np.arange(0, 64, 2, dtype=np.float64) / 64.0))
    ang = np.arange(T, dtype=np.float64)[None, :] * np.tile(invf, 4)[:, None]
    import ml_dtypes
    c["ropec"] = np.cos(ang).astype(ml_dtypes.bfloat16)
    c["ropes"] = np.sin(ang).astype(ml_dtypes.bfloat16)
    kk = np.arange(128)[:, None]
    uu = np.arange(896)[None, :]
    m = np.where(kk <= uu - 384, 0.0, NEG).astype(np.float32)
    import ml_dtypes
    c["l1mask"] = m.astype(ml_dtypes.bfloat16)
    kk = np.arange(128)[:, None]
    qq = np.arange(128)[None, :]
    sm = np.zeros((128, 256), np.float32)
    sm[:, 0:128] = np.where(kk > qq, 0.0, NEG)
    sm[:, 128:256] = np.where(kk <= qq, 0.0, NEG)
    c["swamask"] = sm.astype(ml_dtypes.bfloat16)
    c["trilT"] = np.where(kk <= qq, 1.0, 0.0).astype(np.float32)
    blk16 = np.zeros((128, 8), np.float32)
    for p in range(128):
        blk16[p, (p % 64) // 8] = 1.0
    c["blk16"] = blk16
    c["blk8"] = np.ascontiguousarray(blk16.T)
    fc = np.array(FCONST_VALS, np.float32)
    c["fconsts"] = np.tile(fc[None, :], (128, 1))
    return c


def build_program(nlayers):
    nc = bass.Bass("TRN2", target_bir_lowering=False)
    di = {}

    def dram_in(name, shape, dt=F32):
        di[name] = nc.dram_tensor(name, list(shape), dt, kind="ExternalInput")

    dram_in("x_fm", (D, T))
    dram_in("ipw1T", (D, 2048)); dram_in("ipw2T", (D, 2048))
    dram_in("conv_w", (2048, 4)); dram_in("conv_b", (2048,))
    dram_in("dt_wT", (2048, 2048)); dram_in("dt_b", (2048,))
    dram_in("A_log", (2048, 16))
    dram_in("B_wT", (2048, 16)); dram_in("C_wT", (2048, 16))
    dram_in("out_wT", (2048, D))
    dram_in("WrT", (D, D)); dram_in("WkT", (D, D)); dram_in("WvT", (D, D))
    dram_in("mu_r", (D,)); dram_in("mu_k", (D,))
    if nlayers >= 2:
        dram_in("l1_wqT", (D, D)); dram_in("l1_wkT", (D, D)); dram_in("l1_wvT", (D, D))
        dram_in("l1_w2T", (D, D))
        dram_in("l1f_w1T", (D, 4096)); dram_in("l1f_b1", (4096,))
        dram_in("l1f_w2T", (4096, D)); dram_in("l1f_b2", (D,))
    if nlayers >= 3:
        dram_in("l2_wqT", (D, D)); dram_in("l2_wkT", (D, D)); dram_in("l2_wvT", (D, D))
        dram_in("l2_bq", (D,)); dram_in("l2_bk", (D,)); dram_in("l2_bv", (D,))
        dram_in("l2_woT", (D, D)); dram_in("l2_bo", (D,))
        dram_in("l2f_ws1", (D, D)); dram_in("l2f_bs1", (D,))
        dram_in("l2f_ws2", (D, D)); dram_in("l2f_bs2", (D,))
        dram_in("l2f_We1", (31, D, D)); dram_in("l2f_Be1", (31, D))
        dram_in("l2f_We2", (31, D, D)); dram_in("l2f_Be2", (31, D))
        dram_in("l2f_Wr", (D, 31)); dram_in("l2f_br", (31,))
    if nlayers >= 4:
        dram_in("l3_g1", (D,)); dram_in("l3_b1", (D,))
        dram_in("l3_Wqkv_a", (D, 3 * D)); dram_in("l3_bqkv_a", (3 * D,))
        dram_in("l3_Wp_a", (D, D)); dram_in("l3_bp_a", (D,))
        dram_in("l3_Wqkv_s", (D, 3 * D)); dram_in("l3_bqkv_s", (3 * D,))
        dram_in("l3_Wo_s", (D, D)); dram_in("l3_bo_s", (D,))
        dram_in("l3_Wo", (D, D)); dram_in("l3_bo", (D,))
        dram_in("l3_gm", (D,)); dram_in("l3_bm", (D,))
        dram_in("l3_W1", (D, 4 * D)); dram_in("l3_b1m", (4 * D,))
        dram_in("l3_W2", (4 * D, D)); dram_in("l3_b2m", (D,))
        dram_in("l3f_Wg", (D, 8))
        dram_in("l3f_W1", (8, D, 2048)); dram_in("l3f_W2", (8, 2048, D))
    dram_in("identf", (128, 128)); dram_in("onesf", (128, 128))
    dram_in("ropec", (128, T), BF16); dram_in("ropes", (128, T), BF16)
    dram_in("l1mask", (128, 896), BF16); dram_in("swamask", (128, 256), BF16)
    dram_in("trilT", (128, 128)); dram_in("blk16", (128, 8)); dram_in("blk8", (8, 128))
    dram_in("fconsts", (128, 8))

    out_d = nc.dram_tensor("out_fm", [D, T], F32, kind="ExternalOutput")
    dbg_d = nc.dram_tensor("dbg", [1, 64], F32, kind="ExternalOutput")

    xin_sp = nc.dram_tensor("xin_sp", [2048, T], BF16)
    bc_sp = nc.dram_tensor("bc_sp", [16, 2 * T], BF16)
    dt_sp = nc.dram_tensor("dt_sp", [2048, T], BF16)
    y_sp = nc.dram_tensor("y_sp", [2048, T], BF16)
    h_sp = nc.dram_tensor("h_sp", [4096, T], F32)
    xq_sp = nc.dram_tensor("xq_sp", [D, T], BF16)
    gates_sp = nc.dram_tensor("gates_sp", [31, T], BF16)
    swp_sp = nc.dram_tensor("swp_sp", [D, T], BF16)
    mix_sp = nc.dram_tensor("mix_sp", [D, T], BF16)
    h2_sp = nc.dram_tensor("h2_sp", [4096, T], BF16)
    w2q_sp = nc.dram_tensor("w2q_sp", [4096, D], BF16)
    ar_in = nc.dram_tensor("ar_in", [1, 4], F32)
    ar_out = nc.dram_tensor("ar_out", [1, 4], F32)
    ar2_in = nc.dram_tensor("ar2_in", [1, 4], F32)
    ar2_out = nc.dram_tensor("ar2_out", [1, 4], F32)
    flip_sp = nc.dram_tensor("flip_sp", [4, 128], F32)
    ag_in = nc.dram_tensor("ag_in", [8, T], F32)
    ag_out = nc.dram_tensor("ag_out", [64, T], F32)

    import contextlib
    stack = contextlib.ExitStack()
    with stack:
        P = Prog(nc)
        P.init_sems(stack)
        E = P.emit

        off = BASE
        def fixed(name, shape, dtype):
            nonlocal off
            t = nc.alloc_sbuf_tensor_at(name, list(shape), dtype, offset=off)
            off += (shape[1] * (4 if dtype == F32 else 2) + 31) & ~31
            return t

        identf = fixed("identf_s", (128, 128), F32)
        onesf = fixed("onesf_s", (128, 128), F32)
        ones16 = fixed("ones16_s", (128, 128), BF16)
        ropec = fixed("ropec_s", (128, T), BF16)
        ropes = fixed("ropes_s", (128, T), BF16)
        l1mask = fixed("l1mask_s", (128, 896), BF16)
        swamask = fixed("swamask_s", (128, 256), BF16)
        trilT16 = fixed("trilT16_s", (128, 128), BF16)
        bv3 = fixed("bv3_s", (1, 1024), BF16)
        blk16 = fixed("blk16_s", (128, 8), F32)
        blk8 = fixed("blk8_s", (8, 128), F32)
        fconsts = fixed("fconsts_s", (128, 8), F32)
        misc = fixed("misc_s", (128, 1100), F32)
        misc2 = fixed("misc2_s", (128, 512), F32)
        assert off <= X_B, f"const region overflow: {off}"
        P.bub = misc[0:1, 1096:1100]

        X = nc.alloc_sbuf_tensor_at("X_s", [128, PT * T], F32, offset=X_B)
        XN = nc.alloc_sbuf_tensor_at("XN_s", [128, PT * T], BF16, offset=XN_B)
        Xb = [Buf() for _ in range(PT)]
        XNb = [Buf() for _ in range(PT)]

        A = Arena(nc, ARENA_B, ARENA_E)

        pbig = stack.enter_context(nc.psum_tensor("pbig", [128, 4096], F32))
        pbB = [Buf() for _ in range(8)]
        def pbank(i, n0=0, nlen=512):
            return pbig[:, i * 512 + n0:i * 512 + n0 + nlen]

        CB = Buf()
        flipD = Buf()
        xinD, dtD, yD, hD, bcD = Buf(), Buf(), Buf(), Buf(), Buf()
        xqD, w2qD, gD = Buf(), Buf(), Buf()
        swpD, mixD, h2D, agD, agD2 = Buf(), Buf(), Buf(), Buf(), Buf()
        miscB = Buf()
        misc2B = Buf()

        blk = stack.enter_context(nc.Block())

        for i, v in enumerate(FCONST_VALS):
            nc.const_aps.register(F32, float(v), fconsts[:, i:i + 1])
        for t_, nm in ((identf, "identf"), (onesf, "onesf"), (ropec, "ropec"),
                       (ropes, "ropes"), (l1mask, "l1mask"), (swamask, "swamask"),
                       (blk16, "blk16"), (blk8, "blk8"), (fconsts, "fconsts")):
            E("sp", lambda e, t_=t_, nm=nm: e.dma_start(out=t_[:], in_=di[nm].ap()),
              outs=[CB], inc=16)
        E("gp", lambda e: e.dma_start(out=ones16[:], in_=di["onesf"].ap()), outs=[CB], inc=16)
        E("gp", lambda e: e.dma_start(out=trilT16[:], in_=di["trilT"].ap()), outs=[CB], inc=16)
        bv3b = Buf()
        for k in range(PT):
            E("sp", lambda e, k=k: e.dma_start(
                out=X[:, k * T:(k + 1) * T], in_=di["x_fm"].ap()[k * 128:(k + 1) * 128, :]),
              outs=[Xb[k]], inc=16)

        misc_off = [0]
        def mslot(n):
            o = misc_off[0]
            misc_off[0] += n
            assert misc_off[0] <= 1096
            return o

        def load_vec(dname, n, slot=None):
            nt = (n + 127) // 128
            o = mslot(nt) if slot is None else slot
            E("sp", lambda e: e.dma_start(out=misc[:, o:o + nt],
                                          in_=AP(di[dname].ap().tensor, 0, [[1, 128], [128, nt]]),
                                          allow_slow_non_contiguous=True),
              outs=[miscB], inc=16)
            return o

        def load_rows(dname, nrows, ncols, o=None):
            nt = nrows // 128
            o = mslot(nt * ncols) if o is None else o
            for i in range(nt):
                E("sp", lambda e, i=i: e.dma_start(
                    out=misc[:, o + i * ncols:o + (i + 1) * ncols],
                    in_=di[dname].ap()[i * 128:(i + 1) * 128, :]), outs=[miscB], inc=16)
            return o

        def wload(dst, dstb, dname, kt0, nk, out_w, col0=0, ncols=None, eng="gp"):
            if ncols is None:
                ncols = out_w
            for i in range(nk):
                r0 = (kt0 + i) * 128
                E(eng, lambda e, i=i, r0=r0: e.dma_start(
                    out=dst[:, i * ncols:(i + 1) * ncols],
                    in_=di[dname].ap()[r0:r0 + 128, col0:col0 + ncols]),
                  outs=[dstb], inc=16)

        def mm(bank, lT, lTsl, lb, rT, rsl, rb, start, stop, m0=0, mlen=128, n0=0, nlen=512):
            E("pe", lambda e: e.matmul(pbank(bank, n0, nlen)[m0:m0 + mlen, :],
                                       lT[lTsl[0]:lTsl[1], lTsl[2]:lTsl[3]],
                                       rT[rsl[0]:rsl[1], rsl[2]:rsl[3]],
                                       start=start, stop=stop),
              ins=[lb, rb], outs=[pbB[bank]])

        def layer_norm(want32=False):
            xn32, xn32b = (A.alloc((128, PT * T), F32, "xn32") if want32 else (None, None))
            sq, sqb = A.alloc((128, 512), F32, "lnsq")
            for ch in range(4):
                c0 = ch * 512
                for k in range(PT):
                    mm(0, onesf, (0, 128, 0, 1), CB, X, (0, 128, k * T + c0, k * T + c0 + 512),
                       Xb[k], start=(k == 0), stop=(k == PT - 1), mlen=1)
                for k in range(PT):
                    E("act", lambda e, k=k: e.activation(
                        sq[:], X[:, k * T + c0:k * T + c0 + 512], AF.Square),
                      ins=[Xb[k]], outs=[sqb])
                    mm(1, onesf, (0, 128, 0, 1), CB, sq, (0, 128, 0, 512), sqb,
                       start=(k == 0), stop=(k == PT - 1), mlen=1)
                E("dve", lambda e: e.tensor_scalar(misc2[0:1, 0:512], pbank(0)[0:1, :],
                                                   1.0 / D, None, OP.mult),
                  ins=[pbB[0]], outs=[misc2B])
                mm(2, onesf, (0, 1, 0, 128), CB, misc2, (0, 1, 0, 512), misc2B, True, True)
                E("dve", lambda e: e.tensor_tensor(misc2[0:1, 0:512], misc2[0:1, 0:512],
                                                   misc2[0:1, 0:512], OP.mult),
                  ins=[misc2B], outs=[misc2B])
                E("dve", lambda e: e.scalar_tensor_tensor(
                    misc2[0:1, 0:512], pbank(1)[0:1, :], 1.0 / D, misc2[0:1, 0:512],
                    OP.mult, OP.subtract), ins=[pbB[1], misc2B], outs=[misc2B])
                E("act", lambda e: e.activation(misc2[0:1, 0:512], misc2[0:1, 0:512],
                                                AF.Sqrt, bias=1e-5),
                  ins=[misc2B], outs=[misc2B])
                E("dve", lambda e: e.reciprocal(misc2[0:1, 0:512], misc2[0:1, 0:512]),
                  ins=[misc2B], outs=[misc2B], hz=True)
                mm(3, onesf, (0, 1, 0, 128), CB, misc2, (0, 1, 0, 512), misc2B, True, True)
                for k in range(PT):
                    E("dve", lambda e, k=k: e.tensor_tensor(
                        sq[:], X[:, k * T + c0:k * T + c0 + 512], pbank(2), OP.subtract),
                      ins=[Xb[k], pbB[2]], outs=[sqb])
                    if want32:
                        E("dve", lambda e, k=k: e.tensor_tensor(
                            xn32[:, k * T + c0:k * T + c0 + 512], sq[:], pbank(3), OP.mult),
                          ins=[sqb, pbB[3]], outs=[xn32b])
                        E("dve", lambda e, k=k: e.tensor_copy(
                            XN[:, k * T + c0:k * T + c0 + 512],
                            xn32[:, k * T + c0:k * T + c0 + 512]),
                          ins=[xn32b], outs=[XNb[k]])
                    else:
                        E("dve", lambda e, k=k: e.tensor_tensor(
                            XN[:, k * T + c0:k * T + c0 + 512], sq[:], pbank(3), OP.mult),
                          ins=[sqb, pbB[3]], outs=[XNb[k]])
            return xn32, xn32b

        def x_add_psum(bank, k, c0, nlen=512):
            E("dve", lambda e: e.tensor_tensor(
                X[:, k * T + c0:k * T + c0 + nlen],
                X[:, k * T + c0:k * T + c0 + nlen], pbank(bank, 0, nlen), OP.add),
              ins=[pbB[bank]], outs=[Xb[k]])

        def l0():
            o_A = load_rows("A_log", 2048, 16)
            o_cw = load_rows("conv_w", 2048, 4)
            o_cb = load_vec("conv_b", 2048)
            o_db = load_vec("dt_b", 2048)
            o_mur = load_vec("mu_r", D)
            o_muk = load_vec("mu_k", D)
            o_omr = mslot(8)
            o_omk = mslot(8)
            E("act", lambda e: e.activation(misc[:, o_A:o_A + 256], misc[:, o_A:o_A + 256],
                                            AF.Exp), ins=[miscB], outs=[miscB])
            E("dve", lambda e: e.tensor_scalar(misc[:, o_A:o_A + 256], misc[:, o_A:o_A + 256],
                                               -1.0, None, OP.mult), ins=[miscB], outs=[miscB])
            E("dve", lambda e: e.tensor_scalar(misc[:, o_omr:o_omr + 8],
                                               misc[:, o_mur:o_mur + 8],
                                               -1.0, 1.0, OP.mult, OP.add),
              ins=[miscB], outs=[miscB])
            E("dve", lambda e: e.tensor_scalar(misc[:, o_omk:o_omk + 8],
                                               misc[:, o_muk:o_muk + 8],
                                               -1.0, 1.0, OP.mult, OP.add),
              ins=[miscB], outs=[miscB])

            mpre = A.mark()
            layer_norm(False)
            A.release(mpre)
            m0 = A.mark()
            do_attn = MK_PARTS in ("both", "attn", "f0xin", "f0dt", "f0y") or N_LAYERS > 1
            do_ffn = MK_PARTS in ("both", "ffn", "f0r", "f0k") or N_LAYERS > 1
            if MK_PARTS == "ln":
                for k in range(PT):
                    E("dve", lambda e, k=k: e.tensor_copy(X[:, k * T:(k + 1) * T],
                                                          XN[:, k * T:(k + 1) * T]),
                      ins=[XNb[k]], outs=[Xb[k]])
            def dump_sp(spt, spb):
                tmp, tmpb = A.alloc((128, T), BF16, "dump")
                for k in range(PT):
                    E("gp", lambda e, k=k, spt=spt: e.dma_start(
                        out=tmp[:], in_=spt.ap()[k * 128:(k + 1) * 128, :]),
                      ins=[spb], outs=[tmpb], inc=16)
                    E("dve", lambda e, k=k: e.tensor_copy(X[:, k * T:(k + 1) * T], tmp[:]),
                      ins=[tmpb], outs=[Xb[k]])

            if do_attn:
                w1, w1b = A.alloc((128, PT * 2048), BF16, "ipw1")
                stg, stgb = A.alloc((128, 520), F32, "cstg")
                cvt, cvtb = A.alloc((128, 512), F32, "cvt")
                cv2, cv2b = A.alloc((128, 512), F32, "cv2")
                xo, xob = A.alloc((128, 512), BF16, "xo")
                wload(w1, w1b, "ipw1T", 0, PT, 2048)
                for dt16 in range(16):
                    for ch in range(4):
                        c0 = ch * 512
                        bank = ch % 2
                        for k in range(PT):
                            mm(bank, w1, (0, 128, k * 2048 + dt16 * 128, k * 2048 + dt16 * 128 + 128),
                               w1b, XN, (0, 128, k * T + c0, k * T + c0 + 512), XNb[k],
                               start=(k == 0), stop=(k == PT - 1))
                        if ch == 0:
                            E("dve", lambda e: e.memset(stg[:, 0:3], 0.0), outs=[stgb])
                        E("act", lambda e, bank=bank: e.copy(stg[:, 3:515], pbank(bank)),
                          ins=[pbB[bank]], outs=[stgb])
                        E("dve", lambda e, dt16=dt16: e.tensor_single_scalar(
                            cvt[:], stg[:, 0:512], misc[:, o_cw + dt16 * 4:o_cw + dt16 * 4 + 1],
                            OP.mult), ins=[stgb, miscB], outs=[cvtb], hz=False)
                        for kk in range(1, 4):
                            E("dve", lambda e, dt16=dt16, kk=kk: e.tensor_single_scalar(
                                cv2[:], stg[:, kk:kk + 512],
                                misc[:, o_cw + dt16 * 4 + kk:o_cw + dt16 * 4 + kk + 1], OP.mult),
                              ins=[stgb, miscB], outs=[cv2b], hz=False)
                            E("dve", lambda e: e.tensor_tensor(cvt[:], cvt[:], cv2[:], OP.add),
                              ins=[cv2b], outs=[cvtb], hz=False)
                        if ch < 3:
                            E("dve", lambda e: e.tensor_copy(stg[:, 0:3], stg[:, 512:515]),
                              ins=[stgb], outs=[stgb])
                        E("act", lambda e, dt16=dt16: e.activation(
                            xo[:], cvt[:], AF.Silu, bias=misc[:, o_cb + dt16:o_cb + dt16 + 1]),
                          ins=[cvtb, miscB], outs=[xob])
                        E("sp", lambda e, dt16=dt16, c0=c0: e.dma_start(
                            out=xin_sp.ap()[dt16 * 128:(dt16 + 1) * 128, c0:c0 + 512], in_=xo[:]),
                          ins=[xob], outs=[xinD], inc=16)
                if MK_PARTS == "f0xin":
                    dump_sp(xin_sp, xinD)
                A.release(m0)

                if MK_PARTS == "f0xin":
                    raise _SkipRest
                BmCm, BmCmb = A.alloc((128, 512), BF16, "BmCm")
                bw, bwb = A.alloc((128, 16 * 32), BF16, "bcw")
                for i in range(16):
                    E("gp", lambda e, i=i: e.dma_start(
                        out=bw[:, i * 32:i * 32 + 16],
                        in_=di["B_wT"].ap()[i * 128:(i + 1) * 128, :]), outs=[bwb], inc=16)
                    E("gp", lambda e, i=i: e.dma_start(
                        out=bw[:, i * 32 + 16:i * 32 + 32],
                        in_=di["C_wT"].ap()[i * 128:(i + 1) * 128, :]), outs=[bwb], inc=16)
                m1 = A.mark()
                dw, dwb = A.alloc((128, 16 * 1024), BF16, "dtw")
                xr = [A.alloc((128, 16 * 512), BF16, f"xr{i}") for i in range(2)]
                sp1, sp1b = A.alloc((128, 512), F32, "sp1")
                sp2, sp2b = A.alloc((128, 512), F32, "sp2")
                dto, dtob = A.alloc((128, 512), BF16, "dto")
                for half in range(2):
                    wload(dw, dwb, "dt_wT", 0, 16, 1024, col0=half * 1024, ncols=1024)
                    for ch in range(4):
                        c0 = ch * 512
                        xt, xtb = xr[ch % 2]
                        for i in range(16):
                            E("gp", lambda e, i=i, c0=c0: e.dma_start(
                                out=xt[:, i * 512:(i + 1) * 512],
                                in_=xin_sp.ap()[i * 128:(i + 1) * 128, c0:c0 + 512]),
                              ins=[xinD], outs=[xtb], inc=16)
                        if half == 0:
                            for i in range(16):
                                mm(6, bw, (0, 128, i * 32, i * 32 + 16), bwb,
                                   xt, (0, 128, i * 512, i * 512 + 512), xtb,
                                   start=(i == 0), stop=(i == 15), mlen=16)
                            E("act", lambda e: e.copy(BmCm[0:16, 0:512], pbank(6)[0:16, :]),
                              ins=[pbB[6]], outs=[BmCmb])
                            E("sp", lambda e, c0=c0: e.dma_start(
                                out=bc_sp.ap()[:, c0:c0 + 512], in_=BmCm[0:16, 0:512]),
                              ins=[BmCmb], outs=[bcD], inc=16)
                            for i in range(16):
                                mm(7, bw, (0, 128, i * 32 + 16, i * 32 + 32), bwb,
                                   xt, (0, 128, i * 512, i * 512 + 512), xtb,
                                   start=(i == 0), stop=(i == 15), mlen=16)
                            E("act", lambda e: e.copy(BmCm[0:16, 0:512], pbank(7)[0:16, :]),
                              ins=[pbB[7]], outs=[BmCmb])
                            E("sp", lambda e, c0=c0: e.dma_start(
                                out=bc_sp.ap()[:, T + c0:T + c0 + 512], in_=BmCm[0:16, 0:512]),
                              ins=[BmCmb], outs=[bcD], inc=16)
                        for mt in range(8):
                            dt16 = half * 8 + mt
                            bank = mt % 2 + 2
                            for i in range(16):
                                mm(bank, dw, (0, 128, i * 1024 + mt * 128, i * 1024 + mt * 128 + 128),
                                   dwb, xt, (0, 128, i * 512, i * 512 + 512), xtb,
                                   start=(i == 0), stop=(i == 15))
                            dbs = misc[:, o_db + dt16:o_db + dt16 + 1]
                            E("act", lambda e, bank=bank, dbs=dbs: e.activation(
                                sp1[:], pbank(bank), AF.Abs, bias=dbs),
                              ins=[pbB[bank], miscB], outs=[sp1b])
                            E("act", lambda e: e.activation(sp1[:], sp1[:], AF.Exp, scale=-1.0),
                              ins=[sp1b], outs=[sp1b])
                            E("act", lambda e: e.activation(sp1[:], sp1[:], AF.Ln, bias=1.0),
                              ins=[sp1b], outs=[sp1b])
                            E("act", lambda e, bank=bank, dbs=dbs: e.activation(
                                sp2[:], pbank(bank), AF.Relu, bias=dbs),
                              ins=[pbB[bank], miscB], outs=[sp2b])
                            E("dve", lambda e: e.tensor_tensor(dto[:], sp1[:], sp2[:], OP.add),
                              ins=[sp1b, sp2b], outs=[dtob], hz=False)
                            E("sp", lambda e, dt16=dt16, c0=c0: e.dma_start(
                                out=dt_sp.ap()[dt16 * 128:(dt16 + 1) * 128, c0:c0 + 512],
                                in_=dto[:]), ins=[dtob], outs=[dtD], inc=16)
                A.release(m1)
                if MK_PARTS == "f0dt":
                    dump_sp(dt_sp, dtD)

                if MK_PARTS == "f0dt":
                    raise _SkipRest
                xi = [A.alloc((128, T), BF16, f"sxi{i}") for i in range(2)]
                dti = [A.alloc((128, T), BF16, f"sdt{i}") for i in range(2)]
                br = [A.alloc((1, 2 * T), BF16, f"sbr{i}") for i in range(2)]
                ut, utb = A.alloc((128, T), F32, "su")
                bt2, bt2b = A.alloc((128, T), F32, "sb2")
                at, atb = A.alloc((128, T), F32, "sa")
                bt, btb = A.alloc((128, T), F32, "sb")
                st, stb = A.alloc((128, T), F32, "ss")
                ya, yab = A.alloc((128, T), F32, "sy")
                y16, y16b = A.alloc((128, T), BF16, "sy16")
                for pt16 in range(16):
                    xt, xtb = xi[pt16 % 2]
                    dt_, dtb_ = dti[pt16 % 2]
                    E("gp", lambda e, pt16=pt16: e.dma_start(
                        out=xt[:], in_=xin_sp.ap()[pt16 * 128:(pt16 + 1) * 128, :]),
                      ins=[xinD], outs=[xtb], inc=16)
                    E("gp", lambda e, pt16=pt16: e.dma_start(
                        out=dt_[:], in_=dt_sp.ap()[pt16 * 128:(pt16 + 1) * 128, :]),
                      ins=[dtD], outs=[dtb_], inc=16)
                    E("dve", lambda e: e.tensor_tensor(ut[:], dt_[:], xt[:], OP.mult),
                      ins=[dtb_, xtb], outs=[utb], hz=False)
                    for s in range(16):
                        brt, brtb = br[s % 2]
                        E("gp", lambda e, s=s, brt=brt: e.dma_start(
                            out=brt[:], in_=bc_sp.ap()[s:s + 1, :]),
                          ins=[bcD], outs=[brtb], inc=16)
                        for cc in range(4):
                            mm(cc, ones16, (0, 1, 0, 128), CB, brt,
                               (0, 1, cc * 512, cc * 512 + 512), brtb, True, True)
                            mm(4 + cc, ones16, (0, 1, 0, 128), CB, brt,
                               (0, 1, T + cc * 512, T + cc * 512 + 512), brtb, True, True)
                        Asl = misc[:, o_A + pt16 * 16 + s:o_A + pt16 * 16 + s + 1]
                        E("act", lambda e, Asl=Asl: e.activation(at[:], dt_[:], AF.Exp, scale=Asl),
                          ins=[dtb_, miscB], outs=[atb])
                        E("dve", lambda e: e.tensor_tensor(bt[:], ut[:], pbig[:, 0:2048], OP.mult),
                          ins=[utb, pbB[0], pbB[1], pbB[2], pbB[3]], outs=[btb], hz=False)
                        E("dve", lambda e: e.tensor_tensor_scan(st[:], at[:], bt[:], 0.0,
                                                                OP.mult, OP.add),
                          ins=[atb, btb], outs=[stb], hz=True, hzin=True)
                        E("dve", lambda e: e.tensor_tensor(bt2[:], st[:], pbig[:, 2048:4096],
                                                           OP.mult),
                          ins=[stb, pbB[4], pbB[5], pbB[6], pbB[7]], outs=[bt2b], hz=False)
                        if s == 0:
                            if os.environ.get("MK_SCANDBG", "") == "s0":
                                E("dve", lambda e: e.tensor_copy(ya[:], st[:]), ins=[stb], outs=[yab])
                            else:
                                E("dve", lambda e: e.tensor_copy(ya[:], bt2[:]), ins=[bt2b], outs=[yab], hz=False)
                        elif os.environ.get("MK_SCANDBG", "") != "s0":
                            E("dve", lambda e: e.tensor_tensor(ya[:], ya[:], bt2[:], OP.add),
                              ins=[bt2b], outs=[yab], hz=False)
                    sdbg = os.environ.get("MK_SCANDBG", "")
                    if sdbg not in ("a", "b", "u", "s", "s0", "B", "C"):
                        sdbg = ""
                    if sdbg == "B":
                        E("dve", lambda e: e.tensor_copy(y16[:], pbig[:, 0:2048]),
                          ins=[pbB[0], pbB[1], pbB[2], pbB[3]], outs=[y16b])
                    elif sdbg == "C":
                        E("dve", lambda e: e.tensor_copy(y16[:], pbig[:, 2048:4096]),
                          ins=[pbB[4], pbB[5], pbB[6], pbB[7]], outs=[y16b])
                    else:
                        dbgsrc = {"a": (at, atb), "b": (bt, btb), "u": (ut, utb),
                                  "s": (st, stb), "s0": (ya, yab), "": (ya, yab)}[sdbg]
                        E("dve", lambda e: e.tensor_copy(y16[:], dbgsrc[0][:]),
                          ins=[dbgsrc[1]], outs=[y16b], hz=False)
                    E("sp", lambda e, pt16=pt16: e.dma_start(
                        out=y_sp.ap()[pt16 * 128:(pt16 + 1) * 128, :], in_=y16[:]),
                      ins=[y16b], outs=[yD], inc=16)
                if MK_PARTS == "f0y":
                    dump_sp(y_sp, yD)
                A.release(m0)

                if MK_PARTS == "f0y":
                    raise _SkipRest
                w2, w2b = A.alloc((128, PT * 2048), BF16, "ipw2")
                ow, owb = A.alloc((128, 16 * 1024), BF16, "outw")
                yt, ytb = A.alloc((128, 16 * 512), BF16, "yt")
                sr, srb = A.alloc((128, 512), BF16, "sres")
                wload(w2, w2b, "ipw2T", 0, PT, 2048)
                wload(ow, owb, "out_wT", 0, 16, 1024)
                for ch in range(4):
                    c0 = ch * 512
                    for i in range(16):
                        E("gp", lambda e, i=i, c0=c0: e.dma_start(
                            out=yt[:, i * 512:(i + 1) * 512],
                            in_=y_sp.ap()[i * 128:(i + 1) * 128, c0:c0 + 512]),
                          ins=[yD], outs=[ytb], inc=16)
                    for dt16 in range(16):
                        bank = dt16 % 2
                        for k in range(PT):
                            mm(bank, w2, (0, 128, k * 2048 + dt16 * 128, k * 2048 + dt16 * 128 + 128),
                               w2b, XN, (0, 128, k * T + c0, k * T + c0 + 512), XNb[k],
                               start=(k == 0), stop=(k == PT - 1))
                        E("act", lambda e, bank=bank: e.activation(sr[:], pbank(bank), AF.Silu),
                          ins=[pbB[bank]], outs=[srb])
                        E("dve", lambda e, dt16=dt16: e.tensor_tensor(
                            yt[:, dt16 * 512:(dt16 + 1) * 512],
                            yt[:, dt16 * 512:(dt16 + 1) * 512], sr[:], OP.mult),
                          ins=[srb], outs=[ytb])
                    for mt in range(8):
                        bank = mt % 2 + 2
                        for i in range(16):
                            mm(bank, ow, (0, 128, i * 1024 + mt * 128, i * 1024 + mt * 128 + 128),
                               owb, yt, (0, 128, i * 512, i * 512 + 512), ytb,
                               start=(i == 0), stop=(i == 15))
                        x_add_psum(bank, mt, c0)
                A.release(m0)


            if do_ffn:
                def shift_mix(dst, dstb, o_mu, o_om, k, c0):
                    mu = misc[:, o_mu + k:o_mu + k + 1]
                    om = misc[:, o_om + k:o_om + k + 1]
                    E("dve", lambda e: e.tensor_single_scalar(
                        cvt[:], XN[:, k * T + c0:k * T + c0 + 512], mu, OP.mult),
                      ins=[XNb[k], miscB], outs=[cvtb])
                    if c0 == 0:
                        E("dve", lambda e: e.tensor_single_scalar(
                            cv2[:, 1:512], XN[:, k * T:k * T + 511], om, OP.mult),
                          ins=[XNb[k], miscB], outs=[cv2b])
                        E("dve", lambda e: e.memset(cv2[:, 0:1], 0.0), outs=[cv2b])
                    else:
                        E("dve", lambda e: e.tensor_single_scalar(
                            cv2[:], XN[:, k * T + c0 - 1:k * T + c0 + 511], om, OP.mult),
                          ins=[XNb[k], miscB], outs=[cv2b])
                    E("dve", lambda e: e.tensor_tensor(dst[:], cvt[:], cv2[:], OP.add),
                      ins=[cvtb, cv2b], outs=[dstb])

                cvt, cvtb = A.alloc((128, 512), F32, "f0a")
                cv2, cv2b = A.alloc((128, 512), F32, "f0b")
                xmix, xmixb = A.alloc((128, PT * 512), BF16, "f0mix")
                r16, r16b = A.alloc((128, PT * T), BF16, "f0r")
                k16, k16b = A.alloc((128, PT * T), BF16, "f0k")
                wv, wvb = A.alloc((128, PT * 512), BF16, "f0w")
                for half in range(2):
                    wload(wv, wvb, "WkT", 0, PT, 512, col0=half * 512, ncols=512)
                    for ch in range(4):
                        c0 = ch * 512
                        for k in range(PT):
                            shift_mix(xmix[:, k * 512:(k + 1) * 512], xmixb, o_muk, o_omk, k, c0)
                        for mt4 in range(4):
                            mt = half * 4 + mt4
                            bank = mt4 % 2
                            for k in range(PT):
                                mm(bank, wv, (0, 128, k * 512 + mt4 * 128, k * 512 + mt4 * 128 + 128),
                                   wvb, xmix, (0, 128, k * 512, k * 512 + 512), xmixb,
                                   start=(k == 0), stop=(k == PT - 1))
                            E("act", lambda e, bank=bank: e.activation(cvt[:], pbank(bank), AF.Relu),
                              ins=[pbB[bank]], outs=[cvtb])
                            E("act", lambda e, mt=mt, c0=c0: e.activation(
                                k16[:, mt * T + c0:mt * T + c0 + 512], cvt[:], AF.Square),
                              ins=[cvtb], outs=[k16b])
                for half in range(2):
                    wload(wv, wvb, "WrT", 0, PT, 512, col0=half * 512, ncols=512)
                    for ch in range(4):
                        c0 = ch * 512
                        for k in range(PT):
                            shift_mix(xmix[:, k * 512:(k + 1) * 512], xmixb, o_mur, o_omr, k, c0)
                        for mt4 in range(4):
                            mt = half * 4 + mt4
                            bank = mt4 % 2
                            for k in range(PT):
                                mm(bank, wv, (0, 128, k * 512 + mt4 * 128, k * 512 + mt4 * 128 + 128),
                                   wvb, xmix, (0, 128, k * 512, k * 512 + 512), xmixb,
                                   start=(k == 0), stop=(k == PT - 1))
                            E("act", lambda e, bank=bank, mt=mt, c0=c0: e.activation(
                                r16[:, mt * T + c0:mt * T + c0 + 512], pbank(bank), AF.Sigmoid),
                              ins=[pbB[bank]], outs=[r16b])
                if MK_PARTS in ("f0r", "f0k"):
                    srct = r16 if MK_PARTS == "f0r" else k16
                    for k in range(PT):
                        E("dve", lambda e, k=k: e.tensor_copy(
                            X[:, k * T:(k + 1) * T], srct[:, k * T:(k + 1) * T]),
                          ins=[r16b if MK_PARTS == "f0r" else k16b], outs=[Xb[k]])
                for half in range(2):
                    wload(wv, wvb, "WvT", 0, PT, 512, col0=half * 512, ncols=512)
                    for ch in range(4):
                        c0 = ch * 512
                        for mt4 in range(4):
                            mt = half * 4 + mt4
                            bank = mt4 % 2
                            for k in range(PT):
                                mm(bank, wv, (0, 128, k * 512 + mt4 * 128, k * 512 + mt4 * 128 + 128),
                                   wvb, k16, (0, 128, k * T + c0, k * T + c0 + 512), k16b,
                                   start=(k == 0), stop=(k == PT - 1))
                            E("dve", lambda e, bank=bank, mt=mt, c0=c0: e.scalar_tensor_tensor(
                                cvt[:], pbank(bank), 1.0, r16[:, mt * T + c0:mt * T + c0 + 512],
                                OP.mult, OP.mult), ins=[pbB[bank], r16b], outs=[cvtb])
                            E("dve", lambda e, mt=mt, c0=c0: e.tensor_tensor(
                                X[:, mt * T + c0:mt * T + c0 + 512],
                                X[:, mt * T + c0:mt * T + c0 + 512], cvt[:], OP.add),
                              ins=[cvtb], outs=[Xb[mt]])
                A.release(m0)


        def rope_attn(wq_nm, wk_nm, wv_nm, w2_nm, causal, o_bq=None, o_bk=None,
                      bv16=None, bv16b=None, o_bo=None, o_sc4=0):
            for hg in range(2):
                mhg = A.mark()
                ao, aob = A.alloc((128, 4 * T), BF16, "ao")
                q16, q16b = A.alloc((128, 4 * T), BF16, "q")
                k16, k16b = A.alloc((128, 4 * T), BF16, "k")
                vt, vtb = A.alloc((128, 16 * 520), BF16, "v")
                msc = A.mark()
                rstg, rstgb = A.alloc((128, 512), F32, "rs")
                rot, rotb = A.alloc((128, 512), F32, "rot")
                t1, t1b = A.alloc((128, 512), F32, "t1")
                ws, wsb = A.alloc((128, 8 * 512), BF16, "w")
                pst = [A.alloc((128, 512), BF16, f"p{i}") for i in range(2)]
                nrm, nrmb = A.alloc((128, 520), BF16, "nrm")
                nr32, nr32b = rstg, rstgb

                for which, wnm, dst, dstb, o_bias in ((0, wq_nm, q16, q16b, o_bq),
                                                      (1, wk_nm, k16, k16b, o_bk)):
                    wload(ws, wsb, wnm, 0, PT, 512, col0=hg * 512, ncols=512)
                    for mt4 in range(4):
                        for ch in range(4):
                            c0 = ch * 512
                            bank = (mt4 * 4 + ch) % 2
                            for k in range(PT):
                                mm(bank, ws, (0, 128, k * 512 + mt4 * 128,
                                              k * 512 + mt4 * 128 + 128), wsb,
                                   XN, (0, 128, k * T + c0, k * T + c0 + 512), XNb[k],
                                   start=(k == 0), stop=(k == PT - 1))
                            if o_bias is None:
                                E("act", lambda e, bank=bank: e.copy(rstg[:], pbank(bank)),
                                  ins=[pbB[bank]], outs=[rstgb])
                            else:
                                bc = o_bias + hg * 4 + mt4
                                E("act", lambda e, bank=bank, bc=bc: e.activation(
                                    rstg[:], pbank(bank), AF.Identity, scale=1.0,
                                    bias=misc[:, bc:bc + 1]),
                                  ins=[pbB[bank], miscB], outs=[rstgb])
                            for blk2 in range(2):
                                b0 = blk2 * 64
                                E("dve", lambda e, b0=b0: e.tensor_scalar(
                                    rot[b0:b0 + 32, :], rstg[b0 + 32:b0 + 64, :],
                                    -1.0, None, OP.mult), ins=[rstgb], outs=[rotb], hz=False)
                                E("dve", lambda e, b0=b0: e.tensor_copy(
                                    rot[b0 + 32:b0 + 64, :], rstg[b0:b0 + 32, :]),
                                  ins=[rstgb], outs=[rotb], hz=False)
                            E("dve", lambda e, c0=c0: e.tensor_tensor(
                                t1[:], rstg[:], ropec[:, c0:c0 + 512], OP.mult),
                              ins=[rstgb, CB], outs=[t1b], hz=False)
                            E("dve", lambda e, c0=c0: e.tensor_tensor(
                                rot[:], rot[:], ropes[:, c0:c0 + 512], OP.mult),
                              ins=[rotb, CB], outs=[rotb], hz=False)
                            E("dve", lambda e, mt4=mt4, c0=c0, dst=dst: e.tensor_tensor(
                                dst[:, mt4 * T + c0:mt4 * T + c0 + 512], t1[:], rot[:],
                                OP.add), ins=[t1b, rotb], outs=[dstb], hz=False)
                E("dve", lambda e: e.memset(vt[:], 1.0), outs=[vtb], hz=False)
                wload(ws, wsb, wv_nm, 0, PT, 512, col0=hg * 512, ncols=512)
                for tt in range(16):
                    bank = tt % 2 + 2
                    for k in range(PT):
                        mm(bank, XN, (0, 128, k * T + tt * 128, k * T + tt * 128 + 128),
                           XNb[k], ws, (0, 128, k * 512, k * 512 + 512), wsb,
                           start=(k == 0), stop=(k == PT - 1 and bv16 is None))
                    if bv16 is not None:
                        mm(bank, ones16, (0, 1, 0, 128), CB, bv16,
                           (0, 1, hg * 512, hg * 512 + 512), bv16b,
                           start=False, stop=True)
                    for h8 in range(8):
                        E("act", lambda e, bank=bank, tt=tt, h8=h8: e.copy(
                            vt[:, tt * 520 + h8 * 65:tt * 520 + h8 * 65 + 64],
                            pbank(bank, h8 * 64, 64)), ins=[pbB[bank]], outs=[vtb])
                for h8 in range(8):
                    hb = (h8 % 2) * 64
                    tq = (h8 // 2)
                    qoff = hb * (4 * T) + tq * T
                    E("pe", lambda e, qoff=qoff: e.matmul(
                        pbank(6)[0:128, 0:128],
                        AP(k16.ap().tensor, qoff, [[4 * T, 64], [16, 128]]),
                        AP(q16.ap().tensor, qoff, [[4 * T, 64], [16, 128]]),
                        start=True, stop=True),
                      ins=[k16b, q16b], outs=[pbB[6]])
                    E("dve", lambda e: e.reduce_max(misc2[:, 28:29], pbank(6)[:, 0:128],
                                                    axis=AX),
                      ins=[pbB[6]], outs=[misc2B], hz=True)
                    E("sp", lambda e: e.dma_start(
                        out=flip_sp.ap()[0:1, :].rearrange("a b -> (a b)"),
                        in_=misc2[:, 28:29]), ins=[misc2B], outs=[flipD], inc=16)
                    E("sp", lambda e: e.dma_start(out=misc2[0:1, 32:160],
                                                  in_=flip_sp.ap()[0:1, :]),
                      ins=[flipD], outs=[misc2B], inc=16)
                    E("dve", lambda e: e.reduce_max(misc2[0:1, 30:31],
                                                    misc2[0:1, 32:160], axis=AX),
                      ins=[misc2B], outs=[misc2B], hz=True)
                    E("dve", lambda e: e.tensor_scalar(misc2[0:1, 30:31],
                                                       misc2[0:1, 30:31], -0.125, None,
                                                       OP.mult),
                      ins=[misc2B], outs=[misc2B])
                    mm(6, onesf, (0, 1, 0, 128), CB, misc2, (0, 1, 30, 31), misc2B,
                       True, True, nlen=1)
                    E("act", lambda e: e.copy(misc[:, o_sc4:o_sc4 + 1], pbank(6, 0, 1)),
                      ins=[pbB[6]], outs=[miscB])
                    if os.environ.get("MK_SCANDBG", "") == "nb" and not causal:
                        E("sp", lambda e, hg=hg, h8=h8: e.dma_start(
                            out=dbg_d.ap()[0:1, hg * 8 + h8:hg * 8 + h8 + 1],
                            in_=misc2[0:1, 30:31]), ins=[misc2B], inc=16)
                        E("sp", lambda e, hg=hg, h8=h8: e.dma_start(
                            out=dbg_d.ap()[0:1, 16 + hg * 8 + h8:17 + hg * 8 + h8],
                            in_=misc2[0:1, 28:29]), ins=[misc2B], inc=16)
                    for qc in range(4):
                        q0 = qc * 512
                        nkt = (qc * 4 + 4) if causal else 16
                        for kt in range(nkt):
                            bank = kt % 2
                            mm(bank, k16, (hb, hb + 64, tq * T + kt * 128,
                                           tq * T + kt * 128 + 128), k16b,
                               q16, (hb, hb + 64, tq * T + q0, tq * T + q0 + 512), q16b,
                               start=True, stop=True)
                            if causal:
                                delta = kt * 128 - q0
                                if delta >= 0:
                                    mi = delta // 128
                                    b0 = 384 - mi * 128
                                    E("dve", lambda e, bank=bank, b0=b0: e.tensor_tensor(
                                        pbank(bank), pbank(bank),
                                        l1mask[:, b0:b0 + 512], OP.add),
                                      ins=[CB], outs=[pbB[bank]], hz=False)
                            ps, psb = pst[kt % 2]
                            E("act", lambda e, bank=bank, ps=ps: e.activation(
                                ps[:], pbank(bank), AF.Exp, scale=0.125,
                                bias=misc[:, o_sc4:o_sc4 + 1]),
                              ins=[pbB[bank], miscB], outs=[psb])
                            mm(2, vt, (0, 128, kt * 520 + h8 * 65, kt * 520 + h8 * 65 + 65),
                               vtb, ps, (0, 128, 0, 512), psb,
                               start=(kt == 0), stop=(kt == nkt - 1), mlen=65)
                        E("dve", lambda e: e.reciprocal(nr32[64:65, 0:512],
                                                        pbank(2)[64:65, 0:512]),
                          ins=[pbB[2]], outs=[nr32b], hz=True)
                        E("dve", lambda e: e.tensor_copy(nrm[64:65, 0:512],
                                                         nr32[64:65, 0:512]),
                          ins=[nr32b], outs=[nrmb], hz=True)
                        mm(3, ones16, (64, 65, 0, 64), CB, nrm, (64, 65, 0, 512), nrmb,
                           True, True, mlen=64)
                        E("act", lambda e: e.copy(rstg[0:64, :], pbank(2)[0:64, :]),
                          ins=[pbB[2]], outs=[rstgb])
                        E("dve", lambda e, h8=h8, q0=q0: e.tensor_tensor(
                            ao[(h8 % 2) * 64:(h8 % 2) * 64 + 64,
                               (h8 // 2) * T + q0:(h8 // 2) * T + q0 + 512],
                            rstg[0:64, :], pbank(3)[0:64, :], OP.mult),
                          ins=[rstgb, pbB[3]], outs=[aob], hz=False)
                A.release(msc)
                w2s, w2sb = A.alloc((128, 4 * 1024), BF16, "w2p")
                ev2, ev2b = A.alloc((128, 512), F32, "w2ev")
                wload(w2s, w2sb, w2_nm, hg * 4, 4, 1024)
                for ch in range(4):
                    c0 = ch * 512
                    for mt in range(8):
                        bank = mt % 2
                        for k in range(4):
                            mm(bank, w2s, (0, 128, k * 1024 + mt * 128,
                                           k * 1024 + mt * 128 + 128),
                               w2sb, ao, (0, 128, k * T + c0, k * T + c0 + 512), aob,
                               start=(k == 0), stop=(k == 3))
                        if o_bo is not None and hg == 0:
                            E("act", lambda e, bank=bank, mt=mt: e.activation(
                                ev2[:], pbank(bank), AF.Identity, scale=1.0,
                                bias=misc[:, o_bo + mt:o_bo + mt + 1]),
                              ins=[pbB[bank], miscB], outs=[ev2b])
                            E("dve", lambda e, mt=mt, c0=c0: e.tensor_tensor(
                                X[:, mt * T + c0:mt * T + c0 + 512],
                                X[:, mt * T + c0:mt * T + c0 + 512], ev2[:], OP.add),
                              ins=[ev2b], outs=[Xb[mt]], hz=False)
                        else:
                            x_add_psum(bank, mt, c0)
                A.release(mhg)

        def l1():
            o_b1 = load_vec("l1f_b1", 4096)
            o_b2 = load_vec("l1f_b2", D)
            o_b1s = mslot(32)
            o_b2s = mslot(8)
            o_sc = mslot(8)

            mpre = A.mark()
            xn32, xn32b = layer_norm(want32=True)

            arD, ar2D = Buf(), Buf()
            ax, axb = A.alloc((128, 16), F32, "amax")
            wstg, wstgb = A.alloc((128, 2048), F32, "awstg")
            wstg2, wstg2b = A.alloc((128, 2048), F32, "awstg2")
            for k in range(PT):
                E("dve", lambda e, k=k: e.reduce_max(ax[:, k:k + 1], xn32[:, k * T:(k + 1) * T],
                                                     axis=AX, apply_absolute_value=True),
                  ins=[xn32b], outs=[axb], hz=True)
            w1v = di["l1f_w1T"].ap().rearrange("a (b c) -> (a b) c", c=2048)
            for i in range(16):
                wt = wstg if i % 2 == 0 else wstg2
                wtb = wstgb if i % 2 == 0 else wstg2b
                E("sp", lambda e, i=i, wt=wt: e.dma_start(
                    out=wt[:], in_=w1v[i * 128:(i + 1) * 128, :]),
                  outs=[wtb], inc=16)
                E("dve", lambda e, wt=wt: e.reduce_max(ax[:, 9:10], wt[:], axis=AX,
                                                       apply_absolute_value=True),
                  ins=[wtb], outs=[axb], hz=True)
                E("dve", lambda e, i=i: (e.tensor_copy(ax[:, 8:9], ax[:, 9:10]) if i == 0
                                         else e.tensor_tensor(ax[:, 8:9], ax[:, 8:9],
                                                              ax[:, 9:10], OP.max)),
                  ins=[axb], outs=[axb])
            for i in range(16):
                wt = wstg if i % 2 == 0 else wstg2
                wtb = wstgb if i % 2 == 0 else wstg2b
                E("sp", lambda e, i=i, wt=wt: e.dma_start(
                    out=wt[:, 0:1024], in_=di["l1f_w2T"].ap()[i * 256:i * 256 + 128, :]),
                  outs=[wtb], inc=16)
                E("sp", lambda e, i=i, wt=wt: e.dma_start(
                    out=wt[:, 1024:2048], in_=di["l1f_w2T"].ap()[i * 256 + 128:i * 256 + 256, :]),
                  outs=[wtb], inc=16)
                E("dve", lambda e, wt=wt: e.reduce_max(ax[:, 9:10], wt[:], axis=AX,
                                                       apply_absolute_value=True),
                  ins=[wtb], outs=[axb], hz=True)
                E("dve", lambda e, i=i: (e.tensor_copy(ax[:, 10:11], ax[:, 9:10]) if i == 0
                                         else e.tensor_tensor(ax[:, 10:11], ax[:, 10:11],
                                                              ax[:, 9:10], OP.max)),
                  ins=[axb], outs=[axb])
            for k in range(1, PT):
                E("dve", lambda e, k=k: e.tensor_tensor(ax[:, 0:1], ax[:, 0:1], ax[:, k:k + 1],
                                                        OP.max), ins=[axb], outs=[axb])
            flD = flipD
            for j, col in enumerate((0, 8, 10)):
                E("sp", lambda e, j=j, col=col: e.dma_start(
                    out=flip_sp.ap()[j:j + 1, :].rearrange("a b -> (a b)"),
                    in_=ax[:, col:col + 1]), ins=[axb], outs=[flD], inc=16)
            for j in range(3):
                E("sp", lambda e, j=j: e.dma_start(
                    out=misc2[0:1, 32 + j * 128:32 + (j + 1) * 128],
                    in_=flip_sp.ap()[j:j + 1, :]), ins=[flD], outs=[misc2B], inc=16)
            for j in range(3):
                E("dve", lambda e, j=j: e.reduce_max(
                    misc2[0:1, 16 + j:17 + j], misc2[0:1, 32 + j * 128:32 + (j + 1) * 128],
                    axis=AX), ins=[misc2B], outs=[misc2B], hz=True)
            E("dve", lambda e: e.memset(misc2[0:1, 19:20], 0.0), outs=[misc2B])
            E("gp", lambda e: e.dma_start(out=ar_in.ap()[0:1, 0:4], in_=misc2[0:1, 16:20]),
              ins=[misc2B], outs=[arD], inc=16)
            E("gp", lambda e: e.collective_compute(
                "AllReduce", OP.max, replica_groups=[list(range(8))],
                ins=[ar_in.ap().opt()], outs=[ar_out.ap().opt()]),
              ins=[arD], outs=[ar2D], isasync=True)
            E("gp", lambda e: e.dma_start(out=misc2[0:1, 16:19], in_=ar_out.ap()[0:1, 0:3]),
              ins=[ar2D], outs=[misc2B], inc=16)
            E("dve", lambda e: e.tensor_scalar(misc2[0:1, 20:23], misc2[0:1, 16:19],
                                               1e-6, 1.0 / 127.0, OP.max, OP.mult),
              ins=[misc2B], outs=[misc2B])
            E("dve", lambda e: e.reciprocal(misc2[0:1, 24:25], misc2[0:1, 20:21]),
              ins=[misc2B], outs=[misc2B], hz=True)
            E("dve", lambda e: e.tensor_tensor(misc2[0:1, 25:26], misc2[0:1, 20:21],
                                               misc2[0:1, 21:22], OP.mult),
              ins=[misc2B], outs=[misc2B])
            mm(1, onesf, (0, 1, 0, 128), CB, misc2, (0, 1, 24, 25), misc2B, True, True, nlen=1)
            E("act", lambda e: e.copy(misc[:, o_sc:o_sc + 1], pbank(1, 0, 1)),
              ins=[pbB[1]], outs=[miscB])
            mm(1, onesf, (0, 1, 0, 128), CB, misc2, (0, 1, 25, 26), misc2B, True, True, nlen=1)
            E("act", lambda e: e.copy(misc[:, o_sc + 1:o_sc + 2], pbank(1, 0, 1)),
              ins=[pbB[1]], outs=[miscB])
            E("dve", lambda e: e.reciprocal(misc2[0:1, 29:30], misc2[0:1, 21:22]),
              ins=[misc2B], outs=[misc2B], hz=True)
            mm(1, onesf, (0, 1, 0, 128), CB, misc2, (0, 1, 29, 30), misc2B, True, True, nlen=1)
            E("act", lambda e: e.copy(misc[:, o_sc + 5:o_sc + 6], pbank(1, 0, 1)),
              ins=[pbB[1]], outs=[miscB])
            E("dve", lambda e: e.reciprocal(misc2[0:1, 29:30], misc2[0:1, 22:23]),
              ins=[misc2B], outs=[misc2B], hz=True)
            mm(1, onesf, (0, 1, 0, 128), CB, misc2, (0, 1, 29, 30), misc2B, True, True, nlen=1)
            E("act", lambda e: e.copy(misc[:, o_sc + 6:o_sc + 7], pbank(1, 0, 1)),
              ins=[pbB[1]], outs=[miscB])
            E("dve", lambda e: e.tensor_single_scalar(misc[:, o_b1s:o_b1s + 32],
                                                      misc[:, o_b1:o_b1 + 32],
                                                      misc[:, o_sc + 1:o_sc + 2], OP.mult),
              ins=[miscB], outs=[miscB])
            if os.environ.get("MK_SCANDBG", "") == "sc":
                E("dve", lambda e: e.tensor_copy(X[0:1, 0:16], misc2[0:1, 16:32]),
                  ins=[misc2B], outs=[Xb[0]])
                E("dve", lambda e: e.tensor_copy(X[0:1, 16:24], misc[0:1, o_sc:o_sc + 8]),
                  ins=[miscB], outs=[Xb[0]])
                E("sp", lambda e: e.dma_start(out=X[0:1, 40:44], in_=ar_out.ap()[0:1, 0:4]),
                  ins=[ar2D], outs=[Xb[0]], inc=16)
                E("sp", lambda e: e.dma_start(out=X[0:1, 44:48], in_=ar_in.ap()[0:1, 0:4]),
                  ins=[arD], outs=[Xb[0]], inc=16)
                E("dve", lambda e: e.tensor_copy(X[0:1, 70:198], misc2[0:1, 160:288]),
                  ins=[misc2B], outs=[Xb[0]])
                raise _SkipRest
            xqs, xqsb = A.alloc((128, 512), F32, "xqs")
            xq16, xq16b = A.alloc((128, 512), BF16, "xq16")
            for k in range(PT):
                for ch in range(4):
                    c0 = ch * 512
                    E("dve", lambda e, k=k, c0=c0: e.tensor_single_scalar(
                        xqs[:], xn32[:, k * T + c0:k * T + c0 + 512],
                        misc[:, o_sc:o_sc + 1], OP.mult),
                      ins=[xn32b, miscB], outs=[xqsb])
                    E("dve", lambda e: e.tensor_scalar(xqs[:], xqs[:], MAGIC, None, OP.add),
                      ins=[xqsb], outs=[xqsb])
                    E("dve", lambda e: e.tensor_scalar(xq16[:], xqs[:], MAGIC, None,
                                                       OP.subtract),
                      ins=[xqsb], outs=[xq16b])
                    E("sp", lambda e, k=k, c0=c0: e.dma_start(
                        out=xq_sp.ap()[k * 128:(k + 1) * 128, c0:c0 + 512], in_=xq16[:]),
                      ins=[xq16b], outs=[xqD], inc=16)
            A.release(mpre)


            do_attn1 = not (N_LAYERS == 2 and MK_PARTS == "ffn")
            do_ffn1 = not (N_LAYERS == 2 and MK_PARTS == "attn")
            m0 = A.mark()
            if do_attn1:
                rope_attn("l1_wqT", "l1_wkT", "l1_wvT", "l1_w2T", True, o_sc4=o_sc + 4)
            A.release(m0)

            if not do_ffn1:
                raise _SkipRest
            m0 = A.mark()
            w1q, w1qb = A.alloc((128, 8 * 512), BF16, "w1q")
            fstg, fstgb = A.alloc((128, 512), F32, "fstg")
            f2, f2b = A.alloc((128, 512), F32, "f2")
            xqv = [A.alloc((128, 8 * 512), BF16, f"xqv{i}") for i in range(2)]
            hmax, hmaxb = A.alloc((128, 4), F32, "hmax")
            hst = [A.alloc((128, 512), F32, f"hst{i}") for i in range(2)]
            for oc in range(8):
                for kt in range(PT):
                    E("sp", lambda e, oc=oc, kt=kt: e.dma_start(
                        out=fstg[:], in_=di["l1f_w1T"].ap()[kt * 128:(kt + 1) * 128,
                                                            oc * 512:(oc + 1) * 512]),
                      outs=[fstgb], inc=16)
                    E("dve", lambda e, kt=kt: e.tensor_single_scalar(
                        f2[:], fstg[:], misc[:, o_sc + 5:o_sc + 6], OP.mult),
                      ins=[fstgb, miscB], outs=[f2b])
                    E("dve", lambda e: e.tensor_scalar(f2[:], f2[:], MAGIC, None, OP.add),
                      ins=[f2b], outs=[f2b])
                    E("dve", lambda e, kt=kt: e.tensor_scalar(
                        w1q[:, kt * 512:(kt + 1) * 512], f2[:], MAGIC, None, OP.subtract),
                      ins=[f2b], outs=[w1qb])
                for ch in range(4):
                    c0 = ch * 512
                    xqt, xqtb = xqv[ch % 2]
                    for kt in range(PT):
                        E("gp", lambda e, kt=kt, c0=c0: e.dma_start(
                            out=xqt[:, kt * 512:(kt + 1) * 512],
                            in_=xq_sp.ap()[kt * 128:(kt + 1) * 128, c0:c0 + 512]),
                          ins=[xqD], outs=[xqtb], inc=16)
                    for mt4 in range(4):
                        dglob = oc * 4 + mt4
                        bank = (mt4 + ch) % 2
                        for kt in range(PT):
                            mm(bank, w1q, (0, 128, kt * 512 + mt4 * 128,
                                           kt * 512 + mt4 * 128 + 128), w1qb,
                               xqt, (0, 128, kt * 512, kt * 512 + 512), xqtb,
                               start=(kt == 0), stop=(kt == PT - 1))
                        hs, hsb = hst[bank]
                        E("act", lambda e, bank=bank, dglob=dglob, hs=hs: e.activation(
                            hs[:], pbank(bank), AF.Gelu,
                            scale=misc[:, o_sc + 1:o_sc + 2],
                            bias=misc[:, o_b1s + dglob:o_b1s + dglob + 1]),
                          ins=[pbB[bank], miscB], outs=[hsb])
                        E("dve", lambda e, hs=hs: e.reduce_max(hmax[:, 1:2], hs[:], axis=AX,
                                                               apply_absolute_value=True),
                          ins=[hsb], outs=[hmaxb], hz=True)
                        E("dve", lambda e, oc=oc, ch=ch, mt4=mt4: (
                            e.tensor_copy(hmax[:, 0:1], hmax[:, 1:2])
                            if (oc == 0 and ch == 0 and mt4 == 0) else
                            e.tensor_tensor(hmax[:, 0:1], hmax[:, 0:1], hmax[:, 1:2],
                                            OP.max)), ins=[hmaxb], outs=[hmaxb])
                        E("sp", lambda e, dglob=dglob, c0=c0, hs=hs: e.dma_start(
                            out=h_sp.ap()[dglob * 128:(dglob + 1) * 128, c0:c0 + 512],
                            in_=hs[:]), ins=[hsb], outs=[hD], inc=16)
            flD2 = flipD
            E("sp", lambda e: e.dma_start(
                out=flip_sp.ap()[3:4, :].rearrange("a b -> (a b)"), in_=hmax[:, 0:1]),
              ins=[hmaxb], outs=[flD2], inc=16)
            E("sp", lambda e: e.dma_start(out=misc2[0:1, 32:160], in_=flip_sp.ap()[3:4, :]),
              ins=[flD2], outs=[misc2B], inc=16)
            E("dve", lambda e: e.reduce_max(misc2[0:1, 16:17], misc2[0:1, 32:160], axis=AX),
              ins=[misc2B], outs=[misc2B], hz=True)
            E("gp", lambda e: e.dma_start(out=ar2_in.ap()[0:1, 0:1], in_=misc2[0:1, 16:17]),
              ins=[misc2B], outs=[arD], inc=16)
            E("gp", lambda e: e.collective_compute(
                "AllReduce", OP.max, replica_groups=[list(range(8))],
                ins=[ar2_in.ap().opt()], outs=[ar2_out.ap().opt()]),
              ins=[arD], outs=[ar2D], isasync=True)
            E("gp", lambda e: e.dma_start(out=misc2[0:1, 16:17], in_=ar2_out.ap()[0:1, 0:1]),
              ins=[ar2D], outs=[misc2B], inc=16)
            E("dve", lambda e: e.tensor_scalar(misc2[0:1, 26:27], misc2[0:1, 16:17],
                                               1e-6, 1.0 / 127.0, OP.max, OP.mult),
              ins=[misc2B], outs=[misc2B])
            E("dve", lambda e: e.reciprocal(misc2[0:1, 27:28], misc2[0:1, 26:27]),
              ins=[misc2B], outs=[misc2B], hz=True)
            E("dve", lambda e: e.tensor_tensor(misc2[0:1, 28:29], misc2[0:1, 26:27],
                                               misc2[0:1, 22:23], OP.mult),
              ins=[misc2B], outs=[misc2B])
            mm(1, onesf, (0, 1, 0, 128), CB, misc2, (0, 1, 27, 28), misc2B, True, True, nlen=1)
            E("act", lambda e: e.copy(misc[:, o_sc + 2:o_sc + 3], pbank(1, 0, 1)),
              ins=[pbB[1]], outs=[miscB])
            mm(1, onesf, (0, 1, 0, 128), CB, misc2, (0, 1, 28, 29), misc2B, True, True, nlen=1)
            E("act", lambda e: e.copy(misc[:, o_sc + 3:o_sc + 4], pbank(1, 0, 1)),
              ins=[pbB[1]], outs=[miscB])
            E("dve", lambda e: e.tensor_single_scalar(misc[:, o_b2s:o_b2s + 8],
                                                      misc[:, o_b2:o_b2 + 8],
                                                      misc[:, o_sc + 3:o_sc + 4], OP.mult),
              ins=[miscB], outs=[miscB])
            for kt in range(32):
                E("sp", lambda e, kt=kt: e.dma_start(
                    out=fstg[:], in_=di["l1f_w2T"].ap()[kt * 128:(kt + 1) * 128, 0:512]),
                  outs=[fstgb], inc=16)
                E("sp", lambda e, kt=kt: e.dma_start(
                    out=f2[:], in_=di["l1f_w2T"].ap()[kt * 128:(kt + 1) * 128, 512:1024]),
                  outs=[f2b], inc=16)
                for half, (src_t, src_b) in enumerate(((fstg, fstgb), (f2, f2b))):
                    E("dve", lambda e, src_t=src_t: e.tensor_single_scalar(
                        hst[0][0][:], src_t[:], misc[:, o_sc + 6:o_sc + 7], OP.mult),
                      ins=[src_b, miscB], outs=[hst[0][1]])
                    E("dve", lambda e: e.tensor_scalar(hst[0][0][:], hst[0][0][:], MAGIC,
                                                       None, OP.add),
                      ins=[hst[0][1]], outs=[hst[0][1]])
                    E("dve", lambda e: e.tensor_scalar(xqv[0][0][:, 0:512], hst[0][0][:],
                                                       MAGIC, None, OP.subtract),
                      ins=[hst[0][1]], outs=[xqv[0][1]])
                    E("sp", lambda e, kt=kt, half=half: e.dma_start(
                        out=w2q_sp.ap()[kt * 128:(kt + 1) * 128,
                                        half * 512:(half + 1) * 512],
                        in_=xqv[0][0][:, 0:512]), ins=[xqv[0][1]], outs=[w2qD], inc=16)
            w2qs, w2qsb = A.alloc((128, 4 * 1024), BF16, "w2qs")
            hq16, hq16b = A.alloc((128, 4 * 512), BF16, "hq16")
            for ch in range(4):
                c0 = ch * 512
                first = True
                for kg in range(8):
                    for kt4 in range(4):
                        kt = kg * 4 + kt4
                        E("gp", lambda e, kt=kt, kt4=kt4: e.dma_start(
                            out=w2qs[:, kt4 * 1024:(kt4 + 1) * 1024],
                            in_=w2q_sp.ap()[kt * 128:(kt + 1) * 128, :]),
                          ins=[w2qD], outs=[w2qsb], inc=16)
                        E("sp", lambda e, kt=kt, c0=c0: e.dma_start(
                            out=fstg[:], in_=h_sp.ap()[kt * 128:(kt + 1) * 128, c0:c0 + 512]),
                          ins=[hD], outs=[fstgb], inc=16)
                        E("dve", lambda e: e.tensor_single_scalar(
                            f2[:], fstg[:], misc[:, o_sc + 2:o_sc + 3], OP.mult),
                          ins=[fstgb, miscB], outs=[f2b])
                        E("dve", lambda e: e.tensor_scalar(f2[:], f2[:], MAGIC, None, OP.add),
                          ins=[f2b], outs=[f2b])
                        E("dve", lambda e, kt4=kt4: e.tensor_scalar(
                            hq16[:, kt4 * 512:(kt4 + 1) * 512], f2[:], MAGIC, None,
                            OP.subtract), ins=[f2b], outs=[hq16b])
                    for mt in range(8):
                        for kt4 in range(4):
                            mm(mt, w2qs, (0, 128, kt4 * 1024 + mt * 128,
                                          kt4 * 1024 + mt * 128 + 128), w2qsb,
                               hq16, (0, 128, kt4 * 512, kt4 * 512 + 512), hq16b,
                               start=(kg == 0 and kt4 == 0),
                               stop=(kg == 7 and kt4 == 3))
                for mt in range(8):
                    E("act", lambda e, mt=mt: e.activation(
                        fstg[:], pbank(mt), AF.Identity, scale=misc[:, o_sc + 3:o_sc + 4],
                        bias=misc[:, o_b2s + mt:o_b2s + mt + 1]),
                      ins=[pbB[mt], miscB], outs=[fstgb])
                    E("dve", lambda e, mt=mt, c0=c0: e.tensor_tensor(
                        X[:, mt * T + c0:mt * T + c0 + 512],
                        X[:, mt * T + c0:mt * T + c0 + 512], fstg[:], OP.add),
                      ins=[fstgb], outs=[Xb[mt]])
            A.release(m0)


        def l2():
            o_bq2 = load_vec("l2_bq", D)
            o_bk2 = load_vec("l2_bk", D)
            o_bo2 = load_vec("l2_bo", D)
            o_bs1 = load_vec("l2f_bs1", D)
            o_bs2 = load_vec("l2f_bs2", D)
            o_br = mslot(1)
            E("sp", lambda e: e.dma_start(out=misc[0:31, o_br:o_br + 1],
                                          in_=AP(di["l2f_br"].ap().tensor, 0, [[1, 31], [1, 1]]),
                                          allow_slow_non_contiguous=True),
              outs=[miscB], inc=16)
            bv16, bv16b = A.alloc((1, 1024), BF16, "l2bv16")
            E("gp", lambda e: e.dma_start(out=bv16[:],
                                          in_=di["l2_bv"].ap().rearrange("(a b) -> a b", a=1)),
              outs=[bv16b], inc=16)

            mpre = A.mark()
            layer_norm(False)
            A.release(mpre)
            do_attn2 = not (N_LAYERS == 3 and MK_PARTS == "ffn")
            do_ffn2 = not (N_LAYERS == 3 and MK_PARTS == "attn")
            m0 = A.mark()
            if do_attn2:
                rope_attn("l2_wqT", "l2_wkT", "l2_wvT", "l2_woT", False,
                          o_bq=o_bq2, o_bk=o_bk2, bv16=bv16, bv16b=bv16b, o_bo=o_bo2,
                          o_sc4=o_br + 1 if False else mslot(1))
            A.release(m0)
            if not do_ffn2:
                return

            m0 = A.mark()
            cvt, cvtb = A.alloc((128, 512), F32, "l2cv")
            for mt in range(8):
                for ch in range(4):
                    c0 = ch * 512
                    E("dve", lambda e, mt=mt, c0=c0: e.tensor_tensor(
                        X[:, mt * T + c0:mt * T + c0 + 512],
                        X[:, mt * T + c0:mt * T + c0 + 512],
                        XN[:, mt * T + c0:mt * T + c0 + 512], OP.add),
                      ins=[XNb[mt]], outs=[Xb[mt]], hz=False)
            ws, wsb = A.alloc((128, PT * 1024), BF16, "l2ws")
            h16, h16b = A.alloc((128, PT * T), BF16, "l2h")
            wload(ws, wsb, "l2f_ws1", 0, PT, 1024)
            for ch in range(4):
                c0 = ch * 512
                for mt in range(8):
                    bank = mt % 2
                    for k in range(PT):
                        mm(bank, ws, (0, 128, k * 1024 + mt * 128, k * 1024 + mt * 128 + 128),
                           wsb, XN, (0, 128, k * T + c0, k * T + c0 + 512), XNb[k],
                           start=(k == 0), stop=(k == PT - 1))
                    E("act", lambda e, bank=bank, mt=mt, c0=c0: e.activation(
                        h16[:, mt * T + c0:mt * T + c0 + 512], pbank(bank), AF.Gelu,
                        bias=misc[:, o_bs1 + mt:o_bs1 + mt + 1]),
                      ins=[pbB[bank], miscB], outs=[h16b])
            wload(ws, wsb, "l2f_ws2", 0, PT, 1024)
            for ch in range(4):
                c0 = ch * 512
                for mt in range(8):
                    bank = mt % 2
                    for k in range(PT):
                        mm(bank, ws, (0, 128, k * 1024 + mt * 128, k * 1024 + mt * 128 + 128),
                           wsb, h16, (0, 128, k * T + c0, k * T + c0 + 512), h16b,
                           start=(k == 0), stop=(k == PT - 1))
                    E("act", lambda e, bank=bank, mt=mt: e.activation(
                        cvt[:], pbank(bank), AF.Identity, scale=1.0,
                        bias=misc[:, o_bs2 + mt:o_bs2 + mt + 1]),
                      ins=[pbB[bank], miscB], outs=[cvtb])
                    E("dve", lambda e, mt=mt, c0=c0: e.tensor_tensor(
                        X[:, mt * T + c0:mt * T + c0 + 512],
                        X[:, mt * T + c0:mt * T + c0 + 512], cvt[:], OP.add),
                      ins=[cvtb], outs=[Xb[mt]], hz=False)
            wr, wrb = A.alloc((128, PT * 31), BF16, "l2wr")
            affs, affsb = A.alloc((128, T), F32, "l2aff")
            ptm, ptmb = A.alloc((128, 31), F32, "l2ptm")
            gtm, gtmb = A.alloc((128, 31), F32, "l2gtm")
            s1, s1b = A.alloc((128, 8), F32, "l2s1")
            gfm, gfmb = A.alloc((128, 128), BF16, "l2gfm")
            E("dve", lambda e: e.memset(affs[:], 0.0), outs=[affsb], hz=False)
            wload(wr, wrb, "l2f_Wr", 0, PT, 31)
            for ch in range(4):
                c0 = ch * 512
                bank = ch % 2
                for k in range(PT):
                    mm(bank, wr, (0, 128, k * 31, k * 31 + 31), wrb,
                       XN, (0, 128, k * T + c0, k * T + c0 + 512), XNb[k],
                       start=(k == 0), stop=(k == PT - 1), mlen=31)
                E("act", lambda e, bank=bank, c0=c0: e.activation(
                    affs[0:31, c0:c0 + 512], pbank(bank)[0:31, :], AF.Identity,
                    scale=1.0, bias=misc[0:31, o_br:o_br + 1]),
                  ins=[pbB[bank], miscB], outs=[affsb])
            for tt in range(16):
                E("pe", lambda e, tt=tt: e.transpose(pbank(4)[0:128, 0:128],
                                                     affs[:, tt * 128:(tt + 1) * 128],
                                                     identf[:]),
                  ins=[affsb, CB], outs=[pbB[4]])
                E("dve", lambda e: e.reduce_max(s1[:, 0:1], pbank(4)[:, 0:31], axis=AX,
                                                negate=True),
                  ins=[pbB[4]], outs=[s1b], hz=True)
                E("act", lambda e: e.activation(ptm[:], pbank(4)[:, 0:31], AF.Exp,
                                                bias=s1[:, 0:1]),
                  ins=[pbB[4], s1b], outs=[ptmb])
                E("dve", lambda e: e.reduce_sum(s1[:, 1:2], ptm[:], axis=AX),
                  ins=[ptmb], outs=[s1b], hz=True)
                E("dve", lambda e: e.reciprocal(s1[:, 2:3], s1[:, 1:2]),
                  ins=[s1b], outs=[s1b], hz=True)
                E("dve", lambda e: e.tensor_single_scalar(ptm[:], ptm[:], s1[:, 2:3],
                                                          OP.mult),
                  ins=[s1b], outs=[ptmb])
                E("dve", lambda e: e.reduce_max(s1[:, 3:4], ptm[:], axis=AX),
                  ins=[ptmb], outs=[s1b], hz=True)
                E("dve", lambda e: e.tensor_single_scalar(gtm[:], ptm[:], s1[:, 3:4],
                                                          OP.is_ge),
                  ins=[ptmb, s1b], outs=[gtmb])
                E("dve", lambda e: e.scalar_tensor_tensor(gtm[:], gtm[:], -2.0, ptm[:],
                                                          OP.mult, OP.add),
                  ins=[ptmb], outs=[gtmb], hz=False)
                E("dve", lambda e: e.reduce_max(s1[:, 4:5], gtm[:], axis=AX),
                  ins=[gtmb], outs=[s1b], hz=True)
                E("dve", lambda e: e.tensor_single_scalar(cvt[:, 0:31], gtm[:], s1[:, 4:5],
                                                          OP.is_ge),
                  ins=[gtmb, s1b], outs=[cvtb])
                E("dve", lambda e: e.scalar_tensor_tensor(cvt[:, 0:31], cvt[:, 0:31], -2.0,
                                                          gtm[:], OP.mult, OP.add),
                  ins=[gtmb], outs=[cvtb], hz=False)
                E("dve", lambda e: e.reduce_max(s1[:, 5:6], cvt[:, 0:31], axis=AX),
                  ins=[cvtb], outs=[s1b], hz=True)
                E("dve", lambda e: e.tensor_single_scalar(gtm[:], ptm[:], s1[:, 5:6],
                                                          OP.is_ge),
                  ins=[ptmb, s1b], outs=[gtmb])
                E("dve", lambda e: e.tensor_tensor(gtm[:], gtm[:], ptm[:], OP.mult),
                  ins=[ptmb], outs=[gtmb])
                E("pe", lambda e, tt=tt: e.transpose(pbank(5)[0:31, 0:128], gtm[:, 0:31],
                                                     identf[:]),
                  ins=[gtmb, CB], outs=[pbB[5]])
                E("act", lambda e: e.copy(gfm[0:31, 0:128], pbank(5)[0:31, 0:128]),
                  ins=[pbB[5]], outs=[gfmb])
                E("sp", lambda e, tt=tt: e.dma_start(
                    out=gates_sp.ap()[:, tt * 128:(tt + 1) * 128], in_=gfm[0:31, 0:128]),
                  ins=[gfmb], outs=[gD], inc=16)
            A.release(m0)
            m0 = A.mark()
            cvt, cvtb = A.alloc((128, 512), F32, "l2cv2")
            he, heb = A.alloc((128, PT * T), BF16, "l2he")
            wv = [A.alloc((128, PT * 1024), BF16, f"l2we{i}") for i in range(2)]
            be1r, be1rb = A.alloc((128, 16), F32, "l2be1")
            be2r, be2rb = A.alloc((1, 1024), BF16, "l2be2")
            grow, growb = A.alloc((1, T), BF16, "l2grow")
            for e_ in range(31):
                wt, wtb = wv[e_ % 2]
                for i in range(PT):
                    E("gp", lambda e, e_=e_, i=i, wt=wt: e.dma_start(
                        out=wt[:, i * 1024:(i + 1) * 1024],
                        in_=di["l2f_We1"].ap()[e_, i * 128:(i + 1) * 128, :]),
                      outs=[wtb], inc=16)
                E("sp", lambda e, e_=e_: e.dma_start(
                    out=be1r[:, 0:8],
                    in_=AP(di["l2f_Be1"].ap().tensor, e_ * 1024, [[1, 128], [128, 8]]),
                    allow_slow_non_contiguous=True),
                  outs=[be1rb], inc=16)
                E("gp", lambda e, e_=e_: e.dma_start(
                    out=grow[:], in_=gates_sp.ap()[e_:e_ + 1, :]),
                  ins=[gD], outs=[growb], inc=16)
                for ch in range(4):
                    c0 = ch * 512
                    for mt in range(8):
                        bank = mt % 4
                        for k in range(PT):
                            mm(bank, wt, (0, 128, k * 1024 + mt * 128,
                                          k * 1024 + mt * 128 + 128), wtb,
                               XN, (0, 128, k * T + c0, k * T + c0 + 512), XNb[k],
                               start=(k == 0), stop=(k == PT - 1))
                        E("act", lambda e, bank=bank, mt=mt, c0=c0: e.activation(
                            he[:, mt * T + c0:mt * T + c0 + 512], pbank(bank), AF.Gelu,
                            bias=be1r[:, mt:mt + 1]),
                          ins=[pbB[bank], be1rb], outs=[heb])
                wt2, wt2b = wv[(e_ + 1) % 2]
                for i in range(PT):
                    E("gp", lambda e, e_=e_, i=i, wt2=wt2: e.dma_start(
                        out=wt2[:, i * 1024:(i + 1) * 1024],
                        in_=di["l2f_We2"].ap()[e_, i * 128:(i + 1) * 128, :]),
                      outs=[wt2b], inc=16)
                E("gp", lambda e, e_=e_: e.dma_start(
                    out=be2r[:], in_=di["l2f_Be2"].ap()[e_:e_ + 1, :]),
                  outs=[be2rb], inc=16)
                for ch in range(4):
                    c0 = ch * 512
                    mm(4, ones16, (0, 1, 0, 128), CB, grow, (0, 1, c0, c0 + 512), growb,
                       True, True)
                    for mt in range(8):
                        bank = 2 + mt % 2
                        for k in range(PT):
                            mm(bank, wt2, (0, 128, k * 1024 + mt * 128,
                                           k * 1024 + mt * 128 + 128), wt2b,
                               he, (0, 128, k * T + c0, k * T + c0 + 512), heb,
                               start=(k == 0), stop=False)
                        mm(bank, be2r, (0, 1, mt * 128, mt * 128 + 128), be2rb,
                           grow, (0, 1, c0, c0 + 512), growb, start=False, stop=True)
                        E("act", lambda e, bank=bank: e.copy(cvt[:], pbank(bank)),
                          ins=[pbB[bank]], outs=[cvtb])
                        E("dve", lambda e: e.tensor_tensor(cvt[:], cvt[:], pbank(4),
                                                           OP.mult),
                          ins=[cvtb, pbB[4]], outs=[cvtb], hz=False)
                        E("dve", lambda e, mt=mt, c0=c0: e.tensor_tensor(
                            X[:, mt * T + c0:mt * T + c0 + 512],
                            X[:, mt * T + c0:mt * T + c0 + 512], cvt[:], OP.add),
                          ins=[cvtb], outs=[Xb[mt]], hz=False)
            A.release(m0)


        def l3():
            o_g1 = load_vec("l3_g1", D); o_b1 = load_vec("l3_b1", D)
            o_gm = load_vec("l3_gm", D); o_bm = load_vec("l3_bm", D)
            o_bqa = load_vec("l3_bqkv_a", 3 * D)
            o_bqs = load_vec("l3_bqkv_s", 3 * D)
            o_bpa = load_vec("l3_bp_a", D)
            o_bos = load_vec("l3_bo_s", D)
            o_bo3 = load_vec("l3_bo", D)
            o_b1m = load_vec("l3_b1m", 4 * D)
            o_b2m = load_vec("l3_b2m", D)
            o_sc3 = mslot(4)
            o_bqs8 = mslot(8)
            E("dve", lambda e: e.tensor_scalar(misc[:, o_bqs8:o_bqs8 + 8],
                                               misc[:, o_bqs:o_bqs + 8], 0.125, None,
                                               OP.mult), ins=[miscB], outs=[miscB])

            mpre = A.mark()
            layer_norm(False)
            A.release(mpre)
            do_attn3 = not (N_LAYERS == 4 and MK_PARTS == "ffn")
            do_ffn3 = not (N_LAYERS == 4 and MK_PARTS == "attn")

            def ln_of(src, srcbl, dst, dstb, o_g, o_b):
                sq2, sq2b = A.alloc((128, 512), F32, "lnq")
                for ch in range(4):
                    c0 = ch * 512
                    for k in range(PT):
                        mm(0, ones16, (0, 128, 0, 1), CB, src,
                           (0, 128, k * T + c0, k * T + c0 + 512), srcbl[k],
                           start=(k == 0), stop=(k == PT - 1), mlen=1)
                    for k in range(PT):
                        E("act", lambda e, k=k: e.activation(
                            sq2[:], src[:, k * T + c0:k * T + c0 + 512], AF.Square),
                          ins=[srcbl[k]], outs=[sq2b])
                        mm(1, onesf, (0, 128, 0, 1), CB, sq2, (0, 128, 0, 512), sq2b,
                           start=(k == 0), stop=(k == PT - 1), mlen=1)
                    E("dve", lambda e: e.tensor_scalar(misc2[0:1, 0:512], pbank(0)[0:1, :],
                                                       1.0 / D, None, OP.mult),
                      ins=[pbB[0]], outs=[misc2B])
                    mm(2, onesf, (0, 1, 0, 128), CB, misc2, (0, 1, 0, 512), misc2B,
                       True, True)
                    E("dve", lambda e: e.tensor_tensor(misc2[0:1, 0:512], misc2[0:1, 0:512],
                                                       misc2[0:1, 0:512], OP.mult),
                      ins=[misc2B], outs=[misc2B])
                    E("dve", lambda e: e.scalar_tensor_tensor(
                        misc2[0:1, 0:512], pbank(1)[0:1, :], 1.0 / D, misc2[0:1, 0:512],
                        OP.mult, OP.subtract), ins=[pbB[1], misc2B], outs=[misc2B])
                    E("act", lambda e: e.activation(misc2[0:1, 0:512], misc2[0:1, 0:512],
                                                    AF.Sqrt, bias=1e-5),
                      ins=[misc2B], outs=[misc2B])
                    E("dve", lambda e: e.reciprocal(misc2[0:1, 0:512], misc2[0:1, 0:512]),
                      ins=[misc2B], outs=[misc2B], hz=True)
                    mm(3, onesf, (0, 1, 0, 128), CB, misc2, (0, 1, 0, 512), misc2B,
                       True, True)
                    for k in range(PT):
                        E("dve", lambda e, k=k: e.tensor_tensor(
                            sq2[:], src[:, k * T + c0:k * T + c0 + 512], pbank(2),
                            OP.subtract), ins=[srcbl[k], pbB[2]], outs=[sq2b], hz=False)
                        E("dve", lambda e: e.tensor_tensor(sq2[:], sq2[:], pbank(3),
                                                           OP.mult),
                          ins=[sq2b, pbB[3]], outs=[sq2b], hz=False)
                        E("dve", lambda e, k=k: e.tensor_single_scalar(
                            sq2[:], sq2[:], misc[:, o_g + k:o_g + k + 1], OP.mult),
                          ins=[sq2b, miscB], outs=[sq2b], hz=False)
                        E("dve", lambda e, k=k: e.tensor_single_scalar(
                            dst[:, k * T + c0:k * T + c0 + 512], sq2[:],
                            misc[:, o_b + k:o_b + k + 1], OP.add),
                          ins=[sq2b, miscB], outs=[dstb], hz=False)

            if do_attn3:
                m0 = A.mark()
                xni, xnib = A.alloc((128, PT * T), BF16, "xni")
                ln_of(XN, XNb, xni, [xnib] if False else xnib, o_g1, o_b1)
                m_ao = A.mark()
                ao, aob = A.alloc((128, PT * T), BF16, "l3ao")
                E("gp", lambda e: e.dma_start(
                    out=bv3[:], in_=di["l3_bqkv_a"].ap()
                    .rearrange("(a b) -> a b", a=3)[2:3, :]),
                  outs=[bv3b], inc=16)
                msw = A.mark()
                q16, q16b = A.alloc((128, T), BF16, "swq")
                k16, k16b = A.alloc((128, T), BF16, "swk")
                vth, vthb = A.alloc((128, 16 * 65), BF16, "swv")
                ws3, ws3b = A.alloc((128, 8 * 192), BF16, "sww")
                rst, rstb = A.alloc((128, 512), F32, "swr")
                ps3 = [A.alloc((128, 128), BF16, f"swp{i}") for i in range(2)]
                nr3, nr3b = A.alloc((128, 130), BF16, "swn")
                for h in range(16):
                    for j, wnm in enumerate(("l3_Wqkv_a",)):
                        for kk in range(PT):
                            E("gp", lambda e, kk=kk, h=h: e.dma_start(
                                out=ws3[:, kk * 192:kk * 192 + 64],
                                in_=di["l3_Wqkv_a"].ap()[kk * 128:(kk + 1) * 128,
                                                         h * 64:h * 64 + 64]),
                              outs=[ws3b], inc=16)
                            E("gp", lambda e, kk=kk, h=h: e.dma_start(
                                out=ws3[:, kk * 192 + 64:kk * 192 + 128],
                                in_=di["l3_Wqkv_a"].ap()[kk * 128:(kk + 1) * 128,
                                                         1024 + h * 64:1024 + h * 64 + 64]),
                              outs=[ws3b], inc=16)
                            E("gp", lambda e, kk=kk, h=h: e.dma_start(
                                out=ws3[:, kk * 192 + 128:kk * 192 + 192],
                                in_=di["l3_Wqkv_a"].ap()[kk * 128:(kk + 1) * 128,
                                                         2048 + h * 64:2048 + h * 64 + 64]),
                              outs=[ws3b], inc=16)
                    for which, dst3, dstb3, bofs in ((0, q16, q16b, o_bqa),
                                                     (1, k16, k16b, o_bqa + 8)):
                        for ch in range(4):
                            c0 = ch * 512
                            bank = ch % 2
                            for kk in range(PT):
                                mm(bank, ws3, (0, 128, kk * 192 + which * 64,
                                               kk * 192 + which * 64 + 64), ws3b,
                                   xni, (0, 128, kk * T + c0, kk * T + c0 + 512), xnib,
                                   start=(kk == 0), stop=(kk == PT - 1), mlen=64)
                            bc = bofs + h // 2
                            hb2 = (h % 2) * 64
                            E("act", lambda e, bank=bank, bc=bc, hb2=hb2, dst3=dst3, c0=c0:
                              e.activation(dst3[0:64, c0:c0 + 512], pbank(bank)[0:64, :],
                                           AF.Identity, scale=1.0,
                                           bias=misc[hb2:hb2 + 64, bc:bc + 1]),
                              ins=[pbB[bank], miscB], outs=[dstb3])
                    E("dve", lambda e: e.memset(vth[:], 1.0), outs=[vthb], hz=False)
                    for tt in range(16):
                        bank = tt % 2 + 2
                        for kk in range(PT):
                            mm(bank, xni, (0, 128, kk * T + tt * 128, kk * T + tt * 128 + 128),
                               xnib, ws3, (0, 128, kk * 192 + 128, kk * 192 + 192), ws3b,
                               start=(kk == 0), stop=False, nlen=64)
                        mm(bank, ones16, (0, 1, 0, 128), CB, bv3,
                           (0, 1, h * 64, h * 64 + 64), bv3b, start=False, stop=True,
                           nlen=64)
                        E("act", lambda e, bank=bank, tt=tt: e.copy(
                            vth[:, tt * 65:tt * 65 + 64], pbank(bank, 0, 64)),
                          ins=[pbB[bank]], outs=[vthb])
                    E("pe", lambda e: e.matmul(
                        pbank(6)[0:128, 0:128],
                        AP(k16.ap().tensor, 0, [[T, 64], [16, 128]]),
                        AP(q16.ap().tensor, 0, [[T, 64], [16, 128]]),
                        start=True, stop=True), ins=[k16b, q16b], outs=[pbB[6]])
                    E("dve", lambda e: e.reduce_max(misc2[:, 28:29], pbank(6)[:, 0:128],
                                                    axis=AX),
                      ins=[pbB[6]], outs=[misc2B], hz=True)
                    E("sp", lambda e: e.dma_start(
                        out=flip_sp.ap()[0:1, :].rearrange("a b -> (a b)"),
                        in_=misc2[:, 28:29]), ins=[misc2B], outs=[flipD], inc=16)
                    E("sp", lambda e: e.dma_start(out=misc2[0:1, 32:160],
                                                  in_=flip_sp.ap()[0:1, :]),
                      ins=[flipD], outs=[misc2B], inc=16)
                    E("dve", lambda e: e.reduce_max(misc2[0:1, 30:31], misc2[0:1, 32:160],
                                                    axis=AX),
                      ins=[misc2B], outs=[misc2B], hz=True)
                    E("dve", lambda e: e.tensor_scalar(misc2[0:1, 30:31], misc2[0:1, 30:31],
                                                       -0.125, None, OP.mult),
                      ins=[misc2B], outs=[misc2B])
                    mm(6, onesf, (0, 1, 0, 128), CB, misc2, (0, 1, 30, 31), misc2B,
                       True, True, nlen=1)
                    E("act", lambda e: e.copy(misc[:, o_sc3:o_sc3 + 1], pbank(6, 0, 1)),
                      ins=[pbB[6]], outs=[miscB])
                    for n in range(16):
                        kts = ([n] if n == 0 else [n - 1, n])
                        nk = len(kts)
                        for j, ktn in enumerate(kts):
                            bank = j % 2
                            mm(bank, k16, (0, 64, ktn * 128, ktn * 128 + 128), k16b,
                               q16, (0, 64, n * 128, n * 128 + 128), q16b,
                               start=True, stop=True, nlen=128)
                            moff = 128 if (nk == 1 or j == 1) else 0
                            E("dve", lambda e, bank=bank, moff=moff: e.tensor_tensor(
                                pbank(bank, 0, 128), pbank(bank, 0, 128),
                                swamask[:, moff:moff + 128], OP.add),
                              ins=[CB], outs=[pbB[bank]], hz=False)
                            ps, psb = ps3[j]
                            E("act", lambda e, bank=bank, ps=ps: e.activation(
                                ps[:], pbank(bank, 0, 128), AF.Exp, scale=0.125,
                                bias=misc[:, o_sc3:o_sc3 + 1]),
                              ins=[pbB[bank], miscB], outs=[psb])
                            mm(4, vth, (0, 128, ktn * 65, ktn * 65 + 65), vthb,
                               ps, (0, 128, 0, 128), psb,
                               start=(j == 0), stop=(j == nk - 1), mlen=65, nlen=128)
                        E("dve", lambda e: e.tensor_scalar(rst[64:65, 0:128],
                                                           pbank(4)[64:65, 0:128],
                                                           1e-30, None, OP.add),
                          ins=[pbB[4]], outs=[rstb])
                        E("dve", lambda e: e.reciprocal(rst[64:65, 128:256],
                                                        rst[64:65, 0:128]),
                          ins=[rstb], outs=[rstb], hz=True)
                        E("dve", lambda e: e.tensor_copy(nr3[64:65, 0:128],
                                                         rst[64:65, 128:256]),
                          ins=[rstb], outs=[nr3b], hz=True)
                        mm(5, ones16, (64, 65, 0, 64), CB, nr3, (64, 65, 0, 128), nr3b,
                           True, True, mlen=64, nlen=128)
                        E("act", lambda e: e.copy(rst[0:64, 0:128], pbank(4)[0:64, 0:128]),
                          ins=[pbB[4]], outs=[rstb])
                        E("dve", lambda e, h=h, n=n: e.tensor_tensor(
                            ao[(h % 2) * 64:(h % 2) * 64 + 64,
                               (h // 2) * T + n * 128:(h // 2) * T + n * 128 + 128],
                            rst[0:64, 0:128], pbank(5)[0:64, 0:128], OP.mult),
                          ins=[rstb, pbB[5]], outs=[aob], hz=False)
                A.release(msw)
                wsp, wspb = A.alloc((128, PT * 1024), BF16, "wsp")
                ev3, ev3b = A.alloc((128, 512), BF16, "ev3")
                wload(wsp, wspb, "l3_Wp_a", 0, PT, 1024)
                for ch in range(4):
                    c0 = ch * 512
                    for mt in range(8):
                        bank = mt % 2
                        for kk in range(PT):
                            mm(bank, wsp, (0, 128, kk * 1024 + mt * 128,
                                           kk * 1024 + mt * 128 + 128), wspb,
                               ao, (0, 128, kk * T + c0, kk * T + c0 + 512), aob,
                               start=(kk == 0), stop=(kk == PT - 1))
                        E("act", lambda e, bank=bank, mt=mt: e.activation(
                            ev3[:], pbank(bank), AF.Identity, scale=1.0,
                            bias=misc[:, o_bpa + mt:o_bpa + mt + 1]),
                          ins=[pbB[bank], miscB], outs=[ev3b])
                        E("sp", lambda e, mt=mt, c0=c0: e.dma_start(
                            out=swp_sp.ap()[mt * 128:(mt + 1) * 128, c0:c0 + 512],
                            in_=ev3[:]), ins=[ev3b], outs=[swpD], inc=16)
                A.release(m_ao)
                ao2, ao2b = A.alloc((128, PT * T), BF16, "l3ao2")
                E("gp", lambda e: e.dma_start(
                    out=bv3[:], in_=di["l3_bqkv_s"].ap()
                    .rearrange("(a b) -> a b", a=3)[2:3, :]),
                  outs=[bv3b], inc=16)
                mln = A.mark()
                q16, q16b = A.alloc((128, T), BF16, "liq")
                k16, k16b = A.alloc((128, T), BF16, "lik")
                k32, k32b = A.alloc((128, 512), F32, "lik32")
                ktm, ktmb = A.alloc((128, 64), BF16, "liktm")
                vth, vthb = A.alloc((128, 16 * 65), BF16, "livt")
                ws3b2 = ws3, ws3b = A.alloc((128, 8 * 192), BF16, "liw")
                rst, rstb = A.alloc((128, 512), F32, "lir")
                el1, el1b = A.alloc((128, 512), F32, "liel")
                Ssb, Ssbb = A.alloc((128, 66), BF16, "liS")
                Sf, Sfb = A.alloc((128, 66), F32, "liSf")
                at3, at3b = A.alloc((128, 128), BF16, "liA")
                nr3, nr3b = A.alloc((128, 130), BF16, "lin")
                for h in range(16):
                    for kk in range(PT):
                        for j in range(3):
                            E("gp", lambda e, kk=kk, h=h, j=j: e.dma_start(
                                out=ws3[:, kk * 192 + j * 64:kk * 192 + j * 64 + 64],
                                in_=di["l3_Wqkv_s"].ap()[kk * 128:(kk + 1) * 128,
                                                         j * 1024 + h * 64:j * 1024 + h * 64 + 64]),
                              outs=[ws3b], inc=16)
                    for which, dst3, dstb3, bofs, sc in ((0, q16, q16b, o_bqs8, 0.125),
                                                         (1, k16, k16b, o_bqs + 8, 1.0)):
                        for ch in range(4):
                            c0 = ch * 512
                            bank = ch % 2
                            for kk in range(PT):
                                mm(bank, ws3, (0, 128, kk * 192 + which * 64,
                                               kk * 192 + which * 64 + 64), ws3b,
                                   xni, (0, 128, kk * T + c0, kk * T + c0 + 512), xnib,
                                   start=(kk == 0), stop=(kk == PT - 1), mlen=64)
                            bc = bofs + h // 2
                            hb2 = (h % 2) * 64
                            E("act", lambda e, bank=bank, bc=bc, hb2=hb2, sc=sc:
                              e.activation(rst[0:64, :], pbank(bank)[0:64, :],
                                           AF.Identity, scale=sc,
                                           bias=misc[hb2:hb2 + 64, bc:bc + 1]),
                              ins=[pbB[bank], miscB], outs=[rstb])
                            E("dve", lambda e: e.tensor_scalar(el1[0:64, :], rst[0:64, :],
                                                               0.0, None, OP.min),
                              ins=[rstb], outs=[el1b], hz=False)
                            E("act", lambda e: e.activation(el1[0:64, :], el1[0:64, :],
                                                            AF.Exp),
                              ins=[el1b], outs=[el1b])
                            E("dve", lambda e: e.scalar_tensor_tensor(
                                el1[0:64, :], rst[0:64, :], 0.0, el1[0:64, :],
                                OP.max, OP.add), ins=[rstb, el1b], outs=[el1b], hz=False)
                            E("dve", lambda e, dst3=dst3, c0=c0: e.tensor_scalar(
                                dst3[0:64, c0:c0 + 512], el1[0:64, :], 1e-4, None,
                                OP.add), ins=[el1b], outs=[dstb3], hz=False)
                    E("dve", lambda e: e.memset(vth[:], 1.0), outs=[vthb], hz=False)
                    for tt in range(16):
                        bank = tt % 2 + 2
                        for kk in range(PT):
                            mm(bank, xni, (0, 128, kk * T + tt * 128, kk * T + tt * 128 + 128),
                               xnib, ws3, (0, 128, kk * 192 + 128, kk * 192 + 192), ws3b,
                               start=(kk == 0), stop=False, nlen=64)
                        mm(bank, ones16, (0, 1, 0, 128), CB, bv3,
                           (0, 1, h * 64, h * 64 + 64), bv3b, start=False, stop=True,
                           nlen=64)
                        E("act", lambda e, bank=bank, tt=tt: e.copy(
                            vth[:, tt * 65:tt * 65 + 64], pbank(bank, 0, 64)),
                          ins=[pbB[bank]], outs=[vthb])
                    E("dve", lambda e: e.memset(Sf[:], 0.0), outs=[Sfb], hz=False)
                    E("dve", lambda e: e.memset(Ssb[:], 0.0), outs=[Ssbb], hz=False)
                    for n in range(16):
                        mm(0, k16, (0, 64, n * 128, n * 128 + 128), k16b,
                           q16, (0, 64, n * 128, n * 128 + 128), q16b, True, True,
                           nlen=128)
                        E("dve", lambda e: e.tensor_tensor(at3[:, 0:128], pbank(0, 0, 128),
                                                           trilT16[:], OP.mult),
                          ins=[pbB[0], CB], outs=[at3b], hz=False)
                        mm(1, Ssb, (0, 64, 0, 65), Ssbb,
                           q16, (0, 64, n * 128, n * 128 + 128), q16b,
                           start=True, stop=False, mlen=65, nlen=128)
                        mm(1, vth, (0, 128, n * 65, n * 65 + 65), vthb,
                           at3, (0, 128, 0, 128), at3b, start=False, stop=True,
                           mlen=65, nlen=128)
                        E("dve", lambda e: e.tensor_scalar(rst[64:65, 0:128],
                                                           pbank(1)[64:65, 0:128],
                                                           1e-6, None, OP.add),
                          ins=[pbB[1]], outs=[rstb])
                        E("dve", lambda e: e.reciprocal(rst[64:65, 128:256],
                                                        rst[64:65, 0:128]),
                          ins=[rstb], outs=[rstb], hz=True)
                        E("dve", lambda e: e.tensor_copy(nr3[64:65, 0:128],
                                                         rst[64:65, 128:256]),
                          ins=[rstb], outs=[nr3b], hz=True)
                        mm(5, ones16, (64, 65, 0, 64), CB, nr3, (64, 65, 0, 128), nr3b,
                           True, True, mlen=64, nlen=128)
                        E("act", lambda e: e.copy(rst[0:64, 0:128], pbank(1)[0:64, 0:128]),
                          ins=[pbB[1]], outs=[rstb])
                        E("dve", lambda e, h=h, n=n: e.tensor_tensor(
                            ao2[(h % 2) * 64:(h % 2) * 64 + 64,
                                (h // 2) * T + n * 128:(h // 2) * T + n * 128 + 128],
                            rst[0:64, 0:128], pbank(5)[0:64, 0:128], OP.mult),
                          ins=[rstb, pbB[5]], outs=[ao2b], hz=False)
                        E("act", lambda e, n=n: e.copy(k32[0:64, 0:128],
                                                       k16[0:64, n * 128:n * 128 + 128]),
                          ins=[k16b], outs=[k32b])
                        E("pe", lambda e: e.transpose(pbank(6)[0:128, 0:64],
                                                      k32[0:64, 0:128], identf[0:64, 0:64]),
                          ins=[k32b, CB], outs=[pbB[6]])
                        E("act", lambda e: e.copy(ktm[:, 0:64], pbank(6)[:, 0:64]),
                          ins=[pbB[6]], outs=[ktmb])
                        mm(7, ktm, (0, 128, 0, 64), ktmb, vth,
                           (0, 128, n * 65, n * 65 + 65), vthb, True, True,
                           mlen=64, nlen=65)
                        E("dve", lambda e: e.tensor_tensor(Sf[0:64, 0:65], Sf[0:64, 0:65],
                                                           pbank(7, 0, 65)[0:64, :],
                                                           OP.add),
                          ins=[pbB[7]], outs=[Sfb], hz=False)
                        E("dve", lambda e: e.tensor_copy(Ssb[0:64, 0:65], Sf[0:64, 0:65]),
                          ins=[Sfb], outs=[Ssbb], hz=False)
                A.release(mln)
                wsp, wspb = A.alloc((128, PT * 1024), BF16, "wos")
                ev3, ev3b = A.alloc((128, 512), F32, "ev3b")
                sw16, sw16b = A.alloc((128, 512), BF16, "sw16")
                mx16, mx16b = A.alloc((128, 512), BF16, "mx16")
                wload(wsp, wspb, "l3_Wo_s", 0, PT, 1024)
                for ch in range(4):
                    c0 = ch * 512
                    for mt in range(8):
                        bank = mt % 2
                        for kk in range(PT):
                            mm(bank, wsp, (0, 128, kk * 1024 + mt * 128,
                                           kk * 1024 + mt * 128 + 128), wspb,
                               ao2, (0, 128, kk * T + c0, kk * T + c0 + 512), ao2b,
                               start=(kk == 0), stop=(kk == PT - 1))
                        E("act", lambda e, bank=bank, mt=mt: e.activation(
                            ev3[:], pbank(bank), AF.Identity, scale=1.0,
                            bias=misc[:, o_bos + mt:o_bos + mt + 1]),
                          ins=[pbB[bank], miscB], outs=[ev3b])
                        E("sp", lambda e, mt=mt, c0=c0, sw16=sw16: e.dma_start(
                            out=sw16[:], in_=swp_sp.ap()[mt * 128:(mt + 1) * 128,
                                                         c0:c0 + 512]),
                          ins=[swpD], outs=[sw16b], inc=16)
                        E("dve", lambda e: e.tensor_tensor(mx16[:], ev3[:], sw16[:],
                                                           OP.add),
                          ins=[ev3b, sw16b], outs=[mx16b], hz=False)
                        E("sp", lambda e, mt=mt, c0=c0: e.dma_start(
                            out=mix_sp.ap()[mt * 128:(mt + 1) * 128, c0:c0 + 512],
                            in_=mx16[:]), ins=[mx16b], outs=[mixD], inc=16)
                A.release(m0)
                mx1 = A.mark()
                x1, x1b = A.alloc((128, PT * T), BF16, "l3x1")
                m0 = A.mark()
                wsp, wspb = A.alloc((128, PT * 1024), BF16, "wo3")
                mxr, mxrb = A.alloc((128, 8 * 512), BF16, "mxr")
                ev3, ev3b = A.alloc((128, 512), F32, "ev3c")
                wload(wsp, wspb, "l3_Wo", 0, PT, 1024)
                for ch in range(4):
                    c0 = ch * 512
                    for kk in range(PT):
                        E("gp", lambda e, kk=kk, c0=c0: e.dma_start(
                            out=mxr[:, kk * 512:(kk + 1) * 512],
                            in_=mix_sp.ap()[kk * 128:(kk + 1) * 128, c0:c0 + 512]),
                          ins=[mixD], outs=[mxrb], inc=16)
                    for mt in range(8):
                        bank = mt % 2
                        for kk in range(PT):
                            mm(bank, wsp, (0, 128, kk * 1024 + mt * 128,
                                           kk * 1024 + mt * 128 + 128), wspb,
                               mxr, (0, 128, kk * 512, kk * 512 + 512), mxrb,
                               start=(kk == 0), stop=(kk == PT - 1))
                        E("act", lambda e, bank=bank, mt=mt: e.activation(
                            ev3[:], pbank(bank), AF.Identity, scale=1.0,
                            bias=misc[:, o_bo3 + mt:o_bo3 + mt + 1]),
                          ins=[pbB[bank], miscB], outs=[ev3b])
                        E("dve", lambda e, mt=mt, c0=c0: e.tensor_tensor(
                            x1[:, mt * T + c0:mt * T + c0 + 512], ev3[:],
                            XN[:, mt * T + c0:mt * T + c0 + 512], OP.add),
                          ins=[ev3b, XNb[mt]], outs=[x1b], hz=False)
                        E("dve", lambda e, mt=mt, c0=c0: e.tensor_tensor(
                            X[:, mt * T + c0:mt * T + c0 + 512],
                            X[:, mt * T + c0:mt * T + c0 + 512],
                            x1[:, mt * T + c0:mt * T + c0 + 512], OP.add),
                          ins=[x1b], outs=[Xb[mt]], hz=False)
                A.release(m0)
                ln_of(x1, [x1b] * PT, x1, x1b, o_gm, o_bm)
                h2, h2b = x1, x1b
                m0 = A.mark()
                wsp, wspb = A.alloc((128, PT * 1024), BF16, "w13")
                hstg3, hstg3b = A.alloc((128, 512), BF16, "hst3")
                for half in range(4):
                    wload(wsp, wspb, "l3_W1", 0, PT, 1024, col0=half * 1024, ncols=1024)
                    for ch in range(4):
                        c0 = ch * 512
                        for mt in range(8):
                            dglob = half * 8 + mt
                            bank = mt % 2
                            for kk in range(PT):
                                mm(bank, wsp, (0, 128, kk * 1024 + mt * 128,
                                               kk * 1024 + mt * 128 + 128), wspb,
                                   h2, (0, 128, kk * T + c0, kk * T + c0 + 512), h2b,
                                   start=(kk == 0), stop=(kk == PT - 1))
                            E("act", lambda e, bank=bank, dglob=dglob: e.activation(
                                hstg3[:], pbank(bank), AF.Gelu,
                                bias=misc[:, o_b1m + dglob:o_b1m + dglob + 1]),
                              ins=[pbB[bank], miscB], outs=[hstg3b])
                            E("sp", lambda e, dglob=dglob, c0=c0: e.dma_start(
                                out=h2_sp.ap()[dglob * 128:(dglob + 1) * 128, c0:c0 + 512],
                                in_=hstg3[:]), ins=[hstg3b], outs=[h2D], inc=16)
                A.release(m0)
                m0 = A.mark()
                wsp, wspb = A.alloc((128, 16 * 1024), BF16, "w23")
                hr, hrb = A.alloc((128, 16 * 512), BF16, "hr3")
                ev3, ev3b = A.alloc((128, 512), F32, "ev3d")
                for half in range(2):
                    wload(wsp, wspb, "l3_W2", half * 16, 16, 1024)
                    for ch in range(4):
                        c0 = ch * 512
                        for kk in range(16):
                            E("sp", lambda e, kk=kk, half=half, c0=c0: e.dma_start(
                                out=hr[:, kk * 512:(kk + 1) * 512],
                                in_=h2_sp.ap()[(half * 16 + kk) * 128:
                                               (half * 16 + kk + 1) * 128, c0:c0 + 512]),
                              ins=[h2D], outs=[hrb], inc=16)
                        for mt in range(8):
                            bank = mt % 2
                            for kk in range(16):
                                mm(bank, wsp, (0, 128, kk * 1024 + mt * 128,
                                               kk * 1024 + mt * 128 + 128), wspb,
                                   hr, (0, 128, kk * 512, kk * 512 + 512), hrb,
                                   start=(kk == 0), stop=(kk == 15))
                            if half == 0:
                                E("act", lambda e, bank=bank, mt=mt: e.activation(
                                    ev3[:], pbank(bank), AF.Identity, scale=1.0,
                                    bias=misc[:, o_b2m + mt:o_b2m + mt + 1]),
                                  ins=[pbB[bank], miscB], outs=[ev3b])
                                E("dve", lambda e, mt=mt, c0=c0: e.tensor_tensor(
                                    X[:, mt * T + c0:mt * T + c0 + 512],
                                    X[:, mt * T + c0:mt * T + c0 + 512], ev3[:], OP.add),
                                  ins=[ev3b], outs=[Xb[mt]], hz=False)
                            else:
                                x_add_psum(bank, mt, c0)
                A.release(mx1)

            if not do_ffn3:
                return
            m0 = A.mark()
            wg, wgb = A.alloc((128, PT * 8), BF16, "l3wg")
            ptm3, ptm3b = A.alloc((128, 16 * 8), F32, "l3ptm")
            s3, s3b = A.alloc((128, 16), F32, "l3s")
            pfm, pfmb = A.alloc((128, 128), F32, "l3pfm")
            wload(wg, wgb, "l3f_Wg", 0, PT, 8)
            for tt in range(16):
                bank = tt % 2
                for kk in range(PT):
                    mm(bank, XN, (0, 128, kk * T + tt * 128, kk * T + tt * 128 + 128),
                       XNb[kk], wg, (0, 128, kk * 8, kk * 8 + 8), wgb,
                       start=(kk == 0), stop=(kk == PT - 1), nlen=8)
                E("dve", lambda e, bank=bank: e.reduce_max(s3[:, 0:1], pbank(bank)[:, 0:8],
                                                           axis=AX, negate=True),
                  ins=[pbB[bank]], outs=[s3b], hz=True)
                E("act", lambda e, bank=bank, tt=tt: e.activation(
                    ptm3[:, tt * 8:tt * 8 + 8], pbank(bank)[:, 0:8], AF.Exp,
                    bias=s3[:, 0:1]), ins=[pbB[bank], s3b], outs=[ptm3b])
                E("dve", lambda e, tt=tt: e.reduce_sum(s3[:, 1:2],
                                                       ptm3[:, tt * 8:tt * 8 + 8],
                                                       axis=AX),
                  ins=[ptm3b], outs=[s3b], hz=True)
                E("dve", lambda e: e.reciprocal(s3[:, 2:3], s3[:, 1:2]),
                  ins=[s3b], outs=[s3b], hz=True)
                E("dve", lambda e, tt=tt: e.tensor_single_scalar(
                    ptm3[:, tt * 8:tt * 8 + 8], ptm3[:, tt * 8:tt * 8 + 8], s3[:, 2:3],
                    OP.mult), ins=[s3b], outs=[ptm3b])
                E("pe", lambda e, tt=tt: e.transpose(pbank(4)[0:8, 0:128],
                                                     ptm3[:, tt * 8:tt * 8 + 8],
                                                     identf[:]),
                  ins=[ptm3b, CB], outs=[pbB[4]])
                E("act", lambda e: e.copy(pfm[0:8, 0:128], pbank(4)[0:8, 0:128]),
                  ins=[pbB[4]], outs=[pfmb])
                E("sp", lambda e, tt=tt: e.dma_start(
                    out=ag_in.ap()[:, tt * 128:(tt + 1) * 128], in_=pfm[0:8, 0:128]),
                  ins=[pfmb], outs=[agD], inc=16)
            E("gp", lambda e: e.collective_compute(
                "AllGather", OP.bypass, replica_groups=[list(range(8))],
                ins=[ag_in.ap().opt()], outs=[ag_out.ap().opt()]),
              ins=[agD], outs=[agD2], isasync=True)
            pall, pallb = A.alloc((128, 1024), F32, "l3pall")
            for hh in range(2):
                E("sp", lambda e, hh=hh: e.dma_start(
                    out=pall[hh * 64:(hh + 1) * 64, :],
                    in_=AP(ag_out.ap().tensor, hh * 1024,
                           [[T, 8], [8 * T, 8], [1, 1024]])),
                  ins=[agD2], outs=[pallb], inc=16)
            lo_c, hi_c, mid_c, cnt_c, prd_c = 4, 5, 6, 7, 8
            E("dve", lambda e: e.memset(s3[:, lo_c:lo_c + 1], 0.0), outs=[s3b], hz=False)
            E("dve", lambda e: e.memset(s3[:, hi_c:hi_c + 1], 1.0), outs=[s3b], hz=False)
            cm, cmb = A.alloc((128, 1024), F32, "l3cm")
            for it in range(48):
                E("dve", lambda e: e.scalar_tensor_tensor(
                    s3[:, mid_c:mid_c + 1], s3[:, lo_c:lo_c + 1], 0.5,
                    s3[:, hi_c:hi_c + 1], OP.add, OP.bypass) if False else
                    e.tensor_tensor(s3[:, mid_c:mid_c + 1], s3[:, lo_c:lo_c + 1],
                                    s3[:, hi_c:hi_c + 1], OP.add),
                  ins=[s3b], outs=[s3b])
                E("dve", lambda e: e.tensor_scalar(s3[:, mid_c:mid_c + 1],
                                                   s3[:, mid_c:mid_c + 1], 0.5, None,
                                                   OP.mult), ins=[s3b], outs=[s3b])
                E("dve", lambda e: e.tensor_single_scalar(cm[:], pall[:],
                                                          s3[:, mid_c:mid_c + 1],
                                                          OP.is_ge),
                  ins=[pallb, s3b], outs=[cmb])
                E("dve", lambda e: e.reduce_sum(s3[:, cnt_c:cnt_c + 1], cm[:], axis=AX),
                  ins=[cmb], outs=[s3b], hz=True)
                mm(6, blk16, (0, 128, 0, 8), CB, s3, (0, 128, cnt_c, cnt_c + 1), s3b,
                   True, True, mlen=8, nlen=1)
                E("act", lambda e: e.copy(s3[0:8, 15:16], pbank(6, 0, 1)[0:8, :]),
                  ins=[pbB[6]], outs=[s3b])
                mm(7, blk8, (0, 8, 0, 128), CB, s3, (0, 8, 15, 16), s3b,
                   True, True, mlen=128, nlen=1)
                E("dve", lambda e: e.tensor_scalar(s3[:, cnt_c:cnt_c + 1],
                                                   pbank(7, 0, 1), 2048.0, None,
                                                   OP.is_ge),
                  ins=[pbB[7]], outs=[s3b])
                E("dve", lambda e: e.tensor_tensor(s3[:, prd_c:prd_c + 1],
                                                   s3[:, mid_c:mid_c + 1],
                                                   s3[:, lo_c:lo_c + 1], OP.subtract),
                  ins=[s3b], outs=[s3b])
                E("dve", lambda e: e.tensor_tensor(s3[:, prd_c:prd_c + 1],
                                                   s3[:, prd_c:prd_c + 1],
                                                   s3[:, cnt_c:cnt_c + 1], OP.mult),
                  ins=[s3b], outs=[s3b])
                E("dve", lambda e: e.tensor_tensor(s3[:, lo_c:lo_c + 1],
                                                   s3[:, lo_c:lo_c + 1],
                                                   s3[:, prd_c:prd_c + 1], OP.add),
                  ins=[s3b], outs=[s3b])
                E("dve", lambda e: e.tensor_scalar(s3[:, 9:10], s3[:, cnt_c:cnt_c + 1],
                                                   -1.0, 1.0, OP.mult, OP.add),
                  ins=[s3b], outs=[s3b])
                E("dve", lambda e: e.tensor_tensor(s3[:, prd_c:prd_c + 1],
                                                   s3[:, hi_c:hi_c + 1],
                                                   s3[:, mid_c:mid_c + 1], OP.subtract),
                  ins=[s3b], outs=[s3b])
                E("dve", lambda e: e.tensor_tensor(s3[:, prd_c:prd_c + 1],
                                                   s3[:, prd_c:prd_c + 1],
                                                   s3[:, 9:10], OP.mult),
                  ins=[s3b], outs=[s3b])
                E("dve", lambda e: e.tensor_tensor(s3[:, hi_c:hi_c + 1],
                                                   s3[:, hi_c:hi_c + 1],
                                                   s3[:, prd_c:prd_c + 1], OP.subtract),
                  ins=[s3b], outs=[s3b])
            if os.environ.get("MK_SCANDBG", "") == "th":
                E("sp", lambda e: e.dma_start(
                    out=flip_sp.ap()[0:1, :].rearrange("a b -> (a b)"),
                    in_=s3[:, lo_c:lo_c + 1]), ins=[s3b], outs=[flipD], inc=16)
                E("sp", lambda e: e.dma_start(out=dbg_d.ap()[0:1, 0:64],
                                              in_=flip_sp.ap()[0:1, 0:64]),
                  ins=[flipD], inc=16)
            mm(6, blk16, (0, 128, 0, 8), CB, s3, (0, 128, lo_c, lo_c + 1), s3b,
               True, True, mlen=8, nlen=1)
            E("act", lambda e: e.activation(pfm[0:8, 0:1], pbank(6, 0, 1)[0:8, :],
                                            AF.Identity, scale=1.0 / 16.0, bias=0.0),
              ins=[pbB[6]], outs=[pfmb])
            E("pe", lambda e: e.transpose(pbank(4)[0:1, 0:8], pfm[0:8, 0:1],
                                          identf[0:8, 0:8]),
              ins=[pfmb, CB], outs=[pbB[4]])
            E("act", lambda e: e.copy(pfm[0:1, 8:16], pbank(4)[0:1, 0:8]),
              ins=[pbB[4]], outs=[pfmb])
            mm(5, onesf, (0, 1, 0, 128), CB, pfm, (0, 1, 8, 16), pfmb, True, True, nlen=8)
            g1t, g1tb = A.alloc((128, 8), F32, "l3g1")
            g2t, g2tb = A.alloc((128, 8), F32, "l3g2")
            gfm3, gfm3b = A.alloc((128, 128), BF16, "l3gf")
            for tt in range(16):
                E("dve", lambda e, tt=tt: e.tensor_tensor(
                    g1t[:], ptm3[:, tt * 8:tt * 8 + 8], pbank(5, 0, 8), OP.is_ge),
                  ins=[ptm3b, pbB[5]], outs=[g1tb])
                E("dve", lambda e, tt=tt: e.tensor_tensor(
                    g2t[:], ptm3[:, tt * 8:tt * 8 + 8], g1t[:], OP.mult),
                  ins=[ptm3b, g1tb], outs=[g2tb])
                E("dve", lambda e: e.tensor_scalar(g1t[:], g1t[:], 1e30, -1e30,
                                                   OP.mult, OP.add),
                  ins=[g1tb], outs=[g1tb])
                E("dve", lambda e: e.tensor_tensor(g1t[:], g2t[:], g1t[:], OP.add),
                  ins=[g2tb], outs=[g1tb])
                E("dve", lambda e: e.reduce_max(s3[:, 9:10], g1t[:], axis=AX),
                  ins=[g1tb], outs=[s3b], hz=True)
                E("dve", lambda e: e.tensor_scalar(s3[:, 10:11], s3[:, 9:10], -1e29,
                                                   None, OP.is_gt),
                  ins=[s3b], outs=[s3b])
                E("dve", lambda e, tt=tt: e.reduce_max(s3[:, 11:12],
                                                       ptm3[:, tt * 8:tt * 8 + 8],
                                                       axis=AX),
                  ins=[ptm3b], outs=[s3b], hz=True)
                E("dve", lambda e: e.tensor_tensor(s3[:, 12:13], s3[:, 11:12],
                                                   s3[:, 9:10], OP.subtract),
                  ins=[s3b], outs=[s3b])
                E("dve", lambda e: e.tensor_tensor(s3[:, 12:13], s3[:, 12:13],
                                                   s3[:, 10:11], OP.mult),
                  ins=[s3b], outs=[s3b])
                E("dve", lambda e: e.tensor_tensor(s3[:, 12:13], s3[:, 11:12],
                                                   s3[:, 12:13], OP.subtract),
                  ins=[s3b], outs=[s3b])
                E("dve", lambda e, tt=tt: e.tensor_tensor(
                    g2t[:], g1t[:], ptm3[:, tt * 8:tt * 8 + 8], OP.subtract),
                  ins=[g1tb, ptm3b], outs=[g2tb])
                E("dve", lambda e: e.tensor_single_scalar(g2t[:], g2t[:], s3[:, 10:11],
                                                          OP.mult),
                  ins=[s3b], outs=[g2tb])
                E("dve", lambda e, tt=tt: e.tensor_tensor(
                    g2t[:], g2t[:], ptm3[:, tt * 8:tt * 8 + 8], OP.add),
                  ins=[ptm3b], outs=[g2tb])
                E("dve", lambda e: e.tensor_single_scalar(g2t[:], g2t[:], s3[:, 12:13],
                                                          OP.is_ge),
                  ins=[s3b], outs=[g2tb])
                E("dve", lambda e, tt=tt: e.tensor_tensor(
                    g2t[:], g2t[:], ptm3[:, tt * 8:tt * 8 + 8], OP.mult),
                  ins=[ptm3b], outs=[g2tb])
                E("pe", lambda e: e.transpose(pbank(4)[0:8, 0:128], g2t[:, 0:8],
                                              identf[:]),
                  ins=[g2tb, CB], outs=[pbB[4]])
                E("act", lambda e, tt=tt: e.copy(gfm3[0:8, 0:128], pbank(4)[0:8, 0:128]),
                  ins=[pbB[4]], outs=[gfm3b])
                E("sp", lambda e, tt=tt: e.dma_start(
                    out=gates_sp.ap()[0:8, tt * 128:(tt + 1) * 128], in_=gfm3[0:8, 0:128]),
                  ins=[gfm3b], outs=[gD], inc=16)
            A.release(m0)
            m0 = A.mark()
            w1e, w1eb = A.alloc((128, 8 * 2048), BF16, "l3w1e")
            w2e, w2eb = A.alloc((128, 16 * 1024), BF16, "l3w2e")
            hech, hechb = A.alloc((128, 16 * 512), BF16, "l3hech")
            grow, growb = A.alloc((1, T), BF16, "l3grow")
            cve, cveb = A.alloc((128, 512), F32, "l3cve")
            for e_ in range(8):
                for i in range(PT):
                    E("gp", lambda e, e_=e_, i=i: e.dma_start(
                        out=w1e[:, i * 2048:(i + 1) * 2048],
                        in_=di["l3f_W1"].ap()[e_, i * 128:(i + 1) * 128, :]),
                      outs=[w1eb], inc=16)
                for i in range(16):
                    E("gp", lambda e, e_=e_, i=i: e.dma_start(
                        out=w2e[:, i * 1024:(i + 1) * 1024],
                        in_=di["l3f_W2"].ap()[e_, i * 128:(i + 1) * 128, :]),
                      outs=[w2eb], inc=16)
                E("gp", lambda e, e_=e_: e.dma_start(
                    out=grow[:], in_=gates_sp.ap()[e_:e_ + 1, :]),
                  ins=[gD], outs=[growb], inc=16)
                for ch in range(4):
                    c0 = ch * 512
                    for mt in range(16):
                        bank = mt % 4
                        for kk in range(PT):
                            mm(bank, w1e, (0, 128, kk * 2048 + mt * 128,
                                           kk * 2048 + mt * 128 + 128), w1eb,
                               XN, (0, 128, kk * T + c0, kk * T + c0 + 512), XNb[kk],
                               start=(kk == 0), stop=(kk == PT - 1))
                        E("act", lambda e, bank=bank, mt=mt: e.activation(
                            hech[:, mt * 512:(mt + 1) * 512], pbank(bank), AF.Gelu),
                          ins=[pbB[bank]], outs=[hechb])
                    mm(4, ones16, (0, 1, 0, 128), CB, grow, (0, 1, c0, c0 + 512), growb,
                       True, True)
                    for mt in range(8):
                        bank = 2 + mt % 2
                        for kk in range(16):
                            mm(bank, w2e, (0, 128, kk * 1024 + mt * 128,
                                           kk * 1024 + mt * 128 + 128), w2eb,
                               hech, (0, 128, kk * 512, kk * 512 + 512), hechb,
                               start=(kk == 0), stop=(kk == 15))
                        E("act", lambda e, bank=bank: e.copy(cve[:], pbank(bank)),
                          ins=[pbB[bank]], outs=[cveb])
                        E("dve", lambda e: e.tensor_tensor(cve[:], cve[:], pbank(4),
                                                           OP.mult),
                          ins=[cveb, pbB[4]], outs=[cveb], hz=False)
                        E("dve", lambda e, mt=mt, c0=c0: e.tensor_tensor(
                            X[:, mt * T + c0:mt * T + c0 + 512],
                            X[:, mt * T + c0:mt * T + c0 + 512], cve[:], OP.add),
                          ins=[cveb], outs=[Xb[mt]], hz=False)
            A.release(m0)

        try:
            l0()
            if nlayers >= 2:
                l1()
            if nlayers >= 3:
                l2()
            if nlayers >= 4:
                l3()
        except _SkipRest:
            pass

        fin = Buf()
        for k in range(PT):
            E("sp", lambda e, k=k: e.dma_start(
                out=out_d.ap()[k * 128:(k + 1) * 128, :], in_=X[:, k * T:(k + 1) * T]),
              ins=[Xb[k]], outs=[fin], inc=16)
        nc.sync.wait_ge(P.sem["sp"], P.cnt["sp"])
        nc.vector.wait_ge(P.sem["sp"], P.cnt["sp"])

    return nc


def _prep_inputs(inputs, nlayers):
    ct = host_consts()
    w = {}
    w.update({k: np.ascontiguousarray(v) for k, v in ct.items()})
    f = lambda a: np.ascontiguousarray(np.asarray(a, dtype=np.float32))
    tr = lambda a: np.ascontiguousarray(np.asarray(a, dtype=np.float32).T)
    w["ipw1T"] = tr(inputs["L0_attn_in_proj_w"][:2048])
    w["ipw2T"] = tr(inputs["L0_attn_in_proj_w"][2048:])
    w["conv_w"] = f(inputs["L0_attn_conv_w"])
    w["conv_b"] = f(inputs["L0_attn_conv_b"])
    w["dt_wT"] = tr(inputs["L0_attn_dt_w"])
    w["dt_b"] = f(inputs["L0_attn_dt_b"])
    w["A_log"] = f(inputs["L0_attn_A_log"])
    w["B_wT"] = tr(inputs["L0_attn_B_w"])
    w["C_wT"] = tr(inputs["L0_attn_C_w"])
    w["out_wT"] = tr(inputs["L0_attn_out_w"])
    w["WrT"] = tr(inputs["L0_ffn_W_r"])
    w["WkT"] = tr(inputs["L0_ffn_W_k"])
    w["WvT"] = tr(inputs["L0_ffn_W_v"])
    w["mu_r"] = f(inputs["L0_ffn_mu_r"])
    w["mu_k"] = f(inputs["L0_ffn_mu_k"])
    if nlayers >= 2:
        W1 = np.asarray(inputs["L1_attn_W1"], np.float32)
        w["l1_wqT"] = tr(W1[0:1024]); w["l1_wkT"] = tr(W1[1024:2048]); w["l1_wvT"] = tr(W1[2048:3072])
        w["l1_w2T"] = tr(inputs["L1_attn_W2"])
        w["l1f_w1T"] = tr(inputs["L1_ffn_w1"]); w["l1f_b1"] = f(inputs["L1_ffn_b1"])
        w["l1f_w2T"] = tr(inputs["L1_ffn_w2"]); w["l1f_b2"] = f(inputs["L1_ffn_b2"])
    if nlayers >= 3:
        qkv = np.asarray(inputs["L2_attn_qkv_w"], np.float32)
        qkb = np.asarray(inputs["L2_attn_qkv_b"], np.float32)
        perm = np.concatenate([h * 64 + np.concatenate([np.arange(0, 64, 2),
                                                        np.arange(1, 64, 2)])
                               for h in range(16)])
        w["l2_wqT"] = tr(qkv[0:1024][perm])
        w["l2_wkT"] = tr(qkv[1024:2048][perm])
        w["l2_wvT"] = tr(qkv[2048:3072])
        w["l2_bq"] = f(qkb[0:1024][perm]); w["l2_bk"] = f(qkb[1024:2048][perm])
        w["l2_bv"] = f(qkb[2048:3072])
        w["l2_woT"] = tr(inputs["L2_attn_out_w"]); w["l2_bo"] = f(inputs["L2_attn_out_b"])
        w["l2f_ws1"] = f(inputs["L2_ffn_ws1"]); w["l2f_bs1"] = f(inputs["L2_ffn_bs1"])
        w["l2f_ws2"] = f(inputs["L2_ffn_ws2"]); w["l2f_bs2"] = f(inputs["L2_ffn_bs2"])
        w["l2f_We1"] = f(inputs["L2_ffn_We1"]); w["l2f_Be1"] = f(inputs["L2_ffn_Be1"])
        w["l2f_We2"] = f(inputs["L2_ffn_We2"]); w["l2f_Be2"] = f(inputs["L2_ffn_Be2"])
        w["l2f_Wr"] = f(inputs["L2_ffn_Wr"]); w["l2f_br"] = f(inputs["L2_ffn_br"])
    if nlayers >= 4:
        w["l3_g1"] = f(inputs["L3_attn_g1"]); w["l3_b1"] = f(inputs["L3_attn_b1"])
        w["l3_Wqkv_a"] = f(inputs["L3_attn_Wqkv_a"]); w["l3_bqkv_a"] = f(inputs["L3_attn_bqkv_a"])
        w["l3_Wp_a"] = f(inputs["L3_attn_Wp_a"]); w["l3_bp_a"] = f(inputs["L3_attn_bp_a"])
        w["l3_Wqkv_s"] = f(inputs["L3_attn_Wqkv_s"]); w["l3_bqkv_s"] = f(inputs["L3_attn_bqkv_s"])
        w["l3_Wo_s"] = f(inputs["L3_attn_Wo_s"]); w["l3_bo_s"] = f(inputs["L3_attn_bo_s"])
        w["l3_Wo"] = f(inputs["L3_attn_Wo"]); w["l3_bo"] = f(inputs["L3_attn_bo"])
        w["l3_gm"] = f(inputs["L3_attn_gm"]); w["l3_bm"] = f(inputs["L3_attn_bm"])
        w["l3_W1"] = f(inputs["L3_attn_W1"]); w["l3_b1m"] = f(inputs["L3_attn_b1m"])
        w["l3_W2"] = f(inputs["L3_attn_W2"]); w["l3_b2m"] = f(inputs["L3_attn_b2m"])
        w["l3f_Wg"] = f(inputs["L3_ffn_Wg"])
        w["l3f_W1"] = f(inputs["L3_ffn_W1"]); w["l3f_W2"] = f(inputs["L3_ffn_W2"])
    return w


def kernel(**inputs):
    nlayers = N_LAYERS
    w = _prep_inputs(inputs, nlayers)
    x = np.asarray(inputs["L0_attn_x"], np.float32)
    nc = build_program(nlayers)
    in_maps = []
    for c in range(8):
        m = dict(w)
        m["x_fm"] = np.ascontiguousarray(x[c].T)
        in_maps.append(m)
    res = run_bass_kernel_spmd(nc, in_maps, core_ids=list(range(8)))
    global _LAST_RES
    _LAST_RES = res
    out = np.stack([np.ascontiguousarray(res.results[c]["out_fm"].T) for c in range(8)])
    return out
